# version 01
# baseline (speedup 1.0000x reference)
import math
import numpy as np
import ml_dtypes
import concourse.bass as bass
import concourse.mybir as mybir
from concourse.alu_op_type import AluOpType
from concourse.bass_utils import run_bass_kernel_spmd

F32 = mybir.dt.float32
BF16 = mybir.dt.bfloat16
AF = mybir.ActivationFunctionType
AX = mybir.AxisListType

ENGINES = ("pe", "act", "dve", "pool", "sp")
DMA_QUEUES = ("sp", "act", "pool")
DMA_LANES = 6

LAMBDA_INIT = 0.8 - 0.6 * math.exp(-0.3 * 1.0)



class Region:
    __slots__ = ("name",)

    def __init__(self, name=""):
        self.name = name

    def __repr__(self):
        return f"R({self.name})"


class Step:
    __slots__ = ("eng", "fn", "reads", "writes", "idx", "is_dma", "needs_inc",
                 "waits", "tag", "lane")

    def __init__(self, eng, fn, reads, writes, idx, is_dma, tag=""):
        self.eng = eng
        self.fn = fn
        self.reads = reads
        self.writes = writes
        self.idx = idx
        self.is_dma = is_dma
        self.needs_inc = False
        self.waits = []
        self.tag = tag


class Prog:
    def __init__(self, nc):
        self.nc = nc
        self.steps = {e: [] for e in ENGINES}
        self.last_writer = {}
        self.readers = {}
        self.deps = []
        self._realized = False

    def _record(self, step):
        deps = {}
        for r in step.reads:
            w = self.last_writer.get(r)
            if w is not None and w is not step:
                deps[id(w)] = w
        for r in step.writes:
            w = self.last_writer.get(r)
            if w is not None and w is not step:
                deps[id(w)] = w
            for rd in self.readers.get(r, ()):
                if rd is not step:
                    deps[id(rd)] = rd
        for p in deps.values():
            self.deps.append((step, p))
        for r in step.reads:
            self.readers.setdefault(r, []).append(step)
        for r in step.writes:
            self.last_writer[r] = step
            self.readers[r] = []
        self.steps[step.eng].append(step)
        return step

    def step(self, eng, fn, reads=(), writes=(), tag=""):
        assert eng in ENGINES and not self._realized
        s = Step(eng, fn, list(reads), list(writes), len(self.steps[eng]), False, tag)
        return self._record(s)

    def dma(self, queue, out_ap, in_ap, reads=(), writes=(), tag="", **kw):
        assert queue in DMA_QUEUES
        fn = lambda e: e.dma_start(out_ap, in_ap, **kw)
        s = Step(queue, fn, list(reads), list(writes), len(self.steps[queue]), True, tag)
        return self._record(s)

    def finish(self, regions):
        s = Step("sp", None, list(regions), [], len(self.steps["sp"]), False, "finish")
        return self._record(s)

    def realize(self, block, sems):
        assert not self._realized
        self._realized = True
        for (c, p) in self.deps:
            if c.eng == "pe" and p.eng == "pe":
                continue
            p.needs_inc = True
        comp_sem = {}
        comp_val = {}
        for e in ENGINES:
            lst = self.steps[e]
            lane_cnt = [0] * DMA_LANES
            ndma = 0
            ninc = 0
            for s in lst:
                if s.is_dma:
                    lane = ndma % DMA_LANES
                    s.lane = lane
                    lane_cnt[lane] += 1
                    ndma += 1
                    comp_sem[id(s)] = ("dma", e, lane)
                    comp_val[id(s)] = 16 * lane_cnt[lane]
                    if lane_cnt[lane] >= 2:
                        s.waits.append((("dma", e, lane), 16 * (lane_cnt[lane] - 1)))
                else:
                    comp_sem[id(s)] = ("cmp", e)
                    if s.needs_inc:
                        ninc += 1
                        comp_val[id(s)] = ninc
                    else:
                        comp_val[id(s)] = None
        for (c, p) in self.deps:
            if c.eng == "pe" and p.eng == "pe":
                continue
            sk = comp_sem[id(p)]
            v = comp_val[id(p)]
            assert v is not None, f"dep on non-covered step {p.eng}:{p.idx} {p.tag}"
            c.waits.append((sk, v))

        prog_self = self
        eng_attr = {"pe": "tensor", "act": "scalar", "dve": "vector",
                    "pool": "gpsimd", "sp": "sync"}

        def make_section(e):
            def section(engobj):
                lastwait = {}
                for s in prog_self.steps[e]:
                    for (sk, v) in sorted(set(s.waits), key=str):
                        if lastwait.get(sk, -1) >= v:
                            continue
                        if sk[0] == "dma":
                            sem = sems[sk[1]][1][sk[2]]
                        else:
                            sem = sems[sk[1]][0]
                        engobj.wait_ge(sem, v)
                        lastwait[sk] = v
                    if s.fn is None:
                        continue
                    inst = s.fn(engobj)
                    if s.is_dma:
                        inst.then_inc(sems[e][1][s.lane], 16)
                    elif s.needs_inc:
                        inst.then_inc(sems[e][0], 1)
            return section

        for e in ENGINES:
            if not self.steps[e]:
                continue
            getattr(block, eng_attr[e])(make_section(e))

    def stats(self):
        n = {e: len(self.steps[e]) for e in ENGINES}
        nw = sum(len(s.waits) for e in ENGINES for s in self.steps[e])
        ni = sum(1 for e in ENGINES for s in self.steps[e] if s.needs_inc or s.is_dma)
        return f"steps={n} waits={nw} incs={ni} deps={len(self.deps)}"


def make_sems(nc, es):
    sems = {}
    for e in ENGINES:
        cmp_s = es.enter_context(nc.semaphore(f"c_{e}"))
        dma_s = ([es.enter_context(nc.semaphore(f"d_{e}{i}")) for i in range(DMA_LANES)]
                 if e in DMA_QUEUES else None)
        sems[e] = (cmp_s, dma_s)
    return sems



class Buf:

    def __init__(self, nc, name, nt, tw, dtype):
        self.name = name
        self.nt = nt
        self.tw = tw
        self.dtype = dtype
        self.t = nc.alloc_sbuf_tensor(name, [128, nt * tw], dtype)
        self.regs = [Region(f"{name}{i}") for i in range(nt)]

    def tile(self, i, a=0, b=None, p0=0, p1=128):
        b = self.tw if b is None else b
        return self.t[p0:p1, i * self.tw + a: i * self.tw + b]

    def R(self, i):
        return self.regs[i]

    def raw(self, p0, p1, c0, c1):
        return self.t[p0:p1, c0:c1]

    def allR(self):
        return list(self.regs)


class PsumPool:

    def __init__(self, nc):
        self.banks = [nc.alloc_psum_tensor(f"psb{i}", [128, 512], F32) for i in range(8)]
        self.regs = [Region(f"psb{i}") for i in range(8)]
        self.rr = 0
        self.held = set()

    def take(self, hold=False):
        for k in range(8):
            i = (self.rr + k) % 8
            if i not in self.held:
                self.rr = (i + 1) % 8
                if hold:
                    self.held.add(i)
                return i
        raise RuntimeError("all psum banks held")

    def release(self, i):
        self.held.discard(i)

    def ap(self, i, p0=0, p1=128, a=0, b=512):
        return self.banks[i][p0:p1, a:b]

    def bf16(self, i, p0=0, p1=128, a=0, b=1024):
        return self.banks[i].ap().bitcast(BF16)[p0:p1, a:b]

    def R(self, i):
        return self.regs[i]



class KC:

    def __init__(self, nc, p, cfg):
        self.nc = nc
        self.p = p
        self.cfg = cfg
        self.bufs = {}
        self.dram = {}
        self.dram_regs = {}

    def dreg(self, name):
        if name not in self.dram_regs:
            self.dram_regs[name] = Region(f"D:{name}")
        return self.dram_regs[name]


def ceil_div(a, b):
    return (a + b - 1) // b



def build_constants(T):
    c = {}
    c["c_ident"] = np.eye(128, dtype=ml_dtypes.bfloat16)
    c["c_ones1x128"] = np.ones((1, 128), np.float32)
    c["c_onescol"] = np.ones((128, 1), ml_dtypes.bfloat16)
    dh = 64
    inv_freq = 1.0 / 10000.0 ** (np.arange(0, dh, 2, dtype=np.float64) / dh)
    t = np.arange(T, dtype=np.float64)
    fr = t[None, :] * inv_freq[:, None]
    cos2 = np.repeat(np.cos(fr), 2, axis=0)
    sin2 = np.repeat(np.sin(fr), 2, axis=0)
    c["c_ropec"] = np.concatenate([cos2, cos2], 0).astype(np.float32)
    c["c_ropes"] = np.concatenate([sin2, sin2], 0).astype(np.float32)
    R = np.zeros((64, 64), np.float32)
    for p_ in range(32):
        R[2 * p_, 2 * p_ + 1] = -1.0
        R[2 * p_ + 1, 2 * p_] = 1.0
    Rh = np.zeros((64, 64), np.float32)
    for d in range(32):
        Rh[d, d + 32] = -1.0
        Rh[d + 32, d] = 1.0
    pf = np.zeros((128, 128), np.float32)
    pf[0:64, 0:64] = R.T
    pf[64:128, 64:128] = R.T
    c["c_permf"] = pf.astype(ml_dtypes.bfloat16)
    ph = np.zeros((128, 128), np.float32)
    ph[0:64, 0:64] = Rh.T
    ph[64:128, 64:128] = Rh.T
    c["c_permh"] = ph.astype(ml_dtypes.bfloat16)
    half = np.arange(0, dh, 2, dtype=np.float64)
    invf = 1.0 / 10000.0 ** (half / dh)
    fr2 = t[None, :] * invf[:, None]
    freqs = np.concatenate([fr2, fr2], 0)
    base_scale = (half + 0.4 * dh) / (1.4 * dh)
    power = (t - T // 2) / (256 // 2)
    scale = base_scale[:, None] ** power[None, :]
    scale = np.concatenate([scale, scale], 0)
    def _st(a):
        return np.concatenate([a, a], 0).astype(ml_dtypes.bfloat16)
    c["c_xposcs"] = _st(np.cos(freqs) * scale)
    c["c_xposss"] = _st(np.sin(freqs) * scale)
    c["c_xposck"] = _st(np.cos(freqs) / scale)
    c["c_xpossk"] = _st(np.sin(freqs) / scale)
    masks = []
    for r in (0, 128, 256, 384):
        kg = r + np.arange(128)[:, None]
        qg = np.arange(512)[None, :]
        masks.append((kg <= qg).astype(np.float32))
    c["c_cmask"] = np.stack(masks).astype(ml_dtypes.bfloat16)
    lm = []
    for a0 in (0, 128):
        a = a0 + np.arange(128)[:, None]
        bcol = np.arange(512)[None, :]
        ok = (a <= bcol) & (bcol <= a + 256)
        lm.append(np.where(ok, 0.0, -3.0e38).astype(np.float32))
    c["c_lmask"] = np.stack(lm)
    lm0 = []
    for a0 in (0, 128):
        a = a0 + np.arange(128)[:, None]
        bcol = np.arange(512)[None, :]
        ok = (a <= bcol) & (bcol <= a + 256) & (bcol >= 256)
        lm0.append(np.where(ok, 0.0, -3.0e38).astype(np.float32))
    c["c_lmask0"] = np.stack(lm0)
    c["c_iota8"] = np.broadcast_to(np.arange(8, dtype=np.float32), (128, 8)).copy()
    e2 = np.zeros((128, 2), ml_dtypes.bfloat16)
    e2[0:64, 0] = 1
    e2[64:128, 1] = 1
    c["c_e2"] = e2
    c["c_ones64"] = np.ones((64, 1), ml_dtypes.bfloat16)
    return c



def dma_col_f32(K, dst_ap, dst_regs, src_ap, src_reg, tag="col"):
    K.p.dma("sp", dst_ap, src_ap, reads=[src_reg], writes=list(dst_regs), tag=tag)


class WStream:

    def __init__(self, K, name, w_ap, w_reg, n_in, n_out, trans,
                 ring_buf, wtmp=None, mg=2, mw=128, col0=0, k0=0):
        self.K = K
        self.name = name
        self.w_ap = w_ap
        self.w_reg = w_reg
        self.n_in = n_in
        self.n_out = n_out
        self.trans = trans
        self.KT = ceil_div(n_in, 128)
        self.mg = mg
        self.mw = mw
        self.col0 = col0
        self.k0 = k0
        self.ew = mg * self.KT * mw
        self.ring_buf = ring_buf
        self.nring = ring_buf.nt * ring_buf.tw // self.ew
        assert self.nring >= 2, f"{name}: ring too small"
        self.wtmp = wtmp
        self.n_mt = ceil_div(n_out, mw)
        self.n_groups = ceil_div(self.n_mt, mg)
        self.slot_regs = [Region(f"{name}s{i}") for i in range(self.nring)]
        self._first = True

    def entry_ap(self, slot, m_local, kt, ksz=128, msz=None):
        msz = self.mw if msz is None else msz
        off = slot * self.ew + (m_local * self.KT + kt) * self.mw
        return self.ring_buf.raw(0, ksz, off, off + msz)

    def stage_group(self, g, seq=None):
        K = self.K
        p = K.p
        slot = (g if seq is None else seq) % self.nring
        m0 = g * self.mg
        nmt = min(self.mg, self.n_mt - m0)
        ent_reg = self.slot_regs[slot]
        claims = []
        legacy = getattr(self.ring_buf, "_war", None)
        if legacy:
            self.ring_buf._war = None
            claims += legacy
        if self._first:
            self._first = False
            prev = getattr(self.ring_buf, "_ws_owner", None)
            if prev is not None and prev is not self.slot_regs:
                claims += prev
            self.ring_buf._ws_owner = self.slot_regs
        if claims:
            self.K.p.step("pool", lambda e: e.nop(),
                          reads=[], writes=list(self.slot_regs) + claims,
                          tag=self.name + "war")
        KT, n_in, n_out, mw = self.KT, self.n_in, self.n_out, self.mw
        if not self.trans:
            for m in range(nmt):
                mj = min(mw, n_out - (m0 + m) * mw)
                for kt in range(KT):
                    ksz = min(128, n_in - kt * 128)
                    c0 = self.col0 + (m0 + m) * mw
                    src = self.w_ap[self.k0 + kt * 128: self.k0 + kt * 128 + ksz,
                                    c0:c0 + mj]
                    dst = self.entry_ap(slot, m, kt, ksz, mj)
                    p.dma("pool", dst, src, reads=[self.w_reg],
                          writes=[ent_reg], tag=f"{self.name}g{g}")
        else:
            assert mw == 128
            for m in range(nmt):
                mj = min(mw, n_out - (m0 + m) * mw)
                tslot = (m0 + m) % self.wtmp.nt
                treg = self.wtmp.R(tslot)
                r0 = self.col0 + (m0 + m) * mw
                src = self.w_ap[r0:r0 + mj, self.k0:self.k0 + n_in]
                p.dma("pool", self.wtmp.tile(tslot, 0, n_in, 0, mj), src,
                      reads=[self.w_reg], writes=[treg], tag=f"{self.name}g{g}t")
                for kt in range(KT):
                    ksz = min(128, n_in - kt * 128)
                    bank = K.ps.take()
                    pr = K.ps.R(bank)
                    src_t = self.wtmp.tile(tslot, kt * 128, kt * 128 + ksz, 0, mj)
                    dst_ps = K.ps.bf16(bank, 0, ksz, 0, mj)
                    dst_sb = self.entry_ap(slot, m, kt, ksz, mj)
                    p.step("pe", lambda e, s=src_t, d=dst_ps, i=K.cbuf["ident"]:
                           e.transpose(d, s, i),
                           reads=[treg, K.creg["ident"]], writes=[pr],
                           tag=f"{self.name}tp{g}")
                    p.step("dve", lambda e, s=dst_ps, d=dst_sb: e.tensor_copy(d, s),
                           reads=[pr], writes=[ent_reg], tag=f"{self.name}tc{g}")
        return slot, ent_reg




class View:

    def __init__(self, arena, name, offs, nt, tw, dtype):
        self.arena = arena
        self.name = name
        self.offs = offs
        self.nt = nt
        self.tw = tw
        self.dtype = dtype
        self.mult = 2 if dtype == F32 else 1
        self.regs = [Region(f"{name}{i}") for i in range(nt)]
        self._war = None

    def tile(self, i, a=0, b=None, p0=0, p1=128):
        b = self.tw if b is None else b
        base = self.arena.t.ap() if self.dtype == BF16 else self.arena.t.ap().bitcast(F32)
        o = self.offs[i] // self.mult
        return base[p0:p1, o + a: o + b]

    def R(self, i):
        return self.regs[i]

    def wr(self, i):
        if self._war:
            out = [self.regs[i]] + self._war
            self._war = None
            return out
        return [self.regs[i]]

    def raw(self, p0, p1, c0, c1):
        assert self.dtype == BF16
        base = self.arena.t.ap()
        return base[p0:p1, self.offs[0] + c0: self.offs[0] + c1]

    def spans(self):
        return [(o, o + self.tw * self.mult) for o in self.offs]

    def allR(self):
        return list(self.regs)


class Arena:
    def __init__(self, nc, name, cols):
        self.t = nc.alloc_sbuf_tensor(name, [128, cols], BF16)
        self.cols = cols
        self.views = {}
        self.init_reg = Region(name + "_init")

    def memset_all(self, p):
        for c0 in range(0, self.cols, 32768):
            c1 = min(c0 + 32768, self.cols)
            p.step("dve", lambda e, a=c0, b=c1: e.memset(self.t[0:128, a:b], 0.0),
                   reads=[], writes=[self.init_reg], tag="ainit")

    def carve(self, keep, new):
        legacy = []
        for v in list(self.views.values()):
            if v not in keep:
                legacy.extend(v.regs)
                legacy.extend(getattr(v, "_ws_owner", None) or [])
        self.views = {v.name: v for v in keep}
        used = sorted(s for v in keep for s in v.spans())
        out = {}

        def alloc(sz):
            off = 0
            while True:
                hit = None
                for (a, b) in used:
                    if off < b and off + sz > a:
                        hit = b
                        break
                if hit is None:
                    break
                off = hit
            assert off + sz <= self.cols, \
                f"arena overflow: need {sz} at {off}, cols={self.cols}"
            used.append((off, off + sz))
            used.sort()
            return off
        for spec in new:
            (name, nt, tw, dtype) = spec[:4]
            contig = name.startswith("wring") or name.startswith("wqs") \
                or name.startswith("wos") or (len(spec) > 4 and spec[4])
            mult = 2 if dtype == F32 else 1
            if contig:
                base = alloc(nt * tw * mult)
                offs = [base + i * tw * mult for i in range(nt)]
            else:
                offs = [alloc(tw * mult) for _ in range(nt)]
            v = View(self, name, offs, nt, tw, dtype)
            v._war = list(legacy) + [self.init_reg]
            self.views[name] = v
            out[name] = v
        return out



def linear(K, ws, src_fn, ep, tok0, tok1, chunk=512, tag="lin", seq0=0):
    p = K.p
    chunks = [(t0, min(t0 + chunk, tok1)) for t0 in range(tok0, tok1, chunk)]
    staged = {0: ws.stage_group(0, seq=seq0)}
    for g in range(ws.n_groups):
        if g + 1 < ws.n_groups:
            staged[g + 1] = ws.stage_group(g + 1, seq=seq0 + g + 1)
        slot, ent_reg = staged.pop(g)
        nmt = min(ws.mg, ws.n_mt - g * ws.mg)
        for m in range(nmt):
            mi = g * ws.mg + m
            msz = min(ws.mw, ws.n_out - mi * ws.mw)
            for (t0, t1) in chunks:
                bank = K.ps.take()
                for kt in range(ws.KT):
                    ksz = min(128, ws.n_in - kt * 128)
                    rhs, rregs = src_fn(kt, ksz, t0, t1)
                    lhsT = ws.entry_ap(slot, m, kt, ksz, msz)
                    p.step("pe",
                           lambda e, o=K.ps.ap(bank, 0, msz, 0, t1 - t0), l=lhsT,
                           r=rhs, st=(kt == 0), sp=(kt == ws.KT - 1):
                           e.matmul(o, l, r, start=st, stop=sp),
                           reads=[ent_reg] + rregs, writes=[K.ps.R(bank)],
                           tag=f"{tag}m{mi}")
                ep(mi, msz, bank, t0, t1)


def src_from_view(view, T):
    def f(kt, ksz, t0, t1):
        return view.tile(kt, t0, t1, 0, ksz), [view.R(kt)]
    return f


def ep_to_view(K, view, dtype_scale=None, tag="ep"):
    def ep(mi, msz, bank, t0, t1):
        src = K.ps.ap(bank, 0, msz, 0, t1 - t0)
        dst = view.tile(mi, t0, t1, 0, msz)
        if dtype_scale is None:
            K.p.step("dve", lambda e, s=src, d=dst: e.tensor_copy(d, s),
                     reads=[K.ps.R(bank)], writes=view.wr(mi), tag=tag)
        else:
            K.p.step("act", lambda e, s=src, d=dst: e.mul(d, s, dtype_scale),
                     reads=[K.ps.R(bank)], writes=view.wr(mi), tag=tag)
    return ep


def bcast_row(K, row_ap, row_regs, dst_ap_fn, dst_wregs, t0, t1, nparts=128,
              tag="bc"):
    p = K.p
    for c0 in range(t0, t1, 512):
        c1 = min(c0 + 512, t1)
        bank = K.ps.take()
        p.step("pe", lambda e, o=K.ps.ap(bank, 0, nparts, 0, c1 - c0),
               l=K.cbuf["ones1x128"][0:1, 0:nparts], r=row_ap[0:1, c0 - t0:c1 - t0]:
               e.matmul(o, l, r, start=True, stop=True),
               reads=[K.creg["ones1x128"]] + row_regs, writes=[K.ps.R(bank)],
               tag=tag)
        p.step("dve", lambda e, s=K.ps.ap(bank, 0, nparts, 0, c1 - c0),
               d=dst_ap_fn(c0 - t0, c1 - t0): e.tensor_copy(d, s),
               reads=[K.ps.R(bank)], writes=dst_wregs, tag=tag)


def colsum(K, src_fn, KT, row_ap, row_wregs, t0, t1, scale=None, tag="cs"):
    p = K.p
    for c0 in range(t0, t1, 512):
        c1 = min(c0 + 512, t1)
        bank = K.ps.take()
        for kt in range(KT):
            rhs, rregs, ksz = src_fn(kt, c0, c1)
            p.step("pe", lambda e, o=K.ps.ap(bank, 0, 1, 0, c1 - c0),
                   l=K.cbuf["onescol"][0:ksz, 0:1], r=rhs,
                   st=(kt == 0), sp=(kt == KT - 1):
                   e.matmul(o, l, r, start=st, stop=sp),
                   reads=[K.creg["onescol"]] + rregs, writes=[K.ps.R(bank)],
                   tag=tag)
        if scale is None:
            p.step("dve", lambda e, s=K.ps.ap(bank, 0, 1, 0, c1 - c0),
                   d=row_ap[0:1, c0 - t0:c1 - t0]: e.tensor_copy(d, s),
                   reads=[K.ps.R(bank)], writes=row_wregs, tag=tag)
        else:
            p.step("act", lambda e, s=K.ps.ap(bank, 0, 1, 0, c1 - c0),
                   d=row_ap[0:1, c0 - t0:c1 - t0]: e.mul(d, s, scale),
                   reads=[K.ps.R(bank)], writes=row_wregs, tag=tag)



def ln_emit(K, XT, XB, BC, XNT, T, tag="ln"):
    p = K.p
    RWA, RWB = K.rwa, K.rwb
    rA = K.rwa_reg
    rB = K.rwb_reg
    for kt in range(8):
        p.step("act", lambda e, s=XT.tile(kt), d=XB.tile(kt): e.copy(d, s),
               reads=[XT.R(kt)], writes=XB.wr(kt), tag=tag + "c")
    colsum(K, lambda kt, c0, c1: (XB.tile(kt, c0, c1), [XB.R(kt)], 128), 8,
           RWA[0:1, 0:T], [rA], 0, T, scale=1.0 / 1024.0, tag=tag + "mu")
    bcast_row(K, RWA[0:1, 0:T], [rA], lambda a, b: BC.tile(1, a, b), BC.wr(1),
              0, T, tag=tag + "b1")
    p.step("act", lambda e: e.activation(RWA[0:1, 0:T], RWA[0:1, 0:T],
                                         AF.Square),
           reads=[rA], writes=[rA], tag=tag)
    for kt in range(8):
        p.step("act", lambda e, s=XT.tile(kt), d=XB.tile(kt):
               e.activation(d, s, AF.Square),
               reads=[XT.R(kt)], writes=[XB.R(kt)], tag=tag + "sq")
    colsum(K, lambda kt, c0, c1: (XB.tile(kt, c0, c1), [XB.R(kt)], 128), 8,
           RWB[0:1, 0:T], [rB], 0, T, scale=1.0 / 1024.0, tag=tag + "s2")
    p.step("dve", lambda e: e.tensor_tensor(RWB[0:1, 0:T], RWB[0:1, 0:T],
                                            RWA[0:1, 0:T], op=AluOpType.subtract),
           reads=[rB, rA], writes=[rB], tag=tag)
    p.step("act", lambda e: e.activation(RWB[0:1, 0:T], RWB[0:1, 0:T],
                                         AF.Ln, bias=K.cbuf["misc"][0:1, 0:1]),
           reads=[rB, K.creg["misc"]], writes=[rB], tag=tag)
    p.step("act", lambda e: e.activation(RWB[0:1, 0:T], RWB[0:1, 0:T],
                                         AF.Exp, scale=-0.5),
           reads=[rB], writes=[rB], tag=tag)
    bcast_row(K, RWB[0:1, 0:T], [rB], lambda a, b: BC.tile(0, a, b), BC.wr(0),
              0, T, tag=tag + "b0")
    for kt in range(8):
        for c0 in range(0, T, 1024):
            c1 = min(c0 + 1024, T)
            p.step("dve", lambda e, kt=kt, c0=c0, c1=c1: e.tensor_tensor(
                K.xf[0:128, 0:c1 - c0], XT.tile(kt, c0, c1), BC.tile(1, c0, c1),
                op=AluOpType.subtract),
                reads=[XT.R(kt), BC.R(1)], writes=[K.xf_reg], tag=tag)
            p.step("dve", lambda e, kt=kt, c0=c0, c1=c1: e.tensor_tensor(
                XNT.tile(kt, c0, c1), K.xf[0:128, 0:c1 - c0], BC.tile(0, c0, c1),
                op=AluOpType.mult),
                reads=[K.xf_reg, BC.R(0)], writes=XNT.wr(kt), tag=tag)


def rms_emit(K, SRC, XB, BC, DST, T, gcols, gcol_reg, eps=1e-6, tag="rms",
             LO=None):
    p = K.p
    RWA, RWB = K.rwa, K.rwb
    rA, rB = K.rwa_reg, K.rwb_reg
    for kt in range(8):
        p.step("act", lambda e, s=SRC.tile(kt), d=XB.tile(kt):
               e.activation(d, s, AF.Square),
               reads=[SRC.R(kt)], writes=XB.wr(kt), tag=tag + "sq")
    colsum(K, lambda kt, c0, c1: (XB.tile(kt, c0, c1), [XB.R(kt)], 128), 8,
           RWB[0:1, 0:T], [rB], 0, T, scale=1.0 / 1024.0, tag=tag + "s2")
    epscol = {1e-5: 0, 1e-6: 1, 1e-12: 2}[eps]
    p.step("act", lambda e: e.activation(RWB[0:1, 0:T], RWB[0:1, 0:T],
                                         AF.Ln, bias=K.cbuf["misc"][0:1, epscol:epscol + 1]),
           reads=[rB, K.creg["misc"]], writes=[rB], tag=tag)
    p.step("act", lambda e: e.activation(RWB[0:1, 0:T], RWB[0:1, 0:T],
                                         AF.Exp, scale=-0.5),
           reads=[rB], writes=[rB], tag=tag)
    bcast_row(K, RWB[0:1, 0:T], [rB], lambda a, b: BC.tile(0, a, b), BC.wr(0),
              0, T, tag=tag + "b0")
    for kt in range(8):
        for c0 in range(0, T, 1024):
            c1 = min(c0 + 1024, T)
            W_ = c1 - c0
            p.step("dve", lambda e, kt=kt, c0=c0, c1=c1, W_=W_: e.tensor_tensor(
                K.xf[0:128, 0:W_], SRC.tile(kt, c0, c1), BC.tile(0, c0, c1),
                op=AluOpType.mult),
                reads=[SRC.R(kt), BC.R(0)], writes=[K.xf_reg], tag=tag)
            if LO is None:
                p.step("dve", lambda e, kt=kt, c0=c0, c1=c1, W_=W_,
                       g=gcols[kt]: e.tensor_scalar(
                    DST.tile(kt, c0, c1), K.xf[0:128, 0:W_], g, None,
                    op0=AluOpType.mult),
                    reads=[K.xf_reg, gcol_reg], writes=DST.wr(kt), tag=tag)
            else:
                p.step("dve", lambda e, kt=kt, W_=W_, g=gcols[kt]:
                       e.tensor_scalar(
                    K.xf[0:128, 0:W_], K.xf[0:128, 0:W_], g, None,
                    op0=AluOpType.mult),
                    reads=[K.xf_reg, gcol_reg], writes=[K.xf_reg], tag=tag)
                p.step("dve", lambda e, kt=kt, c0=c0, c1=c1, W_=W_:
                       e.tensor_copy(DST.tile(kt, c0, c1), K.xf[0:128, 0:W_]),
                    reads=[K.xf_reg], writes=DST.wr(kt), tag=tag)
                p.step("act", lambda e, kt=kt, c0=c0, c1=c1:
                       e.copy(BC.tile(1, c0, c1), DST.tile(kt, c0, c1)),
                    reads=[DST.R(kt)], writes=[BC.R(1)], tag=tag)
                p.step("dve", lambda e, kt=kt, c0=c0, c1=c1, W_=W_:
                       e.tensor_tensor(LO.tile(kt, c0, c1), K.xf[0:128, 0:W_],
                                       BC.tile(1, c0, c1),
                                       op=AluOpType.subtract),
                    reads=[K.xf_reg, BC.R(1)], writes=LO.wr(kt), tag=tag)



def load_xt(K, x_ap, x_reg, XT, XN, T, tag="ldx"):
    p = K.p
    nt = T // 128
    for tt in range(nt):
        sl = tt % XN.nt
        p.dma("sp", XN.tile(sl), x_ap[tt * 128:(tt + 1) * 128, 0:1024],
              reads=[x_reg], writes=XN.wr(sl), tag=tag)
        for kt in range(8):
            bank = K.ps.take()
            p.step("pe", lambda e, s=XN.tile(sl, kt * 128, (kt + 1) * 128),
                   d=K.ps.ap(bank, 0, 128, 0, 128), i=K.cbuf["identf"]:
                   e.transpose(d, s, i),
                   reads=[XN.R(sl), K.creg["identf"]], writes=[K.ps.R(bank)],
                   tag=tag)
            p.step("dve", lambda e, s=K.ps.ap(bank, 0, 128, 0, 128),
                   d=XT.tile(kt, tt * 128, (tt + 1) * 128): e.tensor_copy(d, s),
                   reads=[K.ps.R(bank)], writes=XT.wr(kt), tag=tag)


def store_xt(K, XT, out_ap, out_reg, XN, T, tag="stx"):
    p = K.p
    nt = T // 128
    for tt in range(nt):
        sl = tt % XN.nt
        for kt in range(8):
            bank = K.ps.take()
            p.step("pe", lambda e, s=XT.tile(kt, tt * 128, (tt + 1) * 128),
                   d=K.ps.ap(bank, 0, 128, 0, 128), i=K.cbuf["identf"]:
                   e.transpose(d, s, i),
                   reads=[XT.R(kt), K.creg["identf"]], writes=[K.ps.R(bank)],
                   tag=tag)
            p.step("dve", lambda e, s=K.ps.ap(bank, 0, 128, 0, 128),
                   d=XN.tile(sl, kt * 128, (kt + 1) * 128): e.tensor_copy(d, s),
                   reads=[K.ps.R(bank)], writes=XN.wr(sl), tag=tag)
        p.dma("sp", out_ap[tt * 128:(tt + 1) * 128, 0:1024], XN.tile(sl),
              reads=[XN.R(sl)], writes=[out_reg], tag=tag)


def l0a_maps():
    qmap = {}
    for h in range(32):
        qmap[h] = (h // 4, 64 * ((h % 4) // 2))
    kmap = {}
    for h in range(32):
        s = h % 4
        q_ = h // 4
        kmap[h] = (2 * q_ + (s % 2), (0, 32, 64, 96)[s])
    return qmap, kmap


def _l0a_bound(K, QT, KT, SCR, T, tag):
    p = K.p
    C = K.cols
    cr = K.cols_reg
    first = True
    for qt in range(8):
        for c0 in range(0, T, 256):
            p.step("act", lambda e, s=QT.tile(qt, c0, c0 + 256),
                   d=SCR.tile(0, 0, 256): e.activation(d, s, AF.Square),
                   reads=[QT.R(qt)], writes=SCR.wr(0), tag=tag + "bq")
            bank = K.ps.take()
            p.step("pe", lambda e, o=K.ps.ap(bank, 0, 4, 0, 256),
                   l=K.cbuf["e4"][0:128, 0:4], r=SCR.tile(0, 0, 256):
                   e.matmul(o, l, r, start=True, stop=True),
                   reads=[K.creg["e4"], SCR.R(0)], writes=[K.ps.R(bank)],
                   tag=tag + "bq")
            if first:
                p.step("dve", lambda e, s=K.ps.ap(bank, 0, 4, 0, 256):
                       e.reduce_max(C[0:4, 0:1], s, axis=AX.X),
                       reads=[K.ps.R(bank)], writes=[cr], tag=tag + "bq")
                first = False
            else:
                p.step("dve", lambda e, s=K.ps.ap(bank, 0, 4, 0, 256):
                       e.reduce_max(C[0:4, 1:2], s, axis=AX.X),
                       reads=[K.ps.R(bank)], writes=[cr], tag=tag + "bq")
                p.step("dve", lambda e: e.tensor_tensor(
                    C[0:4, 0:1], C[0:4, 0:1], C[0:4, 1:2], op=AluOpType.max),
                    reads=[cr], writes=[cr], tag=tag + "bq")
    first = True
    for kt_ in range(16):
        for c0 in range(0, T, 256):
            p.step("act", lambda e, s=KT.tile(kt_, c0, c0 + 256),
                   d=SCR.tile(0, 0, 256): e.activation(d, s, AF.Square),
                   reads=[KT.R(kt_)], writes=[SCR.R(0)], tag=tag + "bk")
            bank = K.ps.take()
            p.step("pe", lambda e, o=K.ps.ap(bank, 0, 2, 0, 256),
                   l=K.cbuf["e2"][0:128, 0:2], r=SCR.tile(0, 0, 256):
                   e.matmul(o, l, r, start=True, stop=True),
                   reads=[K.creg["e2"], SCR.R(0)], writes=[K.ps.R(bank)],
                   tag=tag + "bk")
            if first:
                p.step("dve", lambda e, s=K.ps.ap(bank, 0, 2, 0, 256):
                       e.reduce_max(C[0:2, 2:3], s, axis=AX.X),
                       reads=[K.ps.R(bank)], writes=[cr], tag=tag + "bk")
                first = False
            else:
                p.step("dve", lambda e, s=K.ps.ap(bank, 0, 2, 0, 256):
                       e.reduce_max(C[0:2, 3:4], s, axis=AX.X),
                       reads=[K.ps.R(bank)], writes=[cr], tag=tag + "bk")
                p.step("dve", lambda e: e.tensor_tensor(
                    C[0:2, 2:3], C[0:2, 2:3], C[0:2, 3:4], op=AluOpType.max),
                    reads=[cr], writes=[cr], tag=tag + "bk")
    bank = K.ps.take()
    p.step("pe", lambda e, o=K.ps.ap(bank, 0, 1, 0, 4):
           e.transpose(o, C[0:4, 0:1], K.cbuf["identf"][0:4, 0:4]),
           reads=[cr, K.creg["identf"]], writes=[K.ps.R(bank)], tag=tag + "bt")
    p.step("dve", lambda e, s=K.ps.ap(bank, 0, 1, 0, 4):
           e.reduce_max(C[0:1, 4:5], s, axis=AX.X),
           reads=[K.ps.R(bank)], writes=[cr], tag=tag + "bt")
    bank2 = K.ps.take()
    p.step("pe", lambda e, o=K.ps.ap(bank2, 0, 1, 0, 2):
           e.transpose(o, C[0:2, 2:3], K.cbuf["identf"][0:2, 0:2]),
           reads=[cr, K.creg["identf"]], writes=[K.ps.R(bank2)], tag=tag + "bt")
    p.step("dve", lambda e, s=K.ps.ap(bank2, 0, 1, 0, 2):
           e.reduce_max(C[0:1, 5:6], s, axis=AX.X),
           reads=[K.ps.R(bank2)], writes=[cr], tag=tag + "bt")
    p.step("dve", lambda e: e.tensor_tensor(C[0:1, 4:5], C[0:1, 4:5],
                                            C[0:1, 5:6], op=AluOpType.mult),
           reads=[cr], writes=[cr], tag=tag + "bt")
    p.step("act", lambda e: e.activation(C[0:1, 5:6], C[0:1, 4:5], AF.Ln),
           reads=[cr], writes=[cr], tag=tag + "bt")
    p.step("act", lambda e: e.activation(C[0:1, 4:5], C[0:1, 5:6], AF.Exp,
                                         scale=0.5),
           reads=[cr], writes=[cr], tag=tag + "bt")
    bank3 = K.ps.take()
    p.step("pe", lambda e, o=K.ps.ap(bank3, 0, 128, 0, 1):
           e.matmul(o, K.cbuf["ones1x128"][0:1, 0:128], C[0:1, 4:5],
                    start=True, stop=True),
           reads=[K.creg["ones1x128"], cr], writes=[K.ps.R(bank3)], tag=tag + "bt")
    p.step("dve", lambda e, s=K.ps.ap(bank3, 0, 128, 0, 1):
           e.tensor_scalar(K.negc[0:128, 0:1], s, -1.0, None, op0=AluOpType.mult),
           reads=[K.ps.R(bank3)], writes=[K.negc_reg], tag=tag + "bt")


def _l0a_combine(K, j, obk, OTc, tag):
    p = K.p
    C, cr = K.cols, K.cols_reg
    xf, xr = K.xf, K.xf_reg
    RWA, rA = K.rwa, K.rwa_reg
    b1, b2 = obk[0], obk[1]
    RWB, rB = K.rwb, K.rwb_reg
    p.step("dve", lambda e, s=K.ps.ap(b1, 64, 65, 0, 256):
           e.reciprocal(RWA[0:1, 0:256], s),
           reads=[K.ps.R(b1)], writes=[rA], tag=tag + "cb")
    p.step("dve", lambda e, s=K.ps.ap(b2, 64, 65, 0, 256):
           e.reciprocal(RWB[0:1, 0:256], s),
           reads=[K.ps.R(b2)], writes=[rB], tag=tag + "cb")
    p.step("dve", lambda e: e.tensor_scalar(
        RWB[0:1, 0:256], RWB[0:1, 0:256], K.lam[0:1, 0:1], None,
        op0=AluOpType.mult),
        reads=[rB, K.lam_reg], writes=[rB], tag=tag + "cb")
    for (rowap, rreg, cofs) in ((RWA, rA, 0), (RWB, rB, 256)):
        bank = K.ps.take()
        p.step("pe", lambda e, o=K.ps.ap(bank, 0, 64, 0, 256),
               l=K.cbuf["ones1x128"][0:1, 0:64], r=rowap[0:1, 0:256]:
               e.matmul(o, l, r, start=True, stop=True),
               reads=[K.creg["ones1x128"], rreg], writes=[K.ps.R(bank)],
               tag=tag + "cb")
        p.step("dve", lambda e, s=K.ps.ap(bank, 0, 64, 0, 256), c=cofs:
               e.tensor_copy(xf[0:64, c:c + 256], s),
               reads=[K.ps.R(bank)], writes=[xr], tag=tag + "cb")
    p.step("dve", lambda e, s=K.ps.ap(b1, 0, 64, 0, 256):
           e.tensor_tensor(xf[0:64, 512:768], s, xf[0:64, 0:256],
                           op=AluOpType.mult),
           reads=[K.ps.R(b1), xr], writes=[xr], tag=tag + "cb")
    p.step("dve", lambda e, s=K.ps.ap(b2, 0, 64, 0, 256):
           e.tensor_tensor(xf[0:64, 768:1024], s, xf[0:64, 256:512],
                           op=AluOpType.mult),
           reads=[K.ps.R(b2), xr], writes=[xr], tag=tag + "cb")
    p.step("dve", lambda e: e.tensor_tensor(
        xf[0:64, 512:768], xf[0:64, 512:768], xf[0:64, 768:1024],
        op=AluOpType.subtract),
        reads=[xr], writes=[xr], tag=tag + "cb")
    p.step("act", lambda e: e.activation(K.sq64[0:64, 0:256],
                                         xf[0:64, 512:768], AF.Square),
           reads=[xr], writes=[K.sq64_reg], tag=tag + "cb")
    bank = K.ps.take()
    p.step("pe", lambda e, o=K.ps.ap(bank, 0, 1, 0, 256):
           e.matmul(o, K.cbuf["ones64"][0:64, 0:1], K.sq64[0:64, 0:256],
                    start=True, stop=True),
           reads=[K.creg["ones64"], K.sq64_reg], writes=[K.ps.R(bank)],
           tag=tag + "cb")
    p.step("act", lambda e, s=K.ps.ap(bank, 0, 1, 0, 256):
           e.activation(RWA[0:1, 0:256], s, AF.Ln, scale=1.0 / 64.0,
                        bias=K.cbuf["misc"][0:1, 1:2]),
           reads=[K.ps.R(bank), K.creg["misc"]], writes=[rA], tag=tag + "cb")
    p.step("act", lambda e: e.activation(RWA[0:1, 0:256], RWA[0:1, 0:256],
                                         AF.Exp, scale=-0.5),
           reads=[rA], writes=[rA], tag=tag + "cb")
    bank2 = K.ps.take()
    p.step("pe", lambda e, o=K.ps.ap(bank2, 0, 64, 0, 256):
           e.matmul(o, K.cbuf["ones1x128"][0:1, 0:64], RWA[0:1, 0:256],
                    start=True, stop=True),
           reads=[K.creg["ones1x128"], rA], writes=[K.ps.R(bank2)], tag=tag + "cb")
    p.step("dve", lambda e, s=K.ps.ap(bank2, 0, 64, 0, 256):
           e.tensor_tensor(xf[0:64, 768:1024], xf[0:64, 512:768], s,
                           op=AluOpType.mult),
           reads=[K.ps.R(bank2), xr], writes=[xr], tag=tag + "cb")
    rb = 64 * (j % 2)
    p.step("dve", lambda e, d=OTc.tile(j // 2, 0, 256, rb, rb + 64):
           e.tensor_scalar(d, xf[0:64, 768:1024], K.rsc[0:64, 0:1], None,
                           op0=AluOpType.mult),
           reads=[xr, K.rsc_reg], writes=OTc.wr(j // 2), tag=tag + "cb")


def stage_l0_attn(K, XNT, T, W, ao_sp, xnt_sp, keeps=(), tag="l0a"):
    p = K.p
    qmap, kmap = l0a_maps()
    NKB = T // 128
    inv_sqrt_h = 1.0 / math.sqrt(32.0)
    keeps = list(keeps)

    V1 = K.arena.carve(keeps + [XNT], [
        ("wring", 4, 2048, BF16), ("wvs", 8, 1024, BF16),
        ("qt", 8, T, BF16), ("vn", NKB, 1040, BF16)])
    QT, VN = V1["qt"], V1["vn"]
    wsq = WStream(K, tag + "q", W["wqkv"][0], W["wqkv"][1], 1024, 1024, False,
                  V1["wring"], mg=2)
    linear(K, wsq, src_from_view(XNT, T),
           ep_to_view(K, QT, dtype_scale=inv_sqrt_h, tag=tag + "qev"),
           0, T, tag=tag + "q")
    WVS = V1["wvs"]
    for t_ in range(NKB):
        p.step("dve", lambda e, t_=t_: e.memset(VN.tile(t_), 1.0),
               reads=[], writes=VN.wr(t_), tag=tag + "vo")
    for kt in range(8):
        p.dma("pool", WVS.tile(kt), W["wqkv"][0][kt * 128:(kt + 1) * 128, 2048:3072],
              reads=[W["wqkv"][1]], writes=WVS.wr(kt), tag=tag + "vw")
    for tt in range(NKB):
        banks = [K.ps.take(hold=True), K.ps.take(hold=True)]
        for kt in range(8):
            for half in range(2):
                p.step("pe", lambda e, o=K.ps.ap(banks[half], 0, 128, 0, 512),
                       l=XNT.tile(kt, tt * 128, (tt + 1) * 128),
                       r=WVS.tile(kt, half * 512, (half + 1) * 512),
                       st=(kt == 0), sp=(kt == 7):
                       e.matmul(o, l, r, start=st, stop=sp),
                       reads=[XNT.R(kt), WVS.R(kt)],
                       writes=[K.ps.R(banks[half])], tag=tag + "v")
        for half in range(2):
            src_ = K.ps.ap(banks[half], 0, 128, 0, 512).rearrange(
                "p (h c) -> p h c", c=64)
            dst = VN.tile(tt, 520 * half, 520 * half + 520).rearrange(
                "p (h c) -> p h c", c=65)[:, :, 0:64]
            p.step("dve", lambda e, s=src_, d=dst: e.tensor_copy(d, s),
                   reads=[K.ps.R(banks[half])], writes=[VN.R(tt)], tag=tag + "vev")
        K.ps.release(banks[0]); K.ps.release(banks[1])

    V2 = K.arena.carve(keeps + [XNT, QT, VN], [
        ("wring", 2, 2048, BF16), ("kt", 16, T, BF16)])
    KT = V2["kt"]
    for t_ in range(16):
        p.step("dve", lambda e, t_=t_: e.memset(KT.tile(t_), 0.0),
               reads=[], writes=KT.wr(t_), tag=tag + "kz")
    wsk = WStream(K, tag + "k", W["wqkv"][0], W["wqkv"][1], 1024, 1024, False,
                  V2["wring"], mg=4, mw=64, col0=1024)

    def ep_k(mi, msz, bank, t0, t1):
        for half in range(2):
            h = 2 * mi + half
            (ktile, dr) = kmap[h]
            if dr < 96:
                s_ = K.ps.ap(bank, 32 * half, 32 * half + 32, 0, t1 - t0)
                d_ = KT.tile(ktile, t0, t1, dr, dr + 32)
                K.p.step("dve", lambda e, s=s_, d=d_: e.tensor_copy(d, s),
                         reads=[K.ps.R(bank)], writes=[KT.R(ktile)],
                         tag=tag + "kev")
            else:
                s_ = K.ps.ap(bank, 0, 64, 0, t1 - t0)
                d_ = KT.tile(ktile, t0, t1, 64, 128)
                K.p.step("dve", lambda e, s=s_, d=d_: e.tensor_copy(d, s),
                         reads=[K.ps.R(bank)], writes=[KT.R(ktile)],
                         tag=tag + "kev")
                K.p.step("dve", lambda e, d=KT.tile(ktile, t0, t1, 64, 96):
                         e.memset(d, 0.0),
                         reads=[], writes=[KT.R(ktile)], tag=tag + "kz3")
    linear(K, wsk, src_from_view(XNT, T), ep_k, 0, T, tag=tag + "k")

    for kt in range(8):
        p.dma("sp", xnt_sp[0][kt * 128:(kt + 1) * 128, 0:T], XNT.tile(kt),
              reads=[XNT.R(kt)], writes=[xnt_sp[1]], tag=tag + "xsp")

    V3 = K.arena.carve(keeps + [QT, KT, VN], [
        ("wring", 4, 2048, BF16), ("p1", NKB, 256, BF16),
        ("p2", NKB, 256, BF16), ("otc", 8, 256, BF16)])
    P1, P2, OTc = V3["p1"], V3["p2"], V3["otc"]
    _l0a_bound(K, QT, KT, P1, T, tag)
    wsp = WStream(K, tag + "p", W["wproj"][0], W["wproj"][1], 1024, 1024, False,
                  V3["wring"], mg=2)
    seq = 0
    for qc in range(T // 256):
        t0, t1 = qc * 256, (qc + 1) * 256
        for j in range(16):
            obk = {}
            for pas in range(2):
                h = j + 16 * pas
                (qt_t, qb) = qmap[h]
                (ktile, _dr) = kmap[h]
                P = P1 if pas == 0 else P2
                for kb in range(NKB):
                    sbank = K.ps.take()
                    p.step("pe", lambda e, o=K.ps.ap(sbank, 0, 128, 0, 256),
                           l=KT.tile(ktile, kb * 128, (kb + 1) * 128, qb, qb + 64),
                           r=QT.tile(qt_t, t0, t1, qb, qb + 64):
                           e.matmul(o, l, r, start=True, stop=True),
                           reads=[KT.R(ktile), QT.R(qt_t)],
                           writes=[K.ps.R(sbank)], tag=tag + "s")
                    p.step("act", lambda e, s=K.ps.ap(sbank, 0, 128, 0, 256),
                           d=P.tile(kb):
                           e.activation(d, s, AF.Exp, bias=K.negc[0:128, 0:1]),
                           reads=[K.ps.R(sbank), K.negc_reg], writes=P.wr(kb),
                           tag=tag + "e")
                obank = K.ps.take(hold=True)
                for kb in range(NKB):
                    p.step("pe", lambda e, o=K.ps.ap(obank, 0, 65, 0, 256),
                           l=VN.tile(kb, 65 * j, 65 * j + 65), r=P.tile(kb),
                           st=(kb == 0), sp=(kb == NKB - 1):
                           e.matmul(o, l, r, start=st, stop=sp),
                           reads=[VN.R(kb), P.R(kb)], writes=[K.ps.R(obank)],
                           tag=tag + "pv")
                obk[pas] = obank
            _l0a_combine(K, j, obk, OTc, tag)
            K.ps.release(obk[0])
            K.ps.release(obk[1])

        def ep_ao(mi, msz, bank, a, b):
            stg = (K.sqs, K.sqs_reg) if mi % 2 == 0 else (K.pb, K.pb_reg)
            K.p.step("dve", lambda e, s=K.ps.ap(bank, 0, msz, 0, b - a),
                     d=stg[0][0:msz, 0:b - a]: e.tensor_copy(d, s),
                     reads=[K.ps.R(bank)], writes=[stg[1]], tag=tag + "ao")
            K.p.dma("sp", ao_sp[0][mi * 128:mi * 128 + msz, t0 + a:t0 + b],
                    stg[0][0:msz, 0:b - a],
                    reads=[stg[1]], writes=[ao_sp[1]], tag=tag + "aosp")
        linear(K, wsp, src_from_view(OTc, 256), ep_ao, 0, 256, chunk=256,
               tag=tag + "proj", seq0=seq)
        seq += wsp.n_groups



CONST_SPECS = [
    ("ident", [128, 128], BF16), ("identf", [128, 128], F32),
    ("ones1x128", [32, 128], F32), ("onescol", [128, 32], BF16),
    ("onescolf", [128, 32], F32), ("ones64", [64, 32], BF16),
    ("e2", [128, 32], BF16), ("e4", [128, 32], BF16),
    ("iota8", [128, 8], F32), ("misc", [128, 8], F32),
    ("permf", [128, 128], BF16), ("permh", [128, 128], BF16),
    ("exp2", [32, 128], BF16),
]


def const_arrays(T):
    base = build_constants(T)
    e4 = np.zeros((128, 4), ml_dtypes.bfloat16)
    for j in range(4):
        e4[32 * j:32 * (j + 1), j] = 1
    ones1x = np.zeros((32, 128), np.float32); ones1x[0, :] = 1.0
    onescol = np.zeros((128, 32), ml_dtypes.bfloat16); onescol[:, 0] = 1
    onescolf = np.zeros((128, 32), np.float32); onescolf[:, 0] = 1.0
    ones64 = np.zeros((64, 32), ml_dtypes.bfloat16); ones64[:, 0] = 1
    e2p = np.zeros((128, 32), ml_dtypes.bfloat16); e2p[:, 0:2] = base["c_e2"]
    e4p = np.zeros((128, 32), ml_dtypes.bfloat16); e4p[:, 0:4] = e4
    return {
        "c_ident": base["c_ident"],
        "c_identf": np.eye(128, dtype=np.float32),
        "c_ones1x128": ones1x,
        "c_onescol": onescol,
        "c_onescolf": onescolf,
        "c_ones64": ones64,
        "c_e2": e2p,
        "c_e4": e4p,
        "c_iota8": base["c_iota8"],
        "c_permf": np.asarray(base["c_permf"]),
        "c_exp2": (lambda a: a)(np.asarray(
            [[1.0 if (m // 64) == k2 else 0.0 for m in range(128)]
             if k2 < 2 else [0.0] * 128 for k2 in range(32)],
            ml_dtypes.bfloat16)),
        "c_permh": np.asarray(base["c_permh"]),
        "c_ropec": base["c_ropec"],
        "c_ropes": base["c_ropes"],
        "c_misc": np.broadcast_to(np.array(
            [1e-5, 1e-6, 1e-12, 1.0, 1e-38, 0.0, 0.0, 0.0], np.float32),
            (128, 8)).copy(),
        "c_xposcs": np.asarray(base["c_xposcs"]),
        "c_xposss": np.asarray(base["c_xposss"]),
        "c_xposck": np.asarray(base["c_xposck"]),
        "c_xpossk": np.asarray(base["c_xpossk"]),
        "c_cmask": np.asarray(base["c_cmask"]),
        "c_lmask": base["c_lmask"],
        "c_lmask0": base["c_lmask0"],
    }


def make_ctx(nc, p, cfg):
    K = KC(nc, p, cfg)
    K.ps = PsumPool(nc)
    K.cbuf = {}
    K.creg = {}
    for (name, shape, dt) in CONST_SPECS:
        d = nc.dram_tensor("c_" + name, shape, dt, kind="ExternalInput").ap()
        t = nc.alloc_sbuf_tensor("cb_" + name, shape, dt)
        K.cbuf[name] = t.ap()
        K.creg[name] = Region("c_" + name)
        p.dma("sp", t.ap(), d, reads=[], writes=[K.creg[name]], tag="const")
    T = cfg["T"]
    K.rwa = nc.alloc_sbuf_tensor("rwa", [128, T], F32).ap()
    K.rwa_reg = Region("rwa")
    K.rwb = nc.alloc_sbuf_tensor("rwb", [128, T], F32).ap()
    K.rwb_reg = Region("rwb")

    K.cols = nc.alloc_sbuf_tensor("colsb", [128, 64], F32).ap()
    K.cols_reg = Region("cols")
    K.xf = nc.alloc_sbuf_tensor("xf", [128, 1024], F32).ap()
    K.xf_reg = Region("xf")
    K.negc = nc.alloc_sbuf_tensor("negc", [128, 4], F32).ap()
    K.negc_reg = Region("negc")
    K.lam = nc.alloc_sbuf_tensor("lamt", [1, 4], F32).ap()
    K.lam_reg = Region("lam")
    K.rsc = nc.alloc_sbuf_tensor("rsct", [64, 4], F32).ap()
    K.rsc_reg = Region("rsc")
    K.sq64 = nc.alloc_sbuf_tensor("sq64", [64, 256], BF16).ap()
    K.sq64_reg = Region("sq64")
    K.ebt = nc.alloc_sbuf_tensor("ebt", [128, 8], F32).ap()
    K.ebt_reg = Region("ebt")
    K.sqs = nc.alloc_sbuf_tensor("sqs", [128, 512], BF16).ap()
    K.sqs_reg = Region("sqs")
    K.pb = nc.alloc_sbuf_tensor("pbt", [128, 512], BF16).ap()
    K.pb_reg = Region("pb")
    K.rlkf = nc.alloc_sbuf_tensor("rlkf", [32, 256], F32).ap()
    K.rlk = nc.alloc_sbuf_tensor("rlkb", [32, 256], BF16).ap()
    K.rlk_reg = Region("rlk")
    for (ap_, reg_) in [(K.rwa, K.rwa_reg), (K.rwb, K.rwb_reg),
                        (K.cols, K.cols_reg), (K.xf, K.xf_reg),
                        (K.negc, K.negc_reg), (K.lam, K.lam_reg),
                        (K.rsc, K.rsc_reg), (K.sq64, K.sq64_reg),
                        (K.ebt, K.ebt_reg),
                        (K.sqs, K.sqs_reg), (K.pb, K.pb_reg),
                        (K.rlkf, K.rlk_reg), (K.rlk, K.rlk_reg)]:
        p.step("dve", lambda e, a=ap_: e.memset(a, 0.0), reads=[],
               writes=[reg_], tag="init")
    return K


def precompute_l0a_scalars(K, lam_aps, lam_regs, rms_ap, rms_reg, tag="pre"):
    p = K.p
    C, cr = K.cols, K.cols_reg
    for i, (ap_, reg_) in enumerate(zip(lam_aps, lam_regs)):
        p.dma("sp", C[0:32, 8 + i:9 + i], ap_, reads=[reg_], writes=[cr], tag=tag)
    p.dma("sp", K.rsc[0:64, 1:2], rms_ap, reads=[rms_reg], writes=[K.rsc_reg],
          tag=tag)
    p.step("dve", lambda e: e.tensor_scalar(
        K.rsc[0:64, 0:1], K.rsc[0:64, 1:2], 1.0 - LAMBDA_INIT, None,
        op0=AluOpType.mult),
        reads=[K.rsc_reg], writes=[K.rsc_reg], tag=tag)
    for pair in range(2):
        p.step("dve", lambda e, a=8 + 2 * pair, pr=pair: e.tensor_tensor(
            C[0:32, 12 + pr:13 + pr], C[0:32, a:a + 1], C[0:32, a + 1:a + 2],
            op=AluOpType.mult),
            reads=[cr], writes=[cr], tag=tag)
        bank = K.ps.take()
        p.step("pe", lambda e, o=K.ps.ap(bank, 0, 1, 0, 1), pair=pair:
               e.matmul(o, K.cbuf["onescolf"][0:32, 0:1],
                        C[0:32, 12 + pair:13 + pair], start=True, stop=True),
               reads=[K.creg["onescolf"], cr], writes=[K.ps.R(bank)], tag=tag)
        p.step("act", lambda e, s=K.ps.ap(bank, 0, 1, 0, 1), pair=pair:
               e.activation(K.lam[0:1, 1 + pair:2 + pair], s, AF.Exp),
               reads=[K.ps.R(bank)], writes=[K.lam_reg], tag=tag)
    p.step("dve", lambda e: e.tensor_tensor(
        K.lam[0:1, 0:1], K.lam[0:1, 1:2], K.lam[0:1, 2:3], op=AluOpType.subtract),
        reads=[K.lam_reg], writes=[K.lam_reg], tag=tag)
    p.step("dve", lambda e: e.tensor_scalar(
        K.lam[0:1, 0:1], K.lam[0:1, 0:1], LAMBDA_INIT, None, op0=AluOpType.add),
        reads=[K.lam_reg], writes=[K.lam_reg], tag=tag)



def bound_sumsq_max(K, tiles_fn, ntiles, T, cell, tmpcell, tag, chunk=512):
    p = K.p
    C, cr = K.cols, K.cols_reg
    first = True
    for c0 in range(0, T, chunk):
        c1 = min(c0 + chunk, T)
        bank = K.ps.take()
        for i in range(ntiles):
            ap_, reg_ = tiles_fn(i, c0, c1)
            p.step("act", lambda e, s=ap_, d=K.sqs[0:128, 0:c1 - c0]:
                   e.activation(d, s, AF.Square),
                   reads=[reg_], writes=[K.sqs_reg], tag=tag)
            p.step("pe", lambda e, o=K.ps.ap(bank, 0, 1, 0, c1 - c0),
                   st=(i == 0), sp=(i == ntiles - 1):
                   e.matmul(o, K.cbuf["onescol"][0:128, 0:1],
                            K.sqs[0:128, 0:c1 - c0], start=st, stop=sp),
                   reads=[K.creg["onescol"], K.sqs_reg], writes=[K.ps.R(bank)],
                   tag=tag)
        dstcell = cell if first else tmpcell
        p.step("dve", lambda e, s=K.ps.ap(bank, 0, 1, 0, c1 - c0), d=dstcell:
               e.reduce_max(d, s, axis=AX.X),
               reads=[K.ps.R(bank)], writes=[cr], tag=tag)
        if not first:
            p.step("dve", lambda e: e.tensor_tensor(cell, cell, tmpcell,
                                                    op=AluOpType.max),
                   reads=[cr], writes=[cr], tag=tag)
        first = False


def bound_to_negcol(K, qcell, kcell, tmpcell, negcol, scale2=1.0, tag="bnd"):
    p = K.p
    cr = K.cols_reg
    p.step("dve", lambda e: e.tensor_tensor(tmpcell, qcell, kcell,
                                            op=AluOpType.mult),
           reads=[cr], writes=[cr], tag=tag)
    if scale2 != 1.0:
        p.step("dve", lambda e: e.tensor_scalar(tmpcell, tmpcell, scale2, None,
                                                op0=AluOpType.mult),
               reads=[cr], writes=[cr], tag=tag)
    p.step("act", lambda e: e.activation(tmpcell, tmpcell, AF.Ln),
           reads=[cr], writes=[cr], tag=tag)
    p.step("act", lambda e: e.activation(tmpcell, tmpcell, AF.Exp, scale=0.5),
           reads=[cr], writes=[cr], tag=tag)
    bank = K.ps.take()
    p.step("pe", lambda e, o=K.ps.ap(bank, 0, 128, 0, 1):
           e.matmul(o, K.cbuf["ones1x128"][0:1, 0:128], tmpcell,
                    start=True, stop=True),
           reads=[K.creg["ones1x128"], cr], writes=[K.ps.R(bank)], tag=tag)
    p.step("dve", lambda e, s=K.ps.ap(bank, 0, 128, 0, 1), d=negcol:
           e.tensor_scalar(d, s, -1.0, None, op0=AluOpType.mult),
           reads=[K.ps.R(bank)], writes=[K.negc_reg], tag=tag)


def load_cols(K, src_ap, src_reg, col0, n, tag="gc"):
    cols = []
    for kt in range(n):
        K.p.dma("sp", K.cols[0:128, col0 + kt:col0 + kt + 1],
                src_ap[kt * 128:(kt + 1) * 128].unsqueeze(1),
                reads=[src_reg], writes=[K.cols_reg], tag=tag)
        cols.append(K.cols[0:128, col0 + kt:col0 + kt + 1])
    return cols



def slice3d(ap3, e, rows, cols):
    return bass.AP(ap3.tensor, e * rows * cols, [[cols, rows], [1, cols]])


def stage_l0_ffn(K, XNT, Y, T, W, keeps=(), tag="l0f"):
    p = K.p
    NKB = T // 128
    keeps = [Y] + list(keeps)

    Va = K.arena.carve(keeps + [XNT], [
        ("xb", 8, T, BF16), ("bc", 2, T, F32), ("nx", 8, T, BF16)])
    for kt in range(8):
        p.step("act", lambda e, kt=kt, d=Y.tile(kt), s=XNT.tile(kt):
               e.copy(d, s), reads=[XNT.R(kt)], writes=Y.wr(kt), tag=tag + "sd")
    g1c = load_cols(K, W["g1"][0], W["g1"][1], 16, 8, tag + "g1")
    NX = Va["nx"]
    rms_emit(K, Y, Va["xb"], Va["bc"], NX, T, g1c, K.cols_reg, eps=1e-6,
             tag=tag + "r1")

    Vb = K.arena.carve(keeps + [NX], [
        ("wring", 4, 2048, BF16), ("wtmp", 2, 1024, BF16),
        ("qt2", 8, T, BF16), ("kct", 2, T, BF16), ("vct", 2, T, BF16),
        ("vcn", NKB, 256, BF16)])
    QT2, KCT, VCT, VCN = Vb["qt2"], Vb["kct"], Vb["vct"], Vb["vcn"]
    wsq = WStream(K, tag + "wq", W["Wq"][0], W["Wq"][1], 1024, 1024, True,
                  Vb["wring"], wtmp=Vb["wtmp"], mg=2)
    linear(K, wsq, src_from_view(NX, T),
           ep_to_view(K, QT2, dtype_scale=1.0 / 16.0, tag=tag + "qe"),
           0, T, tag=tag + "q")
    wskc = WStream(K, tag + "wk", W["Wkc"][0], W["Wkc"][1], 1024, 256, True,
                   Vb["wring"], wtmp=Vb["wtmp"], mg=2)
    linear(K, wskc, src_from_view(NX, T),
           ep_to_view(K, KCT, tag=tag + "ke"), 0, T, tag=tag + "k")
    wsvc = WStream(K, tag + "wv", W["Wvc"][0], W["Wvc"][1], 1024, 256, True,
                   Vb["wring"], wtmp=Vb["wtmp"], mg=2)
    linear(K, wsvc, src_from_view(NX, T),
           ep_to_view(K, VCT, tag=tag + "ve"), 0, T, tag=tag + "v")
    for dt in range(2):
        for tb in range(NKB):
            bank = K.ps.take()
            p.step("pe", lambda e, s=VCT.tile(dt, tb * 128, (tb + 1) * 128),
                   o=K.ps.bf16(bank, 0, 128, 0, 128), i=K.cbuf["ident"]:
                   e.transpose(o, s, i),
                   reads=[VCT.R(dt), K.creg["ident"]], writes=[K.ps.R(bank)],
                   tag=tag + "vt")
            p.step("dve", lambda e, s=K.ps.bf16(bank, 0, 128, 0, 128),
                   d=VCN.tile(tb, dt * 128, (dt + 1) * 128):
                   e.tensor_copy(d, s),
                   reads=[K.ps.R(bank)], writes=VCN.wr(tb), tag=tag + "vt")

    Vc_ = K.arena.carve(keeps + [QT2, KCT, VCN], [
        ("p", NKB, 512, BF16), ("ot2", 8, T, BF16)])
    P, OT2 = Vc_["p"], Vc_["ot2"]
    C = K.cols
    bound_sumsq_max(K, lambda i, c0, c1: (KCT.tile(i, c0, c1), KCT.R(i)), 2, T,
                    C[0:1, 44:45], C[0:1, 45:46], tag + "bk")
    for h in range(4):
        bound_sumsq_max(K, lambda i, c0, c1, h=h: (QT2.tile(2 * h + i, c0, c1),
                                                   QT2.R(2 * h + i)), 2, T,
                        C[0:1, 40:41], C[0:1, 41:42], tag + f"bq{h}")
        bound_to_negcol(K, C[0:1, 40:41], C[0:1, 44:45], C[0:1, 41:42],
                        K.negc[0:128, h:h + 1], tag=tag + f"bn{h}")
    for qc in range(T // 512):
        t0, t1 = qc * 512, (qc + 1) * 512
        for h in range(4):
            for kb in range(NKB):
                sbank = K.ps.take()
                for dt in range(2):
                    p.step("pe", lambda e, o=K.ps.ap(sbank, 0, 128, 0, 512),
                           l=KCT.tile(dt, kb * 128, (kb + 1) * 128),
                           r=QT2.tile(2 * h + dt, t0, t1),
                           st=(dt == 0), sp=(dt == 1):
                           e.matmul(o, l, r, start=st, stop=sp),
                           reads=[KCT.R(dt), QT2.R(2 * h + dt)],
                           writes=[K.ps.R(sbank)], tag=tag + "s")
                p.step("act", lambda e, s=K.ps.ap(sbank, 0, 128, 0, 512),
                       d=P.tile(kb), h=h:
                       e.activation(d, s, AF.Exp, bias=K.negc[0:128, h:h + 1]),
                       reads=[K.ps.R(sbank), K.negc_reg], writes=P.wr(kb),
                       tag=tag + "e")
            obanks = [K.ps.take(hold=True), K.ps.take(hold=True)]
            dbank = K.ps.take(hold=True)
            for kb in range(NKB):
                for half in range(2):
                    p.step("pe", lambda e, o=K.ps.ap(obanks[half], 0, 128, 0, 512),
                           l=VCN.tile(kb, half * 128, (half + 1) * 128),
                           r=P.tile(kb), st=(kb == 0), sp=(kb == NKB - 1):
                           e.matmul(o, l, r, start=st, stop=sp),
                           reads=[VCN.R(kb), P.R(kb)],
                           writes=[K.ps.R(obanks[half])], tag=tag + "pv")
                p.step("pe", lambda e, o=K.ps.ap(dbank, 0, 1, 0, 512),
                       l=K.cbuf["onescol"][0:128, 0:1], r=P.tile(kb),
                       st=(kb == 0), sp=(kb == NKB - 1):
                       e.matmul(o, l, r, start=st, stop=sp),
                       reads=[K.creg["onescol"], P.R(kb)],
                       writes=[K.ps.R(dbank)], tag=tag + "dn")
            p.step("dve", lambda e, s=K.ps.ap(dbank, 0, 1, 0, 512):
                   e.reciprocal(K.rwa[0:1, 0:512], s),
                   reads=[K.ps.R(dbank)], writes=[K.rwa_reg], tag=tag + "nr")
            bank2 = K.ps.take()
            p.step("pe", lambda e, o=K.ps.ap(bank2, 0, 128, 0, 512):
                   e.matmul(o, K.cbuf["ones1x128"][0:1, 0:128],
                            K.rwa[0:1, 0:512], start=True, stop=True),
                   reads=[K.creg["ones1x128"], K.rwa_reg],
                   writes=[K.ps.R(bank2)], tag=tag + "nb")
            p.step("dve", lambda e, s=K.ps.ap(bank2, 0, 128, 0, 512):
                   e.tensor_copy(K.xf[0:128, 0:512], s),
                   reads=[K.ps.R(bank2)], writes=[K.xf_reg], tag=tag + "nb")
            for half in range(2):
                p.step("dve", lambda e, s=K.ps.ap(obanks[half], 0, 128, 0, 512),
                       d=OT2.tile(2 * h + half, t0, t1):
                       e.tensor_tensor(d, s, K.xf[0:128, 0:512],
                                       op=AluOpType.mult),
                       reads=[K.ps.R(obanks[half]), K.xf_reg],
                       writes=OT2.wr(2 * h + half), tag=tag + "oe")
            K.ps.release(obanks[0]); K.ps.release(obanks[1]); K.ps.release(dbank)

    Vd = K.arena.carve(keeps + [OT2], [
        ("wring", 4, 2048, BF16), ("wtmp", 2, 1024, BF16)])
    wso = WStream(K, tag + "wo", W["Wo"][0], W["Wo"][1], 1024, 1024, True,
                  Vd["wring"], wtmp=Vd["wtmp"], mg=2)

    def ep_acc_y(mi, msz, bank, t0, t1):
        K.p.step("dve", lambda e, s=K.ps.ap(bank, 0, msz, 0, t1 - t0),
                 d=Y.tile(mi, t0, t1): e.tensor_tensor(d, d, s, op=AluOpType.add),
                 reads=[K.ps.R(bank), Y.R(mi)], writes=[Y.R(mi)], tag=tag + "ya")
    linear(K, wso, src_from_view(OT2, T), ep_acc_y, 0, T, tag=tag + "wo")

    Ve = K.arena.carve(keeps, [
        ("xb", 8, T, BF16), ("bc", 2, T, F32),
        ("xn2", 8, T, BF16), ("xn2lo", 8, T, BF16)])
    XN2, XN2LO = Ve["xn2"], Ve["xn2lo"]
    g2c = load_cols(K, W["g2"][0], W["g2"][1], 24, 8, tag + "g2")
    rms_emit(K, Y, Ve["xb"], Ve["bc"], XN2, T, g2c, K.cols_reg, eps=1e-6,
             tag=tag + "r2", LO=XN2LO)

    Vf = K.arena.carve(keeps + [XN2, XN2LO], [
        ("cmb", 1, T, F32), ("cmbr", 1, T, F32),
        ("wrgh", 8, 8, BF16), ("wrgl", 8, 8, BF16),
        ("wrsv", 1, 1024, F32)])
    Vf["xn2"] = XN2
    Vf["xn2lo"] = XN2LO
    _l0f_router(K, Vf, T, W, tag)
    CMB = Vf["cmb"]

    Vg = K.arena.carve(keeps + [XN2, CMB], [
        ("wring", 2, 2048, BF16), ("wring2", 2, 2048, BF16),
        ("wtmp", 2, 1024, BF16), ("cbb", 1, T, BF16),
        ("vstg", 2, T, BF16), ("hv", 8, T, BF16)])
    CBB, VSTG, HV = Vg["cbb"], Vg["vstg"], Vg["hv"]
    for e_ in range(9):
        is_shared = (e_ == 8)
        if is_shared:
            w1ap, w1rg = W["sW1"]
            w2ap, w2rg = W["sW2"]
            w3ap, w3rg = W["sW3"]
        else:
            w1ap, w1rg = W["eW1"][0][e_], W["eW1"][1]
            w2ap, w2rg = W["eW2"][0][e_], W["eW2"][1]
            w3ap, w3rg = W["eW3"][0][e_], W["eW3"][1]
        if not is_shared:
            p.dma("sp", K.rwa[0:1, 0:T], CMB.tile(0, 0, T, e_, e_ + 1),
                  reads=[CMB.R(0)], writes=[K.rwa_reg], tag=tag + f"cbr{e_}")
            for c0 in range(0, T, 512):
                c1 = min(c0 + 512, T)
                bank = K.ps.take()
                p.step("pe", lambda e, o=K.ps.ap(bank, 0, 128, 0, c1 - c0),
                       l=K.cbuf["ones1x128"][0:1, 0:128],
                       r=K.rwa[0:1, c0:c1]:
                       e.matmul(o, l, r, start=True, stop=True),
                       reads=[K.creg["ones1x128"], K.rwa_reg],
                       writes=[K.ps.R(bank)], tag=tag + "cb")
                p.step("dve", lambda e, s=K.ps.ap(bank, 0, 128, 0, c1 - c0),
                       d=CBB.tile(0, c0, c1): e.tensor_copy(d, s),
                       reads=[K.ps.R(bank)], writes=CBB.wr(0), tag=tag + "cb")
        ws3 = WStream(K, tag + f"w3_{e_}", w3ap, w3rg, 1024, 1024, True,
                      Vg["wring"], wtmp=Vg["wtmp"], mg=2)
        ws1 = WStream(K, tag + f"w1_{e_}", w1ap, w1rg, 1024, 1024, True,
                      Vg["wring2"], wtmp=Vg["wtmp"], mg=2)
        chunks = [(c0, min(c0 + 512, T)) for c0 in range(0, T, 512)]
        for g in range(ws3.n_groups):
            s3, r3 = ws3.stage_group(g)
            s1, r1 = ws1.stage_group(g)
            for m in range(2):
                mi = g * 2 + m
                for (t0, t1) in chunks:
                    bank = K.ps.take()
                    for kt in range(8):
                        p.step("pe", lambda e, o=K.ps.ap(bank, 0, 128, 0, t1 - t0),
                               l=ws3.entry_ap(s3, m, kt),
                               r=XN2.tile(kt, t0, t1),
                               st=(kt == 0), sp=(kt == 7):
                               e.matmul(o, l, r, start=st, stop=sp),
                               reads=[r3, XN2.R(kt)], writes=[K.ps.R(bank)],
                               tag=tag + "x3")
                    p.step("dve", lambda e, s=K.ps.ap(bank, 0, 128, 0, t1 - t0),
                           d=VSTG.tile(m, t0, t1): e.tensor_copy(d, s),
                           reads=[K.ps.R(bank)], writes=[VSTG.R(m)],
                           tag=tag + "x3e")
                for (t0, t1) in chunks:
                    bank = K.ps.take()
                    for kt in range(8):
                        p.step("pe", lambda e, o=K.ps.ap(bank, 0, 128, 0, t1 - t0),
                               l=ws1.entry_ap(s1, m, kt),
                               r=XN2.tile(kt, t0, t1),
                               st=(kt == 0), sp=(kt == 7):
                               e.matmul(o, l, r, start=st, stop=sp),
                               reads=[r1, XN2.R(kt)], writes=[K.ps.R(bank)],
                               tag=tag + "x1")
                    p.step("act", lambda e, s=K.ps.ap(bank, 0, 128, 0, t1 - t0),
                           d=K.sqs[0:128, 0:t1 - t0]:
                           e.activation(d, s, AF.Silu),
                           reads=[K.ps.R(bank)], writes=[K.sqs_reg],
                           tag=tag + "hs")
                    p.step("dve", lambda e, d=HV.tile(mi, t0, t1),
                           s=VSTG.tile(m, t0, t1):
                           e.tensor_tensor(d, K.sqs[0:128, 0:t1 - t0], s,
                                           op=AluOpType.mult),
                           reads=[K.sqs_reg, VSTG.R(m)], writes=HV.wr(mi),
                           tag=tag + "hm")
                    if not is_shared:
                        p.step("dve", lambda e, d=HV.tile(mi, t0, t1),
                               c=CBB.tile(0, t0, t1):
                               e.tensor_tensor(d, d, c, op=AluOpType.mult),
                               reads=[HV.R(mi), CBB.R(0)], writes=[HV.R(mi)],
                               tag=tag + "hc")
        ws2 = WStream(K, tag + f"w2_{e_}", w2ap, w2rg, 1024, 1024, True,
                      Vg["wring"], wtmp=Vg["wtmp"], mg=2)
        linear(K, ws2, src_from_view(HV, T), ep_acc_y, 0, T,
               tag=tag + f"x2_{e_}")


def _l0f_router(K, V, T, W, tag):
    p = K.p
    wrs = V["wrsv"].tile(0)
    wrs_reg = V["wrsv"].R(0)
    p.step("dve", lambda e: e.memset(V["wrsv"].tile(0, 0, 1024, 0, 32), 0.0),
           reads=[], writes=V["wrsv"].wr(0), tag=tag + "wr")
    p.dma("sp", V["wrsv"].tile(0, 0, 1024, 0, 8), W["Wr"][0],
          reads=[W["Wr"][1]], writes=[wrs_reg], tag=tag + "wr")
    p.dma("sp", K.ebt[0:128, 0:8], W["eb"][0].unsqueeze(0).partition_broadcast(128),
          reads=[W["eb"][1]], writes=[K.ebt_reg], tag=tag + "eb")
    WRGH, WRGL = V["wrgh"], V["wrgl"]
    for kt in range(8):
        bank = K.ps.take()
        p.step("pe", lambda e, s=wrs[0:8, kt * 128:(kt + 1) * 128],
               o=K.ps.ap(bank, 0, 128, 0, 8), i=K.cbuf["identf"][0:8, 0:8]:
               e.transpose(o, s, i),
               reads=[wrs_reg, K.creg["identf"]], writes=[K.ps.R(bank)],
               tag=tag + "wt")
        p.step("dve", lambda e, s=K.ps.ap(bank, 0, 128, 0, 8):
               e.tensor_copy(K.xf[0:128, 0:8], s),
               reads=[K.ps.R(bank)], writes=[K.xf_reg], tag=tag + "wt")
        p.step("dve", lambda e, kt=kt, d=WRGH.tile(kt):
               e.tensor_copy(d, K.xf[0:128, 0:8]),
               reads=[K.xf_reg], writes=WRGH.wr(kt), tag=tag + "wt")
        p.step("act", lambda e, kt=kt, s=WRGH.tile(kt):
               e.copy(K.xf[0:128, 8:16], s),
               reads=[WRGH.R(kt)], writes=[K.xf_reg], tag=tag + "wt")
        p.step("dve", lambda e, kt=kt, d=WRGL.tile(kt):
               e.tensor_tensor(d, K.xf[0:128, 0:8], K.xf[0:128, 8:16],
                               op=AluOpType.subtract),
               reads=[K.xf_reg], writes=WRGL.wr(kt), tag=tag + "wt")
    for tt in range(T // 128):
        bank = K.ps.take()
        passes = [(V["xn2"], WRGH), (V["xn2lo"], WRGH), (V["xn2"], WRGL)]
        idx = 0
        for (srcv, wv) in passes:
            for kt in range(8):
                p.step("pe", lambda e, o=K.ps.ap(bank, 0, 128, 0, 8),
                       l=srcv.tile(kt, tt * 128, (tt + 1) * 128),
                       r=wv.tile(kt), st=(idx == 0), sp=(idx == 23):
                       e.matmul(o, l, r, start=st, stop=sp),
                       reads=[srcv.R(kt), wv.R(kt)], writes=[K.ps.R(bank)],
                       tag=tag + "lg")
                idx += 1
        xf, xr = K.xf, K.xf_reg
        C, cr = K.cols, K.cols_reg
        p.step("dve", lambda e, s=K.ps.ap(bank, 0, 128, 0, 8):
               e.tensor_tensor(xf[0:128, 0:8], s, K.ebt[0:128, 0:8],
                               op=AluOpType.add),
               reads=[K.ps.R(bank), K.ebt_reg], writes=[xr], tag=tag + "lg")
        p.step("dve", lambda e: e.reduce_max(C[0:128, 48:49], xf[0:128, 0:8],
                                             axis=AX.X, negate=True),
               reads=[xr], writes=[cr], tag=tag + "sm")
        p.step("act", lambda e: e.activation(xf[0:128, 0:8], xf[0:128, 0:8],
                                             AF.Exp, bias=C[0:128, 48:49]),
               reads=[xr, cr], writes=[xr], tag=tag + "sm")
        p.step("dve", lambda e: e.tensor_reduce(C[0:128, 48:49], xf[0:128, 0:8],
                                                axis=AX.X, op=AluOpType.add),
               reads=[xr], writes=[cr], tag=tag + "sm")
        p.step("dve", lambda e: e.reciprocal(C[0:128, 49:50], C[0:128, 48:49]),
               reads=[cr], writes=[cr], tag=tag + "sm")
        p.step("dve", lambda e: e.tensor_scalar(xf[0:128, 0:8], xf[0:128, 0:8],
                                                C[0:128, 49:50], None,
                                                op0=AluOpType.mult),
               reads=[xr, cr], writes=[xr], tag=tag + "sm")
        def top1(rw_c0, mask_c0, v_col, j_col):
            p.step("dve", lambda e: e.tensor_reduce(
                C[0:128, v_col:v_col + 1], xf[0:128, rw_c0:rw_c0 + 8],
                axis=AX.X, op=AluOpType.max),
                reads=[xr], writes=[cr], tag=tag + "t1")
            p.step("dve", lambda e: e.tensor_scalar(
                xf[0:128, mask_c0:mask_c0 + 8], xf[0:128, rw_c0:rw_c0 + 8],
                C[0:128, v_col:v_col + 1], None, op0=AluOpType.is_ge),
                reads=[xr, cr], writes=[xr], tag=tag + "t1")
            p.step("dve", lambda e: e.tensor_scalar(
                xf[0:128, 24:32], xf[0:128, mask_c0:mask_c0 + 8], -64.0, 64.0,
                op0=AluOpType.mult, op1=AluOpType.add),
                reads=[xr], writes=[xr], tag=tag + "t1")
            p.step("dve", lambda e: e.tensor_tensor(
                xf[0:128, 24:32], xf[0:128, 24:32], K.cbuf["iota8"][0:128, 0:8],
                op=AluOpType.add),
                reads=[xr, K.creg["iota8"]], writes=[xr], tag=tag + "t1")
            p.step("dve", lambda e: e.tensor_reduce(
                C[0:128, j_col:j_col + 1], xf[0:128, 24:32], axis=AX.X,
                op=AluOpType.min),
                reads=[xr], writes=[cr], tag=tag + "t1")
            p.step("dve", lambda e: e.tensor_scalar(
                xf[0:128, mask_c0:mask_c0 + 8], K.cbuf["iota8"][0:128, 0:8],
                C[0:128, j_col:j_col + 1], None, op0=AluOpType.is_equal),
                reads=[K.creg["iota8"], cr], writes=[xr], tag=tag + "t1")
        top1(0, 8, 50, 51)
        p.step("dve", lambda e: e.tensor_scalar(
            xf[0:128, 24:32], xf[0:128, 8:16], 1e9, None, op0=AluOpType.mult),
            reads=[xr], writes=[xr], tag=tag + "t2")
        p.step("dve", lambda e: e.tensor_tensor(
            xf[0:128, 16:24], xf[0:128, 0:8], xf[0:128, 24:32],
            op=AluOpType.subtract),
            reads=[xr], writes=[xr], tag=tag + "t2")
        top1(16, 16, 52, 53)
        p.step("act", lambda e: e.activation(C[0:128, 54:55], C[0:128, 50:51],
                                             AF.Exp),
               reads=[cr], writes=[cr], tag=tag + "tw")
        p.step("act", lambda e: e.activation(C[0:128, 55:56], C[0:128, 52:53],
                                             AF.Exp),
               reads=[cr], writes=[cr], tag=tag + "tw")
        p.step("dve", lambda e: e.tensor_tensor(C[0:128, 56:57], C[0:128, 54:55],
                                                C[0:128, 55:56], op=AluOpType.add),
               reads=[cr], writes=[cr], tag=tag + "tw")
        p.step("dve", lambda e: e.reciprocal(C[0:128, 57:58], C[0:128, 56:57]),
               reads=[cr], writes=[cr], tag=tag + "tw")
        p.step("dve", lambda e: e.tensor_tensor(C[0:128, 54:55], C[0:128, 54:55],
                                                C[0:128, 57:58], op=AluOpType.mult),
               reads=[cr], writes=[cr], tag=tag + "tw")
        p.step("dve", lambda e: e.tensor_tensor(C[0:128, 55:56], C[0:128, 55:56],
                                                C[0:128, 57:58], op=AluOpType.mult),
               reads=[cr], writes=[cr], tag=tag + "tw")
        p.step("dve", lambda e: e.tensor_scalar(
            xf[0:128, 8:16], xf[0:128, 8:16], C[0:128, 54:55], None,
            op0=AluOpType.mult),
            reads=[xr, cr], writes=[xr], tag=tag + "cmb")
        p.step("dve", lambda e: e.tensor_scalar(
            xf[0:128, 16:24], xf[0:128, 16:24], C[0:128, 55:56], None,
            op0=AluOpType.mult),
            reads=[xr, cr], writes=[xr], tag=tag + "cmb")
        p.step("dve", lambda e: e.tensor_tensor(
            xf[0:128, 0:8], xf[0:128, 8:16], xf[0:128, 16:24], op=AluOpType.add),
            reads=[xr], writes=[xr], tag=tag + "cmb")
        bank2 = K.ps.take()
        p.step("pe", lambda e, o=K.ps.ap(bank2, 0, 8, 0, 128):
               e.transpose(o, xf[0:128, 0:8], K.cbuf["identf"]),
               reads=[xr, K.creg["identf"]], writes=[K.ps.R(bank2)], tag=tag + "ct")
        p.step("dve", lambda e, s=K.ps.ap(bank2, 0, 8, 0, 128), tt=tt:
               e.tensor_copy(V["cmb"].tile(0, tt * 128, (tt + 1) * 128, 0, 8), s),
               reads=[K.ps.R(bank2)], writes=V["cmb"].wr(0), tag=tag + "ct")



def bound_sumsq_max_b2(K, tiles_fn, ntiles, T, cell, tmpcell, tag, chunk=512):
    p = K.p
    C, cr = K.cols, K.cols_reg
    first = True
    for i in range(ntiles):
        for c0 in range(0, T, chunk):
            c1 = min(c0 + chunk, T)
            ap_, reg_ = tiles_fn(i, c0, c1)
            p.step("act", lambda e, s=ap_, d=K.sqs[0:128, 0:c1 - c0]:
                   e.activation(d, s, AF.Square),
                   reads=[reg_], writes=[K.sqs_reg], tag=tag)
            bank = K.ps.take()
            p.step("pe", lambda e, o=K.ps.ap(bank, 0, 2, 0, c1 - c0):
                   e.matmul(o, K.cbuf["e2"][0:128, 0:2],
                            K.sqs[0:128, 0:c1 - c0], start=True, stop=True),
                   reads=[K.creg["e2"], K.sqs_reg], writes=[K.ps.R(bank)],
                   tag=tag)
            dc = C[0:2, 60:61] if first else C[0:2, 61:62]
            p.step("dve", lambda e, s=K.ps.ap(bank, 0, 2, 0, c1 - c0), d=dc:
                   e.reduce_max(d, s, axis=AX.X),
                   reads=[K.ps.R(bank)], writes=[cr], tag=tag)
            if not first:
                p.step("dve", lambda e: e.tensor_tensor(
                    C[0:2, 60:61], C[0:2, 60:61], C[0:2, 61:62], op=AluOpType.max),
                    reads=[cr], writes=[cr], tag=tag)
            first = False
    bank = K.ps.take()
    p.step("pe", lambda e, o=K.ps.ap(bank, 0, 1, 0, 2):
           e.transpose(o, C[0:2, 60:61], K.cbuf["identf"][0:2, 0:2]),
           reads=[cr, K.creg["identf"]], writes=[K.ps.R(bank)], tag=tag)
    p.step("dve", lambda e, s=K.ps.ap(bank, 0, 1, 0, 2), d=cell:
           e.reduce_max(d, s, axis=AX.X),
           reads=[K.ps.R(bank)], writes=[cr], tag=tag)


def stage_l1_attn(K, XNT, T, W, ao_sp, xnt_sp, keeps=(), tag="l1a"):
    p = K.p
    NKB = T // 128
    keeps = list(keeps)

    V1 = K.arena.carve(keeps + [XNT], [
        ("wring", 4, 2048, BF16), ("wtmp", 2, 1024, BF16),
        ("qkt", 16, T, BF16), ("vn", NKB, 1040, BF16),
        ("ropec", 1, T, F32), ("ropes", 1, T, F32)])
    QKT, VN = V1["qkt"], V1["vn"]
    p.dma("sp", V1["ropec"].tile(0), W["ropec"][0], reads=[W["ropec"][1]],
          writes=V1["ropec"].wr(0), tag=tag + "tc")
    p.dma("sp", V1["ropes"].tile(0), W["ropes"][0], reads=[W["ropes"][1]],
          writes=V1["ropes"].wr(0), tag=tag + "ts")
    bq = load_cols(K, W["bqkv"][0], W["bqkv"][1], 8, 24, tag + "bq")

    for t_ in range(NKB):
        p.step("dve", lambda e, t_=t_: e.memset(VN.tile(t_), 1.0),
               reads=[], writes=VN.wr(t_), tag=tag + "vo")
    ws = WStream(K, tag + "w", W["wqkv"][0], W["wqkv"][1], 1024, 3072, True,
                 V1["wring"], wtmp=V1["wtmp"], mg=2)

    def ep_qkv(mi, msz, bank, t0, t1):
        if mi < 8:
            K.p.step("dve", lambda e, s=K.ps.ap(bank, 0, msz, 0, t1 - t0),
                     d=QKT.tile(mi, t0, t1), b=bq[mi]:
                     e.tensor_scalar(d, s, b, 0.125, op0=AluOpType.add,
                                     op1=AluOpType.mult),
                     reads=[K.ps.R(bank), K.cols_reg], writes=[QKT.R(mi)],
                     tag=tag + "eq")
        elif mi < 16:
            K.p.step("dve", lambda e, s=K.ps.ap(bank, 0, msz, 0, t1 - t0),
                     d=QKT.tile(mi, t0, t1), b=bq[mi]:
                     e.tensor_scalar(d, s, b, None, op0=AluOpType.add),
                     reads=[K.ps.R(bank), K.cols_reg], writes=[QKT.R(mi)],
                     tag=tag + "ek")
        else:
            K.p.step("dve", lambda e, s=K.ps.ap(bank, 0, msz, 0, t1 - t0),
                     d=K.sqs[0:128, 0:t1 - t0], b=bq[mi]:
                     e.tensor_scalar(d, s, b, None, op0=AluOpType.add),
                     reads=[K.ps.R(bank), K.cols_reg], writes=[K.sqs_reg],
                     tag=tag + "ev")
            h0 = 2 * (mi - 16)
            for tb in range((t1 - t0) // 128):
                bank2 = K.ps.take()
                K.p.step("pe", lambda e, tb=tb,
                         s=K.sqs[0:128, tb * 128:(tb + 1) * 128],
                         o=K.ps.bf16(bank2, 0, 128, 0, 128), i=K.cbuf["ident"]:
                         e.transpose(o, s, i),
                         reads=[K.sqs_reg, K.creg["ident"]],
                         writes=[K.ps.R(bank2)], tag=tag + "vt")
                vt = (t0 + tb * 128) // 128
                src_ = K.ps.bf16(bank2, 0, 128, 0, 128).rearrange(
                    "p (h c) -> p h c", c=64)
                dst = VN.tile(vt, 65 * h0, 65 * h0 + 130).rearrange(
                    "p (h c) -> p h c", c=65)[:, :, 0:64]
                K.p.step("dve", lambda e, s=src_, d=dst: e.tensor_copy(d, s),
                         reads=[K.ps.R(bank2)], writes=[VN.R(vt)],
                         tag=tag + "vc")
    linear(K, ws, src_from_view(XNT, T), ep_qkv, 0, T, tag=tag + "qkv")

    for t_ in range(16):
        for c0 in range(0, T, 512):
            c1 = c0 + 512
            bank = K.ps.take()
            p.step("pe", lambda e, o=K.ps.ap(bank, 0, 128, 0, 512),
                   l=K.cbuf["permf"], r=QKT.tile(t_, c0, c1):
                   e.matmul(o, l, r, start=True, stop=True),
                   reads=[K.creg["permf"], QKT.R(t_)], writes=[K.ps.R(bank)],
                   tag=tag + "rp")
            p.step("dve", lambda e, s=K.ps.ap(bank, 0, 128, 0, 512),
                   c=V1["ropes"].tile(0, c0, c1):
                   e.tensor_tensor(K.xf[0:128, 0:512], s, c, op=AluOpType.mult),
                   reads=[K.ps.R(bank), V1["ropes"].R(0)], writes=[K.xf_reg],
                   tag=tag + "rp")
            p.step("dve", lambda e, q=QKT.tile(t_, c0, c1),
                   c=V1["ropec"].tile(0, c0, c1):
                   e.tensor_tensor(K.xf[0:128, 512:1024], q, c, op=AluOpType.mult),
                   reads=[QKT.R(t_), V1["ropec"].R(0)], writes=[K.xf_reg],
                   tag=tag + "rp")
            p.step("dve", lambda e, d=QKT.tile(t_, c0, c1):
                   e.tensor_tensor(d, K.xf[0:128, 0:512], K.xf[0:128, 512:1024],
                                   op=AluOpType.add),
                   reads=[K.xf_reg], writes=[QKT.R(t_)], tag=tag + "rp")

    for kt in range(8):
        p.dma("sp", xnt_sp[0][kt * 128:(kt + 1) * 128, 0:T], XNT.tile(kt),
              reads=[XNT.R(kt)], writes=[xnt_sp[1]], tag=tag + "xsp")

    V2 = K.arena.carve(keeps + [QKT, VN], [
        ("p", NKB, 512, BF16), ("cmask", 4, 512, BF16), ("ot", 8, T, BF16)])
    P, OT = V2["p"], V2["ot"]
    for r in range(4):
        p.dma("sp", V2["cmask"].tile(r), W["cmask"][0][r * 128:(r + 1) * 128, 0:512],
              reads=[W["cmask"][1]], writes=V2["cmask"].wr(r), tag=tag + "tm")
    bound_sumsq_max_b2(K, lambda i, c0, c1: (QKT.tile(i, c0, c1), QKT.R(i)),
                       8, T, K.cols[0:1, 40:41], K.cols[0:1, 41:42], tag + "bq")
    bound_sumsq_max_b2(K, lambda i, c0, c1: (QKT.tile(8 + i, c0, c1), QKT.R(8 + i)),
                       8, T, K.cols[0:1, 44:45], K.cols[0:1, 45:46], tag + "bk")
    bound_to_negcol(K, K.cols[0:1, 40:41], K.cols[0:1, 44:45], K.cols[0:1, 41:42],
                    K.negc[0:128, 0:1], tag=tag + "bn")

    for qc in range(T // 512):
        t0, t1 = qc * 512, (qc + 1) * 512
        nkb = (qc + 1) * 4
        for h in range(16):
            qt = h // 2
            b = 64 * (h % 2)
            for kb in range(nkb):
                sbank = K.ps.take()
                p.step("pe", lambda e, o=K.ps.ap(sbank, 0, 128, 0, 512),
                       l=QKT.tile(8 + qt, kb * 128, (kb + 1) * 128, b, b + 64),
                       r=QKT.tile(qt, t0, t1, b, b + 64):
                       e.matmul(o, l, r, start=True, stop=True),
                       reads=[QKT.R(8 + qt), QKT.R(qt)],
                       writes=[K.ps.R(sbank)], tag=tag + "s")
                p.step("act", lambda e, s=K.ps.ap(sbank, 0, 128, 0, 512),
                       d=P.tile(kb):
                       e.activation(d, s, AF.Exp, bias=K.negc[0:128, 0:1]),
                       reads=[K.ps.R(sbank), K.negc_reg], writes=P.wr(kb),
                       tag=tag + "e")
                r = kb * 128 - t0
                if 0 <= r < 512:
                    p.step("dve", lambda e, d=P.tile(kb), m=V2["cmask"].tile(r // 128):
                           e.tensor_tensor(d, d, m, op=AluOpType.mult),
                           reads=[P.R(kb), V2["cmask"].R(r // 128)],
                           writes=[P.R(kb)], tag=tag + "m")
            obank = K.ps.take(hold=True)
            for kb in range(nkb):
                p.step("pe", lambda e, o=K.ps.ap(obank, 0, 65, 0, 512),
                       l=VN.tile(kb, 65 * h, 65 * h + 65), r=P.tile(kb),
                       st=(kb == 0), sp=(kb == nkb - 1):
                       e.matmul(o, l, r, start=st, stop=sp),
                       reads=[VN.R(kb), P.R(kb)], writes=[K.ps.R(obank)],
                       tag=tag + "pv")
            p.step("dve", lambda e, s=K.ps.ap(obank, 64, 65, 0, 512):
                   e.reciprocal(K.rwa[0:1, 0:512], s),
                   reads=[K.ps.R(obank)], writes=[K.rwa_reg], tag=tag + "nr")
            bank2 = K.ps.take()
            p.step("pe", lambda e, o=K.ps.ap(bank2, 0, 64, 0, 512):
                   e.matmul(o, K.cbuf["ones1x128"][0:1, 0:64],
                            K.rwa[0:1, 0:512], start=True, stop=True),
                   reads=[K.creg["ones1x128"], K.rwa_reg],
                   writes=[K.ps.R(bank2)], tag=tag + "nb")
            p.step("dve", lambda e, s=K.ps.ap(bank2, 0, 64, 0, 512):
                   e.tensor_copy(K.xf[0:64, 0:512], s),
                   reads=[K.ps.R(bank2)], writes=[K.xf_reg], tag=tag + "nb")
            rb = 64 * (h % 2)
            p.step("dve", lambda e, s=K.ps.ap(obank, 0, 64, 0, 512),
                   d=OT.tile(h // 2, t0, t1, rb, rb + 64):
                   e.tensor_tensor(d, s, K.xf[0:64, 0:512], op=AluOpType.mult),
                   reads=[K.ps.R(obank), K.xf_reg],
                   writes=[OT.R(h // 2)], tag=tag + "oe")
            K.ps.release(obank)

    V3 = K.arena.carve(keeps + [OT], [
        ("wring", 4, 2048, BF16), ("wtmp", 2, 1024, BF16)])
    bo = load_cols(K, W["bout"][0], W["bout"][1], 32, 8, tag + "bo")
    wso = WStream(K, tag + "wo", W["wout"][0], W["wout"][1], 1024, 1024, True,
                  V3["wring"], wtmp=V3["wtmp"], mg=2)

    def ep_ao(mi, msz, bank, t0, t1):
        stg = (K.sqs, K.sqs_reg) if mi % 2 == 0 else (K.pb, K.pb_reg)
        K.p.step("dve", lambda e, s=K.ps.ap(bank, 0, msz, 0, t1 - t0),
                 d=stg[0][0:msz, 0:t1 - t0], b=bo[mi]:
                 e.tensor_scalar(d, s, b, None, op0=AluOpType.add),
                 reads=[K.ps.R(bank), K.cols_reg], writes=[stg[1]],
                 tag=tag + "ao")
        K.p.dma("sp", ao_sp[0][mi * 128:mi * 128 + msz, t0:t1],
                stg[0][0:msz, 0:t1 - t0],
                reads=[stg[1]], writes=[ao_sp[1]], tag=tag + "aosp")
    linear(K, wso, src_from_view(OT, T), ep_ao, 0, T, tag=tag + "wo")



def ln_shift_emit(K, YB, XN1, t0, t1, gcols, bcols, SQ8, BC, tag="lns"):
    p = K.p
    TL = t1 - t0
    RWA, RWB = K.rwa, K.rwb
    rA, rB = K.rwa_reg, K.rwb_reg

    def zsrc(kt, c0, c1):
        off = 1 if kt < 4 else 0
        return YB.tile(kt, t0 + c0 + off, t0 + c1 + off), [YB.R(kt)], 128
    colsum(K, zsrc, 8, RWA[0:1, 0:TL], [rA], 0, TL, scale=1.0 / 1024.0,
           tag=tag + "mu")
    bcast_row(K, RWA[0:1, 0:TL], [rA], lambda a, b: BC.tile(1, a, b),
              BC.wr(1), 0, TL, tag=tag + "b1")
    p.step("act", lambda e: e.activation(RWA[0:1, 0:TL], RWA[0:1, 0:TL],
                                         AF.Square),
           reads=[rA], writes=[rA], tag=tag)
    for kt in range(8):
        off = 1 if kt < 4 else 0
        p.step("act", lambda e, kt=kt, off=off,
               s=YB.tile(kt, t0 + off, t1 + off), d=SQ8.tile(kt):
               e.activation(d, s, AF.Square),
               reads=[YB.R(kt)], writes=SQ8.wr(kt), tag=tag + "sq")
    colsum(K, lambda kt, c0, c1: (SQ8.tile(kt, c0, c1), [SQ8.R(kt)], 128),
           8, RWB[0:1, 0:TL], [rB], 0, TL, scale=1.0 / 1024.0, tag=tag + "s2")
    p.step("dve", lambda e: e.tensor_tensor(RWB[0:1, 0:TL], RWB[0:1, 0:TL],
                                            RWA[0:1, 0:TL], op=AluOpType.subtract),
           reads=[rA, rB], writes=[rB], tag=tag)
    p.step("act", lambda e: e.activation(RWB[0:1, 0:TL], RWB[0:1, 0:TL],
                                         AF.Ln, bias=K.cbuf["misc"][0:1, 0:1]),
           reads=[rB, K.creg["misc"]], writes=[rB], tag=tag)
    p.step("act", lambda e: e.activation(RWB[0:1, 0:TL], RWB[0:1, 0:TL],
                                         AF.Exp, scale=-0.5),
           reads=[rB], writes=[rB], tag=tag)
    bcast_row(K, RWB[0:1, 0:TL], [rB], lambda a, b: BC.tile(0, a, b),
              BC.wr(0), 0, TL, tag=tag + "b0")
    for kt in range(8):
        off = 1 if kt < 4 else 0
        p.step("dve", lambda e, kt=kt, off=off: e.tensor_tensor(
            K.xf[0:128, 0:TL], YB.tile(kt, t0 + off, t1 + off), BC.tile(1, 0, TL),
            op=AluOpType.subtract),
            reads=[YB.R(kt), BC.R(1)], writes=[K.xf_reg], tag=tag)
        p.step("dve", lambda e, kt=kt: e.tensor_tensor(
            K.xf[0:128, 0:TL], K.xf[0:128, 0:TL], BC.tile(0, 0, TL),
            op=AluOpType.mult),
            reads=[K.xf_reg, BC.R(0)], writes=[K.xf_reg], tag=tag)
        p.step("dve", lambda e, kt=kt, g=gcols[kt], b=bcols[kt]: e.tensor_scalar(
            XN1.tile(kt, t0, t1), K.xf[0:128, 0:TL], g, b,
            op0=AluOpType.mult, op1=AluOpType.add),
            reads=[K.xf_reg, K.cols_reg], writes=XN1.wr(kt), tag=tag)


def stage_l1_ffn(K, XNT, Y, T, W, keeps=(), tag="l1f", tap=None):
    p = K.p
    for kt in range(8):
        p.step("act", lambda e, kt=kt, d=Y.tile(kt), s=XNT.tile(kt):
               e.copy(d, s), reads=[XNT.R(kt)], writes=Y.wr(kt), tag=tag + "sd")
    keeps = [Y] + list(keeps)
    for li in range(6):
        V = K.arena.carve(keeps, [
            ("wqs", 12, 2048, BF16), ("wos", 4, 2048, BF16),
            ("yw", 8, 257, BF16), ("qw", 8, 256, BF16), ("kw", 16, 256, BF16),
            ("vw", 4, 1024, BF16), ("xcs", 1, 256, BF16), ("xss", 1, 256, BF16),
            ("xck", 1, 256, BF16), ("xsk", 1, 256, BF16),
            ("pt", 4, 256, BF16), ("otw", 8, 256, BF16), ("rlq", 2, 32, F32),
            ("lmask", 2, 512, F32), ("lmask0", 2, 512, F32),
        ])
        for qt in range(2):
            p.dma("sp", V["lmask"].tile(qt),
                  W["lmask"][0][qt * 128:(qt + 1) * 128, 0:512],
                  reads=[W["lmask"][1]], writes=V["lmask"].wr(qt), tag=tag + "mk")
            p.dma("sp", V["lmask0"].tile(qt),
                  W["lmask0"][0][qt * 128:(qt + 1) * 128, 0:512],
                  reads=[W["lmask0"][1]], writes=V["lmask0"].wr(qt), tag=tag + "mk")
        _l1f_attn(K, Y, V, T, W, li, tag=f"{tag}a{li}")
        if tap is not None:
            tap(li, "a")
        _l1f_ffn(K, Y, T, W, li, keeps, tag=f"{tag}f{li}")
        if tap is not None:
            tap(li, "b")


def _l1f_attn(K, Y, V, T, W, li, tag):
    p = K.p
    NW = T // 256
    wqk = WStream(K, tag + "wq", slice3d(W["Wqkv"][0], li, 1024, 3072),
                  W["Wqkv"][1], 1024, 3072, False, V["wqs"], mg=2)
    for g in range(wqk.n_groups):
        wqk.stage_group(g)
    wos = WStream(K, tag + "wo", slice3d(W["Wo"][0], li, 1024, 1024),
                  W["Wo"][1], 1024, 1024, False, V["wos"], mg=2)
    for g in range(wos.n_groups):
        wos.stage_group(g)
    for rep in range(2):
        p.dma("sp", K.cols[rep * 64:rep * 64 + 64, 60:61],
              bass.AP(W["qs"][0].tensor, li * 64, [[1, 64], [1, 1]]),
              reads=[W["qs"][1]], writes=[K.cols_reg], tag=tag + "qs")
        p.dma("sp", K.cols[rep * 64:rep * 64 + 64, 61:62],
              bass.AP(W["ks"][0].tensor, li * 64, [[1, 64], [1, 1]]),
              reads=[W["ks"][1]], writes=[K.cols_reg], tag=tag + "ks")
    qscol = K.cols[0:128, 60:61]
    kscol = K.cols[0:128, 61:62]
    for t_ in range(V["kw"].nt):
        p.step("dve", lambda e, t_=t_: e.memset(V["kw"].tile(t_), 0.0),
               reads=[], writes=V["kw"].wr(t_), tag=tag + "z")
    for t_ in range(V["vw"].nt):
        p.step("dve", lambda e, t_=t_: e.memset(V["vw"].tile(t_), 0.0),
               reads=[], writes=V["vw"].wr(t_), tag=tag + "z")

    for w in range(NW):
        t0 = w * 256
        for kt in range(8):
            p.step("act", lambda e, kt=kt, s=Y.tile(kt, t0, t0 + 256),
                   d=V["yw"].tile(kt, 1, 257): e.copy(d, s),
                   reads=[Y.R(kt)], writes=V["yw"].wr(kt), tag=tag + "yc")
            if t0 == 0:
                p.step("dve", lambda e, kt=kt, d=V["yw"].tile(kt, 0, 1):
                       e.memset(d, 0.0),
                       reads=[], writes=[V["yw"].R(kt)], tag=tag + "yz")
            else:
                p.step("act", lambda e, kt=kt, s=Y.tile(kt, t0 - 1, t0),
                       d=V["yw"].tile(kt, 0, 1): e.copy(d, s),
                       reads=[Y.R(kt)], writes=[V["yw"].R(kt)], tag=tag + "yz")
        for (nm, dr) in (("xcs", "xposcs"), ("xss", "xposss"),
                         ("xck", "xposck"), ("xsk", "xpossk")):
            p.dma("sp", V[nm].tile(0), W[dr][0][0:128, t0:t0 + 256],
                  reads=[W[dr][1]], writes=V[nm].wr(0), tag=tag + "tb")
        kslot = w % 2
        for mi in range(24):
            bank = K.ps.take()
            for kt in range(8):
                off = 1 if kt < 4 else 0
                g = mi // wqk.mg
                p.step("pe", lambda e, o=K.ps.ap(bank, 0, 128, 0, 256),
                       l=wqk.entry_ap(g % wqk.nring, mi % wqk.mg, kt),
                       r=V["yw"].tile(kt, off, off + 256),
                       st=(kt == 0), sp=(kt == 7):
                       e.matmul(o, l, r, start=st, stop=sp),
                       reads=[wqk.slot_regs[g % wqk.nring], V["yw"].R(kt)],
                       writes=[K.ps.R(bank)], tag=tag + "qkv")
            if mi < 16:
                isq = mi < 8
                ht = mi % 8
                p.step("act", lambda e, s=K.ps.ap(bank, 0, 128, 0, 256),
                       d=K.sqs[0:128, 0:256]: e.activation(d, s, AF.Square),
                       reads=[K.ps.R(bank)], writes=[K.sqs_reg], tag=tag + "l2")
                bank2 = K.ps.take()
                p.step("pe", lambda e, o=K.ps.ap(bank2, 0, 2, 0, 256):
                       e.matmul(o, K.cbuf["e2"][0:128, 0:2], K.sqs[0:128, 0:256],
                                start=True, stop=True),
                       reads=[K.creg["e2"], K.sqs_reg], writes=[K.ps.R(bank2)],
                       tag=tag + "l2")
                p.step("act", lambda e, s=K.ps.ap(bank2, 0, 2, 0, 256):
                       e.activation(K.rlkf[0:2, 0:256], s, AF.Ln,
                                    bias=K.cbuf["misc"][0:2, 4:5]),
                       reads=[K.ps.R(bank2), K.creg["misc"]], writes=[K.rlk_reg],
                       tag=tag + "l2")
                p.step("act", lambda e: e.activation(K.rlkf[0:2, 0:256],
                                                     K.rlkf[0:2, 0:256],
                                                     AF.Exp, scale=0.5),
                       reads=[K.rlk_reg], writes=[K.rlk_reg], tag=tag + "l2")
                p.step("dve", lambda e: e.tensor_scalar(
                    K.rlkf[0:2, 0:256], K.rlkf[0:2, 0:256], 1e-12, None,
                    op0=AluOpType.max),
                    reads=[K.rlk_reg], writes=[K.rlk_reg], tag=tag + "l2")
                p.step("dve", lambda e: e.reciprocal(K.rlkf[0:2, 0:256],
                                                     K.rlkf[0:2, 0:256]),
                       reads=[K.rlk_reg], writes=[K.rlk_reg], tag=tag + "l2")
                gc = qscol if isq else kscol
                p.step("dve", lambda e, s=K.ps.ap(bank, 0, 128, 0, 256), g=gc:
                       e.tensor_scalar(K.sqs[0:128, 0:256], s, g, None,
                                       op0=AluOpType.mult),
                       reads=[K.ps.R(bank), K.cols_reg], writes=[K.sqs_reg],
                       tag=tag + "g")
                bank3 = K.ps.take()
                p.step("pe", lambda e, o=K.ps.ap(bank3, 0, 128, 0, 256):
                       e.matmul(o, K.cbuf["permh"], K.sqs[0:128, 0:256],
                                start=True, stop=True),
                       reads=[K.creg["permh"], K.sqs_reg], writes=[K.ps.R(bank3)],
                       tag=tag + "rp")
                ct = V["xcs"] if isq else V["xck"]
                st_ = V["xss"] if isq else V["xsk"]
                p.step("dve", lambda e, c=ct.tile(0):
                       e.tensor_tensor(K.xf[0:128, 0:256], K.sqs[0:128, 0:256],
                                       c, op=AluOpType.mult),
                       reads=[K.sqs_reg, ct.R(0)], writes=[K.xf_reg], tag=tag + "rp")
                p.step("dve", lambda e, s=K.ps.ap(bank3, 0, 128, 0, 256),
                       c=st_.tile(0):
                       e.tensor_tensor(K.xf[0:128, 256:512], s, c,
                                       op=AluOpType.mult),
                       reads=[K.ps.R(bank3), st_.R(0)], writes=[K.xf_reg],
                       tag=tag + "rp")
                if isq:
                    p.step("dve", lambda e, ht=ht, d=V["qw"].tile(ht):
                           e.tensor_tensor(d, K.xf[0:128, 0:256],
                                           K.xf[0:128, 256:512], op=AluOpType.add),
                           reads=[K.xf_reg], writes=V["qw"].wr(ht), tag=tag + "q")
                    for qt in range(2):
                        bank4 = K.ps.take()
                        p.step("pe", lambda e, qt=qt,
                               o=K.ps.ap(bank4, 0, 128, 0, 2):
                               e.transpose(o, K.rlkf[0:2, qt * 128:(qt + 1) * 128],
                                           K.cbuf["identf"][0:2, 0:2]),
                               reads=[K.rlk_reg, K.creg["identf"]],
                               writes=[K.ps.R(bank4)], tag=tag + "rq")
                        p.step("dve", lambda e, qt=qt, ht=ht,
                               s=K.ps.ap(bank4, 0, 128, 0, 2):
                               e.tensor_copy(V["rlq"].tile(qt, 2 * ht, 2 * ht + 2),
                                             s), reads=[K.ps.R(bank4)],
                               writes=[V["rlq"].R(qt)], tag=tag + "rq")
                else:
                    p.step("dve", lambda e: e.tensor_tensor(
                        K.xf[0:128, 0:256], K.xf[0:128, 0:256],
                        K.xf[0:128, 256:512], op=AluOpType.add),
                        reads=[K.xf_reg], writes=[K.xf_reg], tag=tag + "k")
                    p.step("dve", lambda e: e.tensor_copy(K.rlk[0:2, 0:256],
                                                          K.rlkf[0:2, 0:256]),
                           reads=[K.rlk_reg], writes=[K.rlk_reg], tag=tag + "k")
                    bank4 = K.ps.take()
                    p.step("pe", lambda e, o=K.ps.ap(bank4, 0, 128, 0, 256):
                           e.matmul(o, K.cbuf["exp2"][0:2, 0:128],
                                    K.rlk[0:2, 0:256], start=True, stop=True),
                           reads=[K.creg["exp2"], K.rlk_reg],
                           writes=[K.ps.R(bank4)], tag=tag + "k")
                    p.step("dve", lambda e, ht=ht, kslot=kslot,
                           s=K.ps.ap(bank4, 0, 128, 0, 256),
                           d=V["kw"].tile(kslot * 8 + ht):
                           e.tensor_tensor(d, K.xf[0:128, 0:256], s,
                                           op=AluOpType.mult),
                           reads=[K.xf_reg, K.ps.R(bank4)],
                           writes=[V["kw"].R(kslot * 8 + ht)], tag=tag + "k")
            else:
                h0 = 2 * (mi - 16)
                p.step("dve", lambda e, s=K.ps.ap(bank, 0, 128, 0, 256):
                       e.tensor_copy(K.sqs[0:128, 0:256], s),
                       reads=[K.ps.R(bank)], writes=[K.sqs_reg], tag=tag + "v")
                for tb in range(2):
                    bank2 = K.ps.take()
                    p.step("pe", lambda e, tb=tb,
                           s=K.sqs[0:128, tb * 128:(tb + 1) * 128],
                           o=K.ps.bf16(bank2, 0, 128, 0, 128), i=K.cbuf["ident"]:
                           e.transpose(o, s, i),
                           reads=[K.sqs_reg, K.creg["ident"]],
                           writes=[K.ps.R(bank2)], tag=tag + "v")
                    vt = kslot * 2 + tb
                    p.step("dve", lambda e, s=K.ps.bf16(bank2, 0, 128, 0, 128),
                           d=V["vw"].tile(vt, 128 * (mi - 16), 128 * (mi - 16) + 128):
                           e.tensor_copy(d, s),
                           reads=[K.ps.R(bank2)], writes=[V["vw"].R(vt)],
                           tag=tag + "v")
        pslot = (w + 1) % 2
        for h in range(16):
            qt_t = h // 2
            b = 64 * (h % 2)
            for qt in range(2):
                sbank = K.ps.take(hold=True)
                for kh in range(2):
                    ks_ = pslot if kh == 0 else kslot
                    p.step("pe", lambda e, o=K.ps.ap(sbank, 0, 128, kh * 256,
                                                     kh * 256 + 256),
                           l=V["qw"].tile(qt_t, qt * 128, (qt + 1) * 128, b, b + 64),
                           r=V["kw"].tile(ks_ * 8 + qt_t, 0, 256, b, b + 64):
                           e.matmul(o, l, r, start=True, stop=True),
                           reads=[V["qw"].R(qt_t), V["kw"].R(ks_ * 8 + qt_t)],
                           writes=[K.ps.R(sbank)], tag=tag + "s")
                p.step("dve", lambda e, s=K.ps.ap(sbank, 0, 128, 0, 512),
                       r=V["rlq"].tile(qt, h, h + 1):
                       e.tensor_scalar(K.xf[0:128, 0:512], s, r, 8.0,
                                       op0=AluOpType.mult, op1=AluOpType.mult),
                       reads=[K.ps.R(sbank), V["rlq"].R(qt)], writes=[K.xf_reg],
                       tag=tag + "sc")
                K.ps.release(sbank)
                mvar = "lmask0" if w == 0 else "lmask"
                p.step("dve", lambda e, m=V[mvar].tile(qt):
                       e.tensor_tensor(K.xf[0:128, 0:512], K.xf[0:128, 0:512],
                                       m, op=AluOpType.add),
                       reads=[K.xf_reg, V[mvar].R(qt)], writes=[K.xf_reg],
                       tag=tag + "mk")
                p.step("dve", lambda e: e.reduce_max(K.cols[0:128, 62:63],
                                                     K.xf[0:128, 0:512],
                                                     axis=AX.X, negate=True),
                       reads=[K.xf_reg], writes=[K.cols_reg], tag=tag + "mx")
                p.step("act", lambda e: e.activation(
                    K.pb[0:128, 0:512], K.xf[0:128, 0:512], AF.Exp,
                    bias=K.cols[0:128, 62:63], accum_out=K.cols[0:128, 63:64]),
                    reads=[K.xf_reg, K.cols_reg], writes=[K.pb_reg, K.cols_reg],
                    tag=tag + "ex")
                p.step("dve", lambda e: e.reciprocal(K.cols[0:128, 62:63],
                                                     K.cols[0:128, 63:64]),
                       reads=[K.cols_reg], writes=[K.cols_reg], tag=tag + "rc")
                p.step("dve", lambda e: e.tensor_scalar(
                    K.pb[0:128, 0:512], K.pb[0:128, 0:512], K.cols[0:128, 62:63],
                    None, op0=AluOpType.mult),
                    reads=[K.pb_reg, K.cols_reg], writes=[K.pb_reg], tag=tag + "nm")
                for kt4 in range(4):
                    bank5 = K.ps.take()
                    p.step("pe", lambda e, kt4=kt4,
                           s=K.pb[0:128, kt4 * 128:(kt4 + 1) * 128],
                           o=K.ps.bf16(bank5, 0, 128, 0, 128), i=K.cbuf["ident"]:
                           e.transpose(o, s, i),
                           reads=[K.pb_reg, K.creg["ident"]],
                           writes=[K.ps.R(bank5)], tag=tag + "pt")
                    p.step("dve", lambda e, s=K.ps.bf16(bank5, 0, 128, 0, 128),
                           d=V["pt"].tile(kt4, qt * 128, (qt + 1) * 128):
                           e.tensor_copy(d, s),
                           reads=[K.ps.R(bank5)], writes=V["pt"].wr(kt4),
                           tag=tag + "pt")
            obank = K.ps.take(hold=True)
            for kt4 in range(4):
                vt = (pslot if kt4 < 2 else kslot) * 2 + (kt4 % 2)
                p.step("pe", lambda e, o=K.ps.ap(obank, 0, 64, 0, 256),
                       l=V["vw"].tile(vt, 64 * h, 64 * h + 64),
                       r=V["pt"].tile(kt4), st=(kt4 == 0), sp=(kt4 == 3):
                       e.matmul(o, l, r, start=st, stop=sp),
                       reads=[V["vw"].R(vt), V["pt"].R(kt4)],
                       writes=[K.ps.R(obank)], tag=tag + "pv")
            rb = 64 * (h % 2)
            p.step("dve", lambda e, s=K.ps.ap(obank, 0, 64, 0, 256),
                   d=V["otw"].tile(h // 2, 0, 256, rb, rb + 64):
                   e.tensor_copy(d, s),
                   reads=[K.ps.R(obank)], writes=[V["otw"].R(h // 2)],
                   tag=tag + "oe")
            K.ps.release(obank)
        for mi in range(8):
            bank = K.ps.take()
            for kt in range(8):
                g = mi // wos.mg
                p.step("pe", lambda e, o=K.ps.ap(bank, 0, 128, 0, 256),
                       l=wos.entry_ap(g % wos.nring, mi % wos.mg, kt),
                       r=V["otw"].tile(kt), st=(kt == 0), sp=(kt == 7):
                       e.matmul(o, l, r, start=st, stop=sp),
                       reads=[wos.slot_regs[g % wos.nring], V["otw"].R(kt)],
                       writes=[K.ps.R(bank)], tag=tag + "woA")
            p.step("dve", lambda e, mi=mi, s=K.ps.ap(bank, 0, 128, 0, 256),
                   d=Y.tile(mi, t0, t0 + 256):
                   e.tensor_tensor(d, d, s, op=AluOpType.add),
                   reads=[K.ps.R(bank), Y.R(mi)], writes=[Y.R(mi)], tag=tag + "ya")


def _l1f_ffn(K, Y, T, W, li, keeps, tag=""):
    p = K.p
    DI = 2730
    TCW = min(1024, T)
    Va = K.arena.carve(keeps, [
        ("ybf", 8, T + 1, BF16), ("xn1", 8, T, BF16),
        ("sq8", 8, TCW, BF16), ("bcl", 2, TCW, F32)])
    YB, XN1 = Va["ybf"], Va["xn1"]
    for kt in range(8):
        p.step("act", lambda e, kt=kt, s=Y.tile(kt), d=YB.tile(kt, 1, T + 1):
               e.copy(d, s), reads=[Y.R(kt)], writes=YB.wr(kt), tag=tag + "yb")
        p.step("dve", lambda e, kt=kt, d=YB.tile(kt, 0, 1): e.memset(d, 0.0),
               reads=[], writes=[YB.R(kt)], tag=tag + "yb")
    lgc = load_cols(K, bass.AP(W["lng"][0].tensor, li * 1024, [[1, 1024]]),
                    W["lng"][1], 16, 8, tag + "lg")
    lbc = load_cols(K, bass.AP(W["lnb"][0].tensor, li * 1024, [[1, 1024]]),
                    W["lnb"][1], 24, 8, tag + "lb")
    for tc in range(T // TCW):
        ln_shift_emit(K, YB, XN1, tc * TCW, (tc + 1) * TCW, lgc, lbc,
                      Va["sq8"], Va["bcl"], tag=tag + "ln")

    Vb = K.arena.carve(keeps + [XN1], [
        ("wring", 4, 2048, BF16), ("wring2", 4, 2048, BF16),
        ("gtq", 6, T, BF16), ("hbq", 6, T, BF16)])
    GTQ, HBQ = Vb["gtq"], Vb["hbq"]
    b2c = load_cols(K, bass.AP(W["b2"][0].tensor, li * 1024, [[1, 1024]]),
                    W["b2"][1], 0, 8, tag + "b2")
    xsrc = src_from_view(XN1, T)
    qsz = [768, 768, 768, 426]
    qoff = [0, 768, 1536, 2304]
    for fq in range(4):
        f0, fn = qoff[fq], qsz[fq]
        nqt = ceil_div(fn, 128)
        wsg = WStream(K, tag + f"wg{fq}", slice3d(W["W1"][0], li, 1024, 5460),
                      W["W1"][1], 1024, fn, False, Vb["wring"], mg=2,
                      col0=DI + f0)
        def ep_g(mi, msz, bank, a, b, f0=f0):
            K.p.dma("sp", K.cols[0:msz, 59:60],
                    bass.AP(W["b1"][0].tensor, li * 5460 + DI + f0 + mi * 128,
                            [[1, msz], [1, 1]]),
                    reads=[W["b1"][1]], writes=[K.cols_reg], tag=tag + "b1g")
            K.p.step("dve", lambda e, s=K.ps.ap(bank, 0, msz, 0, b - a):
                     e.tensor_scalar(K.sqs[0:msz, 0:b - a], s,
                                     K.cols[0:msz, 59:60], None,
                                     op0=AluOpType.add),
                     reads=[K.ps.R(bank), K.cols_reg], writes=[K.sqs_reg],
                     tag=tag + "eg")
            K.p.step("act", lambda e, mi=mi, msz=msz, a=a, b=b:
                     e.activation(GTQ.tile(mi, a, b, 0, msz),
                                  K.sqs[0:msz, 0:b - a], AF.Gelu),
                     reads=[K.sqs_reg], writes=GTQ.wr(mi), tag=tag + "eg")
        linear(K, wsg, xsrc, ep_g, 0, T, tag=tag + f"g{fq}")
        wsa = WStream(K, tag + f"wa{fq}", slice3d(W["W1"][0], li, 1024, 5460),
                      W["W1"][1], 1024, fn, False, Vb["wring2"], mg=2, col0=f0)
        def ep_a(mi, msz, bank, a, b, f0=f0):
            K.p.dma("sp", K.cols[0:msz, 58:59],
                    bass.AP(W["b1"][0].tensor, li * 5460 + f0 + mi * 128,
                            [[1, msz], [1, 1]]),
                    reads=[W["b1"][1]], writes=[K.cols_reg], tag=tag + "b1a")
            K.p.step("dve", lambda e, s=K.ps.ap(bank, 0, msz, 0, b - a):
                     e.tensor_scalar(K.sqs[0:msz, 0:b - a], s,
                                     K.cols[0:msz, 58:59], None,
                                     op0=AluOpType.add),
                     reads=[K.ps.R(bank), K.cols_reg], writes=[K.sqs_reg],
                     tag=tag + "ea")
            K.p.step("dve", lambda e, mi=mi, msz=msz, a=a, b=b,
                     g=GTQ.tile(mi, a, b, 0, msz):
                     e.tensor_tensor(HBQ.tile(mi, a, b, 0, msz),
                                     K.sqs[0:msz, 0:b - a], g,
                                     op=AluOpType.mult),
                     reads=[K.sqs_reg, GTQ.R(mi)], writes=HBQ.wr(mi),
                     tag=tag + "ea")
        linear(K, wsa, xsrc, ep_a, 0, T, tag=tag + f"a{fq}")
        hsrc = lambda kt, ksz, a, b: (HBQ.tile(kt, a, b, 0, ksz),
                                      [HBQ.R(kt)])
        ws2 = WStream(K, tag + f"w2{fq}", slice3d(W["W2"][0], li, 2730, 1024),
                      W["W2"][1], fn, 1024, False, Vb["wring"], mg=2, k0=f0)
        def ep_y(mi, msz, bank, a, b, fq=fq):
            if fq == 0:
                K.p.step("dve", lambda e, s=K.ps.ap(bank, 0, msz, 0, b - a),
                         bc=b2c[mi]:
                         e.tensor_scalar(K.xf[0:msz, 0:b - a], s, bc, None,
                                         op0=AluOpType.add),
                         reads=[K.ps.R(bank), K.cols_reg], writes=[K.xf_reg],
                         tag=tag + "ey")
            else:
                K.p.step("dve", lambda e, s=K.ps.ap(bank, 0, msz, 0, b - a):
                         e.tensor_copy(K.xf[0:msz, 0:b - a], s),
                         reads=[K.ps.R(bank)], writes=[K.xf_reg], tag=tag + "ey")
            K.p.step("dve", lambda e, mi=mi, msz=msz, d=Y.tile(mi, a, b):
                     e.tensor_tensor(d, d, K.xf[0:msz, 0:b - a],
                                     op=AluOpType.add),
                     reads=[K.xf_reg, Y.R(mi)], writes=[Y.R(mi)], tag=tag + "ey")
        linear(K, ws2, hsrc, ep_y, 0, T, tag=tag + f"y{fq}")



def emit_batch(K, x_in, out, T, WD, SCR, tag="b"):
    p = K.p
    tp = getattr(K, "tapd", None) if tag == "b0" else None
    V0 = K.arena.carve([], [("xt", 8, T, F32), ("xnr", 2, 1024, F32)])
    XT = V0["xt"]
    load_xt(K, x_in[0], x_in[1], XT, V0["xnr"], T, tag=tag + "ld")

    for lay in range(2):
        lt = f"{tag}L{lay}"
        Vl = K.arena.carve([XT], [("xb", 8, T, BF16), ("bc", 2, T, F32),
                                  ("xnt", 8, T, BF16)])
        XNT = Vl["xnt"]
        ln_emit(K, XT, Vl["xb"], Vl["bc"], XNT, T, tag=lt + "ln")
        if tp:
            for kt in range(8):
                r0 = 2 * lay * 1024 + kt * 128
                p.dma("sp", tp["b"][0][r0:r0 + 128, 0:T], XNT.tile(kt),
                      reads=[XNT.R(kt)], writes=[tp["b"][1]], tag=lt + "tpx")
        for kt in range(8):
            p.dma("sp", SCR["xt"][0][kt * 128:(kt + 1) * 128, 0:T], XT.tile(kt),
                  reads=[XT.R(kt)], writes=[SCR["xt"][1]], tag=lt + "xtsp")
        if lay == 0:
            stage_l0_attn(K, XNT, T, WD["l0a"], SCR["ao"], SCR["xnt"],
                          tag=lt + "A")
        else:
            stage_l1_attn(K, XNT, T, WD["l1a"], SCR["ao"], SCR["xnt"],
                          tag=lt + "A")
        Vf = K.arena.carve([], [("y", 8, T, F32), ("xntr", 8, T, BF16)])
        Y, XNTR = Vf["y"], Vf["xntr"]
        for kt in range(8):
            p.dma("sp", XNTR.tile(kt), SCR["xnt"][0][kt * 128:(kt + 1) * 128, 0:T],
                  reads=[SCR["xnt"][1]], writes=XNTR.wr(kt), tag=lt + "xnr")
        if lay == 0:
            stage_l0_ffn(K, XNTR, Y, T, WD["l0f"], tag=lt + "F")
        else:
            l1cb = None
            if tp:
                tslot = {(0, "a"): 5, (0, "b"): 6, (1, "b"): 7,
                         (3, "b"): 8, (5, "b"): 9}

                def l1cb(li, ph):
                    sl = tslot.get((li, ph))
                    if sl is None:
                        return
                    for kt in range(8):
                        r0 = sl * 1024 + kt * 128
                        p.dma("sp", tp["f"][0][r0:r0 + 128, 0:T], Y.tile(kt),
                              reads=[Y.R(kt)], writes=[tp["f"][1]],
                              tag=lt + f"tpy{sl}")
            stage_l1_ffn(K, XNTR, Y, T, WD["l1f"], tag=lt + "F", tap=l1cb)
        Vc = K.arena.carve([Y], [("xt", 8, T, F32), ("aor", 8, T, BF16)])
        XT = Vc["xt"]
        AOR = Vc["aor"]
        for kt in range(8):
            p.dma("sp", XT.tile(kt), SCR["xt"][0][kt * 128:(kt + 1) * 128, 0:T],
                  reads=[SCR["xt"][1]], writes=XT.wr(kt), tag=lt + "xtr")
            p.dma("sp", AOR.tile(kt), SCR["ao"][0][kt * 128:(kt + 1) * 128, 0:T],
                  reads=[SCR["ao"][1]], writes=AOR.wr(kt), tag=lt + "aor")
        if tp:
            for kt in range(8):
                r0 = (2 * lay + 1) * 1024 + kt * 128
                p.dma("sp", tp["b"][0][r0:r0 + 128, 0:T], AOR.tile(kt),
                      reads=[AOR.R(kt)], writes=[tp["b"][1]], tag=lt + "tpa")
                r1 = 2 * lay * 1024 + kt * 128
                p.dma("sp", tp["f"][0][r1:r1 + 128, 0:T], Y.tile(kt),
                      reads=[Y.R(kt)], writes=[tp["f"][1]], tag=lt + "tpy")
        for kt in range(8):
            p.step("dve", lambda e, kt=kt, d=XT.tile(kt), s=Y.tile(kt):
                   e.tensor_tensor(d, d, s, op=AluOpType.add),
                   reads=[Y.R(kt), XT.R(kt)], writes=[XT.R(kt)], tag=lt + "cb1")
            p.step("dve", lambda e, kt=kt, d=XT.tile(kt), s=AOR.tile(kt):
                   e.tensor_tensor(d, d, s, op=AluOpType.add),
                   reads=[AOR.R(kt), XT.R(kt)], writes=[XT.R(kt)], tag=lt + "cb2")
        if tp and lay == 0:
            for kt in range(8):
                r0 = 4096 + kt * 128
                p.dma("sp", tp["f"][0][r0:r0 + 128, 0:T], XT.tile(kt),
                      reads=[XT.R(kt)], writes=[tp["f"][1]], tag=lt + "tpx1")

    Vs = K.arena.carve([XT], [("xnr", 2, 1024, F32)])
    store_xt(K, XT, out[0], out[1], Vs["xnr"], T, tag=tag + "st")


def declare_weights(nc, T, b):
    W = {}

    def D(name, shape):
        return (nc.dram_tensor(name, shape, F32, kind="ExternalInput").ap(),
                Region("d_" + name))
    w = {}
    w["L0_attn_Wqkv"] = D("L0_attn_Wqkv", [1024, 3072])
    w["L0_attn_Wproj"] = D("L0_attn_Wproj", [1024, 1024])
    for n in ["lambda_q1", "lambda_k1", "lambda_q2", "lambda_k2"]:
        w["L0_attn_" + n] = D("L0_attn_" + n, [32])
    w["L0_attn_rms_scale"] = D("L0_attn_rms_scale", [64])
    for n, sh in [("g1", [1024]), ("g2", [1024]), ("Wq", [1024, 1024]),
                  ("Wkc", [256, 1024]), ("Wvc", [256, 1024]),
                  ("Wo", [1024, 1024]), ("Wr", [8, 1024]),
                  ("expert_bias", [8]), ("sW1", [1024, 1024]),
                  ("sW2", [1024, 1024]), ("sW3", [1024, 1024]),
                  ("eW1", [8, 1024, 1024]), ("eW2", [8, 1024, 1024]),
                  ("eW3", [8, 1024, 1024])]:
        w["L0_ffn_" + n] = D("L0_ffn_" + n, sh)
    for n, sh in [("w_qkv", [3072, 1024]), ("b_qkv", [3072]),
                  ("w_out", [1024, 1024]), ("b_out", [1024])]:
        w["L1_attn_" + n] = D("L1_attn_" + n, sh)
    for n, sh in [("Wqkv", [6, 1024, 3072]), ("q_scale", [6, 64]),
                  ("k_scale", [6, 64]), ("Wo", [6, 1024, 1024]),
                  ("ln_g", [6, 1024]), ("ln_b", [6, 1024]),
                  ("W1", [6, 1024, 5460]), ("b1", [6, 5460]),
                  ("W2", [6, 2730, 1024]), ("b2", [6, 1024])]:
        w["L1_ffn_" + n] = D("L1_ffn_" + n, sh)
    def DB(name, shape, dt):
        return (nc.dram_tensor(name, shape, dt, kind="ExternalInput").ap(),
                Region("d_" + name))
    w["t_ropec"] = DB("t_ropec", [128, T], F32)
    w["t_ropes"] = DB("t_ropes", [128, T], F32)
    w["t_cmask"] = DB("t_cmask", [512, 512], BF16)
    for n in ["xposcs", "xposss", "xposck", "xpossk"]:
        w["t_" + n] = DB("t_" + n, [128, T], BF16)
    w["t_lmask"] = DB("t_lmask", [256, 512], F32)
    w["t_lmask0"] = DB("t_lmask0", [256, 512], F32)

    WD = {
        "l0a": {"wqkv": w["L0_attn_Wqkv"], "wproj": w["L0_attn_Wproj"]},
        "l0f": {"Wq": w["L0_ffn_Wq"], "Wkc": w["L0_ffn_Wkc"],
                "Wvc": w["L0_ffn_Wvc"], "Wo": w["L0_ffn_Wo"],
                "Wr": w["L0_ffn_Wr"], "eb": w["L0_ffn_expert_bias"],
                "g1": w["L0_ffn_g1"], "g2": w["L0_ffn_g2"],
                "sW1": w["L0_ffn_sW1"], "sW2": w["L0_ffn_sW2"],
                "sW3": w["L0_ffn_sW3"],
                "eW1": ([slice3d(w["L0_ffn_eW1"][0], e, 1024, 1024)
                         for e in range(8)], w["L0_ffn_eW1"][1]),
                "eW2": ([slice3d(w["L0_ffn_eW2"][0], e, 1024, 1024)
                         for e in range(8)], w["L0_ffn_eW2"][1]),
                "eW3": ([slice3d(w["L0_ffn_eW3"][0], e, 1024, 1024)
                         for e in range(8)], w["L0_ffn_eW3"][1])},
        "l1a": {"wqkv": w["L1_attn_w_qkv"], "bqkv": w["L1_attn_b_qkv"],
                "wout": w["L1_attn_w_out"], "bout": w["L1_attn_b_out"],
                "ropec": w["t_ropec"], "ropes": w["t_ropes"],
                "cmask": w["t_cmask"]},
        "l1f": {"Wqkv": w["L1_ffn_Wqkv"], "qs": w["L1_ffn_q_scale"],
                "ks": w["L1_ffn_k_scale"], "Wo": w["L1_ffn_Wo"],
                "lng": w["L1_ffn_ln_g"], "lnb": w["L1_ffn_ln_b"],
                "W1": w["L1_ffn_W1"], "b1": w["L1_ffn_b1"],
                "W2": w["L1_ffn_W2"], "b2": w["L1_ffn_b2"],
                "xposcs": w["t_xposcs"], "xposss": w["t_xposss"],
                "xposck": w["t_xposck"], "xpossk": w["t_xpossk"],
                "lmask": w["t_lmask"], "lmask0": w["t_lmask0"]},
        "raw": w,
    }
    return WD


def build_core_program(T, NB, tap=False):
    import contextlib
    nc = bass.Bass("TRN2", target_bir_lowering=False)
    p = Prog(nc)
    cfg = {"T": T}
    x_d = nc.dram_tensor("x", [NB, T, 1024], F32, kind="ExternalInput").ap()
    o_d = nc.dram_tensor("out", [NB, T, 1024], F32, kind="ExternalOutput").ap()
    r_x = Region("d_x")
    r_o = Region("d_out")
    WD = declare_weights(nc, T, 0)
    K = make_ctx(nc, p, cfg)
    K.arena = Arena(nc, "arena", (nc.sbuf_bytes_remaining // 2) - 1024)
    K.arena.memset_all(p)
    SCR = {}
    for n, dt in [("xt", F32), ("ao", BF16), ("xnt", BF16)]:
        t = nc.dram_tensor("scr_" + n, [1024, T], dt, kind="Internal").ap()
        SCR[n] = (t, Region("scr_" + n))
    if tap:
        tb = nc.dram_tensor("o_tapb", [4096, T], BF16, kind="ExternalOutput").ap()
        tf = nc.dram_tensor("o_tapf", [10240, T], F32, kind="ExternalOutput").ap()
        K.tapd = {"b": (tb, Region("d_tapb")), "f": (tf, Region("d_tapf"))}
    raw = WD["raw"]
    precompute_l0a_scalars(
        K, [raw["L0_attn_" + n][0].unsqueeze(1) for n in
            ["lambda_q1", "lambda_k1", "lambda_q2", "lambda_k2"]],
        [raw["L0_attn_" + n][1] for n in
         ["lambda_q1", "lambda_k1", "lambda_q2", "lambda_k2"]],
        raw["L0_attn_rms_scale"][0].unsqueeze(1), raw["L0_attn_rms_scale"][1])
    for b in range(NB):
        xb_ap = bass.AP(x_d.tensor, b * T * 1024, [[1024, T], [1, 1024]])
        ob_ap = bass.AP(o_d.tensor, b * T * 1024, [[1024, T], [1, 1024]])
        emit_batch(K, (xb_ap, r_x), (ob_ap, r_o), T, WD, SCR, tag=f"b{b}")
    fin = [r_o]
    if tap:
        fin += [K.tapd["b"][1], K.tapd["f"][1]]
    p.finish(fin)
    es = contextlib.ExitStack()
    sems = make_sems(nc, es)
    with nc.Block() as block:
        p.realize(block, sems)
    es.close()
    return nc, p


def input_map_for_core(xs, ins, T):
    ca = const_arrays(T)
    m = {"x": np.ascontiguousarray(xs)}
    for k in ins:
        if k != "L0_attn_x":
            m[k] = np.ascontiguousarray(np.asarray(ins[k], np.float32))
    m["t_ropec"] = ca["c_ropec"]
    m["t_ropes"] = ca["c_ropes"]
    m["t_cmask"] = np.asarray(ca["c_cmask"]).reshape(512, 512)
    for n in ["xposcs", "xposss", "xposck", "xpossk"]:
        m["t_" + n] = np.asarray(ca["c_" + n])
    m["t_lmask"] = np.asarray(ca["c_lmask"]).reshape(256, 512)
    m["t_lmask0"] = np.asarray(ca["c_lmask0"]).reshape(256, 512)
    for (name, shape, dt) in CONST_SPECS:
        m["c_" + name] = ca["c_" + name]
    return m


def kernel(**inputs):
    T = 2048
    NB = 2
    x = np.asarray(inputs["L0_attn_x"], np.float32)
    nc, _ = build_core_program(T, NB)
    in_maps = []
    for c in range(8):
        xs = x[c * NB:(c + 1) * NB]
        in_maps.append(input_map_for_core(xs, inputs, T))
    res = run_bass_kernel_spmd(nc, in_maps, core_ids=list(range(8)))
    return np.concatenate([r["out"] for r in res.results], axis=0)


# version 02
# speedup vs baseline: 1.1135x; 1.1135x over previous
import math
import numpy as np
import ml_dtypes
import concourse.bass as bass
import concourse.mybir as mybir
from concourse.alu_op_type import AluOpType
from concourse.bass_utils import run_bass_kernel_spmd

F32 = mybir.dt.float32
BF16 = mybir.dt.bfloat16
AF = mybir.ActivationFunctionType
AX = mybir.AxisListType

ENGINES = ("pe", "act", "dve", "pool", "sp")
DMA_QUEUES = ("sp", "act", "pool")
DMA_LANES = 6

LAMBDA_INIT = 0.8 - 0.6 * math.exp(-0.3 * 1.0)



class Region:
    __slots__ = ("name",)

    def __init__(self, name=""):
        self.name = name

    def __repr__(self):
        return f"R({self.name})"


class Step:
    __slots__ = ("eng", "fn", "reads", "writes", "idx", "is_dma", "needs_inc",
                 "waits", "tag", "lane")

    def __init__(self, eng, fn, reads, writes, idx, is_dma, tag=""):
        self.eng = eng
        self.fn = fn
        self.reads = reads
        self.writes = writes
        self.idx = idx
        self.is_dma = is_dma
        self.needs_inc = False
        self.waits = []
        self.tag = tag


class Prog:
    def __init__(self, nc):
        self.nc = nc
        self.steps = {e: [] for e in ENGINES}
        self.last_writer = {}
        self.readers = {}
        self.deps = []
        self._realized = False

    def _record(self, step):
        deps = {}
        for r in step.reads:
            w = self.last_writer.get(r)
            if w is not None and w is not step:
                deps[id(w)] = w
        for r in step.writes:
            w = self.last_writer.get(r)
            if w is not None and w is not step:
                deps[id(w)] = w
            for rd in self.readers.get(r, ()):
                if rd is not step:
                    deps[id(rd)] = rd
        for p in deps.values():
            self.deps.append((step, p))
        for r in step.reads:
            self.readers.setdefault(r, []).append(step)
        for r in step.writes:
            self.last_writer[r] = step
            self.readers[r] = []
        self.steps[step.eng].append(step)
        return step

    def step(self, eng, fn, reads=(), writes=(), tag=""):
        assert eng in ENGINES and not self._realized
        s = Step(eng, fn, list(reads), list(writes), len(self.steps[eng]), False, tag)
        return self._record(s)

    def dma(self, queue, out_ap, in_ap, reads=(), writes=(), tag="", **kw):
        assert queue in DMA_QUEUES
        fn = lambda e: e.dma_start(out_ap, in_ap, **kw)
        s = Step(queue, fn, list(reads), list(writes), len(self.steps[queue]), True, tag)
        return self._record(s)

    def finish(self, regions):
        s = Step("sp", None, list(regions), [], len(self.steps["sp"]), False, "finish")
        return self._record(s)

    def realize(self, block, sems):
        assert not self._realized
        self._realized = True
        for (c, p) in self.deps:
            if c.eng == "pe" and p.eng == "pe":
                continue
            p.needs_inc = True
        comp_sem = {}
        comp_val = {}
        for e in ENGINES:
            lst = self.steps[e]
            lane_cnt = [0] * DMA_LANES
            ndma = 0
            ninc = 0
            for s in lst:
                if s.is_dma:
                    lane = ndma % DMA_LANES
                    s.lane = lane
                    lane_cnt[lane] += 1
                    ndma += 1
                    comp_sem[id(s)] = ("dma", e, lane)
                    comp_val[id(s)] = 16 * lane_cnt[lane]
                    if lane_cnt[lane] >= 2:
                        s.waits.append((("dma", e, lane), 16 * (lane_cnt[lane] - 1)))
                else:
                    comp_sem[id(s)] = ("cmp", e)
                    if s.needs_inc:
                        ninc += 1
                        comp_val[id(s)] = ninc
                    else:
                        comp_val[id(s)] = None
        for (c, p) in self.deps:
            if c.eng == "pe" and p.eng == "pe":
                continue
            sk = comp_sem[id(p)]
            v = comp_val[id(p)]
            assert v is not None, f"dep on non-covered step {p.eng}:{p.idx} {p.tag}"
            c.waits.append((sk, v))

        prog_self = self
        eng_attr = {"pe": "tensor", "act": "scalar", "dve": "vector",
                    "pool": "gpsimd", "sp": "sync"}

        def make_section(e):
            def section(engobj):
                lastwait = {}
                for s in prog_self.steps[e]:
                    for (sk, v) in sorted(set(s.waits), key=str):
                        if lastwait.get(sk, -1) >= v:
                            continue
                        if sk[0] == "dma":
                            sem = sems[sk[1]][1][sk[2]]
                        else:
                            sem = sems[sk[1]][0]
                        engobj.wait_ge(sem, v)
                        lastwait[sk] = v
                    if s.fn is None:
                        continue
                    inst = s.fn(engobj)
                    if s.is_dma:
                        inst.then_inc(sems[e][1][s.lane], 16)
                    elif s.needs_inc:
                        inst.then_inc(sems[e][0], 1)
            return section

        for e in ENGINES:
            if not self.steps[e]:
                continue
            getattr(block, eng_attr[e])(make_section(e))

    def stats(self):
        n = {e: len(self.steps[e]) for e in ENGINES}
        nw = sum(len(s.waits) for e in ENGINES for s in self.steps[e])
        ni = sum(1 for e in ENGINES for s in self.steps[e] if s.needs_inc or s.is_dma)
        return f"steps={n} waits={nw} incs={ni} deps={len(self.deps)}"


def make_sems(nc, es):
    sems = {}
    for e in ENGINES:
        cmp_s = es.enter_context(nc.semaphore(f"c_{e}"))
        dma_s = ([es.enter_context(nc.semaphore(f"d_{e}{i}")) for i in range(DMA_LANES)]
                 if e in DMA_QUEUES else None)
        sems[e] = (cmp_s, dma_s)
    return sems



class Buf:

    def __init__(self, nc, name, nt, tw, dtype):
        self.name = name
        self.nt = nt
        self.tw = tw
        self.dtype = dtype
        self.t = nc.alloc_sbuf_tensor(name, [128, nt * tw], dtype)
        self.regs = [Region(f"{name}{i}") for i in range(nt)]

    def tile(self, i, a=0, b=None, p0=0, p1=128):
        b = self.tw if b is None else b
        return self.t[p0:p1, i * self.tw + a: i * self.tw + b]

    def R(self, i):
        return self.regs[i]

    def raw(self, p0, p1, c0, c1):
        return self.t[p0:p1, c0:c1]

    def allR(self):
        return list(self.regs)


class PsumPool:

    def __init__(self, nc):
        self.banks = [nc.alloc_psum_tensor(f"psb{i}", [128, 512], F32) for i in range(8)]
        self.regs = [Region(f"psb{i}") for i in range(8)]
        self.rr = 0
        self.held = set()

    def take(self, hold=False):
        for k in range(8):
            i = (self.rr + k) % 8
            if i not in self.held:
                self.rr = (i + 1) % 8
                if hold:
                    self.held.add(i)
                return i
        raise RuntimeError("all psum banks held")

    def release(self, i):
        self.held.discard(i)

    def ap(self, i, p0=0, p1=128, a=0, b=512):
        return self.banks[i][p0:p1, a:b]

    def bf16(self, i, p0=0, p1=128, a=0, b=1024):
        return self.banks[i].ap().bitcast(BF16)[p0:p1, a:b]

    def R(self, i):
        return self.regs[i]



class KC:

    def __init__(self, nc, p, cfg):
        self.nc = nc
        self.p = p
        self.cfg = cfg
        self.bufs = {}
        self.dram = {}
        self.dram_regs = {}

    def dreg(self, name):
        if name not in self.dram_regs:
            self.dram_regs[name] = Region(f"D:{name}")
        return self.dram_regs[name]


def ceil_div(a, b):
    return (a + b - 1) // b



def build_constants(T):
    c = {}
    c["c_ident"] = np.eye(128, dtype=ml_dtypes.bfloat16)
    c["c_ones1x128"] = np.ones((1, 128), np.float32)
    c["c_onescol"] = np.ones((128, 1), ml_dtypes.bfloat16)
    dh = 64
    inv_freq = 1.0 / 10000.0 ** (np.arange(0, dh, 2, dtype=np.float64) / dh)
    t = np.arange(T, dtype=np.float64)
    fr = t[None, :] * inv_freq[:, None]
    cos2 = np.repeat(np.cos(fr), 2, axis=0)
    sin2 = np.repeat(np.sin(fr), 2, axis=0)
    c["c_ropec"] = np.concatenate([cos2, cos2], 0).astype(np.float32)
    c["c_ropes"] = np.concatenate([sin2, sin2], 0).astype(np.float32)
    R = np.zeros((64, 64), np.float32)
    for p_ in range(32):
        R[2 * p_, 2 * p_ + 1] = -1.0
        R[2 * p_ + 1, 2 * p_] = 1.0
    Rh = np.zeros((64, 64), np.float32)
    for d in range(32):
        Rh[d, d + 32] = -1.0
        Rh[d + 32, d] = 1.0
    pf = np.zeros((128, 128), np.float32)
    pf[0:64, 0:64] = R.T
    pf[64:128, 64:128] = R.T
    c["c_permf"] = pf.astype(ml_dtypes.bfloat16)
    ph = np.zeros((128, 128), np.float32)
    ph[0:64, 0:64] = Rh.T
    ph[64:128, 64:128] = Rh.T
    c["c_permh"] = ph.astype(ml_dtypes.bfloat16)
    half = np.arange(0, dh, 2, dtype=np.float64)
    invf = 1.0 / 10000.0 ** (half / dh)
    fr2 = t[None, :] * invf[:, None]
    freqs = np.concatenate([fr2, fr2], 0)
    base_scale = (half + 0.4 * dh) / (1.4 * dh)
    power = (t - T // 2) / (256 // 2)
    scale = base_scale[:, None] ** power[None, :]
    scale = np.concatenate([scale, scale], 0)
    def _st(a):
        return np.concatenate([a, a], 0).astype(ml_dtypes.bfloat16)
    c["c_xposcs"] = _st(np.cos(freqs) * scale)
    c["c_xposss"] = _st(np.sin(freqs) * scale)
    c["c_xposck"] = _st(np.cos(freqs) / scale)
    c["c_xpossk"] = _st(np.sin(freqs) / scale)
    masks = []
    for r in (0, 128, 256, 384):
        kg = r + np.arange(128)[:, None]
        qg = np.arange(512)[None, :]
        masks.append((kg <= qg).astype(np.float32))
    c["c_cmask"] = np.stack(masks).astype(ml_dtypes.bfloat16)
    lm = []
    for a0 in (0, 128):
        a = a0 + np.arange(128)[:, None]
        bcol = np.arange(512)[None, :]
        ok = (a <= bcol) & (bcol <= a + 256)
        lm.append(np.where(ok, 0.0, -3.0e38).astype(np.float32))
    c["c_lmask"] = np.stack(lm)
    lm0 = []
    for a0 in (0, 128):
        a = a0 + np.arange(128)[:, None]
        bcol = np.arange(512)[None, :]
        ok = (a <= bcol) & (bcol <= a + 256) & (bcol >= 256)
        lm0.append(np.where(ok, 0.0, -3.0e38).astype(np.float32))
    c["c_lmask0"] = np.stack(lm0)
    c["c_iota8"] = np.broadcast_to(np.arange(8, dtype=np.float32), (128, 8)).copy()
    e2 = np.zeros((128, 2), ml_dtypes.bfloat16)
    e2[0:64, 0] = 1
    e2[64:128, 1] = 1
    c["c_e2"] = e2
    c["c_ones64"] = np.ones((64, 1), ml_dtypes.bfloat16)
    return c



def dma_col_f32(K, dst_ap, dst_regs, src_ap, src_reg, tag="col"):
    K.p.dma("sp", dst_ap, src_ap, reads=[src_reg], writes=list(dst_regs), tag=tag)


class WStream:

    def __init__(self, K, name, w_ap, w_reg, n_in, n_out, trans,
                 ring_buf, wtmp=None, mg=2, mw=128, col0=0, k0=0):
        self.K = K
        self.name = name
        self.w_ap = w_ap
        self.w_reg = w_reg
        self.n_in = n_in
        self.n_out = n_out
        self.trans = trans
        self.KT = ceil_div(n_in, 128)
        self.mg = mg
        self.mw = mw
        self.col0 = col0
        self.k0 = k0
        self.ew = mg * self.KT * mw
        self.ring_buf = ring_buf
        self.nring = ring_buf.nt * ring_buf.tw // self.ew
        assert self.nring >= 2, f"{name}: ring too small"
        self.wtmp = wtmp
        self.n_mt = ceil_div(n_out, mw)
        self.n_groups = ceil_div(self.n_mt, mg)
        self.slot_regs = [Region(f"{name}s{i}") for i in range(self.nring)]
        self._first = True

    def entry_ap(self, slot, m_local, kt, ksz=128, msz=None):
        msz = self.mw if msz is None else msz
        off = slot * self.ew + (m_local * self.KT + kt) * self.mw
        return self.ring_buf.raw(0, ksz, off, off + msz)

    def stage_group(self, g, seq=None):
        K = self.K
        p = K.p
        slot = (g if seq is None else seq) % self.nring
        m0 = g * self.mg
        nmt = min(self.mg, self.n_mt - m0)
        ent_reg = self.slot_regs[slot]
        claims = []
        legacy = getattr(self.ring_buf, "_war", None)
        if legacy:
            self.ring_buf._war = None
            claims += legacy
        if self._first:
            self._first = False
            prev = getattr(self.ring_buf, "_ws_owner", None)
            if prev is not None and prev is not self.slot_regs:
                claims += prev
            self.ring_buf._ws_owner = self.slot_regs
        if claims:
            self.K.p.step("pool", lambda e: e.nop(),
                          reads=[], writes=list(self.slot_regs) + claims,
                          tag=self.name + "war")
        KT, n_in, n_out, mw = self.KT, self.n_in, self.n_out, self.mw
        if not self.trans:
            for m in range(nmt):
                mj = min(mw, n_out - (m0 + m) * mw)
                for kt in range(KT):
                    ksz = min(128, n_in - kt * 128)
                    c0 = self.col0 + (m0 + m) * mw
                    src = self.w_ap[self.k0 + kt * 128: self.k0 + kt * 128 + ksz,
                                    c0:c0 + mj]
                    dst = self.entry_ap(slot, m, kt, ksz, mj)
                    p.dma("pool", dst, src, reads=[self.w_reg],
                          writes=[ent_reg], tag=f"{self.name}g{g}")
        else:
            assert mw == 128
            for m in range(nmt):
                mj = min(mw, n_out - (m0 + m) * mw)
                tslot = (m0 + m) % self.wtmp.nt
                treg = self.wtmp.R(tslot)
                r0 = self.col0 + (m0 + m) * mw
                src = self.w_ap[r0:r0 + mj, self.k0:self.k0 + n_in]
                p.dma("pool", self.wtmp.tile(tslot, 0, n_in, 0, mj), src,
                      reads=[self.w_reg], writes=[treg], tag=f"{self.name}g{g}t")
                for kt in range(KT):
                    ksz = min(128, n_in - kt * 128)
                    bank = K.ps.take()
                    pr = K.ps.R(bank)
                    src_t = self.wtmp.tile(tslot, kt * 128, kt * 128 + ksz, 0, mj)
                    dst_ps = K.ps.bf16(bank, 0, ksz, 0, mj)
                    dst_sb = self.entry_ap(slot, m, kt, ksz, mj)
                    p.step("pe", lambda e, s=src_t, d=dst_ps, i=K.cbuf["ident"]:
                           e.transpose(d, s, i),
                           reads=[treg, K.creg["ident"]], writes=[pr],
                           tag=f"{self.name}tp{g}")
                    p.step("dve", lambda e, s=dst_ps, d=dst_sb: e.tensor_copy(d, s),
                           reads=[pr], writes=[ent_reg], tag=f"{self.name}tc{g}")
        return slot, ent_reg




class View:

    def __init__(self, arena, name, offs, nt, tw, dtype):
        self.arena = arena
        self.name = name
        self.offs = offs
        self.nt = nt
        self.tw = tw
        self.dtype = dtype
        self.mult = 2 if dtype == F32 else 1
        self.regs = [Region(f"{name}{i}") for i in range(nt)]
        self._war = None
        self._claimed = set()

    def tile(self, i, a=0, b=None, p0=0, p1=128):
        b = self.tw if b is None else b
        base = self.arena.t.ap() if self.dtype == BF16 else self.arena.t.ap().bitcast(F32)
        o = self.offs[i] // self.mult
        return base[p0:p1, o + a: o + b]

    def R(self, i):
        return self.regs[i]

    def wr(self, i):
        if self._war and i not in self._claimed:
            self._claimed.add(i)
            return [self.regs[i]] + self._war
        return [self.regs[i]]

    def raw(self, p0, p1, c0, c1):
        assert self.dtype == BF16
        base = self.arena.t.ap()
        return base[p0:p1, self.offs[0] + c0: self.offs[0] + c1]

    def spans(self):
        return [(o, o + self.tw * self.mult) for o in self.offs]

    def allR(self):
        return list(self.regs)


class Arena:
    def __init__(self, nc, name, cols):
        self.t = nc.alloc_sbuf_tensor(name, [128, cols], BF16)
        self.cols = cols
        self.views = {}
        self.init_reg = Region(name + "_init")

    def memset_all(self, p):
        for c0 in range(0, self.cols, 32768):
            c1 = min(c0 + 32768, self.cols)
            p.step("dve", lambda e, a=c0, b=c1: e.memset(self.t[0:128, a:b], 0.0),
                   reads=[], writes=[self.init_reg], tag="ainit")

    def carve(self, keep, new):
        legacy = []
        for v in list(self.views.values()):
            if v not in keep:
                legacy.extend(v.regs)
                legacy.extend(getattr(v, "_ws_owner", None) or [])
        self.views = {v.name: v for v in keep}
        used = sorted(s for v in keep for s in v.spans())
        out = {}

        def alloc(sz):
            off = 0
            while True:
                hit = None
                for (a, b) in used:
                    if off < b and off + sz > a:
                        hit = b
                        break
                if hit is None:
                    break
                off = hit
            assert off + sz <= self.cols, \
                f"arena overflow: need {sz} at {off}, cols={self.cols}"
            used.append((off, off + sz))
            used.sort()
            return off
        for spec in new:
            (name, nt, tw, dtype) = spec[:4]
            contig = name.startswith("wring") or name.startswith("wqs") \
                or name.startswith("wos") or (len(spec) > 4 and spec[4])
            mult = 2 if dtype == F32 else 1
            if contig:
                base = alloc(nt * tw * mult)
                offs = [base + i * tw * mult for i in range(nt)]
            else:
                offs = [alloc(tw * mult) for _ in range(nt)]
            v = View(self, name, offs, nt, tw, dtype)
            v._war = list(legacy) + [self.init_reg]
            self.views[name] = v
            out[name] = v
        return out



def linear(K, ws, src_fn, ep, tok0, tok1, chunk=512, tag="lin", seq0=0):
    p = K.p
    chunks = [(t0, min(t0 + chunk, tok1)) for t0 in range(tok0, tok1, chunk)]
    staged = {0: ws.stage_group(0, seq=seq0)}
    for g in range(ws.n_groups):
        if g + 1 < ws.n_groups:
            staged[g + 1] = ws.stage_group(g + 1, seq=seq0 + g + 1)
        slot, ent_reg = staged.pop(g)
        nmt = min(ws.mg, ws.n_mt - g * ws.mg)
        for m in range(nmt):
            mi = g * ws.mg + m
            msz = min(ws.mw, ws.n_out - mi * ws.mw)
            for (t0, t1) in chunks:
                bank = K.ps.take()
                for kt in range(ws.KT):
                    ksz = min(128, ws.n_in - kt * 128)
                    rhs, rregs = src_fn(kt, ksz, t0, t1)
                    lhsT = ws.entry_ap(slot, m, kt, ksz, msz)
                    p.step("pe",
                           lambda e, o=K.ps.ap(bank, 0, msz, 0, t1 - t0), l=lhsT,
                           r=rhs, st=(kt == 0), sp=(kt == ws.KT - 1):
                           e.matmul(o, l, r, start=st, stop=sp),
                           reads=[ent_reg] + rregs, writes=[K.ps.R(bank)],
                           tag=f"{tag}m{mi}")
                ep(mi, msz, bank, t0, t1)


def src_from_view(view, T):
    def f(kt, ksz, t0, t1):
        return view.tile(kt, t0, t1, 0, ksz), [view.R(kt)]
    return f


def ep_to_view(K, view, dtype_scale=None, tag="ep"):
    def ep(mi, msz, bank, t0, t1):
        src = K.ps.ap(bank, 0, msz, 0, t1 - t0)
        dst = view.tile(mi, t0, t1, 0, msz)
        if dtype_scale is None:
            K.p.step("dve", lambda e, s=src, d=dst: e.tensor_copy(d, s),
                     reads=[K.ps.R(bank)], writes=view.wr(mi), tag=tag)
        else:
            K.p.step("act", lambda e, s=src, d=dst: e.mul(d, s, dtype_scale),
                     reads=[K.ps.R(bank)], writes=view.wr(mi), tag=tag)
    return ep


def bcast_row(K, row_ap, row_regs, dst_ap_fn, dst_wregs, t0, t1, nparts=128,
              tag="bc"):
    p = K.p
    for c0 in range(t0, t1, 512):
        c1 = min(c0 + 512, t1)
        bank = K.ps.take()
        p.step("pe", lambda e, o=K.ps.ap(bank, 0, nparts, 0, c1 - c0),
               l=K.cbuf["ones1x128"][0:1, 0:nparts], r=row_ap[0:1, c0 - t0:c1 - t0]:
               e.matmul(o, l, r, start=True, stop=True),
               reads=[K.creg["ones1x128"]] + row_regs, writes=[K.ps.R(bank)],
               tag=tag)
        p.step("dve", lambda e, s=K.ps.ap(bank, 0, nparts, 0, c1 - c0),
               d=dst_ap_fn(c0 - t0, c1 - t0): e.tensor_copy(d, s),
               reads=[K.ps.R(bank)], writes=dst_wregs, tag=tag)


def colsum(K, src_fn, KT, row_ap, row_wregs, t0, t1, scale=None, tag="cs"):
    p = K.p
    for c0 in range(t0, t1, 512):
        c1 = min(c0 + 512, t1)
        bank = K.ps.take()
        for kt in range(KT):
            rhs, rregs, ksz = src_fn(kt, c0, c1)
            p.step("pe", lambda e, o=K.ps.ap(bank, 0, 1, 0, c1 - c0),
                   l=K.cbuf["onescol"][0:ksz, 0:1], r=rhs,
                   st=(kt == 0), sp=(kt == KT - 1):
                   e.matmul(o, l, r, start=st, stop=sp),
                   reads=[K.creg["onescol"]] + rregs, writes=[K.ps.R(bank)],
                   tag=tag)
        if scale is None:
            p.step("dve", lambda e, s=K.ps.ap(bank, 0, 1, 0, c1 - c0),
                   d=row_ap[0:1, c0 - t0:c1 - t0]: e.tensor_copy(d, s),
                   reads=[K.ps.R(bank)], writes=row_wregs, tag=tag)
        else:
            p.step("act", lambda e, s=K.ps.ap(bank, 0, 1, 0, c1 - c0),
                   d=row_ap[0:1, c0 - t0:c1 - t0]: e.mul(d, s, scale),
                   reads=[K.ps.R(bank)], writes=row_wregs, tag=tag)



def ln_emit(K, XT, XB, BC, XNT, T, tag="ln"):
    p = K.p
    RWA, RWB = K.rwa, K.rwb
    rA = K.rwa_reg
    rB = K.rwb_reg
    for kt in range(8):
        p.step("act", lambda e, s=XT.tile(kt), d=XB.tile(kt): e.copy(d, s),
               reads=[XT.R(kt)], writes=XB.wr(kt), tag=tag + "c")
    colsum(K, lambda kt, c0, c1: (XB.tile(kt, c0, c1), [XB.R(kt)], 128), 8,
           RWA[0:1, 0:T], [rA], 0, T, scale=1.0 / 1024.0, tag=tag + "mu")
    bcast_row(K, RWA[0:1, 0:T], [rA], lambda a, b: BC.tile(1, a, b), BC.wr(1),
              0, T, tag=tag + "b1")
    p.step("act", lambda e: e.activation(RWA[0:1, 0:T], RWA[0:1, 0:T],
                                         AF.Square),
           reads=[rA], writes=[rA], tag=tag)
    for kt in range(8):
        p.step("act", lambda e, s=XT.tile(kt), d=XB.tile(kt):
               e.activation(d, s, AF.Square),
               reads=[XT.R(kt)], writes=[XB.R(kt)], tag=tag + "sq")
    colsum(K, lambda kt, c0, c1: (XB.tile(kt, c0, c1), [XB.R(kt)], 128), 8,
           RWB[0:1, 0:T], [rB], 0, T, scale=1.0 / 1024.0, tag=tag + "s2")
    p.step("dve", lambda e: e.tensor_tensor(RWB[0:1, 0:T], RWB[0:1, 0:T],
                                            RWA[0:1, 0:T], op=AluOpType.subtract),
           reads=[rB, rA], writes=[rB], tag=tag)
    p.step("act", lambda e: e.activation(RWB[0:1, 0:T], RWB[0:1, 0:T],
                                         AF.Ln, bias=K.cbuf["misc"][0:1, 0:1]),
           reads=[rB, K.creg["misc"]], writes=[rB], tag=tag)
    p.step("act", lambda e: e.activation(RWB[0:1, 0:T], RWB[0:1, 0:T],
                                         AF.Exp, scale=-0.5),
           reads=[rB], writes=[rB], tag=tag)
    bcast_row(K, RWB[0:1, 0:T], [rB], lambda a, b: BC.tile(0, a, b), BC.wr(0),
              0, T, tag=tag + "b0")
    for kt in range(8):
        for c0 in range(0, T, 1024):
            c1 = min(c0 + 1024, T)
            p.step("dve", lambda e, kt=kt, c0=c0, c1=c1: e.tensor_tensor(
                K.xf[0:128, 0:c1 - c0], XT.tile(kt, c0, c1), BC.tile(1, c0, c1),
                op=AluOpType.subtract),
                reads=[XT.R(kt), BC.R(1)], writes=[K.xf_reg], tag=tag)
            p.step("dve", lambda e, kt=kt, c0=c0, c1=c1: e.tensor_tensor(
                XNT.tile(kt, c0, c1), K.xf[0:128, 0:c1 - c0], BC.tile(0, c0, c1),
                op=AluOpType.mult),
                reads=[K.xf_reg, BC.R(0)], writes=XNT.wr(kt), tag=tag)


def rms_emit(K, SRC, XB, BC, DST, T, gcols, gcol_reg, eps=1e-6, tag="rms",
             LO=None):
    p = K.p
    RWA, RWB = K.rwa, K.rwb
    rA, rB = K.rwa_reg, K.rwb_reg
    for kt in range(8):
        p.step("act", lambda e, s=SRC.tile(kt), d=XB.tile(kt):
               e.activation(d, s, AF.Square),
               reads=[SRC.R(kt)], writes=XB.wr(kt), tag=tag + "sq")
    colsum(K, lambda kt, c0, c1: (XB.tile(kt, c0, c1), [XB.R(kt)], 128), 8,
           RWB[0:1, 0:T], [rB], 0, T, scale=1.0 / 1024.0, tag=tag + "s2")
    epscol = {1e-5: 0, 1e-6: 1, 1e-12: 2}[eps]
    p.step("act", lambda e: e.activation(RWB[0:1, 0:T], RWB[0:1, 0:T],
                                         AF.Ln, bias=K.cbuf["misc"][0:1, epscol:epscol + 1]),
           reads=[rB, K.creg["misc"]], writes=[rB], tag=tag)
    p.step("act", lambda e: e.activation(RWB[0:1, 0:T], RWB[0:1, 0:T],
                                         AF.Exp, scale=-0.5),
           reads=[rB], writes=[rB], tag=tag)
    bcast_row(K, RWB[0:1, 0:T], [rB], lambda a, b: BC.tile(0, a, b), BC.wr(0),
              0, T, tag=tag + "b0")
    for kt in range(8):
        for c0 in range(0, T, 1024):
            c1 = min(c0 + 1024, T)
            W_ = c1 - c0
            p.step("dve", lambda e, kt=kt, c0=c0, c1=c1, W_=W_: e.tensor_tensor(
                K.xf[0:128, 0:W_], SRC.tile(kt, c0, c1), BC.tile(0, c0, c1),
                op=AluOpType.mult),
                reads=[SRC.R(kt), BC.R(0)], writes=[K.xf_reg], tag=tag)
            if LO is None:
                p.step("dve", lambda e, kt=kt, c0=c0, c1=c1, W_=W_,
                       g=gcols[kt]: e.tensor_scalar(
                    DST.tile(kt, c0, c1), K.xf[0:128, 0:W_], g, None,
                    op0=AluOpType.mult),
                    reads=[K.xf_reg, gcol_reg], writes=DST.wr(kt), tag=tag)
            else:
                p.step("dve", lambda e, kt=kt, W_=W_, g=gcols[kt]:
                       e.tensor_scalar(
                    K.xf[0:128, 0:W_], K.xf[0:128, 0:W_], g, None,
                    op0=AluOpType.mult),
                    reads=[K.xf_reg, gcol_reg], writes=[K.xf_reg], tag=tag)
                p.step("dve", lambda e, kt=kt, c0=c0, c1=c1, W_=W_:
                       e.tensor_copy(DST.tile(kt, c0, c1), K.xf[0:128, 0:W_]),
                    reads=[K.xf_reg], writes=DST.wr(kt), tag=tag)
                p.step("act", lambda e, kt=kt, c0=c0, c1=c1:
                       e.copy(BC.tile(1, c0, c1), DST.tile(kt, c0, c1)),
                    reads=[DST.R(kt)], writes=[BC.R(1)], tag=tag)
                p.step("dve", lambda e, kt=kt, c0=c0, c1=c1, W_=W_:
                       e.tensor_tensor(LO.tile(kt, c0, c1), K.xf[0:128, 0:W_],
                                       BC.tile(1, c0, c1),
                                       op=AluOpType.subtract),
                    reads=[K.xf_reg, BC.R(1)], writes=LO.wr(kt), tag=tag)



def load_xt(K, x_ap, x_reg, XT, XN, T, tag="ldx"):
    p = K.p
    nt = T // 128
    for tt in range(nt):
        sl = tt % XN.nt
        p.dma("sp", XN.tile(sl), x_ap[tt * 128:(tt + 1) * 128, 0:1024],
              reads=[x_reg], writes=XN.wr(sl), tag=tag)
        for kt in range(8):
            bank = K.ps.take()
            p.step("pe", lambda e, s=XN.tile(sl, kt * 128, (kt + 1) * 128),
                   d=K.ps.ap(bank, 0, 128, 0, 128), i=K.cbuf["identf"]:
                   e.transpose(d, s, i),
                   reads=[XN.R(sl), K.creg["identf"]], writes=[K.ps.R(bank)],
                   tag=tag)
            p.step("dve", lambda e, s=K.ps.ap(bank, 0, 128, 0, 128),
                   d=XT.tile(kt, tt * 128, (tt + 1) * 128): e.tensor_copy(d, s),
                   reads=[K.ps.R(bank)], writes=XT.wr(kt), tag=tag)


def store_xt(K, XT, out_ap, out_reg, XN, T, tag="stx"):
    p = K.p
    nt = T // 128
    for tt in range(nt):
        sl = tt % XN.nt
        for kt in range(8):
            bank = K.ps.take()
            p.step("pe", lambda e, s=XT.tile(kt, tt * 128, (tt + 1) * 128),
                   d=K.ps.ap(bank, 0, 128, 0, 128), i=K.cbuf["identf"]:
                   e.transpose(d, s, i),
                   reads=[XT.R(kt), K.creg["identf"]], writes=[K.ps.R(bank)],
                   tag=tag)
            p.step("dve", lambda e, s=K.ps.ap(bank, 0, 128, 0, 128),
                   d=XN.tile(sl, kt * 128, (kt + 1) * 128): e.tensor_copy(d, s),
                   reads=[K.ps.R(bank)], writes=XN.wr(sl), tag=tag)
        p.dma("sp", out_ap[tt * 128:(tt + 1) * 128, 0:1024], XN.tile(sl),
              reads=[XN.R(sl)], writes=[out_reg], tag=tag)


def l0a_maps():
    qmap = {}
    for h in range(32):
        qmap[h] = (h // 4, 64 * ((h % 4) // 2))
    kmap = {}
    for h in range(32):
        s = h % 4
        q_ = h // 4
        kmap[h] = (2 * q_ + (s % 2), (0, 32, 64, 96)[s])
    return qmap, kmap


def _l0a_bound(K, QT, KT, SCR, T, tag):
    p = K.p
    C = K.cols
    cr = K.cols_reg
    first = True
    for qt in range(8):
        for c0 in range(0, T, 256):
            p.step("act", lambda e, s=QT.tile(qt, c0, c0 + 256),
                   d=SCR.tile(0, 0, 256): e.activation(d, s, AF.Square),
                   reads=[QT.R(qt)], writes=SCR.wr(0), tag=tag + "bq")
            bank = K.ps.take()
            p.step("pe", lambda e, o=K.ps.ap(bank, 0, 4, 0, 256),
                   l=K.cbuf["e4"][0:128, 0:4], r=SCR.tile(0, 0, 256):
                   e.matmul(o, l, r, start=True, stop=True),
                   reads=[K.creg["e4"], SCR.R(0)], writes=[K.ps.R(bank)],
                   tag=tag + "bq")
            if first:
                p.step("dve", lambda e, s=K.ps.ap(bank, 0, 4, 0, 256):
                       e.reduce_max(C[0:4, 0:1], s, axis=AX.X),
                       reads=[K.ps.R(bank)], writes=[cr], tag=tag + "bq")
                first = False
            else:
                p.step("dve", lambda e, s=K.ps.ap(bank, 0, 4, 0, 256):
                       e.reduce_max(C[0:4, 1:2], s, axis=AX.X),
                       reads=[K.ps.R(bank)], writes=[cr], tag=tag + "bq")
                p.step("dve", lambda e: e.tensor_tensor(
                    C[0:4, 0:1], C[0:4, 0:1], C[0:4, 1:2], op=AluOpType.max),
                    reads=[cr], writes=[cr], tag=tag + "bq")
    first = True
    for kt_ in range(16):
        for c0 in range(0, T, 256):
            p.step("act", lambda e, s=KT.tile(kt_, c0, c0 + 256),
                   d=SCR.tile(0, 0, 256): e.activation(d, s, AF.Square),
                   reads=[KT.R(kt_)], writes=[SCR.R(0)], tag=tag + "bk")
            bank = K.ps.take()
            p.step("pe", lambda e, o=K.ps.ap(bank, 0, 2, 0, 256),
                   l=K.cbuf["e2"][0:128, 0:2], r=SCR.tile(0, 0, 256):
                   e.matmul(o, l, r, start=True, stop=True),
                   reads=[K.creg["e2"], SCR.R(0)], writes=[K.ps.R(bank)],
                   tag=tag + "bk")
            if first:
                p.step("dve", lambda e, s=K.ps.ap(bank, 0, 2, 0, 256):
                       e.reduce_max(C[0:2, 2:3], s, axis=AX.X),
                       reads=[K.ps.R(bank)], writes=[cr], tag=tag + "bk")
                first = False
            else:
                p.step("dve", lambda e, s=K.ps.ap(bank, 0, 2, 0, 256):
                       e.reduce_max(C[0:2, 3:4], s, axis=AX.X),
                       reads=[K.ps.R(bank)], writes=[cr], tag=tag + "bk")
                p.step("dve", lambda e: e.tensor_tensor(
                    C[0:2, 2:3], C[0:2, 2:3], C[0:2, 3:4], op=AluOpType.max),
                    reads=[cr], writes=[cr], tag=tag + "bk")
    bank = K.ps.take()
    p.step("pe", lambda e, o=K.ps.ap(bank, 0, 1, 0, 4):
           e.transpose(o, C[0:4, 0:1], K.cbuf["identf"][0:4, 0:4]),
           reads=[cr, K.creg["identf"]], writes=[K.ps.R(bank)], tag=tag + "bt")
    p.step("dve", lambda e, s=K.ps.ap(bank, 0, 1, 0, 4):
           e.reduce_max(C[0:1, 4:5], s, axis=AX.X),
           reads=[K.ps.R(bank)], writes=[cr], tag=tag + "bt")
    bank2 = K.ps.take()
    p.step("pe", lambda e, o=K.ps.ap(bank2, 0, 1, 0, 2):
           e.transpose(o, C[0:2, 2:3], K.cbuf["identf"][0:2, 0:2]),
           reads=[cr, K.creg["identf"]], writes=[K.ps.R(bank2)], tag=tag + "bt")
    p.step("dve", lambda e, s=K.ps.ap(bank2, 0, 1, 0, 2):
           e.reduce_max(C[0:1, 5:6], s, axis=AX.X),
           reads=[K.ps.R(bank2)], writes=[cr], tag=tag + "bt")
    p.step("dve", lambda e: e.tensor_tensor(C[0:1, 4:5], C[0:1, 4:5],
                                            C[0:1, 5:6], op=AluOpType.mult),
           reads=[cr], writes=[cr], tag=tag + "bt")
    p.step("act", lambda e: e.activation(C[0:1, 5:6], C[0:1, 4:5], AF.Ln),
           reads=[cr], writes=[cr], tag=tag + "bt")
    p.step("act", lambda e: e.activation(C[0:1, 4:5], C[0:1, 5:6], AF.Exp,
                                         scale=0.5),
           reads=[cr], writes=[cr], tag=tag + "bt")
    bank3 = K.ps.take()
    p.step("pe", lambda e, o=K.ps.ap(bank3, 0, 128, 0, 1):
           e.matmul(o, K.cbuf["ones1x128"][0:1, 0:128], C[0:1, 4:5],
                    start=True, stop=True),
           reads=[K.creg["ones1x128"], cr], writes=[K.ps.R(bank3)], tag=tag + "bt")
    p.step("dve", lambda e, s=K.ps.ap(bank3, 0, 128, 0, 1):
           e.tensor_scalar(K.negc[0:128, 0:1], s, -1.0, None, op0=AluOpType.mult),
           reads=[K.ps.R(bank3)], writes=[K.negc_reg], tag=tag + "bt")


def _l0a_combine(K, j, obk, OTc, tag):
    p = K.p
    C, cr = K.cols, K.cols_reg
    xf, xr = K.xf, K.xf_reg
    RWA, rA = K.rwa, K.rwa_reg
    b1, b2 = obk[0], obk[1]
    RWB, rB = K.rwb, K.rwb_reg
    p.step("dve", lambda e, s=K.ps.ap(b1, 64, 65, 0, 256):
           e.reciprocal(RWA[0:1, 0:256], s),
           reads=[K.ps.R(b1)], writes=[rA], tag=tag + "cb")
    p.step("dve", lambda e, s=K.ps.ap(b2, 64, 65, 0, 256):
           e.reciprocal(RWB[0:1, 0:256], s),
           reads=[K.ps.R(b2)], writes=[rB], tag=tag + "cb")
    p.step("dve", lambda e: e.tensor_scalar(
        RWB[0:1, 0:256], RWB[0:1, 0:256], K.lam[0:1, 0:1], None,
        op0=AluOpType.mult),
        reads=[rB, K.lam_reg], writes=[rB], tag=tag + "cb")
    for (rowap, rreg, cofs) in ((RWA, rA, 0), (RWB, rB, 256)):
        bank = K.ps.take()
        p.step("pe", lambda e, o=K.ps.ap(bank, 0, 64, 0, 256),
               l=K.cbuf["ones1x128"][0:1, 0:64], r=rowap[0:1, 0:256]:
               e.matmul(o, l, r, start=True, stop=True),
               reads=[K.creg["ones1x128"], rreg], writes=[K.ps.R(bank)],
               tag=tag + "cb")
        p.step("dve", lambda e, s=K.ps.ap(bank, 0, 64, 0, 256), c=cofs:
               e.tensor_copy(xf[0:64, c:c + 256], s),
               reads=[K.ps.R(bank)], writes=[xr], tag=tag + "cb")
    p.step("dve", lambda e, s=K.ps.ap(b1, 0, 64, 0, 256):
           e.tensor_tensor(xf[0:64, 512:768], s, xf[0:64, 0:256],
                           op=AluOpType.mult),
           reads=[K.ps.R(b1), xr], writes=[xr], tag=tag + "cb")
    p.step("dve", lambda e, s=K.ps.ap(b2, 0, 64, 0, 256):
           e.tensor_tensor(xf[0:64, 768:1024], s, xf[0:64, 256:512],
                           op=AluOpType.mult),
           reads=[K.ps.R(b2), xr], writes=[xr], tag=tag + "cb")
    p.step("dve", lambda e: e.tensor_tensor(
        xf[0:64, 512:768], xf[0:64, 512:768], xf[0:64, 768:1024],
        op=AluOpType.subtract),
        reads=[xr], writes=[xr], tag=tag + "cb")
    p.step("act", lambda e: e.activation(K.sq64[0:64, 0:256],
                                         xf[0:64, 512:768], AF.Square),
           reads=[xr], writes=[K.sq64_reg], tag=tag + "cb")
    bank = K.ps.take()
    p.step("pe", lambda e, o=K.ps.ap(bank, 0, 1, 0, 256):
           e.matmul(o, K.cbuf["ones64"][0:64, 0:1], K.sq64[0:64, 0:256],
                    start=True, stop=True),
           reads=[K.creg["ones64"], K.sq64_reg], writes=[K.ps.R(bank)],
           tag=tag + "cb")
    p.step("act", lambda e, s=K.ps.ap(bank, 0, 1, 0, 256):
           e.activation(RWA[0:1, 0:256], s, AF.Ln, scale=1.0 / 64.0,
                        bias=K.cbuf["misc"][0:1, 1:2]),
           reads=[K.ps.R(bank), K.creg["misc"]], writes=[rA], tag=tag + "cb")
    p.step("act", lambda e: e.activation(RWA[0:1, 0:256], RWA[0:1, 0:256],
                                         AF.Exp, scale=-0.5),
           reads=[rA], writes=[rA], tag=tag + "cb")
    bank2 = K.ps.take()
    p.step("pe", lambda e, o=K.ps.ap(bank2, 0, 64, 0, 256):
           e.matmul(o, K.cbuf["ones1x128"][0:1, 0:64], RWA[0:1, 0:256],
                    start=True, stop=True),
           reads=[K.creg["ones1x128"], rA], writes=[K.ps.R(bank2)], tag=tag + "cb")
    p.step("dve", lambda e, s=K.ps.ap(bank2, 0, 64, 0, 256):
           e.tensor_tensor(xf[0:64, 768:1024], xf[0:64, 512:768], s,
                           op=AluOpType.mult),
           reads=[K.ps.R(bank2), xr], writes=[xr], tag=tag + "cb")
    rb = 64 * (j % 2)
    p.step("dve", lambda e, d=OTc.tile(j // 2, 0, 256, rb, rb + 64):
           e.tensor_scalar(d, xf[0:64, 768:1024], K.rsc[0:64, 0:1], None,
                           op0=AluOpType.mult),
           reads=[xr, K.rsc_reg], writes=OTc.wr(j // 2), tag=tag + "cb")


def stage_l0_attn(K, XNT, T, W, ao_sp, xnt_sp, keeps=(), tag="l0a"):
    p = K.p
    qmap, kmap = l0a_maps()
    NKB = T // 128
    inv_sqrt_h = 1.0 / math.sqrt(32.0)
    keeps = list(keeps)

    V1 = K.arena.carve(keeps + [XNT], [
        ("wring", 4, 2048, BF16), ("wvs", 8, 1024, BF16),
        ("qt", 8, T, BF16), ("vn", NKB, 1040, BF16)])
    QT, VN = V1["qt"], V1["vn"]
    wsq = WStream(K, tag + "q", W["wqkv"][0], W["wqkv"][1], 1024, 1024, False,
                  V1["wring"], mg=2)
    linear(K, wsq, src_from_view(XNT, T),
           ep_to_view(K, QT, dtype_scale=inv_sqrt_h, tag=tag + "qev"),
           0, T, tag=tag + "q")
    WVS = V1["wvs"]
    for t_ in range(NKB):
        p.step("dve", lambda e, t_=t_: e.memset(VN.tile(t_), 1.0),
               reads=[], writes=VN.wr(t_), tag=tag + "vo")
    for kt in range(8):
        p.dma("pool", WVS.tile(kt), W["wqkv"][0][kt * 128:(kt + 1) * 128, 2048:3072],
              reads=[W["wqkv"][1]], writes=WVS.wr(kt), tag=tag + "vw")
    for tt in range(NKB):
        banks = [K.ps.take(hold=True), K.ps.take(hold=True)]
        for kt in range(8):
            for half in range(2):
                p.step("pe", lambda e, o=K.ps.ap(banks[half], 0, 128, 0, 512),
                       l=XNT.tile(kt, tt * 128, (tt + 1) * 128),
                       r=WVS.tile(kt, half * 512, (half + 1) * 512),
                       st=(kt == 0), sp=(kt == 7):
                       e.matmul(o, l, r, start=st, stop=sp),
                       reads=[XNT.R(kt), WVS.R(kt)],
                       writes=[K.ps.R(banks[half])], tag=tag + "v")
        for half in range(2):
            src_ = K.ps.ap(banks[half], 0, 128, 0, 512).rearrange(
                "p (h c) -> p h c", c=64)
            dst = VN.tile(tt, 520 * half, 520 * half + 520).rearrange(
                "p (h c) -> p h c", c=65)[:, :, 0:64]
            p.step("dve", lambda e, s=src_, d=dst: e.tensor_copy(d, s),
                   reads=[K.ps.R(banks[half])], writes=[VN.R(tt)], tag=tag + "vev")
        K.ps.release(banks[0]); K.ps.release(banks[1])

    V2 = K.arena.carve(keeps + [XNT, QT, VN], [
        ("wring", 2, 2048, BF16), ("kt", 16, T, BF16)])
    KT = V2["kt"]
    for t_ in range(16):
        p.step("dve", lambda e, t_=t_: e.memset(KT.tile(t_), 0.0),
               reads=[], writes=KT.wr(t_), tag=tag + "kz")
    wsk = WStream(K, tag + "k", W["wqkv"][0], W["wqkv"][1], 1024, 1024, False,
                  V2["wring"], mg=4, mw=64, col0=1024)

    def ep_k(mi, msz, bank, t0, t1):
        for half in range(2):
            h = 2 * mi + half
            (ktile, dr) = kmap[h]
            if dr < 96:
                s_ = K.ps.ap(bank, 32 * half, 32 * half + 32, 0, t1 - t0)
                d_ = KT.tile(ktile, t0, t1, dr, dr + 32)
                K.p.step("dve", lambda e, s=s_, d=d_: e.tensor_copy(d, s),
                         reads=[K.ps.R(bank)], writes=[KT.R(ktile)],
                         tag=tag + "kev")
            else:
                s_ = K.ps.ap(bank, 0, 64, 0, t1 - t0)
                d_ = KT.tile(ktile, t0, t1, 64, 128)
                K.p.step("dve", lambda e, s=s_, d=d_: e.tensor_copy(d, s),
                         reads=[K.ps.R(bank)], writes=[KT.R(ktile)],
                         tag=tag + "kev")
                K.p.step("dve", lambda e, d=KT.tile(ktile, t0, t1, 64, 96):
                         e.memset(d, 0.0),
                         reads=[], writes=[KT.R(ktile)], tag=tag + "kz3")
    linear(K, wsk, src_from_view(XNT, T), ep_k, 0, T, tag=tag + "k")

    for kt in range(8):
        p.dma("sp", xnt_sp[0][kt * 128:(kt + 1) * 128, 0:T], XNT.tile(kt),
              reads=[XNT.R(kt)], writes=[xnt_sp[1]], tag=tag + "xsp")

    V3 = K.arena.carve(keeps + [QT, KT, VN], [
        ("wring", 4, 2048, BF16), ("p1", NKB, 256, BF16),
        ("p2", NKB, 256, BF16), ("otc", 8, 256, BF16)])
    P1, P2, OTc = V3["p1"], V3["p2"], V3["otc"]
    _l0a_bound(K, QT, KT, P1, T, tag)
    wsp = WStream(K, tag + "p", W["wproj"][0], W["wproj"][1], 1024, 1024, False,
                  V3["wring"], mg=2)
    seq = 0
    for qc in range(T // 256):
        t0, t1 = qc * 256, (qc + 1) * 256
        for j in range(16):
            obk = {}
            for pas in range(2):
                h = j + 16 * pas
                (qt_t, qb) = qmap[h]
                (ktile, _dr) = kmap[h]
                P = P1 if pas == 0 else P2
                for kb in range(NKB):
                    sbank = K.ps.take()
                    p.step("pe", lambda e, o=K.ps.ap(sbank, 0, 128, 0, 256),
                           l=KT.tile(ktile, kb * 128, (kb + 1) * 128, qb, qb + 64),
                           r=QT.tile(qt_t, t0, t1, qb, qb + 64):
                           e.matmul(o, l, r, start=True, stop=True),
                           reads=[KT.R(ktile), QT.R(qt_t)],
                           writes=[K.ps.R(sbank)], tag=tag + "s")
                    p.step("act", lambda e, s=K.ps.ap(sbank, 0, 128, 0, 256),
                           d=P.tile(kb):
                           e.activation(d, s, AF.Exp, bias=K.negc[0:128, 0:1]),
                           reads=[K.ps.R(sbank), K.negc_reg], writes=P.wr(kb),
                           tag=tag + "e")
                obank = K.ps.take(hold=True)
                for kb in range(NKB):
                    p.step("pe", lambda e, o=K.ps.ap(obank, 0, 65, 0, 256),
                           l=VN.tile(kb, 65 * j, 65 * j + 65), r=P.tile(kb),
                           st=(kb == 0), sp=(kb == NKB - 1):
                           e.matmul(o, l, r, start=st, stop=sp),
                           reads=[VN.R(kb), P.R(kb)], writes=[K.ps.R(obank)],
                           tag=tag + "pv")
                obk[pas] = obank
            _l0a_combine(K, j, obk, OTc, tag)
            K.ps.release(obk[0])
            K.ps.release(obk[1])

        def ep_ao(mi, msz, bank, a, b):
            stg = (K.sqs, K.sqs_reg) if mi % 2 == 0 else (K.pb, K.pb_reg)
            K.p.step("dve", lambda e, s=K.ps.ap(bank, 0, msz, 0, b - a),
                     d=stg[0][0:msz, 0:b - a]: e.tensor_copy(d, s),
                     reads=[K.ps.R(bank)], writes=[stg[1]], tag=tag + "ao")
            K.p.dma("sp", ao_sp[0][mi * 128:mi * 128 + msz, t0 + a:t0 + b],
                    stg[0][0:msz, 0:b - a],
                    reads=[stg[1]], writes=[ao_sp[1]], tag=tag + "aosp")
        linear(K, wsp, src_from_view(OTc, 256), ep_ao, 0, 256, chunk=256,
               tag=tag + "proj", seq0=seq)
        seq += wsp.n_groups



CONST_SPECS = [
    ("ident", [128, 128], BF16), ("identf", [128, 128], F32),
    ("ones1x128", [32, 128], F32), ("onescol", [128, 32], BF16),
    ("onescolf", [128, 32], F32), ("ones64", [64, 32], BF16),
    ("e2", [128, 32], BF16), ("e4", [128, 32], BF16),
    ("iota8", [128, 8], F32), ("misc", [128, 8], F32),
    ("permf", [128, 128], BF16), ("permh", [128, 128], BF16),
    ("exp2", [32, 128], BF16),
]


def const_arrays(T):
    base = build_constants(T)
    e4 = np.zeros((128, 4), ml_dtypes.bfloat16)
    for j in range(4):
        e4[32 * j:32 * (j + 1), j] = 1
    ones1x = np.zeros((32, 128), np.float32); ones1x[0, :] = 1.0
    onescol = np.zeros((128, 32), ml_dtypes.bfloat16); onescol[:, 0] = 1
    onescolf = np.zeros((128, 32), np.float32); onescolf[:, 0] = 1.0
    ones64 = np.zeros((64, 32), ml_dtypes.bfloat16); ones64[:, 0] = 1
    e2p = np.zeros((128, 32), ml_dtypes.bfloat16); e2p[:, 0:2] = base["c_e2"]
    e4p = np.zeros((128, 32), ml_dtypes.bfloat16); e4p[:, 0:4] = e4
    return {
        "c_ident": base["c_ident"],
        "c_identf": np.eye(128, dtype=np.float32),
        "c_ones1x128": ones1x,
        "c_onescol": onescol,
        "c_onescolf": onescolf,
        "c_ones64": ones64,
        "c_e2": e2p,
        "c_e4": e4p,
        "c_iota8": base["c_iota8"],
        "c_permf": np.asarray(base["c_permf"]),
        "c_exp2": (lambda a: a)(np.asarray(
            [[1.0 if (m // 64) == k2 else 0.0 for m in range(128)]
             if k2 < 2 else [0.0] * 128 for k2 in range(32)],
            ml_dtypes.bfloat16)),
        "c_permh": np.asarray(base["c_permh"]),
        "c_ropec": base["c_ropec"],
        "c_ropes": base["c_ropes"],
        "c_misc": np.broadcast_to(np.array(
            [1e-5, 1e-6, 1e-12, 1.0, 1e-38, 0.0, 0.0, 0.0], np.float32),
            (128, 8)).copy(),
        "c_xposcs": np.asarray(base["c_xposcs"]),
        "c_xposss": np.asarray(base["c_xposss"]),
        "c_xposck": np.asarray(base["c_xposck"]),
        "c_xpossk": np.asarray(base["c_xpossk"]),
        "c_cmask": np.asarray(base["c_cmask"]),
        "c_lmask": base["c_lmask"],
        "c_lmask0": base["c_lmask0"],
    }


def make_ctx(nc, p, cfg):
    K = KC(nc, p, cfg)
    K.ps = PsumPool(nc)
    K.cbuf = {}
    K.creg = {}
    for (name, shape, dt) in CONST_SPECS:
        d = nc.dram_tensor("c_" + name, shape, dt, kind="ExternalInput").ap()
        t = nc.alloc_sbuf_tensor("cb_" + name, shape, dt)
        K.cbuf[name] = t.ap()
        K.creg[name] = Region("c_" + name)
        p.dma("sp", t.ap(), d, reads=[], writes=[K.creg[name]], tag="const")
    T = cfg["T"]
    K.rwa = nc.alloc_sbuf_tensor("rwa", [128, T], F32).ap()
    K.rwa_reg = Region("rwa")
    K.rwb = nc.alloc_sbuf_tensor("rwb", [128, T], F32).ap()
    K.rwb_reg = Region("rwb")

    K.cols = nc.alloc_sbuf_tensor("colsb", [128, 64], F32).ap()
    K.cols_reg = Region("cols")
    K.xf = nc.alloc_sbuf_tensor("xf", [128, 1024], F32).ap()
    K.xf_reg = Region("xf")
    K.negc = nc.alloc_sbuf_tensor("negc", [128, 4], F32).ap()
    K.negc_reg = Region("negc")
    K.lam = nc.alloc_sbuf_tensor("lamt", [1, 4], F32).ap()
    K.lam_reg = Region("lam")
    K.rsc = nc.alloc_sbuf_tensor("rsct", [64, 4], F32).ap()
    K.rsc_reg = Region("rsc")
    K.sq64 = nc.alloc_sbuf_tensor("sq64", [64, 256], BF16).ap()
    K.sq64_reg = Region("sq64")
    K.ebt = nc.alloc_sbuf_tensor("ebt", [128, 8], F32).ap()
    K.ebt_reg = Region("ebt")
    K.sqs = nc.alloc_sbuf_tensor("sqs", [128, 512], BF16).ap()
    K.sqs_reg = Region("sqs")
    K.pb = nc.alloc_sbuf_tensor("pbt", [128, 512], BF16).ap()
    K.pb_reg = Region("pb")
    K.rlkf = nc.alloc_sbuf_tensor("rlkf", [32, 256], F32).ap()
    K.rlk = nc.alloc_sbuf_tensor("rlkb", [32, 256], BF16).ap()
    K.rlk_reg = Region("rlk")
    for (ap_, reg_) in [(K.rwa, K.rwa_reg), (K.rwb, K.rwb_reg),
                        (K.cols, K.cols_reg), (K.xf, K.xf_reg),
                        (K.negc, K.negc_reg), (K.lam, K.lam_reg),
                        (K.rsc, K.rsc_reg), (K.sq64, K.sq64_reg),
                        (K.ebt, K.ebt_reg),
                        (K.sqs, K.sqs_reg), (K.pb, K.pb_reg),
                        (K.rlkf, K.rlk_reg), (K.rlk, K.rlk_reg)]:
        p.step("dve", lambda e, a=ap_: e.memset(a, 0.0), reads=[],
               writes=[reg_], tag="init")
    return K


def precompute_l0a_scalars(K, lam_aps, lam_regs, rms_ap, rms_reg, tag="pre"):
    p = K.p
    C, cr = K.cols, K.cols_reg
    for i, (ap_, reg_) in enumerate(zip(lam_aps, lam_regs)):
        p.dma("sp", C[0:32, 8 + i:9 + i], ap_, reads=[reg_], writes=[cr], tag=tag)
    p.dma("sp", K.rsc[0:64, 1:2], rms_ap, reads=[rms_reg], writes=[K.rsc_reg],
          tag=tag)
    p.step("dve", lambda e: e.tensor_scalar(
        K.rsc[0:64, 0:1], K.rsc[0:64, 1:2], 1.0 - LAMBDA_INIT, None,
        op0=AluOpType.mult),
        reads=[K.rsc_reg], writes=[K.rsc_reg], tag=tag)
    for pair in range(2):
        p.step("dve", lambda e, a=8 + 2 * pair, pr=pair: e.tensor_tensor(
            C[0:32, 12 + pr:13 + pr], C[0:32, a:a + 1], C[0:32, a + 1:a + 2],
            op=AluOpType.mult),
            reads=[cr], writes=[cr], tag=tag)
        bank = K.ps.take()
        p.step("pe", lambda e, o=K.ps.ap(bank, 0, 1, 0, 1), pair=pair:
               e.matmul(o, K.cbuf["onescolf"][0:32, 0:1],
                        C[0:32, 12 + pair:13 + pair], start=True, stop=True),
               reads=[K.creg["onescolf"], cr], writes=[K.ps.R(bank)], tag=tag)
        p.step("act", lambda e, s=K.ps.ap(bank, 0, 1, 0, 1), pair=pair:
               e.activation(K.lam[0:1, 1 + pair:2 + pair], s, AF.Exp),
               reads=[K.ps.R(bank)], writes=[K.lam_reg], tag=tag)
    p.step("dve", lambda e: e.tensor_tensor(
        K.lam[0:1, 0:1], K.lam[0:1, 1:2], K.lam[0:1, 2:3], op=AluOpType.subtract),
        reads=[K.lam_reg], writes=[K.lam_reg], tag=tag)
    p.step("dve", lambda e: e.tensor_scalar(
        K.lam[0:1, 0:1], K.lam[0:1, 0:1], LAMBDA_INIT, None, op0=AluOpType.add),
        reads=[K.lam_reg], writes=[K.lam_reg], tag=tag)



def bound_sumsq_max(K, tiles_fn, ntiles, T, cell, tmpcell, tag, chunk=512):
    p = K.p
    C, cr = K.cols, K.cols_reg
    first = True
    for c0 in range(0, T, chunk):
        c1 = min(c0 + chunk, T)
        bank = K.ps.take()
        for i in range(ntiles):
            ap_, reg_ = tiles_fn(i, c0, c1)
            p.step("act", lambda e, s=ap_, d=K.sqs[0:128, 0:c1 - c0]:
                   e.activation(d, s, AF.Square),
                   reads=[reg_], writes=[K.sqs_reg], tag=tag)
            p.step("pe", lambda e, o=K.ps.ap(bank, 0, 1, 0, c1 - c0),
                   st=(i == 0), sp=(i == ntiles - 1):
                   e.matmul(o, K.cbuf["onescol"][0:128, 0:1],
                            K.sqs[0:128, 0:c1 - c0], start=st, stop=sp),
                   reads=[K.creg["onescol"], K.sqs_reg], writes=[K.ps.R(bank)],
                   tag=tag)
        dstcell = cell if first else tmpcell
        p.step("dve", lambda e, s=K.ps.ap(bank, 0, 1, 0, c1 - c0), d=dstcell:
               e.reduce_max(d, s, axis=AX.X),
               reads=[K.ps.R(bank)], writes=[cr], tag=tag)
        if not first:
            p.step("dve", lambda e: e.tensor_tensor(cell, cell, tmpcell,
                                                    op=AluOpType.max),
                   reads=[cr], writes=[cr], tag=tag)
        first = False


def bound_to_negcol(K, qcell, kcell, tmpcell, negcol, scale2=1.0, tag="bnd"):
    p = K.p
    cr = K.cols_reg
    p.step("dve", lambda e: e.tensor_tensor(tmpcell, qcell, kcell,
                                            op=AluOpType.mult),
           reads=[cr], writes=[cr], tag=tag)
    if scale2 != 1.0:
        p.step("dve", lambda e: e.tensor_scalar(tmpcell, tmpcell, scale2, None,
                                                op0=AluOpType.mult),
               reads=[cr], writes=[cr], tag=tag)
    p.step("act", lambda e: e.activation(tmpcell, tmpcell, AF.Ln),
           reads=[cr], writes=[cr], tag=tag)
    p.step("act", lambda e: e.activation(tmpcell, tmpcell, AF.Exp, scale=0.5),
           reads=[cr], writes=[cr], tag=tag)
    bank = K.ps.take()
    p.step("pe", lambda e, o=K.ps.ap(bank, 0, 128, 0, 1):
           e.matmul(o, K.cbuf["ones1x128"][0:1, 0:128], tmpcell,
                    start=True, stop=True),
           reads=[K.creg["ones1x128"], cr], writes=[K.ps.R(bank)], tag=tag)
    p.step("dve", lambda e, s=K.ps.ap(bank, 0, 128, 0, 1), d=negcol:
           e.tensor_scalar(d, s, -1.0, None, op0=AluOpType.mult),
           reads=[K.ps.R(bank)], writes=[K.negc_reg], tag=tag)


def load_cols(K, src_ap, src_reg, col0, n, tag="gc"):
    cols = []
    for kt in range(n):
        K.p.dma("sp", K.cols[0:128, col0 + kt:col0 + kt + 1],
                src_ap[kt * 128:(kt + 1) * 128].unsqueeze(1),
                reads=[src_reg], writes=[K.cols_reg], tag=tag)
        cols.append(K.cols[0:128, col0 + kt:col0 + kt + 1])
    return cols



def slice3d(ap3, e, rows, cols):
    return bass.AP(ap3.tensor, e * rows * cols, [[cols, rows], [1, cols]])


def stage_l0_ffn(K, XNT, Y, T, W, keeps=(), tag="l0f"):
    p = K.p
    NKB = T // 128
    keeps = [Y] + list(keeps)

    Va = K.arena.carve(keeps + [XNT], [
        ("xb", 8, T, BF16), ("bc", 2, T, F32), ("nx", 8, T, BF16)])
    for kt in range(8):
        p.step("act", lambda e, kt=kt, d=Y.tile(kt), s=XNT.tile(kt):
               e.copy(d, s), reads=[XNT.R(kt)], writes=Y.wr(kt), tag=tag + "sd")
    g1c = load_cols(K, W["g1"][0], W["g1"][1], 16, 8, tag + "g1")
    NX = Va["nx"]
    rms_emit(K, Y, Va["xb"], Va["bc"], NX, T, g1c, K.cols_reg, eps=1e-6,
             tag=tag + "r1")

    Vb = K.arena.carve(keeps + [NX], [
        ("wring", 4, 2048, BF16), ("wtmp", 2, 1024, BF16),
        ("qt2", 8, T, BF16), ("kct", 2, T, BF16), ("vct", 2, T, BF16),
        ("vcn", NKB, 256, BF16)])
    QT2, KCT, VCT, VCN = Vb["qt2"], Vb["kct"], Vb["vct"], Vb["vcn"]
    wsq = WStream(K, tag + "wq", W["Wq"][0], W["Wq"][1], 1024, 1024, True,
                  Vb["wring"], wtmp=Vb["wtmp"], mg=2)
    linear(K, wsq, src_from_view(NX, T),
           ep_to_view(K, QT2, dtype_scale=1.0 / 16.0, tag=tag + "qe"),
           0, T, tag=tag + "q")
    wskc = WStream(K, tag + "wk", W["Wkc"][0], W["Wkc"][1], 1024, 256, True,
                   Vb["wring"], wtmp=Vb["wtmp"], mg=2)
    linear(K, wskc, src_from_view(NX, T),
           ep_to_view(K, KCT, tag=tag + "ke"), 0, T, tag=tag + "k")
    wsvc = WStream(K, tag + "wv", W["Wvc"][0], W["Wvc"][1], 1024, 256, True,
                   Vb["wring"], wtmp=Vb["wtmp"], mg=2)
    linear(K, wsvc, src_from_view(NX, T),
           ep_to_view(K, VCT, tag=tag + "ve"), 0, T, tag=tag + "v")
    for dt in range(2):
        for tb in range(NKB):
            bank = K.ps.take()
            p.step("pe", lambda e, s=VCT.tile(dt, tb * 128, (tb + 1) * 128),
                   o=K.ps.bf16(bank, 0, 128, 0, 128), i=K.cbuf["ident"]:
                   e.transpose(o, s, i),
                   reads=[VCT.R(dt), K.creg["ident"]], writes=[K.ps.R(bank)],
                   tag=tag + "vt")
            p.step("dve", lambda e, s=K.ps.bf16(bank, 0, 128, 0, 128),
                   d=VCN.tile(tb, dt * 128, (dt + 1) * 128):
                   e.tensor_copy(d, s),
                   reads=[K.ps.R(bank)], writes=VCN.wr(tb), tag=tag + "vt")

    Vc_ = K.arena.carve(keeps + [QT2, KCT, VCN], [
        ("p", NKB, 512, BF16), ("ot2", 8, T, BF16)])
    P, OT2 = Vc_["p"], Vc_["ot2"]
    C = K.cols
    bound_sumsq_max(K, lambda i, c0, c1: (KCT.tile(i, c0, c1), KCT.R(i)), 2, T,
                    C[0:1, 44:45], C[0:1, 45:46], tag + "bk")
    for h in range(4):
        bound_sumsq_max(K, lambda i, c0, c1, h=h: (QT2.tile(2 * h + i, c0, c1),
                                                   QT2.R(2 * h + i)), 2, T,
                        C[0:1, 40:41], C[0:1, 41:42], tag + f"bq{h}")
        bound_to_negcol(K, C[0:1, 40:41], C[0:1, 44:45], C[0:1, 41:42],
                        K.negc[0:128, h:h + 1], tag=tag + f"bn{h}")
    for qc in range(T // 512):
        t0, t1 = qc * 512, (qc + 1) * 512
        for h in range(4):
            for kb in range(NKB):
                sbank = K.ps.take()
                for dt in range(2):
                    p.step("pe", lambda e, o=K.ps.ap(sbank, 0, 128, 0, 512),
                           l=KCT.tile(dt, kb * 128, (kb + 1) * 128),
                           r=QT2.tile(2 * h + dt, t0, t1),
                           st=(dt == 0), sp=(dt == 1):
                           e.matmul(o, l, r, start=st, stop=sp),
                           reads=[KCT.R(dt), QT2.R(2 * h + dt)],
                           writes=[K.ps.R(sbank)], tag=tag + "s")
                p.step("act", lambda e, s=K.ps.ap(sbank, 0, 128, 0, 512),
                       d=P.tile(kb), h=h:
                       e.activation(d, s, AF.Exp, bias=K.negc[0:128, h:h + 1]),
                       reads=[K.ps.R(sbank), K.negc_reg], writes=P.wr(kb),
                       tag=tag + "e")
            obanks = [K.ps.take(hold=True), K.ps.take(hold=True)]
            dbank = K.ps.take(hold=True)
            for kb in range(NKB):
                for half in range(2):
                    p.step("pe", lambda e, o=K.ps.ap(obanks[half], 0, 128, 0, 512),
                           l=VCN.tile(kb, half * 128, (half + 1) * 128),
                           r=P.tile(kb), st=(kb == 0), sp=(kb == NKB - 1):
                           e.matmul(o, l, r, start=st, stop=sp),
                           reads=[VCN.R(kb), P.R(kb)],
                           writes=[K.ps.R(obanks[half])], tag=tag + "pv")
                p.step("pe", lambda e, o=K.ps.ap(dbank, 0, 1, 0, 512),
                       l=K.cbuf["onescol"][0:128, 0:1], r=P.tile(kb),
                       st=(kb == 0), sp=(kb == NKB - 1):
                       e.matmul(o, l, r, start=st, stop=sp),
                       reads=[K.creg["onescol"], P.R(kb)],
                       writes=[K.ps.R(dbank)], tag=tag + "dn")
            p.step("dve", lambda e, s=K.ps.ap(dbank, 0, 1, 0, 512):
                   e.reciprocal(K.rwa[0:1, 0:512], s),
                   reads=[K.ps.R(dbank)], writes=[K.rwa_reg], tag=tag + "nr")
            bank2 = K.ps.take()
            p.step("pe", lambda e, o=K.ps.ap(bank2, 0, 128, 0, 512):
                   e.matmul(o, K.cbuf["ones1x128"][0:1, 0:128],
                            K.rwa[0:1, 0:512], start=True, stop=True),
                   reads=[K.creg["ones1x128"], K.rwa_reg],
                   writes=[K.ps.R(bank2)], tag=tag + "nb")
            p.step("dve", lambda e, s=K.ps.ap(bank2, 0, 128, 0, 512):
                   e.tensor_copy(K.xf[0:128, 0:512], s),
                   reads=[K.ps.R(bank2)], writes=[K.xf_reg], tag=tag + "nb")
            for half in range(2):
                p.step("dve", lambda e, s=K.ps.ap(obanks[half], 0, 128, 0, 512),
                       d=OT2.tile(2 * h + half, t0, t1):
                       e.tensor_tensor(d, s, K.xf[0:128, 0:512],
                                       op=AluOpType.mult),
                       reads=[K.ps.R(obanks[half]), K.xf_reg],
                       writes=OT2.wr(2 * h + half), tag=tag + "oe")
            K.ps.release(obanks[0]); K.ps.release(obanks[1]); K.ps.release(dbank)

    Vd = K.arena.carve(keeps + [OT2], [
        ("wring", 4, 2048, BF16), ("wtmp", 2, 1024, BF16)])
    wso = WStream(K, tag + "wo", W["Wo"][0], W["Wo"][1], 1024, 1024, True,
                  Vd["wring"], wtmp=Vd["wtmp"], mg=2)

    def ep_acc_y(mi, msz, bank, t0, t1):
        K.p.step("dve", lambda e, s=K.ps.ap(bank, 0, msz, 0, t1 - t0),
                 d=Y.tile(mi, t0, t1): e.tensor_tensor(d, d, s, op=AluOpType.add),
                 reads=[K.ps.R(bank), Y.R(mi)], writes=[Y.R(mi)], tag=tag + "ya")
    linear(K, wso, src_from_view(OT2, T), ep_acc_y, 0, T, tag=tag + "wo")

    Ve = K.arena.carve(keeps, [
        ("xb", 8, T, BF16), ("bc", 2, T, F32),
        ("xn2", 8, T, BF16), ("xn2lo", 8, T, BF16)])
    XN2, XN2LO = Ve["xn2"], Ve["xn2lo"]
    g2c = load_cols(K, W["g2"][0], W["g2"][1], 24, 8, tag + "g2")
    rms_emit(K, Y, Ve["xb"], Ve["bc"], XN2, T, g2c, K.cols_reg, eps=1e-6,
             tag=tag + "r2", LO=XN2LO)

    Vf = K.arena.carve(keeps + [XN2, XN2LO], [
        ("cmb", 1, T, F32), ("cmbr", 1, T, F32),
        ("wrgh", 8, 8, BF16), ("wrgl", 8, 8, BF16),
        ("wrsv", 1, 1024, F32)])
    Vf["xn2"] = XN2
    Vf["xn2lo"] = XN2LO
    _l0f_router(K, Vf, T, W, tag)
    CMB = Vf["cmb"]

    Vg = K.arena.carve(keeps + [XN2, CMB], [
        ("wring", 2, 2048, BF16), ("wring2", 2, 2048, BF16),
        ("wtmp", 2, 1024, BF16), ("cbb", 1, T, BF16),
        ("vstg", 2, T, BF16), ("hv", 8, T, BF16)])
    CBB, VSTG, HV = Vg["cbb"], Vg["vstg"], Vg["hv"]
    for e_ in range(9):
        is_shared = (e_ == 8)
        if is_shared:
            w1ap, w1rg = W["sW1"]
            w2ap, w2rg = W["sW2"]
            w3ap, w3rg = W["sW3"]
        else:
            w1ap, w1rg = W["eW1"][0][e_], W["eW1"][1]
            w2ap, w2rg = W["eW2"][0][e_], W["eW2"][1]
            w3ap, w3rg = W["eW3"][0][e_], W["eW3"][1]
        if not is_shared:
            p.dma("sp", K.rwa[0:1, 0:T], CMB.tile(0, 0, T, e_, e_ + 1),
                  reads=[CMB.R(0)], writes=[K.rwa_reg], tag=tag + f"cbr{e_}")
            for c0 in range(0, T, 512):
                c1 = min(c0 + 512, T)
                bank = K.ps.take()
                p.step("pe", lambda e, o=K.ps.ap(bank, 0, 128, 0, c1 - c0),
                       l=K.cbuf["ones1x128"][0:1, 0:128],
                       r=K.rwa[0:1, c0:c1]:
                       e.matmul(o, l, r, start=True, stop=True),
                       reads=[K.creg["ones1x128"], K.rwa_reg],
                       writes=[K.ps.R(bank)], tag=tag + "cb")
                p.step("dve", lambda e, s=K.ps.ap(bank, 0, 128, 0, c1 - c0),
                       d=CBB.tile(0, c0, c1): e.tensor_copy(d, s),
                       reads=[K.ps.R(bank)], writes=CBB.wr(0), tag=tag + "cb")
        ws3 = WStream(K, tag + f"w3_{e_}", w3ap, w3rg, 1024, 1024, True,
                      Vg["wring"], wtmp=Vg["wtmp"], mg=2)
        ws1 = WStream(K, tag + f"w1_{e_}", w1ap, w1rg, 1024, 1024, True,
                      Vg["wring2"], wtmp=Vg["wtmp"], mg=2)
        chunks = [(c0, min(c0 + 512, T)) for c0 in range(0, T, 512)]
        for g in range(ws3.n_groups):
            s3, r3 = ws3.stage_group(g)
            s1, r1 = ws1.stage_group(g)
            for m in range(2):
                mi = g * 2 + m
                for (t0, t1) in chunks:
                    bank = K.ps.take()
                    for kt in range(8):
                        p.step("pe", lambda e, o=K.ps.ap(bank, 0, 128, 0, t1 - t0),
                               l=ws3.entry_ap(s3, m, kt),
                               r=XN2.tile(kt, t0, t1),
                               st=(kt == 0), sp=(kt == 7):
                               e.matmul(o, l, r, start=st, stop=sp),
                               reads=[r3, XN2.R(kt)], writes=[K.ps.R(bank)],
                               tag=tag + "x3")
                    p.step("dve", lambda e, s=K.ps.ap(bank, 0, 128, 0, t1 - t0),
                           d=VSTG.tile(m, t0, t1): e.tensor_copy(d, s),
                           reads=[K.ps.R(bank)], writes=[VSTG.R(m)],
                           tag=tag + "x3e")
                for (t0, t1) in chunks:
                    bank = K.ps.take()
                    for kt in range(8):
                        p.step("pe", lambda e, o=K.ps.ap(bank, 0, 128, 0, t1 - t0),
                               l=ws1.entry_ap(s1, m, kt),
                               r=XN2.tile(kt, t0, t1),
                               st=(kt == 0), sp=(kt == 7):
                               e.matmul(o, l, r, start=st, stop=sp),
                               reads=[r1, XN2.R(kt)], writes=[K.ps.R(bank)],
                               tag=tag + "x1")
                    p.step("act", lambda e, s=K.ps.ap(bank, 0, 128, 0, t1 - t0),
                           d=K.sqs[0:128, 0:t1 - t0]:
                           e.activation(d, s, AF.Silu),
                           reads=[K.ps.R(bank)], writes=[K.sqs_reg],
                           tag=tag + "hs")
                    p.step("dve", lambda e, d=HV.tile(mi, t0, t1),
                           s=VSTG.tile(m, t0, t1):
                           e.tensor_tensor(d, K.sqs[0:128, 0:t1 - t0], s,
                                           op=AluOpType.mult),
                           reads=[K.sqs_reg, VSTG.R(m)], writes=HV.wr(mi),
                           tag=tag + "hm")
                    if not is_shared:
                        p.step("dve", lambda e, d=HV.tile(mi, t0, t1),
                               c=CBB.tile(0, t0, t1):
                               e.tensor_tensor(d, d, c, op=AluOpType.mult),
                               reads=[HV.R(mi), CBB.R(0)], writes=[HV.R(mi)],
                               tag=tag + "hc")
        ws2 = WStream(K, tag + f"w2_{e_}", w2ap, w2rg, 1024, 1024, True,
                      Vg["wring"], wtmp=Vg["wtmp"], mg=2)
        linear(K, ws2, src_from_view(HV, T), ep_acc_y, 0, T,
               tag=tag + f"x2_{e_}")


def _l0f_router(K, V, T, W, tag):
    p = K.p
    wrs = V["wrsv"].tile(0)
    wrs_reg = V["wrsv"].R(0)
    p.step("dve", lambda e: e.memset(V["wrsv"].tile(0, 0, 1024, 0, 32), 0.0),
           reads=[], writes=V["wrsv"].wr(0), tag=tag + "wr")
    p.dma("sp", V["wrsv"].tile(0, 0, 1024, 0, 8), W["Wr"][0],
          reads=[W["Wr"][1]], writes=[wrs_reg], tag=tag + "wr")
    p.dma("sp", K.ebt[0:128, 0:8], W["eb"][0].unsqueeze(0).partition_broadcast(128),
          reads=[W["eb"][1]], writes=[K.ebt_reg], tag=tag + "eb")
    WRGH, WRGL = V["wrgh"], V["wrgl"]
    for kt in range(8):
        bank = K.ps.take()
        p.step("pe", lambda e, s=wrs[0:8, kt * 128:(kt + 1) * 128],
               o=K.ps.ap(bank, 0, 128, 0, 8), i=K.cbuf["identf"][0:8, 0:8]:
               e.transpose(o, s, i),
               reads=[wrs_reg, K.creg["identf"]], writes=[K.ps.R(bank)],
               tag=tag + "wt")
        p.step("dve", lambda e, s=K.ps.ap(bank, 0, 128, 0, 8):
               e.tensor_copy(K.xf[0:128, 0:8], s),
               reads=[K.ps.R(bank)], writes=[K.xf_reg], tag=tag + "wt")
        p.step("dve", lambda e, kt=kt, d=WRGH.tile(kt):
               e.tensor_copy(d, K.xf[0:128, 0:8]),
               reads=[K.xf_reg], writes=WRGH.wr(kt), tag=tag + "wt")
        p.step("act", lambda e, kt=kt, s=WRGH.tile(kt):
               e.copy(K.xf[0:128, 8:16], s),
               reads=[WRGH.R(kt)], writes=[K.xf_reg], tag=tag + "wt")
        p.step("dve", lambda e, kt=kt, d=WRGL.tile(kt):
               e.tensor_tensor(d, K.xf[0:128, 0:8], K.xf[0:128, 8:16],
                               op=AluOpType.subtract),
               reads=[K.xf_reg], writes=WRGL.wr(kt), tag=tag + "wt")
    for tt in range(T // 128):
        bank = K.ps.take()
        passes = [(V["xn2"], WRGH), (V["xn2lo"], WRGH), (V["xn2"], WRGL)]
        idx = 0
        for (srcv, wv) in passes:
            for kt in range(8):
                p.step("pe", lambda e, o=K.ps.ap(bank, 0, 128, 0, 8),
                       l=srcv.tile(kt, tt * 128, (tt + 1) * 128),
                       r=wv.tile(kt), st=(idx == 0), sp=(idx == 23):
                       e.matmul(o, l, r, start=st, stop=sp),
                       reads=[srcv.R(kt), wv.R(kt)], writes=[K.ps.R(bank)],
                       tag=tag + "lg")
                idx += 1
        xf, xr = K.xf, K.xf_reg
        C, cr = K.cols, K.cols_reg
        p.step("dve", lambda e, s=K.ps.ap(bank, 0, 128, 0, 8):
               e.tensor_tensor(xf[0:128, 0:8], s, K.ebt[0:128, 0:8],
                               op=AluOpType.add),
               reads=[K.ps.R(bank), K.ebt_reg], writes=[xr], tag=tag + "lg")
        p.step("dve", lambda e: e.reduce_max(C[0:128, 48:49], xf[0:128, 0:8],
                                             axis=AX.X, negate=True),
               reads=[xr], writes=[cr], tag=tag + "sm")
        p.step("act", lambda e: e.activation(xf[0:128, 0:8], xf[0:128, 0:8],
                                             AF.Exp, bias=C[0:128, 48:49]),
               reads=[xr, cr], writes=[xr], tag=tag + "sm")
        p.step("dve", lambda e: e.tensor_reduce(C[0:128, 48:49], xf[0:128, 0:8],
                                                axis=AX.X, op=AluOpType.add),
               reads=[xr], writes=[cr], tag=tag + "sm")
        p.step("dve", lambda e: e.reciprocal(C[0:128, 49:50], C[0:128, 48:49]),
               reads=[cr], writes=[cr], tag=tag + "sm")
        p.step("dve", lambda e: e.tensor_scalar(xf[0:128, 0:8], xf[0:128, 0:8],
                                                C[0:128, 49:50], None,
                                                op0=AluOpType.mult),
               reads=[xr, cr], writes=[xr], tag=tag + "sm")
        def top1(rw_c0, mask_c0, v_col, j_col):
            p.step("dve", lambda e: e.tensor_reduce(
                C[0:128, v_col:v_col + 1], xf[0:128, rw_c0:rw_c0 + 8],
                axis=AX.X, op=AluOpType.max),
                reads=[xr], writes=[cr], tag=tag + "t1")
            p.step("dve", lambda e: e.tensor_scalar(
                xf[0:128, mask_c0:mask_c0 + 8], xf[0:128, rw_c0:rw_c0 + 8],
                C[0:128, v_col:v_col + 1], None, op0=AluOpType.is_ge),
                reads=[xr, cr], writes=[xr], tag=tag + "t1")
            p.step("dve", lambda e: e.tensor_scalar(
                xf[0:128, 24:32], xf[0:128, mask_c0:mask_c0 + 8], -64.0, 64.0,
                op0=AluOpType.mult, op1=AluOpType.add),
                reads=[xr], writes=[xr], tag=tag + "t1")
            p.step("dve", lambda e: e.tensor_tensor(
                xf[0:128, 24:32], xf[0:128, 24:32], K.cbuf["iota8"][0:128, 0:8],
                op=AluOpType.add),
                reads=[xr, K.creg["iota8"]], writes=[xr], tag=tag + "t1")
            p.step("dve", lambda e: e.tensor_reduce(
                C[0:128, j_col:j_col + 1], xf[0:128, 24:32], axis=AX.X,
                op=AluOpType.min),
                reads=[xr], writes=[cr], tag=tag + "t1")
            p.step("dve", lambda e: e.tensor_scalar(
                xf[0:128, mask_c0:mask_c0 + 8], K.cbuf["iota8"][0:128, 0:8],
                C[0:128, j_col:j_col + 1], None, op0=AluOpType.is_equal),
                reads=[K.creg["iota8"], cr], writes=[xr], tag=tag + "t1")
        top1(0, 8, 50, 51)
        p.step("dve", lambda e: e.tensor_scalar(
            xf[0:128, 24:32], xf[0:128, 8:16], 1e9, None, op0=AluOpType.mult),
            reads=[xr], writes=[xr], tag=tag + "t2")
        p.step("dve", lambda e: e.tensor_tensor(
            xf[0:128, 16:24], xf[0:128, 0:8], xf[0:128, 24:32],
            op=AluOpType.subtract),
            reads=[xr], writes=[xr], tag=tag + "t2")
        top1(16, 16, 52, 53)
        p.step("act", lambda e: e.activation(C[0:128, 54:55], C[0:128, 50:51],
                                             AF.Exp),
               reads=[cr], writes=[cr], tag=tag + "tw")
        p.step("act", lambda e: e.activation(C[0:128, 55:56], C[0:128, 52:53],
                                             AF.Exp),
               reads=[cr], writes=[cr], tag=tag + "tw")
        p.step("dve", lambda e: e.tensor_tensor(C[0:128, 56:57], C[0:128, 54:55],
                                                C[0:128, 55:56], op=AluOpType.add),
               reads=[cr], writes=[cr], tag=tag + "tw")
        p.step("dve", lambda e: e.reciprocal(C[0:128, 57:58], C[0:128, 56:57]),
               reads=[cr], writes=[cr], tag=tag + "tw")
        p.step("dve", lambda e: e.tensor_tensor(C[0:128, 54:55], C[0:128, 54:55],
                                                C[0:128, 57:58], op=AluOpType.mult),
               reads=[cr], writes=[cr], tag=tag + "tw")
        p.step("dve", lambda e: e.tensor_tensor(C[0:128, 55:56], C[0:128, 55:56],
                                                C[0:128, 57:58], op=AluOpType.mult),
               reads=[cr], writes=[cr], tag=tag + "tw")
        p.step("dve", lambda e: e.tensor_scalar(
            xf[0:128, 8:16], xf[0:128, 8:16], C[0:128, 54:55], None,
            op0=AluOpType.mult),
            reads=[xr, cr], writes=[xr], tag=tag + "cmb")
        p.step("dve", lambda e: e.tensor_scalar(
            xf[0:128, 16:24], xf[0:128, 16:24], C[0:128, 55:56], None,
            op0=AluOpType.mult),
            reads=[xr, cr], writes=[xr], tag=tag + "cmb")
        p.step("dve", lambda e: e.tensor_tensor(
            xf[0:128, 0:8], xf[0:128, 8:16], xf[0:128, 16:24], op=AluOpType.add),
            reads=[xr], writes=[xr], tag=tag + "cmb")
        bank2 = K.ps.take()
        p.step("pe", lambda e, o=K.ps.ap(bank2, 0, 8, 0, 128):
               e.transpose(o, xf[0:128, 0:8], K.cbuf["identf"]),
               reads=[xr, K.creg["identf"]], writes=[K.ps.R(bank2)], tag=tag + "ct")
        p.step("dve", lambda e, s=K.ps.ap(bank2, 0, 8, 0, 128), tt=tt:
               e.tensor_copy(V["cmb"].tile(0, tt * 128, (tt + 1) * 128, 0, 8), s),
               reads=[K.ps.R(bank2)], writes=V["cmb"].wr(0), tag=tag + "ct")



def bound_sumsq_max_b2(K, tiles_fn, ntiles, T, cell, tmpcell, tag, chunk=512):
    p = K.p
    C, cr = K.cols, K.cols_reg
    first = True
    for i in range(ntiles):
        for c0 in range(0, T, chunk):
            c1 = min(c0 + chunk, T)
            ap_, reg_ = tiles_fn(i, c0, c1)
            p.step("act", lambda e, s=ap_, d=K.sqs[0:128, 0:c1 - c0]:
                   e.activation(d, s, AF.Square),
                   reads=[reg_], writes=[K.sqs_reg], tag=tag)
            bank = K.ps.take()
            p.step("pe", lambda e, o=K.ps.ap(bank, 0, 2, 0, c1 - c0):
                   e.matmul(o, K.cbuf["e2"][0:128, 0:2],
                            K.sqs[0:128, 0:c1 - c0], start=True, stop=True),
                   reads=[K.creg["e2"], K.sqs_reg], writes=[K.ps.R(bank)],
                   tag=tag)
            dc = C[0:2, 60:61] if first else C[0:2, 61:62]
            p.step("dve", lambda e, s=K.ps.ap(bank, 0, 2, 0, c1 - c0), d=dc:
                   e.reduce_max(d, s, axis=AX.X),
                   reads=[K.ps.R(bank)], writes=[cr], tag=tag)
            if not first:
                p.step("dve", lambda e: e.tensor_tensor(
                    C[0:2, 60:61], C[0:2, 60:61], C[0:2, 61:62], op=AluOpType.max),
                    reads=[cr], writes=[cr], tag=tag)
            first = False
    bank = K.ps.take()
    p.step("pe", lambda e, o=K.ps.ap(bank, 0, 1, 0, 2):
           e.transpose(o, C[0:2, 60:61], K.cbuf["identf"][0:2, 0:2]),
           reads=[cr, K.creg["identf"]], writes=[K.ps.R(bank)], tag=tag)
    p.step("dve", lambda e, s=K.ps.ap(bank, 0, 1, 0, 2), d=cell:
           e.reduce_max(d, s, axis=AX.X),
           reads=[K.ps.R(bank)], writes=[cr], tag=tag)


def stage_l1_attn(K, XNT, T, W, ao_sp, xnt_sp, keeps=(), tag="l1a"):
    p = K.p
    NKB = T // 128
    keeps = list(keeps)

    V1 = K.arena.carve(keeps + [XNT], [
        ("wring", 4, 2048, BF16), ("wtmp", 2, 1024, BF16),
        ("qkt", 16, T, BF16), ("vn", NKB, 1040, BF16),
        ("ropec", 1, T, F32), ("ropes", 1, T, F32)])
    QKT, VN = V1["qkt"], V1["vn"]
    p.dma("sp", V1["ropec"].tile(0), W["ropec"][0], reads=[W["ropec"][1]],
          writes=V1["ropec"].wr(0), tag=tag + "tc")
    p.dma("sp", V1["ropes"].tile(0), W["ropes"][0], reads=[W["ropes"][1]],
          writes=V1["ropes"].wr(0), tag=tag + "ts")
    bq = load_cols(K, W["bqkv"][0], W["bqkv"][1], 8, 24, tag + "bq")

    for t_ in range(NKB):
        p.step("dve", lambda e, t_=t_: e.memset(VN.tile(t_), 1.0),
               reads=[], writes=VN.wr(t_), tag=tag + "vo")
    ws = WStream(K, tag + "w", W["wqkv"][0], W["wqkv"][1], 1024, 3072, True,
                 V1["wring"], wtmp=V1["wtmp"], mg=2)

    def ep_qkv(mi, msz, bank, t0, t1):
        if mi < 8:
            K.p.step("dve", lambda e, s=K.ps.ap(bank, 0, msz, 0, t1 - t0),
                     d=QKT.tile(mi, t0, t1), b=bq[mi]:
                     e.tensor_scalar(d, s, b, 0.125, op0=AluOpType.add,
                                     op1=AluOpType.mult),
                     reads=[K.ps.R(bank), K.cols_reg], writes=[QKT.R(mi)],
                     tag=tag + "eq")
        elif mi < 16:
            K.p.step("dve", lambda e, s=K.ps.ap(bank, 0, msz, 0, t1 - t0),
                     d=QKT.tile(mi, t0, t1), b=bq[mi]:
                     e.tensor_scalar(d, s, b, None, op0=AluOpType.add),
                     reads=[K.ps.R(bank), K.cols_reg], writes=[QKT.R(mi)],
                     tag=tag + "ek")
        else:
            K.p.step("dve", lambda e, s=K.ps.ap(bank, 0, msz, 0, t1 - t0),
                     d=K.sqs[0:128, 0:t1 - t0], b=bq[mi]:
                     e.tensor_scalar(d, s, b, None, op0=AluOpType.add),
                     reads=[K.ps.R(bank), K.cols_reg], writes=[K.sqs_reg],
                     tag=tag + "ev")
            h0 = 2 * (mi - 16)
            for tb in range((t1 - t0) // 128):
                bank2 = K.ps.take()
                K.p.step("pe", lambda e, tb=tb,
                         s=K.sqs[0:128, tb * 128:(tb + 1) * 128],
                         o=K.ps.bf16(bank2, 0, 128, 0, 128), i=K.cbuf["ident"]:
                         e.transpose(o, s, i),
                         reads=[K.sqs_reg, K.creg["ident"]],
                         writes=[K.ps.R(bank2)], tag=tag + "vt")
                vt = (t0 + tb * 128) // 128
                src_ = K.ps.bf16(bank2, 0, 128, 0, 128).rearrange(
                    "p (h c) -> p h c", c=64)
                dst = VN.tile(vt, 65 * h0, 65 * h0 + 130).rearrange(
                    "p (h c) -> p h c", c=65)[:, :, 0:64]
                K.p.step("dve", lambda e, s=src_, d=dst: e.tensor_copy(d, s),
                         reads=[K.ps.R(bank2)], writes=[VN.R(vt)],
                         tag=tag + "vc")
    linear(K, ws, src_from_view(XNT, T), ep_qkv, 0, T, tag=tag + "qkv")

    for t_ in range(16):
        for c0 in range(0, T, 512):
            c1 = c0 + 512
            bank = K.ps.take()
            p.step("pe", lambda e, o=K.ps.ap(bank, 0, 128, 0, 512),
                   l=K.cbuf["permf"], r=QKT.tile(t_, c0, c1):
                   e.matmul(o, l, r, start=True, stop=True),
                   reads=[K.creg["permf"], QKT.R(t_)], writes=[K.ps.R(bank)],
                   tag=tag + "rp")
            p.step("dve", lambda e, s=K.ps.ap(bank, 0, 128, 0, 512),
                   c=V1["ropes"].tile(0, c0, c1):
                   e.tensor_tensor(K.xf[0:128, 0:512], s, c, op=AluOpType.mult),
                   reads=[K.ps.R(bank), V1["ropes"].R(0)], writes=[K.xf_reg],
                   tag=tag + "rp")
            p.step("dve", lambda e, q=QKT.tile(t_, c0, c1),
                   c=V1["ropec"].tile(0, c0, c1):
                   e.tensor_tensor(K.xf[0:128, 512:1024], q, c, op=AluOpType.mult),
                   reads=[QKT.R(t_), V1["ropec"].R(0)], writes=[K.xf_reg],
                   tag=tag + "rp")
            p.step("dve", lambda e, d=QKT.tile(t_, c0, c1):
                   e.tensor_tensor(d, K.xf[0:128, 0:512], K.xf[0:128, 512:1024],
                                   op=AluOpType.add),
                   reads=[K.xf_reg], writes=[QKT.R(t_)], tag=tag + "rp")

    for kt in range(8):
        p.dma("sp", xnt_sp[0][kt * 128:(kt + 1) * 128, 0:T], XNT.tile(kt),
              reads=[XNT.R(kt)], writes=[xnt_sp[1]], tag=tag + "xsp")

    V2 = K.arena.carve(keeps + [QKT, VN], [
        ("p", NKB, 512, BF16), ("cmask", 4, 512, BF16), ("ot", 8, T, BF16)])
    P, OT = V2["p"], V2["ot"]
    for r in range(4):
        p.dma("sp", V2["cmask"].tile(r), W["cmask"][0][r * 128:(r + 1) * 128, 0:512],
              reads=[W["cmask"][1]], writes=V2["cmask"].wr(r), tag=tag + "tm")
    bound_sumsq_max_b2(K, lambda i, c0, c1: (QKT.tile(i, c0, c1), QKT.R(i)),
                       8, T, K.cols[0:1, 40:41], K.cols[0:1, 41:42], tag + "bq")
    bound_sumsq_max_b2(K, lambda i, c0, c1: (QKT.tile(8 + i, c0, c1), QKT.R(8 + i)),
                       8, T, K.cols[0:1, 44:45], K.cols[0:1, 45:46], tag + "bk")
    bound_to_negcol(K, K.cols[0:1, 40:41], K.cols[0:1, 44:45], K.cols[0:1, 41:42],
                    K.negc[0:128, 0:1], tag=tag + "bn")

    for qc in range(T // 512):
        t0, t1 = qc * 512, (qc + 1) * 512
        nkb = (qc + 1) * 4
        for h in range(16):
            qt = h // 2
            b = 64 * (h % 2)
            for kb in range(nkb):
                sbank = K.ps.take()
                p.step("pe", lambda e, o=K.ps.ap(sbank, 0, 128, 0, 512),
                       l=QKT.tile(8 + qt, kb * 128, (kb + 1) * 128, b, b + 64),
                       r=QKT.tile(qt, t0, t1, b, b + 64):
                       e.matmul(o, l, r, start=True, stop=True),
                       reads=[QKT.R(8 + qt), QKT.R(qt)],
                       writes=[K.ps.R(sbank)], tag=tag + "s")
                p.step("act", lambda e, s=K.ps.ap(sbank, 0, 128, 0, 512),
                       d=P.tile(kb):
                       e.activation(d, s, AF.Exp, bias=K.negc[0:128, 0:1]),
                       reads=[K.ps.R(sbank), K.negc_reg], writes=P.wr(kb),
                       tag=tag + "e")
                r = kb * 128 - t0
                if 0 <= r < 512:
                    p.step("dve", lambda e, d=P.tile(kb), m=V2["cmask"].tile(r // 128):
                           e.tensor_tensor(d, d, m, op=AluOpType.mult),
                           reads=[P.R(kb), V2["cmask"].R(r // 128)],
                           writes=[P.R(kb)], tag=tag + "m")
            obank = K.ps.take(hold=True)
            for kb in range(nkb):
                p.step("pe", lambda e, o=K.ps.ap(obank, 0, 65, 0, 512),
                       l=VN.tile(kb, 65 * h, 65 * h + 65), r=P.tile(kb),
                       st=(kb == 0), sp=(kb == nkb - 1):
                       e.matmul(o, l, r, start=st, stop=sp),
                       reads=[VN.R(kb), P.R(kb)], writes=[K.ps.R(obank)],
                       tag=tag + "pv")
            p.step("dve", lambda e, s=K.ps.ap(obank, 64, 65, 0, 512):
                   e.reciprocal(K.rwa[0:1, 0:512], s),
                   reads=[K.ps.R(obank)], writes=[K.rwa_reg], tag=tag + "nr")
            bank2 = K.ps.take()
            p.step("pe", lambda e, o=K.ps.ap(bank2, 0, 64, 0, 512):
                   e.matmul(o, K.cbuf["ones1x128"][0:1, 0:64],
                            K.rwa[0:1, 0:512], start=True, stop=True),
                   reads=[K.creg["ones1x128"], K.rwa_reg],
                   writes=[K.ps.R(bank2)], tag=tag + "nb")
            p.step("dve", lambda e, s=K.ps.ap(bank2, 0, 64, 0, 512):
                   e.tensor_copy(K.xf[0:64, 0:512], s),
                   reads=[K.ps.R(bank2)], writes=[K.xf_reg], tag=tag + "nb")
            rb = 64 * (h % 2)
            p.step("dve", lambda e, s=K.ps.ap(obank, 0, 64, 0, 512),
                   d=OT.tile(h // 2, t0, t1, rb, rb + 64):
                   e.tensor_tensor(d, s, K.xf[0:64, 0:512], op=AluOpType.mult),
                   reads=[K.ps.R(obank), K.xf_reg],
                   writes=[OT.R(h // 2)], tag=tag + "oe")
            K.ps.release(obank)

    V3 = K.arena.carve(keeps + [OT], [
        ("wring", 4, 2048, BF16), ("wtmp", 2, 1024, BF16)])
    bo = load_cols(K, W["bout"][0], W["bout"][1], 32, 8, tag + "bo")
    wso = WStream(K, tag + "wo", W["wout"][0], W["wout"][1], 1024, 1024, True,
                  V3["wring"], wtmp=V3["wtmp"], mg=2)

    def ep_ao(mi, msz, bank, t0, t1):
        stg = (K.sqs, K.sqs_reg) if mi % 2 == 0 else (K.pb, K.pb_reg)
        K.p.step("dve", lambda e, s=K.ps.ap(bank, 0, msz, 0, t1 - t0),
                 d=stg[0][0:msz, 0:t1 - t0], b=bo[mi]:
                 e.tensor_scalar(d, s, b, None, op0=AluOpType.add),
                 reads=[K.ps.R(bank), K.cols_reg], writes=[stg[1]],
                 tag=tag + "ao")
        K.p.dma("sp", ao_sp[0][mi * 128:mi * 128 + msz, t0:t1],
                stg[0][0:msz, 0:t1 - t0],
                reads=[stg[1]], writes=[ao_sp[1]], tag=tag + "aosp")
    linear(K, wso, src_from_view(OT, T), ep_ao, 0, T, tag=tag + "wo")



def ln_shift_emit(K, YB, XN1, t0, t1, gcols, bcols, SQ8, BC, XF2, tag="lns"):
    p = K.p
    TL = t1 - t0
    RWA, RWB = K.rwa, K.rwb
    rA, rB = K.rwa_reg, K.rwb_reg

    def zsrc(kt, c0, c1):
        off = 1 if kt < 4 else 0
        return YB.tile(kt, t0 + c0 + off, t0 + c1 + off), [YB.R(kt)], 128
    colsum(K, zsrc, 8, RWA[0:1, 0:TL], [rA], 0, TL, scale=1.0 / 1024.0,
           tag=tag + "mu")
    bcast_row(K, RWA[0:1, 0:TL], [rA], lambda a, b: BC.tile(1, a, b),
              BC.wr(1), 0, TL, tag=tag + "b1")
    p.step("act", lambda e: e.activation(RWA[0:1, 0:TL], RWA[0:1, 0:TL],
                                         AF.Square),
           reads=[rA], writes=[rA], tag=tag)
    for kt in range(8):
        off = 1 if kt < 4 else 0
        p.step("act", lambda e, kt=kt, off=off,
               s=YB.tile(kt, t0 + off, t1 + off), d=SQ8.tile(kt):
               e.activation(d, s, AF.Square),
               reads=[YB.R(kt)], writes=SQ8.wr(kt), tag=tag + "sq")
    colsum(K, lambda kt, c0, c1: (SQ8.tile(kt, c0, c1), [SQ8.R(kt)], 128),
           8, RWB[0:1, 0:TL], [rB], 0, TL, scale=1.0 / 1024.0, tag=tag + "s2")
    p.step("dve", lambda e: e.tensor_tensor(RWB[0:1, 0:TL], RWB[0:1, 0:TL],
                                            RWA[0:1, 0:TL], op=AluOpType.subtract),
           reads=[rA, rB], writes=[rB], tag=tag)
    p.step("act", lambda e: e.activation(RWB[0:1, 0:TL], RWB[0:1, 0:TL],
                                         AF.Ln, bias=K.cbuf["misc"][0:1, 0:1]),
           reads=[rB, K.creg["misc"]], writes=[rB], tag=tag)
    p.step("act", lambda e: e.activation(RWB[0:1, 0:TL], RWB[0:1, 0:TL],
                                         AF.Exp, scale=-0.5),
           reads=[rB], writes=[rB], tag=tag)
    bcast_row(K, RWB[0:1, 0:TL], [rB], lambda a, b: BC.tile(0, a, b),
              BC.wr(0), 0, TL, tag=tag + "b0")
    for kt in range(8):
        off = 1 if kt < 4 else 0
        x2 = XF2.tile(kt % 2)
        x2r = XF2.R(kt % 2)
        p.step("dve", lambda e, kt=kt, off=off, x2=x2: e.tensor_tensor(
            x2, YB.tile(kt, t0 + off, t1 + off), BC.tile(1, 0, TL),
            op=AluOpType.subtract),
            reads=[YB.R(kt), BC.R(1)], writes=XF2.wr(kt % 2), tag=tag)
        p.step("dve", lambda e, x2=x2: e.tensor_tensor(
            x2, x2, BC.tile(0, 0, TL), op=AluOpType.mult),
            reads=[x2r, BC.R(0)], writes=[x2r], tag=tag)
        p.step("dve", lambda e, kt=kt, x2=x2, g=gcols[kt], b=bcols[kt]:
               e.tensor_scalar(XN1.tile(kt, t0, t1), x2, g, b,
                               op0=AluOpType.mult, op1=AluOpType.add),
               reads=[x2r, K.cols_reg], writes=XN1.wr(kt), tag=tag)


def stage_l1_ffn(K, XNT, Y, T, W, keeps=(), tag="l1f", tap=None):
    p = K.p
    for kt in range(8):
        p.step("act", lambda e, kt=kt, d=Y.tile(kt), s=XNT.tile(kt):
               e.copy(d, s), reads=[XNT.R(kt)], writes=Y.wr(kt), tag=tag + "sd")
    keeps = [Y] + list(keeps)
    for li in range(6):
        V = K.arena.carve(keeps, [
            ("wqs", 12, 2048, BF16), ("wos", 4, 2048, BF16),
            ("yw", 16, 257, BF16), ("qw", 8, 256, BF16), ("kw", 16, 256, BF16),
            ("vw", 4, 1024, BF16), ("xcs", 1, 256, BF16), ("xss", 1, 256, BF16),
            ("xck", 1, 256, BF16), ("xsk", 1, 256, BF16),
            ("pt", 4, 256, BF16), ("otw", 8, 256, BF16), ("rlq", 2, 32, F32),
            ("lmask", 2, 512, BF16),
            ("sx", 2, 512, F32), ("spb", 2, 512, BF16), ("scl", 8, 2, F32),
            ("sq2", 2, 256, BF16), ("srl", 1, 512, F32), ("rlf", 2, 256, F32),
            ("rlb", 1, 256, BF16),
        ])
        _l1f_attn(K, Y, V, T, W, li, tag=f"{tag}a{li}")
        if tap is not None:
            tap(li, "a")
        _l1f_ffn(K, Y, T, W, li, keeps, tag=f"{tag}f{li}")
        if tap is not None:
            tap(li, "b")


def _l1f_attn(K, Y, V, T, W, li, tag):
    p = K.p
    NW = T // 256
    wqk = WStream(K, tag + "wq", slice3d(W["Wqkv"][0], li, 1024, 3072),
                  W["Wqkv"][1], 1024, 3072, False, V["wqs"], mg=2)
    for g in range(wqk.n_groups):
        wqk.stage_group(g)
    wos = WStream(K, tag + "wo", slice3d(W["Wo"][0], li, 1024, 1024),
                  W["Wo"][1], 1024, 1024, False, V["wos"], mg=2)
    for g in range(wos.n_groups):
        wos.stage_group(g)
    for rep in range(2):
        p.dma("sp", K.cols[rep * 64:rep * 64 + 64, 60:61],
              bass.AP(W["qs"][0].tensor, li * 64, [[1, 64], [1, 1]]),
              reads=[W["qs"][1]], writes=[K.cols_reg], tag=tag + "qs")
        p.dma("sp", K.cols[rep * 64:rep * 64 + 64, 61:62],
              bass.AP(W["ks"][0].tensor, li * 64, [[1, 64], [1, 1]]),
              reads=[W["ks"][1]], writes=[K.cols_reg], tag=tag + "ks")
    qscol = K.cols[0:128, 60:61]
    kscol = K.cols[0:128, 61:62]
    for qt in range(2):
        p.dma("pool", V["lmask"].tile(qt),
              W["lmask0"][0][qt * 128:(qt + 1) * 128, 0:512],
              reads=[W["lmask0"][1]], writes=V["lmask"].wr(qt), tag=tag + "mk0")
    for t_ in range(V["kw"].nt):
        p.step("dve", lambda e, t_=t_: e.memset(V["kw"].tile(t_), 0.0),
               reads=[], writes=V["kw"].wr(t_), tag=tag + "z")
    for t_ in range(V["vw"].nt):
        p.step("dve", lambda e, t_=t_: e.memset(V["vw"].tile(t_), 0.0),
               reads=[], writes=V["vw"].wr(t_), tag=tag + "z")
    for kt in range(8):
        p.step("dve", lambda e, kt=kt, d=V["yw"].tile(kt, 0, 1): e.memset(d, 0.0),
               reads=[], writes=V["yw"].wr(kt), tag=tag + "yz")
    YLOC = [Region(f"{tag}yl{kt}") for kt in range(8)]

    for w in range(NW):
        t0 = w * 256
        ys = (w % 2) * 8
        nb = ((w + 1) % 2) * 8
        for kt in range(8):
            p.step("act", lambda e, kt=kt, ys=ys, s=Y.tile(kt, t0, t0 + 256),
                   d=V["yw"].tile(ys + kt, 1, 257): e.copy(d, s),
                   reads=[Y.R(kt)], writes=V["yw"].wr(ys + kt), tag=tag + "yc")
            if w + 1 < NW:
                p.step("act", lambda e, kt=kt, nb=nb,
                       s=Y.tile(kt, t0 + 255, t0 + 256),
                       d=V["yw"].tile(nb + kt, 0, 1): e.copy(d, s),
                       reads=[Y.R(kt)], writes=V["yw"].wr(nb + kt),
                       tag=tag + "yb")
        for (nm, dr) in (("xcs", "xposcs"), ("xss", "xposss"),
                         ("xck", "xposck"), ("xsk", "xpossk")):
            p.dma("sp", V[nm].tile(0), W[dr][0][0:128, t0:t0 + 256],
                  reads=[W[dr][1]], writes=V[nm].wr(0), tag=tag + "tb")
        kslot = w % 2
        for mi in range(24):
            bank = K.ps.take()
            for kt in range(8):
                off = 1 if kt < 4 else 0
                g = mi // wqk.mg
                p.step("pe", lambda e, o=K.ps.ap(bank, 0, 128, 0, 256),
                       l=wqk.entry_ap(g % wqk.nring, mi % wqk.mg, kt),
                       r=V["yw"].tile(ys + kt, off, off + 256),
                       st=(kt == 0), sp=(kt == 7):
                       e.matmul(o, l, r, start=st, stop=sp),
                       reads=[wqk.slot_regs[g % wqk.nring], V["yw"].R(ys + kt)],
                       writes=[K.ps.R(bank)], tag=tag + "qkv")
            if mi < 16:
                isq = mi < 8
                ht = mi % 8
                sq = V["sq2"].tile(mi % 2)
                sqr = V["sq2"].R(mi % 2)
                rlf = V["rlf"].tile(mi % 2, 0, 256, 0, 2)
                rlr = V["rlf"].R(mi % 2)
                p.step("act", lambda e, s=K.ps.ap(bank, 0, 128, 0, 256), sq=sq:
                       e.activation(sq, s, AF.Square),
                       reads=[K.ps.R(bank)], writes=V["sq2"].wr(mi % 2),
                       tag=tag + "l2")
                bank2 = K.ps.take()
                p.step("pe", lambda e, o=K.ps.ap(bank2, 0, 2, 0, 256), sq=sq:
                       e.matmul(o, K.cbuf["e2"][0:128, 0:2], sq,
                                start=True, stop=True),
                       reads=[K.creg["e2"], sqr], writes=[K.ps.R(bank2)],
                       tag=tag + "l2")
                p.step("act", lambda e, s=K.ps.ap(bank2, 0, 2, 0, 256), d=rlf:
                       e.activation(d, s, AF.Ln,
                                    bias=K.cbuf["misc"][0:2, 4:5]),
                       reads=[K.ps.R(bank2), K.creg["misc"]],
                       writes=V["rlf"].wr(mi % 2), tag=tag + "l2")
                p.step("act", lambda e, d=rlf: e.activation(d, d, AF.Exp,
                                                            scale=0.5),
                       reads=[rlr], writes=[rlr], tag=tag + "l2")
                p.step("dve", lambda e, d=rlf: e.tensor_scalar(
                    d, d, 1e-12, None, op0=AluOpType.max),
                    reads=[rlr], writes=[rlr], tag=tag + "l2")
                p.step("dve", lambda e, d=rlf: e.reciprocal(d, d),
                       reads=[rlr], writes=[rlr], tag=tag + "l2")
                gc = qscol if isq else kscol
                p.step("dve", lambda e, s=K.ps.ap(bank, 0, 128, 0, 256), g=gc,
                       sq=sq:
                       e.tensor_scalar(sq, s, g, None, op0=AluOpType.mult),
                       reads=[K.ps.R(bank), K.cols_reg], writes=[sqr],
                       tag=tag + "g")
                bank3 = K.ps.take()
                p.step("pe", lambda e, o=K.ps.ap(bank3, 0, 128, 0, 256), sq=sq:
                       e.matmul(o, K.cbuf["permh"], sq, start=True, stop=True),
                       reads=[K.creg["permh"], sqr], writes=[K.ps.R(bank3)],
                       tag=tag + "rp")
                ct = V["xcs"] if isq else V["xck"]
                st_ = V["xss"] if isq else V["xsk"]
                srl = V["srl"]
                p.step("dve", lambda e, c=ct.tile(0), sq=sq:
                       e.tensor_tensor(srl.tile(0, 0, 256), sq, c,
                                       op=AluOpType.mult),
                       reads=[sqr, ct.R(0)], writes=srl.wr(0), tag=tag + "rp")
                p.step("dve", lambda e, s=K.ps.ap(bank3, 0, 128, 0, 256),
                       c=st_.tile(0):
                       e.tensor_tensor(srl.tile(0, 256, 512), s, c,
                                       op=AluOpType.mult),
                       reads=[K.ps.R(bank3), st_.R(0)], writes=[srl.R(0)],
                       tag=tag + "rp")
                if isq:
                    p.step("dve", lambda e, ht=ht, d=V["qw"].tile(ht):
                           e.tensor_tensor(d, srl.tile(0, 0, 256),
                                           srl.tile(0, 256, 512),
                                           op=AluOpType.add),
                           reads=[srl.R(0)], writes=V["qw"].wr(ht), tag=tag + "q")
                    for qt in range(2):
                        bank4 = K.ps.take()
                        p.step("pe", lambda e, qt=qt,
                               o=K.ps.ap(bank4, 0, 128, 0, 2),
                               s=V["rlf"].tile(mi % 2, qt * 128,
                                               (qt + 1) * 128, 0, 2):
                               e.transpose(o, s, K.cbuf["identf"][0:2, 0:2]),
                               reads=[rlr, K.creg["identf"]],
                               writes=[K.ps.R(bank4)], tag=tag + "rq")
                        p.step("dve", lambda e, qt=qt, ht=ht,
                               s=K.ps.ap(bank4, 0, 128, 0, 2):
                               e.tensor_copy(V["rlq"].tile(qt, 2 * ht, 2 * ht + 2),
                                             s), reads=[K.ps.R(bank4)],
                               writes=[V["rlq"].R(qt)], tag=tag + "rq")
                else:
                    p.step("dve", lambda e: e.tensor_tensor(
                        srl.tile(0, 0, 256), srl.tile(0, 0, 256),
                        srl.tile(0, 256, 512), op=AluOpType.add),
                        reads=[srl.R(0)], writes=[srl.R(0)], tag=tag + "k")
                    p.step("dve", lambda e, d=V["rlb"].tile(0, 0, 256, 0, 2),
                           s=rlf: e.tensor_copy(d, s),
                           reads=[rlr], writes=V["rlb"].wr(0), tag=tag + "k")
                    bank4 = K.ps.take()
                    p.step("pe", lambda e, o=K.ps.ap(bank4, 0, 128, 0, 256):
                           e.matmul(o, K.cbuf["exp2"][0:2, 0:128],
                                    V["rlb"].tile(0, 0, 256, 0, 2),
                                    start=True, stop=True),
                           reads=[K.creg["exp2"], V["rlb"].R(0)],
                           writes=[K.ps.R(bank4)], tag=tag + "k")
                    p.step("dve", lambda e, ht=ht, kslot=kslot,
                           s=K.ps.ap(bank4, 0, 128, 0, 256),
                           d=V["kw"].tile(kslot * 8 + ht):
                           e.tensor_tensor(d, srl.tile(0, 0, 256), s,
                                           op=AluOpType.mult),
                           reads=[srl.R(0), K.ps.R(bank4)],
                           writes=[V["kw"].R(kslot * 8 + ht)], tag=tag + "k")
            else:
                sq = V["sq2"].tile(mi % 2)
                sqr = V["sq2"].R(mi % 2)
                p.step("dve", lambda e, s=K.ps.ap(bank, 0, 128, 0, 256), sq=sq:
                       e.tensor_copy(sq, s),
                       reads=[K.ps.R(bank)], writes=V["sq2"].wr(mi % 2),
                       tag=tag + "v")
                bank2 = K.ps.take()
                for tb in range(2):
                    p.step("pe", lambda e, tb=tb,
                           s=V["sq2"].tile(mi % 2, tb * 128, (tb + 1) * 128),
                           o=K.ps.bf16(bank2, 0, 128, tb * 128, tb * 128 + 128),
                           i=K.cbuf["ident"]:
                           e.transpose(o, s, i),
                           reads=[sqr, K.creg["ident"]],
                           writes=[K.ps.R(bank2)], tag=tag + "v")
                for tb in range(2):
                    vt = kslot * 2 + tb
                    p.step("dve", lambda e, tb=tb,
                           s=K.ps.bf16(bank2, 0, 128, tb * 128, tb * 128 + 128),
                           d=V["vw"].tile(vt, 128 * (mi - 16), 128 * (mi - 16) + 128):
                           e.tensor_copy(d, s),
                           reads=[K.ps.R(bank2)], writes=[V["vw"].R(vt)],
                           tag=tag + "v")
        pslot = (w + 1) % 2
        for h in range(16):
            qt_t = h // 2
            b = 64 * (h % 2)
            obank = K.ps.take(hold=True)
            for qt in range(2):
                u = (2 * h + qt) % 2
                uc = (2 * h + qt) % 8
                SX = V["sx"]
                PB = V["spb"]
                SCt = V["scl"].tile(uc, 0, 2, 0, 128)
                SCr = V["scl"].R(uc)
                sbank = K.ps.take(hold=True)
                for kh in range(2):
                    ks_ = pslot if kh == 0 else kslot
                    p.step("pe", lambda e, o=K.ps.ap(sbank, 0, 128, kh * 256,
                                                     kh * 256 + 256),
                           l=V["qw"].tile(qt_t, qt * 128, (qt + 1) * 128, b, b + 64),
                           r=V["kw"].tile(ks_ * 8 + qt_t, 0, 256, b, b + 64):
                           e.matmul(o, l, r, start=True, stop=True),
                           reads=[V["qw"].R(qt_t), V["kw"].R(ks_ * 8 + qt_t)],
                           writes=[K.ps.R(sbank)], tag=tag + "s")
                p.step("dve", lambda e, u=u, s=K.ps.ap(sbank, 0, 128, 0, 512),
                       r=V["rlq"].tile(qt, h, h + 1):
                       e.tensor_scalar(SX.tile(u), s, r, 8.0,
                                       op0=AluOpType.mult, op1=AluOpType.mult),
                       reads=[K.ps.R(sbank), V["rlq"].R(qt)], writes=SX.wr(u),
                       tag=tag + "sc")
                K.ps.release(sbank)
                p.step("dve", lambda e, u=u, m=V["lmask"].tile(qt):
                       e.tensor_tensor(SX.tile(u), SX.tile(u), m,
                                       op=AluOpType.add),
                       reads=[SX.R(u), V["lmask"].R(qt)], writes=[SX.R(u)],
                       tag=tag + "mk")
                p.step("dve", lambda e, u=u, d=SCt[0:128, 0:1]:
                       e.reduce_max(d, SX.tile(u), axis=AX.X, negate=True),
                       reads=[SX.R(u)], writes=V["scl"].wr(uc), tag=tag + "mx")
                p.step("act", lambda e, u=u: e.activation(
                    PB.tile(u), SX.tile(u), AF.Exp,
                    bias=SCt[0:128, 0:1], accum_out=SCt[0:128, 1:2]),
                    reads=[SX.R(u), SCr], writes=[*PB.wr(u), SCr],
                    tag=tag + "ex")
                p.step("dve", lambda e: e.reciprocal(SCt[0:128, 0:1],
                                                     SCt[0:128, 1:2]),
                       reads=[SCr], writes=[SCr], tag=tag + "rc")
                p.step("dve", lambda e, u=u: e.tensor_scalar(
                    PB.tile(u), PB.tile(u), SCt[0:128, 0:1],
                    None, op0=AluOpType.mult),
                    reads=[PB.R(u), SCr], writes=[PB.R(u)], tag=tag + "nm")
                ptb = K.ps.take()
                for kt4 in range(4):
                    p.step("pe", lambda e, kt4=kt4, u=u,
                           s=PB.tile(u, kt4 * 128, (kt4 + 1) * 128),
                           o=K.ps.bf16(ptb, 0, 128, kt4 * 128, kt4 * 128 + 128),
                           i=K.cbuf["ident"]:
                           e.transpose(o, s, i),
                           reads=[PB.R(u), K.creg["ident"]],
                           writes=[K.ps.R(ptb)], tag=tag + "pt")
                for kt4 in range(4):
                    p.step("dve", lambda e, kt4=kt4,
                           s=K.ps.bf16(ptb, 0, 128, kt4 * 128, kt4 * 128 + 128),
                           d=V["pt"].tile(kt4, qt * 128, (qt + 1) * 128):
                           e.tensor_copy(d, s),
                           reads=[K.ps.R(ptb)], writes=V["pt"].wr(kt4),
                           tag=tag + "pt")
            for kt4 in range(4):
                vt = (pslot if kt4 < 2 else kslot) * 2 + (kt4 % 2)
                p.step("pe", lambda e, o=K.ps.ap(obank, 0, 64, 0, 256),
                       l=V["vw"].tile(vt, 64 * h, 64 * h + 64),
                       r=V["pt"].tile(kt4), st=(kt4 == 0), sp=(kt4 == 3):
                       e.matmul(o, l, r, start=st, stop=sp),
                       reads=[V["vw"].R(vt), V["pt"].R(kt4)],
                       writes=[K.ps.R(obank)], tag=tag + "pv")
            rb = 64 * (h % 2)
            p.step("dve", lambda e, s=K.ps.ap(obank, 0, 64, 0, 256),
                   d=V["otw"].tile(h // 2, 0, 256, rb, rb + 64):
                   e.tensor_copy(d, s),
                   reads=[K.ps.R(obank)], writes=[V["otw"].R(h // 2)],
                   tag=tag + "oe")
            K.ps.release(obank)
        if w == 0:
            for qt in range(2):
                p.dma("pool", V["lmask"].tile(qt),
                      W["lmask"][0][qt * 128:(qt + 1) * 128, 0:512],
                      reads=[W["lmask"][1]], writes=[V["lmask"].R(qt)],
                      tag=tag + "mk1")
        for mi in range(8):
            bank = K.ps.take()
            for kt in range(8):
                g = mi // wos.mg
                p.step("pe", lambda e, o=K.ps.ap(bank, 0, 128, 0, 256),
                       l=wos.entry_ap(g % wos.nring, mi % wos.mg, kt),
                       r=V["otw"].tile(kt), st=(kt == 0), sp=(kt == 7):
                       e.matmul(o, l, r, start=st, stop=sp),
                       reads=[wos.slot_regs[g % wos.nring], V["otw"].R(kt)],
                       writes=[K.ps.R(bank)], tag=tag + "woA")
            rd = [K.ps.R(bank), Y.R(mi)]
            if w + 1 < NW:
                rd.append(V["yw"].R(nb + mi))
            wr_ = [YLOC[mi], Y.R(mi)] if w == NW - 1 else [YLOC[mi]]
            p.step("dve", lambda e, mi=mi, s=K.ps.ap(bank, 0, 128, 0, 256),
                   d=Y.tile(mi, t0, t0 + 256):
                   e.tensor_tensor(d, d, s, op=AluOpType.add),
                   reads=rd, writes=wr_, tag=tag + "ya")


def _l1f_ffn(K, Y, T, W, li, keeps, tag=""):
    p = K.p
    DI = 2730
    TCW = min(1024, T)
    NCH = ceil_div(T, 512)
    Va = K.arena.carve(keeps, [
        ("ybf", 8, T + 1, BF16), ("xn1", 8, T, BF16),
        ("sq8", 8, TCW, BF16), ("bcl", 2, TCW, F32), ("lxf", 2, TCW, F32)])
    YB, XN1 = Va["ybf"], Va["xn1"]
    for kt in range(8):
        p.step("act", lambda e, kt=kt, s=Y.tile(kt), d=YB.tile(kt, 1, T + 1):
               e.copy(d, s), reads=[Y.R(kt)], writes=YB.wr(kt), tag=tag + "yb")
        p.step("dve", lambda e, kt=kt, d=YB.tile(kt, 0, 1): e.memset(d, 0.0),
               reads=[], writes=[YB.R(kt)], tag=tag + "yb")
    lgc = load_cols(K, bass.AP(W["lng"][0].tensor, li * 1024, [[1, 1024]]),
                    W["lng"][1], 16, 8, tag + "lg")
    lbc = load_cols(K, bass.AP(W["lnb"][0].tensor, li * 1024, [[1, 1024]]),
                    W["lnb"][1], 24, 8, tag + "lb")
    for tc in range(T // TCW):
        ln_shift_emit(K, YB, XN1, tc * TCW, (tc + 1) * TCW, lgc, lbc,
                      Va["sq8"], Va["bcl"], Va["lxf"], tag=tag + "ln")

    Vb = K.arena.carve(keeps + [XN1], [
        ("wring", 4, 2048, BF16), ("wring2", 4, 2048, BF16),
        ("gtq", 6 * NCH, 512, BF16), ("hbq", 6 * NCH, 512, BF16),
        ("fb1", 1, 16, F32), ("fst", 2, 512, BF16)])
    GTQ, HBQ = Vb["gtq"], Vb["hbq"]
    FB = Vb["fb1"]
    b2c = load_cols(K, bass.AP(W["b2"][0].tensor, li * 1024, [[1, 1024]]),
                    W["b2"][1], 0, 8, tag + "b2")
    for kt in range(8):
        p.step("dve", lambda e, kt=kt, d=Y.tile(kt), b=b2c[kt]:
               e.tensor_scalar(d, d, b, None, op0=AluOpType.add),
               reads=[Y.R(kt), K.cols_reg], writes=[Y.R(kt)], tag=tag + "b2a")
    xsrc = src_from_view(XN1, T)
    qsz = [768, 768, 768, 426]
    qoff = [0, 768, 1536, 2304]
    for fq in range(4):
        f0, fn = qoff[fq], qsz[fq]
        nqt = ceil_div(fn, 128)
        for mi in range(nqt):
            msz = min(128, fn - mi * 128)
            p.dma("sp", FB.tile(0, mi, mi + 1, 0, msz),
                  bass.AP(W["b1"][0].tensor, li * 5460 + DI + f0 + mi * 128,
                          [[1, msz], [1, 1]]),
                  reads=[W["b1"][1]], writes=FB.wr(0), tag=tag + "b1g")
            p.dma("sp", FB.tile(0, 8 + mi, 9 + mi, 0, msz),
                  bass.AP(W["b1"][0].tensor, li * 5460 + f0 + mi * 128,
                          [[1, msz], [1, 1]]),
                  reads=[W["b1"][1]], writes=[FB.R(0)], tag=tag + "b1a")
        wsg = WStream(K, tag + f"wg{fq}", slice3d(W["W1"][0], li, 1024, 5460),
                      W["W1"][1], 1024, fn, False, Vb["wring"], mg=2,
                      col0=DI + f0)

        def ep_g(mi, msz, bank, a, b):
            ci = a // 512
            K.p.step("act", lambda e, s=K.ps.ap(bank, 0, msz, 0, b - a),
                     d=GTQ.tile(mi * NCH + ci, 0, b - a, 0, msz),
                     bc=FB.tile(0, mi, mi + 1, 0, msz):
                     e.activation(d, s, AF.Gelu, bias=bc),
                     reads=[K.ps.R(bank), FB.R(0)],
                     writes=GTQ.wr(mi * NCH + ci), tag=tag + "eg")
        linear(K, wsg, xsrc, ep_g, 0, T, tag=tag + f"g{fq}")
        wsa = WStream(K, tag + f"wa{fq}", slice3d(W["W1"][0], li, 1024, 5460),
                      W["W1"][1], 1024, fn, False, Vb["wring2"], mg=2, col0=f0)

        def ep_a(mi, msz, bank, a, b):
            ci = a // 512
            r = (mi + ci) % 2
            K.p.step("dve", lambda e, s=K.ps.ap(bank, 0, msz, 0, b - a),
                     d=Vb["fst"].tile(r, 0, b - a, 0, msz),
                     bc=FB.tile(0, 8 + mi, 9 + mi, 0, msz):
                     e.tensor_scalar(d, s, bc, None, op0=AluOpType.add),
                     reads=[K.ps.R(bank), FB.R(0)], writes=Vb["fst"].wr(r),
                     tag=tag + "ea")
            K.p.step("dve", lambda e, s=Vb["fst"].tile(r, 0, b - a, 0, msz),
                     g=GTQ.tile(mi * NCH + ci, 0, b - a, 0, msz),
                     d=HBQ.tile(mi * NCH + ci, 0, b - a, 0, msz):
                     e.tensor_tensor(d, s, g, op=AluOpType.mult),
                     reads=[Vb["fst"].R(r), GTQ.R(mi * NCH + ci)],
                     writes=HBQ.wr(mi * NCH + ci), tag=tag + "ea")
        linear(K, wsa, xsrc, ep_a, 0, T, tag=tag + f"a{fq}")

        def hsrc(kt, ksz, a, b):
            ti = kt * NCH + a // 512
            return HBQ.tile(ti, 0, b - a, 0, ksz), [HBQ.R(ti)]
        ws2 = WStream(K, tag + f"w2{fq}", slice3d(W["W2"][0], li, 2730, 1024),
                      W["W2"][1], fn, 1024, False, Vb["wring"], mg=2, k0=f0)

        def ep_y(mi, msz, bank, a, b):
            K.p.step("dve", lambda e, mi=mi, s=K.ps.ap(bank, 0, msz, 0, b - a),
                     d=Y.tile(mi, a, b):
                     e.tensor_tensor(d, d, s, op=AluOpType.add),
                     reads=[K.ps.R(bank), Y.R(mi)], writes=[Y.R(mi)],
                     tag=tag + "ey")
        linear(K, ws2, hsrc, ep_y, 0, T, tag=tag + f"y{fq}")



def emit_batch(K, x_in, out, T, WD, SCR, tag="b"):
    p = K.p
    tp = getattr(K, "tapd", None) if tag == "b0" else None
    V0 = K.arena.carve([], [("xt", 8, T, F32), ("xnr", 2, 1024, F32)])
    XT = V0["xt"]
    load_xt(K, x_in[0], x_in[1], XT, V0["xnr"], T, tag=tag + "ld")

    for lay in range(2):
        lt = f"{tag}L{lay}"
        Vl = K.arena.carve([XT], [("xb", 8, T, BF16), ("bc", 2, T, F32),
                                  ("xnt", 8, T, BF16)])
        XNT = Vl["xnt"]
        ln_emit(K, XT, Vl["xb"], Vl["bc"], XNT, T, tag=lt + "ln")
        if tp:
            for kt in range(8):
                r0 = 2 * lay * 1024 + kt * 128
                p.dma("sp", tp["b"][0][r0:r0 + 128, 0:T], XNT.tile(kt),
                      reads=[XNT.R(kt)], writes=[tp["b"][1]], tag=lt + "tpx")
        for kt in range(8):
            p.dma("sp", SCR["xt"][0][kt * 128:(kt + 1) * 128, 0:T], XT.tile(kt),
                  reads=[XT.R(kt)], writes=[SCR["xt"][1]], tag=lt + "xtsp")
        if lay == 0:
            stage_l0_attn(K, XNT, T, WD["l0a"], SCR["ao"], SCR["xnt"],
                          tag=lt + "A")
        else:
            stage_l1_attn(K, XNT, T, WD["l1a"], SCR["ao"], SCR["xnt"],
                          tag=lt + "A")
        Vf = K.arena.carve([], [("y", 8, T, F32), ("xntr", 8, T, BF16)])
        Y, XNTR = Vf["y"], Vf["xntr"]
        for kt in range(8):
            p.dma("sp", XNTR.tile(kt), SCR["xnt"][0][kt * 128:(kt + 1) * 128, 0:T],
                  reads=[SCR["xnt"][1]], writes=XNTR.wr(kt), tag=lt + "xnr")
        if lay == 0:
            stage_l0_ffn(K, XNTR, Y, T, WD["l0f"], tag=lt + "F")
        else:
            l1cb = None
            if tp:
                tslot = {(0, "a"): 5, (0, "b"): 6, (1, "b"): 7,
                         (3, "b"): 8, (5, "b"): 9}

                def l1cb(li, ph):
                    sl = tslot.get((li, ph))
                    if sl is None:
                        return
                    for kt in range(8):
                        r0 = sl * 1024 + kt * 128
                        p.dma("sp", tp["f"][0][r0:r0 + 128, 0:T], Y.tile(kt),
                              reads=[Y.R(kt)], writes=[tp["f"][1]],
                              tag=lt + f"tpy{sl}")
            stage_l1_ffn(K, XNTR, Y, T, WD["l1f"], tag=lt + "F", tap=l1cb)
        Vc = K.arena.carve([Y], [("xt", 8, T, F32), ("aor", 8, T, BF16)])
        XT = Vc["xt"]
        AOR = Vc["aor"]
        for kt in range(8):
            p.dma("sp", XT.tile(kt), SCR["xt"][0][kt * 128:(kt + 1) * 128, 0:T],
                  reads=[SCR["xt"][1]], writes=XT.wr(kt), tag=lt + "xtr")
            p.dma("sp", AOR.tile(kt), SCR["ao"][0][kt * 128:(kt + 1) * 128, 0:T],
                  reads=[SCR["ao"][1]], writes=AOR.wr(kt), tag=lt + "aor")
        if tp:
            for kt in range(8):
                r0 = (2 * lay + 1) * 1024 + kt * 128
                p.dma("sp", tp["b"][0][r0:r0 + 128, 0:T], AOR.tile(kt),
                      reads=[AOR.R(kt)], writes=[tp["b"][1]], tag=lt + "tpa")
                r1 = 2 * lay * 1024 + kt * 128
                p.dma("sp", tp["f"][0][r1:r1 + 128, 0:T], Y.tile(kt),
                      reads=[Y.R(kt)], writes=[tp["f"][1]], tag=lt + "tpy")
        for kt in range(8):
            p.step("dve", lambda e, kt=kt, d=XT.tile(kt), s=Y.tile(kt):
                   e.tensor_tensor(d, d, s, op=AluOpType.add),
                   reads=[Y.R(kt), XT.R(kt)], writes=[XT.R(kt)], tag=lt + "cb1")
            p.step("dve", lambda e, kt=kt, d=XT.tile(kt), s=AOR.tile(kt):
                   e.tensor_tensor(d, d, s, op=AluOpType.add),
                   reads=[AOR.R(kt), XT.R(kt)], writes=[XT.R(kt)], tag=lt + "cb2")
        if tp and lay == 0:
            for kt in range(8):
                r0 = 4096 + kt * 128
                p.dma("sp", tp["f"][0][r0:r0 + 128, 0:T], XT.tile(kt),
                      reads=[XT.R(kt)], writes=[tp["f"][1]], tag=lt + "tpx1")

    Vs = K.arena.carve([XT], [("xnr", 2, 1024, F32)])
    store_xt(K, XT, out[0], out[1], Vs["xnr"], T, tag=tag + "st")


def declare_weights(nc, T, b):
    W = {}

    def D(name, shape):
        return (nc.dram_tensor(name, shape, F32, kind="ExternalInput").ap(),
                Region("d_" + name))
    w = {}
    w["L0_attn_Wqkv"] = D("L0_attn_Wqkv", [1024, 3072])
    w["L0_attn_Wproj"] = D("L0_attn_Wproj", [1024, 1024])
    for n in ["lambda_q1", "lambda_k1", "lambda_q2", "lambda_k2"]:
        w["L0_attn_" + n] = D("L0_attn_" + n, [32])
    w["L0_attn_rms_scale"] = D("L0_attn_rms_scale", [64])
    for n, sh in [("g1", [1024]), ("g2", [1024]), ("Wq", [1024, 1024]),
                  ("Wkc", [256, 1024]), ("Wvc", [256, 1024]),
                  ("Wo", [1024, 1024]), ("Wr", [8, 1024]),
                  ("expert_bias", [8]), ("sW1", [1024, 1024]),
                  ("sW2", [1024, 1024]), ("sW3", [1024, 1024]),
                  ("eW1", [8, 1024, 1024]), ("eW2", [8, 1024, 1024]),
                  ("eW3", [8, 1024, 1024])]:
        w["L0_ffn_" + n] = D("L0_ffn_" + n, sh)
    for n, sh in [("w_qkv", [3072, 1024]), ("b_qkv", [3072]),
                  ("w_out", [1024, 1024]), ("b_out", [1024])]:
        w["L1_attn_" + n] = D("L1_attn_" + n, sh)
    for n, sh in [("Wqkv", [6, 1024, 3072]), ("q_scale", [6, 64]),
                  ("k_scale", [6, 64]), ("Wo", [6, 1024, 1024]),
                  ("ln_g", [6, 1024]), ("ln_b", [6, 1024]),
                  ("W1", [6, 1024, 5460]), ("b1", [6, 5460]),
                  ("W2", [6, 2730, 1024]), ("b2", [6, 1024])]:
        w["L1_ffn_" + n] = D("L1_ffn_" + n, sh)
    def DB(name, shape, dt):
        return (nc.dram_tensor(name, shape, dt, kind="ExternalInput").ap(),
                Region("d_" + name))
    w["t_ropec"] = DB("t_ropec", [128, T], F32)
    w["t_ropes"] = DB("t_ropes", [128, T], F32)
    w["t_cmask"] = DB("t_cmask", [512, 512], BF16)
    for n in ["xposcs", "xposss", "xposck", "xpossk"]:
        w["t_" + n] = DB("t_" + n, [128, T], BF16)
    w["t_lmask"] = DB("t_lmask", [256, 512], F32)
    w["t_lmask0"] = DB("t_lmask0", [256, 512], F32)

    WD = {
        "l0a": {"wqkv": w["L0_attn_Wqkv"], "wproj": w["L0_attn_Wproj"]},
        "l0f": {"Wq": w["L0_ffn_Wq"], "Wkc": w["L0_ffn_Wkc"],
                "Wvc": w["L0_ffn_Wvc"], "Wo": w["L0_ffn_Wo"],
                "Wr": w["L0_ffn_Wr"], "eb": w["L0_ffn_expert_bias"],
                "g1": w["L0_ffn_g1"], "g2": w["L0_ffn_g2"],
                "sW1": w["L0_ffn_sW1"], "sW2": w["L0_ffn_sW2"],
                "sW3": w["L0_ffn_sW3"],
                "eW1": ([slice3d(w["L0_ffn_eW1"][0], e, 1024, 1024)
                         for e in range(8)], w["L0_ffn_eW1"][1]),
                "eW2": ([slice3d(w["L0_ffn_eW2"][0], e, 1024, 1024)
                         for e in range(8)], w["L0_ffn_eW2"][1]),
                "eW3": ([slice3d(w["L0_ffn_eW3"][0], e, 1024, 1024)
                         for e in range(8)], w["L0_ffn_eW3"][1])},
        "l1a": {"wqkv": w["L1_attn_w_qkv"], "bqkv": w["L1_attn_b_qkv"],
                "wout": w["L1_attn_w_out"], "bout": w["L1_attn_b_out"],
                "ropec": w["t_ropec"], "ropes": w["t_ropes"],
                "cmask": w["t_cmask"]},
        "l1f": {"Wqkv": w["L1_ffn_Wqkv"], "qs": w["L1_ffn_q_scale"],
                "ks": w["L1_ffn_k_scale"], "Wo": w["L1_ffn_Wo"],
                "lng": w["L1_ffn_ln_g"], "lnb": w["L1_ffn_ln_b"],
                "W1": w["L1_ffn_W1"], "b1": w["L1_ffn_b1"],
                "W2": w["L1_ffn_W2"], "b2": w["L1_ffn_b2"],
                "xposcs": w["t_xposcs"], "xposss": w["t_xposss"],
                "xposck": w["t_xposck"], "xpossk": w["t_xpossk"],
                "lmask": w["t_lmask"], "lmask0": w["t_lmask0"]},
        "raw": w,
    }
    return WD


def build_core_program(T, NB, tap=False):
    import contextlib
    nc = bass.Bass("TRN2", target_bir_lowering=False)
    p = Prog(nc)
    cfg = {"T": T}
    x_d = nc.dram_tensor("x", [NB, T, 1024], F32, kind="ExternalInput").ap()
    o_d = nc.dram_tensor("out", [NB, T, 1024], F32, kind="ExternalOutput").ap()
    r_x = Region("d_x")
    r_o = Region("d_out")
    WD = declare_weights(nc, T, 0)
    K = make_ctx(nc, p, cfg)
    K.arena = Arena(nc, "arena", (nc.sbuf_bytes_remaining // 2) - 1024)
    K.arena.memset_all(p)
    SCR = {}
    for n, dt in [("xt", F32), ("ao", BF16), ("xnt", BF16)]:
        t = nc.dram_tensor("scr_" + n, [1024, T], dt, kind="Internal").ap()
        SCR[n] = (t, Region("scr_" + n))
    if tap:
        tb = nc.dram_tensor("o_tapb", [4096, T], BF16, kind="ExternalOutput").ap()
        tf = nc.dram_tensor("o_tapf", [10240, T], F32, kind="ExternalOutput").ap()
        K.tapd = {"b": (tb, Region("d_tapb")), "f": (tf, Region("d_tapf"))}
    raw = WD["raw"]
    precompute_l0a_scalars(
        K, [raw["L0_attn_" + n][0].unsqueeze(1) for n in
            ["lambda_q1", "lambda_k1", "lambda_q2", "lambda_k2"]],
        [raw["L0_attn_" + n][1] for n in
         ["lambda_q1", "lambda_k1", "lambda_q2", "lambda_k2"]],
        raw["L0_attn_rms_scale"][0].unsqueeze(1), raw["L0_attn_rms_scale"][1])
    for b in range(NB):
        xb_ap = bass.AP(x_d.tensor, b * T * 1024, [[1024, T], [1, 1024]])
        ob_ap = bass.AP(o_d.tensor, b * T * 1024, [[1024, T], [1, 1024]])
        emit_batch(K, (xb_ap, r_x), (ob_ap, r_o), T, WD, SCR, tag=f"b{b}")
    fin = [r_o]
    if tap:
        fin += [K.tapd["b"][1], K.tapd["f"][1]]
    p.finish(fin)
    es = contextlib.ExitStack()
    sems = make_sems(nc, es)
    with nc.Block() as block:
        p.realize(block, sems)
    es.close()
    return nc, p


def input_map_for_core(xs, ins, T):
    ca = const_arrays(T)
    m = {"x": np.ascontiguousarray(xs)}
    for k in ins:
        if k != "L0_attn_x":
            m[k] = np.ascontiguousarray(np.asarray(ins[k], np.float32))
    m["t_ropec"] = ca["c_ropec"]
    m["t_ropes"] = ca["c_ropes"]
    m["t_cmask"] = np.asarray(ca["c_cmask"]).reshape(512, 512)
    for n in ["xposcs", "xposss", "xposck", "xpossk"]:
        m["t_" + n] = np.asarray(ca["c_" + n])
    m["t_lmask"] = np.asarray(ca["c_lmask"]).reshape(256, 512)
    m["t_lmask0"] = np.asarray(ca["c_lmask0"]).reshape(256, 512)
    for (name, shape, dt) in CONST_SPECS:
        m["c_" + name] = ca["c_" + name]
    return m


def kernel(**inputs):
    T = 2048
    NB = 2
    x = np.asarray(inputs["L0_attn_x"], np.float32)
    nc, _ = build_core_program(T, NB)
    in_maps = []
    for c in range(8):
        xs = x[c * NB:(c + 1) * NB]
        in_maps.append(input_map_for_core(xs, inputs, T))
    res = run_bass_kernel_spmd(nc, in_maps, core_ids=list(range(8)))
    return np.concatenate([r["out"] for r in res.results], axis=0)


# version 03
# speedup vs baseline: 1.4500x; 1.3022x over previous
import math
import numpy as np
import ml_dtypes
import concourse.bass as bass
import concourse.mybir as mybir
from concourse.alu_op_type import AluOpType
from concourse.bass_utils import run_bass_kernel_spmd

F32 = mybir.dt.float32
BF16 = mybir.dt.bfloat16
AF = mybir.ActivationFunctionType
AX = mybir.AxisListType

ENGINES = ("pe", "act", "dve", "pool", "sp")
DMA_QUEUES = ("sp", "act", "pool")
DMA_LANES = 6

LAMBDA_INIT = 0.8 - 0.6 * math.exp(-0.3 * 1.0)



class Region:
    __slots__ = ("name",)

    def __init__(self, name=""):
        self.name = name

    def __repr__(self):
        return f"R({self.name})"


class Step:
    __slots__ = ("eng", "fn", "reads", "writes", "idx", "is_dma", "needs_inc",
                 "waits", "tag", "lane")

    def __init__(self, eng, fn, reads, writes, idx, is_dma, tag=""):
        self.eng = eng
        self.fn = fn
        self.reads = reads
        self.writes = writes
        self.idx = idx
        self.is_dma = is_dma
        self.needs_inc = False
        self.waits = []
        self.tag = tag


class Prog:
    def __init__(self, nc):
        self.nc = nc
        self.steps = {e: [] for e in ENGINES}
        self.last_writer = {}
        self.readers = {}
        self.deps = []
        self._realized = False

    def _record(self, step):
        deps = {}
        for r in step.reads:
            w = self.last_writer.get(r)
            if w is not None and w is not step:
                deps[id(w)] = w
        for r in step.writes:
            w = self.last_writer.get(r)
            if w is not None and w is not step:
                deps[id(w)] = w
            for rd in self.readers.get(r, ()):
                if rd is not step:
                    deps[id(rd)] = rd
        for p in deps.values():
            self.deps.append((step, p))
        for r in step.reads:
            self.readers.setdefault(r, []).append(step)
        for r in step.writes:
            self.last_writer[r] = step
            self.readers[r] = []
        self.steps[step.eng].append(step)
        return step

    def step(self, eng, fn, reads=(), writes=(), tag=""):
        assert eng in ENGINES and not self._realized
        s = Step(eng, fn, list(reads), list(writes), len(self.steps[eng]), False, tag)
        return self._record(s)

    def dma(self, queue, out_ap, in_ap, reads=(), writes=(), tag="", **kw):
        assert queue in DMA_QUEUES
        fn = lambda e: e.dma_start(out_ap, in_ap, **kw)
        s = Step(queue, fn, list(reads), list(writes), len(self.steps[queue]), True, tag)
        return self._record(s)

    def finish(self, regions):
        s = Step("sp", None, list(regions), [], len(self.steps["sp"]), False, "finish")
        return self._record(s)

    def realize(self, block, sems):
        assert not self._realized
        self._realized = True
        for (c, p) in self.deps:
            if c.eng == "pe" and p.eng == "pe":
                continue
            p.needs_inc = True
        comp_sem = {}
        comp_val = {}
        for e in ENGINES:
            lst = self.steps[e]
            lane_cnt = [0] * DMA_LANES
            ndma = 0
            ninc = 0
            for s in lst:
                if s.is_dma:
                    lane = ndma % DMA_LANES
                    s.lane = lane
                    lane_cnt[lane] += 1
                    ndma += 1
                    comp_sem[id(s)] = ("dma", e, lane)
                    comp_val[id(s)] = 16 * lane_cnt[lane]
                    if lane_cnt[lane] >= 2:
                        s.waits.append((("dma", e, lane), 16 * (lane_cnt[lane] - 1)))
                else:
                    comp_sem[id(s)] = ("cmp", e)
                    if s.needs_inc:
                        ninc += 1
                        comp_val[id(s)] = ninc
                    else:
                        comp_val[id(s)] = None
        for (c, p) in self.deps:
            if c.eng == "pe" and p.eng == "pe":
                continue
            sk = comp_sem[id(p)]
            v = comp_val[id(p)]
            assert v is not None, f"dep on non-covered step {p.eng}:{p.idx} {p.tag}"
            c.waits.append((sk, v))

        prog_self = self
        eng_attr = {"pe": "tensor", "act": "scalar", "dve": "vector",
                    "pool": "gpsimd", "sp": "sync"}

        def make_section(e):
            def section(engobj):
                lastwait = {}
                for s in prog_self.steps[e]:
                    for (sk, v) in sorted(set(s.waits), key=str):
                        if lastwait.get(sk, -1) >= v:
                            continue
                        if sk[0] == "dma":
                            sem = sems[sk[1]][1][sk[2]]
                        else:
                            sem = sems[sk[1]][0]
                        engobj.wait_ge(sem, v)
                        lastwait[sk] = v
                    if s.fn is None:
                        continue
                    inst = s.fn(engobj)
                    if s.is_dma:
                        inst.then_inc(sems[e][1][s.lane], 16)
                    elif s.needs_inc:
                        inst.then_inc(sems[e][0], 1)
            return section

        for e in ENGINES:
            if not self.steps[e]:
                continue
            getattr(block, eng_attr[e])(make_section(e))

    def stats(self):
        n = {e: len(self.steps[e]) for e in ENGINES}
        nw = sum(len(s.waits) for e in ENGINES for s in self.steps[e])
        ni = sum(1 for e in ENGINES for s in self.steps[e] if s.needs_inc or s.is_dma)
        return f"steps={n} waits={nw} incs={ni} deps={len(self.deps)}"


def make_sems(nc, es):
    sems = {}
    for e in ENGINES:
        cmp_s = es.enter_context(nc.semaphore(f"c_{e}"))
        dma_s = ([es.enter_context(nc.semaphore(f"d_{e}{i}")) for i in range(DMA_LANES)]
                 if e in DMA_QUEUES else None)
        sems[e] = (cmp_s, dma_s)
    return sems



class Buf:

    def __init__(self, nc, name, nt, tw, dtype):
        self.name = name
        self.nt = nt
        self.tw = tw
        self.dtype = dtype
        self.t = nc.alloc_sbuf_tensor(name, [128, nt * tw], dtype)
        self.regs = [Region(f"{name}{i}") for i in range(nt)]

    def tile(self, i, a=0, b=None, p0=0, p1=128):
        b = self.tw if b is None else b
        return self.t[p0:p1, i * self.tw + a: i * self.tw + b]

    def R(self, i):
        return self.regs[i]

    def raw(self, p0, p1, c0, c1):
        return self.t[p0:p1, c0:c1]

    def allR(self):
        return list(self.regs)


class PsumPool:

    def __init__(self, nc):
        self.banks = [nc.alloc_psum_tensor(f"psb{i}", [128, 512], F32) for i in range(8)]
        self.regs = [Region(f"psb{i}") for i in range(8)]
        self.rr = 0
        self.held = set()

    def take(self, hold=False):
        for k in range(8):
            i = (self.rr + k) % 8
            if i not in self.held:
                self.rr = (i + 1) % 8
                if hold:
                    self.held.add(i)
                return i
        raise RuntimeError("all psum banks held")

    def release(self, i):
        self.held.discard(i)

    def ap(self, i, p0=0, p1=128, a=0, b=512):
        return self.banks[i][p0:p1, a:b]

    def bf16(self, i, p0=0, p1=128, a=0, b=1024):
        return self.banks[i].ap().bitcast(BF16)[p0:p1, a:b]

    def R(self, i):
        return self.regs[i]



class KC:

    def __init__(self, nc, p, cfg):
        self.nc = nc
        self.p = p
        self.cfg = cfg
        self.bufs = {}
        self.dram = {}
        self.dram_regs = {}

    def dreg(self, name):
        if name not in self.dram_regs:
            self.dram_regs[name] = Region(f"D:{name}")
        return self.dram_regs[name]


def ceil_div(a, b):
    return (a + b - 1) // b



def build_constants(T):
    c = {}
    c["c_ident"] = np.eye(128, dtype=ml_dtypes.bfloat16)
    c["c_ones1x128"] = np.ones((1, 128), np.float32)
    c["c_onescol"] = np.ones((128, 1), ml_dtypes.bfloat16)
    dh = 64
    inv_freq = 1.0 / 10000.0 ** (np.arange(0, dh, 2, dtype=np.float64) / dh)
    t = np.arange(T, dtype=np.float64)
    fr = t[None, :] * inv_freq[:, None]
    cos2 = np.repeat(np.cos(fr), 2, axis=0)
    sin2 = np.repeat(np.sin(fr), 2, axis=0)
    c["c_ropec"] = np.concatenate([cos2, cos2], 0).astype(np.float32)
    c["c_ropes"] = np.concatenate([sin2, sin2], 0).astype(np.float32)
    R = np.zeros((64, 64), np.float32)
    for p_ in range(32):
        R[2 * p_, 2 * p_ + 1] = -1.0
        R[2 * p_ + 1, 2 * p_] = 1.0
    Rh = np.zeros((64, 64), np.float32)
    for d in range(32):
        Rh[d, d + 32] = -1.0
        Rh[d + 32, d] = 1.0
    pf = np.zeros((128, 128), np.float32)
    pf[0:64, 0:64] = R.T
    pf[64:128, 64:128] = R.T
    c["c_permf"] = pf.astype(ml_dtypes.bfloat16)
    ph = np.zeros((128, 128), np.float32)
    ph[0:64, 0:64] = Rh.T
    ph[64:128, 64:128] = Rh.T
    c["c_permh"] = ph.astype(ml_dtypes.bfloat16)
    half = np.arange(0, dh, 2, dtype=np.float64)
    invf = 1.0 / 10000.0 ** (half / dh)
    fr2 = t[None, :] * invf[:, None]
    freqs = np.concatenate([fr2, fr2], 0)
    base_scale = (half + 0.4 * dh) / (1.4 * dh)
    power = (t - T // 2) / (256 // 2)
    scale = base_scale[:, None] ** power[None, :]
    scale = np.concatenate([scale, scale], 0)
    def _st(a):
        return np.concatenate([a, a], 0).astype(ml_dtypes.bfloat16)
    c["c_xposcs"] = _st(np.cos(freqs) * scale)
    c["c_xposss"] = _st(np.sin(freqs) * scale)
    c["c_xposck"] = _st(np.cos(freqs) / scale)
    c["c_xpossk"] = _st(np.sin(freqs) / scale)
    masks = []
    for r in (0, 128, 256, 384):
        kg = r + np.arange(128)[:, None]
        qg = np.arange(512)[None, :]
        masks.append((kg <= qg).astype(np.float32))
    c["c_cmask"] = np.stack(masks).astype(ml_dtypes.bfloat16)
    lm = []
    for a0 in (0, 128):
        a = a0 + np.arange(128)[:, None]
        bcol = np.arange(512)[None, :]
        ok = (a <= bcol) & (bcol <= a + 256)
        lm.append(np.where(ok, 0.0, -3.0e38).astype(np.float32))
    c["c_lmask"] = np.stack(lm)
    lm0 = []
    for a0 in (0, 128):
        a = a0 + np.arange(128)[:, None]
        bcol = np.arange(512)[None, :]
        ok = (a <= bcol) & (bcol <= a + 256) & (bcol >= 256)
        lm0.append(np.where(ok, 0.0, -3.0e38).astype(np.float32))
    c["c_lmask0"] = np.stack(lm0)
    c["c_iota8"] = np.broadcast_to(np.arange(8, dtype=np.float32), (128, 8)).copy()
    e2 = np.zeros((128, 2), ml_dtypes.bfloat16)
    e2[0:64, 0] = 1
    e2[64:128, 1] = 1
    c["c_e2"] = e2
    c["c_ones64"] = np.ones((64, 1), ml_dtypes.bfloat16)
    return c



def dma_col_f32(K, dst_ap, dst_regs, src_ap, src_reg, tag="col"):
    K.p.dma("sp", dst_ap, src_ap, reads=[src_reg], writes=list(dst_regs), tag=tag)


class WStream:

    def __init__(self, K, name, w_ap, w_reg, n_in, n_out, trans,
                 ring_buf, wtmp=None, mg=2, mw=128, col0=0, k0=0):
        self.K = K
        self.name = name
        self.w_ap = w_ap
        self.w_reg = w_reg
        self.n_in = n_in
        self.n_out = n_out
        self.trans = trans
        self.KT = ceil_div(n_in, 128)
        self.mg = mg
        self.mw = mw
        self.col0 = col0
        self.k0 = k0
        self.ew = mg * self.KT * mw
        self.ring_buf = ring_buf
        self.nring = ring_buf.nt * ring_buf.tw // self.ew
        assert self.nring >= 2, f"{name}: ring too small"
        self.wtmp = wtmp
        self.n_mt = ceil_div(n_out, mw)
        self.n_groups = ceil_div(self.n_mt, mg)
        self.slot_regs = [Region(f"{name}s{i}") for i in range(self.nring)]
        self._first = True

    def entry_ap(self, slot, m_local, kt, ksz=128, msz=None):
        msz = self.mw if msz is None else msz
        off = slot * self.ew + (m_local * self.KT + kt) * self.mw
        return self.ring_buf.raw(0, ksz, off, off + msz)

    def stage_group(self, g, seq=None):
        K = self.K
        p = K.p
        slot = (g if seq is None else seq) % self.nring
        m0 = g * self.mg
        nmt = min(self.mg, self.n_mt - m0)
        ent_reg = self.slot_regs[slot]
        claims = []
        legacy = getattr(self.ring_buf, "_war", None)
        if legacy:
            self.ring_buf._war = None
            claims += legacy
        if self._first:
            self._first = False
            prev = getattr(self.ring_buf, "_ws_owner", None)
            if prev is not None and prev is not self.slot_regs:
                claims += prev
            self.ring_buf._ws_owner = self.slot_regs
        if claims:
            self.K.p.step("pool", lambda e: e.nop(),
                          reads=[], writes=list(self.slot_regs) + claims,
                          tag=self.name + "war")
        KT, n_in, n_out, mw = self.KT, self.n_in, self.n_out, self.mw
        if not self.trans:
            rs = self.w_ap.shape[1]
            w_off = self.w_ap.offset
            kfull = n_in // 128
            ktail = n_in - kfull * 128
            for m in range(nmt):
                mj = min(mw, n_out - (m0 + m) * mw)
                c0 = self.col0 + (m0 + m) * mw
                bc = slot * self.ew + m * KT * mw
                if kfull and mj == mw:
                    s3 = bass.AP(self.w_ap.tensor, w_off + self.k0 * rs + c0,
                                 [[rs, 128], [128 * rs, kfull], [1, mj]])
                    dst = self.ring_buf.raw(0, 128, bc, bc + kfull * mw)
                    p.dma("pool", dst, s3, reads=[self.w_reg],
                          writes=[ent_reg], tag=f"{self.name}g{g}")
                else:
                    for kt in range(kfull):
                        src = self.w_ap[self.k0 + kt * 128:
                                        self.k0 + kt * 128 + 128, c0:c0 + mj]
                        dst = self.entry_ap(slot, m, kt, 128, mj)
                        p.dma("pool", dst, src, reads=[self.w_reg],
                              writes=[ent_reg], tag=f"{self.name}g{g}")
                if ktail:
                    kr = 32 if ktail <= 32 else (64 if ktail <= 64 else 128)
                    zb = 0 if ktail <= 32 else (32 if ktail <= 64 else 64)
                    dcol = bc + kfull * mw
                    p.step("dve", lambda e, zb=zb, kr=kr, a=dcol, b=dcol + mj:
                           e.memset(self.ring_buf.raw(zb, kr, a, b), 0.0),
                           reads=[], writes=[ent_reg], tag=f"{self.name}z{g}")
                    src = self.w_ap[self.k0 + kfull * 128: self.k0 + n_in,
                                    c0:c0 + mj]
                    dst = self.entry_ap(slot, m, kfull, ktail, mj)
                    p.dma("pool", dst, src, reads=[self.w_reg],
                          writes=[ent_reg], tag=f"{self.name}g{g}")
        else:
            assert mw == 128
            for m in range(nmt):
                mj = min(mw, n_out - (m0 + m) * mw)
                tslot = (m0 + m) % self.wtmp.nt
                treg = self.wtmp.R(tslot)
                r0 = self.col0 + (m0 + m) * mw
                src = self.w_ap[r0:r0 + mj, self.k0:self.k0 + n_in]
                p.dma("pool", self.wtmp.tile(tslot, 0, n_in, 0, mj), src,
                      reads=[self.w_reg], writes=[treg], tag=f"{self.name}g{g}t")
                for kt in range(KT):
                    ksz = min(128, n_in - kt * 128)
                    bank = K.ps.take()
                    pr = K.ps.R(bank)
                    src_t = self.wtmp.tile(tslot, kt * 128, kt * 128 + ksz, 0, mj)
                    dst_ps = K.ps.bf16(bank, 0, ksz, 0, mj)
                    dst_sb = self.entry_ap(slot, m, kt, ksz, mj)
                    p.step("pe", lambda e, s=src_t, d=dst_ps, i=K.cbuf["ident"]:
                           e.transpose(d, s, i),
                           reads=[treg, K.creg["ident"]], writes=[pr],
                           tag=f"{self.name}tp{g}")
                    p.step("dve", lambda e, s=dst_ps, d=dst_sb: e.tensor_copy(d, s),
                           reads=[pr], writes=[ent_reg], tag=f"{self.name}tc{g}")
        return slot, ent_reg




class View:

    def __init__(self, arena, name, offs, nt, tw, dtype):
        self.arena = arena
        self.name = name
        self.offs = offs
        self.nt = nt
        self.tw = tw
        self.dtype = dtype
        self.mult = 2 if dtype == F32 else 1
        self.regs = [Region(f"{name}{i}") for i in range(nt)]
        self._war = None
        self._claimed = set()

    def tile(self, i, a=0, b=None, p0=0, p1=128):
        b = self.tw if b is None else b
        base = self.arena.t.ap() if self.dtype == BF16 else self.arena.t.ap().bitcast(F32)
        o = self.offs[i] // self.mult
        return base[p0:p1, o + a: o + b]

    def R(self, i):
        return self.regs[i]

    def wr(self, i):
        if self._war and i not in self._claimed:
            self._claimed.add(i)
            return [self.regs[i]] + self._war
        return [self.regs[i]]

    def raw(self, p0, p1, c0, c1):
        assert self.dtype == BF16
        base = self.arena.t.ap()
        return base[p0:p1, self.offs[0] + c0: self.offs[0] + c1]

    def spans(self):
        return [(o, o + self.tw * self.mult) for o in self.offs]

    def allR(self):
        return list(self.regs)


class Arena:
    def __init__(self, nc, name, cols):
        self.t = nc.alloc_sbuf_tensor(name, [128, cols], BF16)
        self.cols = cols
        self.views = {}
        self.init_reg = Region(name + "_init")

    def memset_all(self, p):
        for c0 in range(0, self.cols, 32768):
            c1 = min(c0 + 32768, self.cols)
            p.step("dve", lambda e, a=c0, b=c1: e.memset(self.t[0:128, a:b], 0.0),
                   reads=[], writes=[self.init_reg], tag="ainit")

    def carve(self, keep, new):
        legacy = []
        for v in list(self.views.values()):
            if v not in keep:
                legacy.extend(v.regs)
                legacy.extend(getattr(v, "_ws_owner", None) or [])
        self.views = {v.name: v for v in keep}
        used = sorted(s for v in keep for s in v.spans())
        out = {}

        def alloc(sz):
            off = 0
            while True:
                hit = None
                for (a, b) in used:
                    if off < b and off + sz > a:
                        hit = b
                        break
                if hit is None:
                    break
                off = hit
            assert off + sz <= self.cols, \
                f"arena overflow: need {sz} at {off}, cols={self.cols}"
            used.append((off, off + sz))
            used.sort()
            return off
        for spec in new:
            (name, nt, tw, dtype) = spec[:4]
            contig = name.startswith("wring") or name.startswith("wqs") \
                or name.startswith("wos") or (len(spec) > 4 and spec[4])
            mult = 2 if dtype == F32 else 1
            if contig:
                base = alloc(nt * tw * mult)
                offs = [base + i * tw * mult for i in range(nt)]
            else:
                offs = [alloc(tw * mult) for _ in range(nt)]
            v = View(self, name, offs, nt, tw, dtype)
            v._war = list(legacy) + [self.init_reg]
            self.views[name] = v
            out[name] = v
        return out



def linear(K, ws, src_fn, ep, tok0, tok1, chunk=512, tag="lin", seq0=0):
    p = K.p
    chunks = [(t0, min(t0 + chunk, tok1)) for t0 in range(tok0, tok1, chunk)]
    staged = {0: ws.stage_group(0, seq=seq0)}
    for g in range(ws.n_groups):
        if g + 1 < ws.n_groups:
            staged[g + 1] = ws.stage_group(g + 1, seq=seq0 + g + 1)
        slot, ent_reg = staged.pop(g)
        nmt = min(ws.mg, ws.n_mt - g * ws.mg)
        for m in range(nmt):
            mi = g * ws.mg + m
            msz = min(ws.mw, ws.n_out - mi * ws.mw)
            for (t0, t1) in chunks:
                bank = K.ps.take()
                for kt in range(ws.KT):
                    ksz = min(128, ws.n_in - kt * 128)
                    rhs, rregs = src_fn(kt, ksz, t0, t1)
                    lhsT = ws.entry_ap(slot, m, kt, ksz, msz)
                    p.step("pe",
                           lambda e, o=K.ps.ap(bank, 0, msz, 0, t1 - t0), l=lhsT,
                           r=rhs, st=(kt == 0), sp=(kt == ws.KT - 1):
                           e.matmul(o, l, r, start=st, stop=sp),
                           reads=[ent_reg] + rregs, writes=[K.ps.R(bank)],
                           tag=f"{tag}m{mi}")
                ep(mi, msz, bank, t0, t1)


def src_from_view(view, T):
    def f(kt, ksz, t0, t1):
        return view.tile(kt, t0, t1, 0, ksz), [view.R(kt)]
    return f


def ep_to_view(K, view, dtype_scale=None, tag="ep"):
    def ep(mi, msz, bank, t0, t1):
        src = K.ps.ap(bank, 0, msz, 0, t1 - t0)
        dst = view.tile(mi, t0, t1, 0, msz)
        if dtype_scale is None:
            K.p.step("dve", lambda e, s=src, d=dst: e.tensor_copy(d, s),
                     reads=[K.ps.R(bank)], writes=view.wr(mi), tag=tag)
        else:
            K.p.step("act", lambda e, s=src, d=dst: e.mul(d, s, dtype_scale),
                     reads=[K.ps.R(bank)], writes=view.wr(mi), tag=tag)
    return ep


def bcast_row(K, row_ap, row_regs, dst_ap_fn, dst_wregs, t0, t1, nparts=128,
              tag="bc"):
    p = K.p
    for c0 in range(t0, t1, 512):
        c1 = min(c0 + 512, t1)
        bank = K.ps.take()
        p.step("pe", lambda e, o=K.ps.ap(bank, 0, nparts, 0, c1 - c0),
               l=K.cbuf["ones1x128"][0:1, 0:nparts], r=row_ap[0:1, c0 - t0:c1 - t0]:
               e.matmul(o, l, r, start=True, stop=True),
               reads=[K.creg["ones1x128"]] + row_regs, writes=[K.ps.R(bank)],
               tag=tag)
        p.step("dve", lambda e, s=K.ps.ap(bank, 0, nparts, 0, c1 - c0),
               d=dst_ap_fn(c0 - t0, c1 - t0): e.tensor_copy(d, s),
               reads=[K.ps.R(bank)], writes=dst_wregs, tag=tag)


def colsum(K, src_fn, KT, row_ap, row_wregs, t0, t1, scale=None, tag="cs"):
    p = K.p
    for c0 in range(t0, t1, 512):
        c1 = min(c0 + 512, t1)
        bank = K.ps.take()
        for kt in range(KT):
            rhs, rregs, ksz = src_fn(kt, c0, c1)
            p.step("pe", lambda e, o=K.ps.ap(bank, 0, 1, 0, c1 - c0),
                   l=K.cbuf["onescol"][0:ksz, 0:1], r=rhs,
                   st=(kt == 0), sp=(kt == KT - 1):
                   e.matmul(o, l, r, start=st, stop=sp),
                   reads=[K.creg["onescol"]] + rregs, writes=[K.ps.R(bank)],
                   tag=tag)
        if scale is None:
            p.step("dve", lambda e, s=K.ps.ap(bank, 0, 1, 0, c1 - c0),
                   d=row_ap[0:1, c0 - t0:c1 - t0]: e.tensor_copy(d, s),
                   reads=[K.ps.R(bank)], writes=row_wregs, tag=tag)
        else:
            p.step("act", lambda e, s=K.ps.ap(bank, 0, 1, 0, c1 - c0),
                   d=row_ap[0:1, c0 - t0:c1 - t0]: e.mul(d, s, scale),
                   reads=[K.ps.R(bank)], writes=row_wregs, tag=tag)



def ln_emit(K, XT, XB, BC, XNT, T, tag="ln"):
    p = K.p
    RWA, RWB = K.rwa, K.rwb
    rA = K.rwa_reg
    rB = K.rwb_reg
    for kt in range(8):
        p.step("act", lambda e, s=XT.tile(kt), d=XB.tile(kt): e.copy(d, s),
               reads=[XT.R(kt)], writes=XB.wr(kt), tag=tag + "c")
    colsum(K, lambda kt, c0, c1: (XB.tile(kt, c0, c1), [XB.R(kt)], 128), 8,
           RWA[0:1, 0:T], [rA], 0, T, scale=1.0 / 1024.0, tag=tag + "mu")
    bcast_row(K, RWA[0:1, 0:T], [rA], lambda a, b: BC.tile(1, a, b), BC.wr(1),
              0, T, tag=tag + "b1")
    p.step("act", lambda e: e.activation(RWA[0:1, 0:T], RWA[0:1, 0:T],
                                         AF.Square),
           reads=[rA], writes=[rA], tag=tag)
    for kt in range(8):
        p.step("act", lambda e, s=XT.tile(kt), d=XB.tile(kt):
               e.activation(d, s, AF.Square),
               reads=[XT.R(kt)], writes=[XB.R(kt)], tag=tag + "sq")
    colsum(K, lambda kt, c0, c1: (XB.tile(kt, c0, c1), [XB.R(kt)], 128), 8,
           RWB[0:1, 0:T], [rB], 0, T, scale=1.0 / 1024.0, tag=tag + "s2")
    p.step("dve", lambda e: e.tensor_tensor(RWB[0:1, 0:T], RWB[0:1, 0:T],
                                            RWA[0:1, 0:T], op=AluOpType.subtract),
           reads=[rB, rA], writes=[rB], tag=tag)
    p.step("act", lambda e: e.activation(RWB[0:1, 0:T], RWB[0:1, 0:T],
                                         AF.Ln, bias=K.cbuf["misc"][0:1, 0:1]),
           reads=[rB, K.creg["misc"]], writes=[rB], tag=tag)
    p.step("act", lambda e: e.activation(RWB[0:1, 0:T], RWB[0:1, 0:T],
                                         AF.Exp, scale=-0.5),
           reads=[rB], writes=[rB], tag=tag)
    bcast_row(K, RWB[0:1, 0:T], [rB], lambda a, b: BC.tile(0, a, b), BC.wr(0),
              0, T, tag=tag + "b0")
    for kt in range(8):
        for c0 in range(0, T, 1024):
            c1 = min(c0 + 1024, T)
            p.step("dve", lambda e, kt=kt, c0=c0, c1=c1: e.tensor_tensor(
                K.xf[0:128, 0:c1 - c0], XT.tile(kt, c0, c1), BC.tile(1, c0, c1),
                op=AluOpType.subtract),
                reads=[XT.R(kt), BC.R(1)], writes=[K.xf_reg], tag=tag)
            p.step("dve", lambda e, kt=kt, c0=c0, c1=c1: e.tensor_tensor(
                XNT.tile(kt, c0, c1), K.xf[0:128, 0:c1 - c0], BC.tile(0, c0, c1),
                op=AluOpType.mult),
                reads=[K.xf_reg, BC.R(0)], writes=XNT.wr(kt), tag=tag)


def rms_emit(K, SRC, XB, BC, DST, T, gcols, gcol_reg, eps=1e-6, tag="rms",
             LO=None):
    p = K.p
    RWA, RWB = K.rwa, K.rwb
    rA, rB = K.rwa_reg, K.rwb_reg
    for kt in range(8):
        p.step("act", lambda e, s=SRC.tile(kt), d=XB.tile(kt):
               e.activation(d, s, AF.Square),
               reads=[SRC.R(kt)], writes=XB.wr(kt), tag=tag + "sq")
    colsum(K, lambda kt, c0, c1: (XB.tile(kt, c0, c1), [XB.R(kt)], 128), 8,
           RWB[0:1, 0:T], [rB], 0, T, scale=1.0 / 1024.0, tag=tag + "s2")
    epscol = {1e-5: 0, 1e-6: 1, 1e-12: 2}[eps]
    p.step("act", lambda e: e.activation(RWB[0:1, 0:T], RWB[0:1, 0:T],
                                         AF.Ln, bias=K.cbuf["misc"][0:1, epscol:epscol + 1]),
           reads=[rB, K.creg["misc"]], writes=[rB], tag=tag)
    p.step("act", lambda e: e.activation(RWB[0:1, 0:T], RWB[0:1, 0:T],
                                         AF.Exp, scale=-0.5),
           reads=[rB], writes=[rB], tag=tag)
    bcast_row(K, RWB[0:1, 0:T], [rB], lambda a, b: BC.tile(0, a, b), BC.wr(0),
              0, T, tag=tag + "b0")
    for kt in range(8):
        for c0 in range(0, T, 1024):
            c1 = min(c0 + 1024, T)
            W_ = c1 - c0
            p.step("dve", lambda e, kt=kt, c0=c0, c1=c1, W_=W_: e.tensor_tensor(
                K.xf[0:128, 0:W_], SRC.tile(kt, c0, c1), BC.tile(0, c0, c1),
                op=AluOpType.mult),
                reads=[SRC.R(kt), BC.R(0)], writes=[K.xf_reg], tag=tag)
            if LO is None:
                p.step("dve", lambda e, kt=kt, c0=c0, c1=c1, W_=W_,
                       g=gcols[kt]: e.tensor_scalar(
                    DST.tile(kt, c0, c1), K.xf[0:128, 0:W_], g, None,
                    op0=AluOpType.mult),
                    reads=[K.xf_reg, gcol_reg], writes=DST.wr(kt), tag=tag)
            else:
                p.step("dve", lambda e, kt=kt, W_=W_, g=gcols[kt]:
                       e.tensor_scalar(
                    K.xf[0:128, 0:W_], K.xf[0:128, 0:W_], g, None,
                    op0=AluOpType.mult),
                    reads=[K.xf_reg, gcol_reg], writes=[K.xf_reg], tag=tag)
                p.step("dve", lambda e, kt=kt, c0=c0, c1=c1, W_=W_:
                       e.tensor_copy(DST.tile(kt, c0, c1), K.xf[0:128, 0:W_]),
                    reads=[K.xf_reg], writes=DST.wr(kt), tag=tag)
                p.step("act", lambda e, kt=kt, c0=c0, c1=c1:
                       e.copy(BC.tile(1, c0, c1), DST.tile(kt, c0, c1)),
                    reads=[DST.R(kt)], writes=[BC.R(1)], tag=tag)
                p.step("dve", lambda e, kt=kt, c0=c0, c1=c1, W_=W_:
                       e.tensor_tensor(LO.tile(kt, c0, c1), K.xf[0:128, 0:W_],
                                       BC.tile(1, c0, c1),
                                       op=AluOpType.subtract),
                    reads=[K.xf_reg, BC.R(1)], writes=LO.wr(kt), tag=tag)



def load_xt(K, x_ap, x_reg, XT, XN, T, tag="ldx"):
    p = K.p
    nt = T // 128
    for tt in range(nt):
        sl = tt % XN.nt
        p.dma("sp", XN.tile(sl), x_ap[tt * 128:(tt + 1) * 128, 0:1024],
              reads=[x_reg], writes=XN.wr(sl), tag=tag)
        for kt in range(8):
            bank = K.ps.take()
            p.step("pe", lambda e, s=XN.tile(sl, kt * 128, (kt + 1) * 128),
                   d=K.ps.ap(bank, 0, 128, 0, 128), i=K.cbuf["identf"]:
                   e.transpose(d, s, i),
                   reads=[XN.R(sl), K.creg["identf"]], writes=[K.ps.R(bank)],
                   tag=tag)
            p.step("dve", lambda e, s=K.ps.ap(bank, 0, 128, 0, 128),
                   d=XT.tile(kt, tt * 128, (tt + 1) * 128): e.tensor_copy(d, s),
                   reads=[K.ps.R(bank)], writes=XT.wr(kt), tag=tag)


def store_xt(K, XT, out_ap, out_reg, XN, T, tag="stx"):
    p = K.p
    nt = T // 128
    for tt in range(nt):
        sl = tt % XN.nt
        for kt in range(8):
            bank = K.ps.take()
            p.step("pe", lambda e, s=XT.tile(kt, tt * 128, (tt + 1) * 128),
                   d=K.ps.ap(bank, 0, 128, 0, 128), i=K.cbuf["identf"]:
                   e.transpose(d, s, i),
                   reads=[XT.R(kt), K.creg["identf"]], writes=[K.ps.R(bank)],
                   tag=tag)
            p.step("dve", lambda e, s=K.ps.ap(bank, 0, 128, 0, 128),
                   d=XN.tile(sl, kt * 128, (kt + 1) * 128): e.tensor_copy(d, s),
                   reads=[K.ps.R(bank)], writes=XN.wr(sl), tag=tag)
        p.dma("sp", out_ap[tt * 128:(tt + 1) * 128, 0:1024], XN.tile(sl),
              reads=[XN.R(sl)], writes=[out_reg], tag=tag)


def l0a_maps():
    qmap = {}
    for h in range(32):
        qmap[h] = (h // 4, 64 * ((h % 4) // 2))
    kmap = {}
    for h in range(32):
        s = h % 4
        q_ = h // 4
        kmap[h] = (2 * q_ + (s % 2), (0, 32, 64, 96)[s])
    return qmap, kmap


def _l0a_bound(K, QT, KT, SCR, T, tag):
    p = K.p
    C = K.cols
    cr = K.cols_reg
    first = True
    for qt in range(8):
        for c0 in range(0, T, 256):
            p.step("act", lambda e, s=QT.tile(qt, c0, c0 + 256),
                   d=SCR.tile(0, 0, 256): e.activation(d, s, AF.Square),
                   reads=[QT.R(qt)], writes=SCR.wr(0), tag=tag + "bq")
            bank = K.ps.take()
            p.step("pe", lambda e, o=K.ps.ap(bank, 0, 4, 0, 256),
                   l=K.cbuf["e4"][0:128, 0:4], r=SCR.tile(0, 0, 256):
                   e.matmul(o, l, r, start=True, stop=True),
                   reads=[K.creg["e4"], SCR.R(0)], writes=[K.ps.R(bank)],
                   tag=tag + "bq")
            if first:
                p.step("dve", lambda e, s=K.ps.ap(bank, 0, 4, 0, 256):
                       e.reduce_max(C[0:4, 0:1], s, axis=AX.X),
                       reads=[K.ps.R(bank)], writes=[cr], tag=tag + "bq")
                first = False
            else:
                p.step("dve", lambda e, s=K.ps.ap(bank, 0, 4, 0, 256):
                       e.reduce_max(C[0:4, 1:2], s, axis=AX.X),
                       reads=[K.ps.R(bank)], writes=[cr], tag=tag + "bq")
                p.step("dve", lambda e: e.tensor_tensor(
                    C[0:4, 0:1], C[0:4, 0:1], C[0:4, 1:2], op=AluOpType.max),
                    reads=[cr], writes=[cr], tag=tag + "bq")
    first = True
    for kt_ in range(16):
        for c0 in range(0, T, 256):
            p.step("act", lambda e, s=KT.tile(kt_, c0, c0 + 256),
                   d=SCR.tile(0, 0, 256): e.activation(d, s, AF.Square),
                   reads=[KT.R(kt_)], writes=[SCR.R(0)], tag=tag + "bk")
            bank = K.ps.take()
            p.step("pe", lambda e, o=K.ps.ap(bank, 0, 2, 0, 256),
                   l=K.cbuf["e2"][0:128, 0:2], r=SCR.tile(0, 0, 256):
                   e.matmul(o, l, r, start=True, stop=True),
                   reads=[K.creg["e2"], SCR.R(0)], writes=[K.ps.R(bank)],
                   tag=tag + "bk")
            if first:
                p.step("dve", lambda e, s=K.ps.ap(bank, 0, 2, 0, 256):
                       e.reduce_max(C[0:2, 2:3], s, axis=AX.X),
                       reads=[K.ps.R(bank)], writes=[cr], tag=tag + "bk")
                first = False
            else:
                p.step("dve", lambda e, s=K.ps.ap(bank, 0, 2, 0, 256):
                       e.reduce_max(C[0:2, 3:4], s, axis=AX.X),
                       reads=[K.ps.R(bank)], writes=[cr], tag=tag + "bk")
                p.step("dve", lambda e: e.tensor_tensor(
                    C[0:2, 2:3], C[0:2, 2:3], C[0:2, 3:4], op=AluOpType.max),
                    reads=[cr], writes=[cr], tag=tag + "bk")
    bank = K.ps.take()
    p.step("pe", lambda e, o=K.ps.ap(bank, 0, 1, 0, 4):
           e.transpose(o, C[0:4, 0:1], K.cbuf["identf"][0:4, 0:4]),
           reads=[cr, K.creg["identf"]], writes=[K.ps.R(bank)], tag=tag + "bt")
    p.step("dve", lambda e, s=K.ps.ap(bank, 0, 1, 0, 4):
           e.reduce_max(C[0:1, 4:5], s, axis=AX.X),
           reads=[K.ps.R(bank)], writes=[cr], tag=tag + "bt")
    bank2 = K.ps.take()
    p.step("pe", lambda e, o=K.ps.ap(bank2, 0, 1, 0, 2):
           e.transpose(o, C[0:2, 2:3], K.cbuf["identf"][0:2, 0:2]),
           reads=[cr, K.creg["identf"]], writes=[K.ps.R(bank2)], tag=tag + "bt")
    p.step("dve", lambda e, s=K.ps.ap(bank2, 0, 1, 0, 2):
           e.reduce_max(C[0:1, 5:6], s, axis=AX.X),
           reads=[K.ps.R(bank2)], writes=[cr], tag=tag + "bt")
    p.step("dve", lambda e: e.tensor_tensor(C[0:1, 4:5], C[0:1, 4:5],
                                            C[0:1, 5:6], op=AluOpType.mult),
           reads=[cr], writes=[cr], tag=tag + "bt")
    p.step("act", lambda e: e.activation(C[0:1, 5:6], C[0:1, 4:5], AF.Ln),
           reads=[cr], writes=[cr], tag=tag + "bt")
    p.step("act", lambda e: e.activation(C[0:1, 4:5], C[0:1, 5:6], AF.Exp,
                                         scale=0.5),
           reads=[cr], writes=[cr], tag=tag + "bt")
    bank3 = K.ps.take()
    p.step("pe", lambda e, o=K.ps.ap(bank3, 0, 128, 0, 1):
           e.matmul(o, K.cbuf["ones1x128"][0:1, 0:128], C[0:1, 4:5],
                    start=True, stop=True),
           reads=[K.creg["ones1x128"], cr], writes=[K.ps.R(bank3)], tag=tag + "bt")
    p.step("dve", lambda e, s=K.ps.ap(bank3, 0, 128, 0, 1):
           e.tensor_scalar(K.negc[0:128, 0:1], s, -1.0, None, op0=AluOpType.mult),
           reads=[K.ps.R(bank3)], writes=[K.negc_reg], tag=tag + "bt")


def _l0a_combine(K, j, obk, OTc, tag):
    p = K.p
    C, cr = K.cols, K.cols_reg
    xf, xr = K.xf, K.xf_reg
    RWA, rA = K.rwa, K.rwa_reg
    b1, b2 = obk[0], obk[1]
    RWB, rB = K.rwb, K.rwb_reg
    p.step("dve", lambda e, s=K.ps.ap(b1, 64, 65, 0, 256):
           e.reciprocal(RWA[0:1, 0:256], s),
           reads=[K.ps.R(b1)], writes=[rA], tag=tag + "cb")
    p.step("dve", lambda e, s=K.ps.ap(b2, 64, 65, 0, 256):
           e.reciprocal(RWB[0:1, 0:256], s),
           reads=[K.ps.R(b2)], writes=[rB], tag=tag + "cb")
    p.step("dve", lambda e: e.tensor_scalar(
        RWB[0:1, 0:256], RWB[0:1, 0:256], K.lam[0:1, 0:1], None,
        op0=AluOpType.mult),
        reads=[rB, K.lam_reg], writes=[rB], tag=tag + "cb")
    for (rowap, rreg, cofs) in ((RWA, rA, 0), (RWB, rB, 256)):
        bank = K.ps.take()
        p.step("pe", lambda e, o=K.ps.ap(bank, 0, 64, 0, 256),
               l=K.cbuf["ones1x128"][0:1, 0:64], r=rowap[0:1, 0:256]:
               e.matmul(o, l, r, start=True, stop=True),
               reads=[K.creg["ones1x128"], rreg], writes=[K.ps.R(bank)],
               tag=tag + "cb")
        p.step("dve", lambda e, s=K.ps.ap(bank, 0, 64, 0, 256), c=cofs:
               e.tensor_copy(xf[0:64, c:c + 256], s),
               reads=[K.ps.R(bank)], writes=[xr], tag=tag + "cb")
    p.step("dve", lambda e, s=K.ps.ap(b1, 0, 64, 0, 256):
           e.tensor_tensor(xf[0:64, 512:768], s, xf[0:64, 0:256],
                           op=AluOpType.mult),
           reads=[K.ps.R(b1), xr], writes=[xr], tag=tag + "cb")
    p.step("dve", lambda e, s=K.ps.ap(b2, 0, 64, 0, 256):
           e.tensor_tensor(xf[0:64, 768:1024], s, xf[0:64, 256:512],
                           op=AluOpType.mult),
           reads=[K.ps.R(b2), xr], writes=[xr], tag=tag + "cb")
    p.step("dve", lambda e: e.tensor_tensor(
        xf[0:64, 512:768], xf[0:64, 512:768], xf[0:64, 768:1024],
        op=AluOpType.subtract),
        reads=[xr], writes=[xr], tag=tag + "cb")
    p.step("act", lambda e: e.activation(K.sq64[0:64, 0:256],
                                         xf[0:64, 512:768], AF.Square),
           reads=[xr], writes=[K.sq64_reg], tag=tag + "cb")
    bank = K.ps.take()
    p.step("pe", lambda e, o=K.ps.ap(bank, 0, 1, 0, 256):
           e.matmul(o, K.cbuf["ones64"][0:64, 0:1], K.sq64[0:64, 0:256],
                    start=True, stop=True),
           reads=[K.creg["ones64"], K.sq64_reg], writes=[K.ps.R(bank)],
           tag=tag + "cb")
    p.step("act", lambda e, s=K.ps.ap(bank, 0, 1, 0, 256):
           e.activation(RWA[0:1, 0:256], s, AF.Ln, scale=1.0 / 64.0,
                        bias=K.cbuf["misc"][0:1, 1:2]),
           reads=[K.ps.R(bank), K.creg["misc"]], writes=[rA], tag=tag + "cb")
    p.step("act", lambda e: e.activation(RWA[0:1, 0:256], RWA[0:1, 0:256],
                                         AF.Exp, scale=-0.5),
           reads=[rA], writes=[rA], tag=tag + "cb")
    bank2 = K.ps.take()
    p.step("pe", lambda e, o=K.ps.ap(bank2, 0, 64, 0, 256):
           e.matmul(o, K.cbuf["ones1x128"][0:1, 0:64], RWA[0:1, 0:256],
                    start=True, stop=True),
           reads=[K.creg["ones1x128"], rA], writes=[K.ps.R(bank2)], tag=tag + "cb")
    p.step("dve", lambda e, s=K.ps.ap(bank2, 0, 64, 0, 256):
           e.tensor_tensor(xf[0:64, 768:1024], xf[0:64, 512:768], s,
                           op=AluOpType.mult),
           reads=[K.ps.R(bank2), xr], writes=[xr], tag=tag + "cb")
    rb = 64 * (j % 2)
    p.step("dve", lambda e, d=OTc.tile(j // 2, 0, 256, rb, rb + 64):
           e.tensor_scalar(d, xf[0:64, 768:1024], K.rsc[0:64, 0:1], None,
                           op0=AluOpType.mult),
           reads=[xr, K.rsc_reg], writes=OTc.wr(j // 2), tag=tag + "cb")


def stage_l0_attn(K, XNT, T, W, ao_sp, xnt_sp, keeps=(), tag="l0a"):
    p = K.p
    qmap, kmap = l0a_maps()
    NKB = T // 128
    inv_sqrt_h = 1.0 / math.sqrt(32.0)
    keeps = list(keeps)

    V1 = K.arena.carve(keeps + [XNT], [
        ("wring", 4, 2048, BF16), ("wvs", 8, 1024, BF16),
        ("qt", 8, T, BF16), ("vn", NKB, 1040, BF16)])
    QT, VN = V1["qt"], V1["vn"]
    wsq = WStream(K, tag + "q", W["wqkv"][0], W["wqkv"][1], 1024, 1024, False,
                  V1["wring"], mg=2)
    linear(K, wsq, src_from_view(XNT, T),
           ep_to_view(K, QT, dtype_scale=inv_sqrt_h, tag=tag + "qev"),
           0, T, tag=tag + "q")
    WVS = V1["wvs"]
    for t_ in range(NKB):
        p.step("dve", lambda e, t_=t_: e.memset(VN.tile(t_), 1.0),
               reads=[], writes=VN.wr(t_), tag=tag + "vo")
    for kt in range(8):
        p.dma("pool", WVS.tile(kt), W["wqkv"][0][kt * 128:(kt + 1) * 128, 2048:3072],
              reads=[W["wqkv"][1]], writes=WVS.wr(kt), tag=tag + "vw")
    for tt in range(NKB):
        banks = [K.ps.take(hold=True), K.ps.take(hold=True)]
        for kt in range(8):
            for half in range(2):
                p.step("pe", lambda e, o=K.ps.ap(banks[half], 0, 128, 0, 512),
                       l=XNT.tile(kt, tt * 128, (tt + 1) * 128),
                       r=WVS.tile(kt, half * 512, (half + 1) * 512),
                       st=(kt == 0), sp=(kt == 7):
                       e.matmul(o, l, r, start=st, stop=sp),
                       reads=[XNT.R(kt), WVS.R(kt)],
                       writes=[K.ps.R(banks[half])], tag=tag + "v")
        for half in range(2):
            src_ = K.ps.ap(banks[half], 0, 128, 0, 512).rearrange(
                "p (h c) -> p h c", c=64)
            dst = VN.tile(tt, 520 * half, 520 * half + 520).rearrange(
                "p (h c) -> p h c", c=65)[:, :, 0:64]
            p.step("dve", lambda e, s=src_, d=dst: e.tensor_copy(d, s),
                   reads=[K.ps.R(banks[half])], writes=[VN.R(tt)], tag=tag + "vev")
        K.ps.release(banks[0]); K.ps.release(banks[1])

    V2 = K.arena.carve(keeps + [XNT, QT, VN], [
        ("wring", 2, 2048, BF16), ("kt", 16, T, BF16)])
    KT = V2["kt"]
    for t_ in range(16):
        p.step("dve", lambda e, t_=t_: e.memset(KT.tile(t_), 0.0),
               reads=[], writes=KT.wr(t_), tag=tag + "kz")
    wsk = WStream(K, tag + "k", W["wqkv"][0], W["wqkv"][1], 1024, 1024, False,
                  V2["wring"], mg=4, mw=64, col0=1024)

    def ep_k(mi, msz, bank, t0, t1):
        for half in range(2):
            h = 2 * mi + half
            (ktile, dr) = kmap[h]
            if dr < 96:
                s_ = K.ps.ap(bank, 32 * half, 32 * half + 32, 0, t1 - t0)
                d_ = KT.tile(ktile, t0, t1, dr, dr + 32)
                K.p.step("dve", lambda e, s=s_, d=d_: e.tensor_copy(d, s),
                         reads=[K.ps.R(bank)], writes=[KT.R(ktile)],
                         tag=tag + "kev")
            else:
                s_ = K.ps.ap(bank, 0, 64, 0, t1 - t0)
                d_ = KT.tile(ktile, t0, t1, 64, 128)
                K.p.step("dve", lambda e, s=s_, d=d_: e.tensor_copy(d, s),
                         reads=[K.ps.R(bank)], writes=[KT.R(ktile)],
                         tag=tag + "kev")
                K.p.step("dve", lambda e, d=KT.tile(ktile, t0, t1, 64, 96):
                         e.memset(d, 0.0),
                         reads=[], writes=[KT.R(ktile)], tag=tag + "kz3")
    linear(K, wsk, src_from_view(XNT, T), ep_k, 0, T, tag=tag + "k")

    for kt in range(8):
        p.dma("sp", xnt_sp[0][kt * 128:(kt + 1) * 128, 0:T], XNT.tile(kt),
              reads=[XNT.R(kt)], writes=[xnt_sp[1]], tag=tag + "xsp")

    V3 = K.arena.carve(keeps + [QT, KT, VN], [
        ("wring", 4, 2048, BF16), ("p1", NKB, 256, BF16),
        ("p2", NKB, 256, BF16), ("otc", 8, 256, BF16)])
    P1, P2, OTc = V3["p1"], V3["p2"], V3["otc"]
    _l0a_bound(K, QT, KT, P1, T, tag)
    wsp = WStream(K, tag + "p", W["wproj"][0], W["wproj"][1], 1024, 1024, False,
                  V3["wring"], mg=2)
    seq = 0
    for qc in range(T // 256):
        t0, t1 = qc * 256, (qc + 1) * 256
        for j in range(16):
            obk = {}
            for pas in range(2):
                h = j + 16 * pas
                (qt_t, qb) = qmap[h]
                (ktile, _dr) = kmap[h]
                P = P1 if pas == 0 else P2
                for kb in range(NKB):
                    sbank = K.ps.take()
                    p.step("pe", lambda e, o=K.ps.ap(sbank, 0, 128, 0, 256),
                           l=KT.tile(ktile, kb * 128, (kb + 1) * 128, qb, qb + 64),
                           r=QT.tile(qt_t, t0, t1, qb, qb + 64):
                           e.matmul(o, l, r, start=True, stop=True),
                           reads=[KT.R(ktile), QT.R(qt_t)],
                           writes=[K.ps.R(sbank)], tag=tag + "s")
                    p.step("act", lambda e, s=K.ps.ap(sbank, 0, 128, 0, 256),
                           d=P.tile(kb):
                           e.activation(d, s, AF.Exp, bias=K.negc[0:128, 0:1]),
                           reads=[K.ps.R(sbank), K.negc_reg], writes=P.wr(kb),
                           tag=tag + "e")
                obank = K.ps.take(hold=True)
                for kb in range(NKB):
                    p.step("pe", lambda e, o=K.ps.ap(obank, 0, 65, 0, 256),
                           l=VN.tile(kb, 65 * j, 65 * j + 65), r=P.tile(kb),
                           st=(kb == 0), sp=(kb == NKB - 1):
                           e.matmul(o, l, r, start=st, stop=sp),
                           reads=[VN.R(kb), P.R(kb)], writes=[K.ps.R(obank)],
                           tag=tag + "pv")
                obk[pas] = obank
            _l0a_combine(K, j, obk, OTc, tag)
            K.ps.release(obk[0])
            K.ps.release(obk[1])

        def ep_ao(mi, msz, bank, a, b):
            stg = (K.sqs, K.sqs_reg) if mi % 2 == 0 else (K.pb, K.pb_reg)
            K.p.step("dve", lambda e, s=K.ps.ap(bank, 0, msz, 0, b - a),
                     d=stg[0][0:msz, 0:b - a]: e.tensor_copy(d, s),
                     reads=[K.ps.R(bank)], writes=[stg[1]], tag=tag + "ao")
            K.p.dma("sp", ao_sp[0][mi * 128:mi * 128 + msz, t0 + a:t0 + b],
                    stg[0][0:msz, 0:b - a],
                    reads=[stg[1]], writes=[ao_sp[1]], tag=tag + "aosp")
        linear(K, wsp, src_from_view(OTc, 256), ep_ao, 0, 256, chunk=256,
               tag=tag + "proj", seq0=seq)
        seq += wsp.n_groups



CONST_SPECS = [
    ("ident", [128, 128], BF16), ("identf", [128, 128], F32),
    ("ones1x128", [32, 128], F32), ("onescol", [128, 32], BF16),
    ("onescolf", [128, 32], F32), ("ones64", [64, 32], BF16),
    ("e2", [128, 32], BF16), ("e4", [128, 32], BF16),
    ("iota8", [128, 8], F32), ("misc", [128, 8], F32),
    ("permf", [128, 128], BF16), ("permh", [128, 128], BF16),
    ("exp2", [32, 128], BF16),
]


def const_arrays(T):
    base = build_constants(T)
    e4 = np.zeros((128, 4), ml_dtypes.bfloat16)
    for j in range(4):
        e4[32 * j:32 * (j + 1), j] = 1
    ones1x = np.zeros((32, 128), np.float32); ones1x[0, :] = 1.0
    onescol = np.zeros((128, 32), ml_dtypes.bfloat16); onescol[:, 0] = 1
    onescolf = np.zeros((128, 32), np.float32); onescolf[:, 0] = 1.0
    ones64 = np.zeros((64, 32), ml_dtypes.bfloat16); ones64[:, 0] = 1
    e2p = np.zeros((128, 32), ml_dtypes.bfloat16); e2p[:, 0:2] = base["c_e2"]
    e4p = np.zeros((128, 32), ml_dtypes.bfloat16); e4p[:, 0:4] = e4
    return {
        "c_ident": base["c_ident"],
        "c_identf": np.eye(128, dtype=np.float32),
        "c_ones1x128": ones1x,
        "c_onescol": onescol,
        "c_onescolf": onescolf,
        "c_ones64": ones64,
        "c_e2": e2p,
        "c_e4": e4p,
        "c_iota8": base["c_iota8"],
        "c_permf": np.asarray(base["c_permf"]),
        "c_exp2": (lambda a: a)(np.asarray(
            [[1.0 if (m // 64) == k2 else 0.0 for m in range(128)]
             if k2 < 2 else [0.0] * 128 for k2 in range(32)],
            ml_dtypes.bfloat16)),
        "c_permh": np.asarray(base["c_permh"]),
        "c_ropec": base["c_ropec"],
        "c_ropes": base["c_ropes"],
        "c_misc": np.broadcast_to(np.array(
            [1e-5, 1e-6, 1e-12, 1.0, 1e-38, 0.0, 0.0, 0.0], np.float32),
            (128, 8)).copy(),
        "c_xposcs": np.asarray(base["c_xposcs"]),
        "c_xposss": np.asarray(base["c_xposss"]),
        "c_xposck": np.asarray(base["c_xposck"]),
        "c_xpossk": np.asarray(base["c_xpossk"]),
        "c_cmask": np.asarray(base["c_cmask"]),
        "c_lmask": base["c_lmask"],
        "c_lmask0": base["c_lmask0"],
    }


def make_ctx(nc, p, cfg):
    K = KC(nc, p, cfg)
    K.ps = PsumPool(nc)
    K.cbuf = {}
    K.creg = {}
    for (name, shape, dt) in CONST_SPECS:
        d = nc.dram_tensor("c_" + name, shape, dt, kind="ExternalInput").ap()
        t = nc.alloc_sbuf_tensor("cb_" + name, shape, dt)
        K.cbuf[name] = t.ap()
        K.creg[name] = Region("c_" + name)
        p.dma("sp", t.ap(), d, reads=[], writes=[K.creg[name]], tag="const")
    T = cfg["T"]
    K.rwa = nc.alloc_sbuf_tensor("rwa", [128, T], F32).ap()
    K.rwa_reg = Region("rwa")
    K.rwb = nc.alloc_sbuf_tensor("rwb", [128, T], F32).ap()
    K.rwb_reg = Region("rwb")

    K.cols = nc.alloc_sbuf_tensor("colsb", [128, 64], F32).ap()
    K.cols_reg = Region("cols")
    K.xf = nc.alloc_sbuf_tensor("xf", [128, 1024], F32).ap()
    K.xf_reg = Region("xf")
    K.negc = nc.alloc_sbuf_tensor("negc", [128, 4], F32).ap()
    K.negc_reg = Region("negc")
    K.lam = nc.alloc_sbuf_tensor("lamt", [1, 4], F32).ap()
    K.lam_reg = Region("lam")
    K.rsc = nc.alloc_sbuf_tensor("rsct", [64, 4], F32).ap()
    K.rsc_reg = Region("rsc")
    K.sq64 = nc.alloc_sbuf_tensor("sq64", [64, 256], BF16).ap()
    K.sq64_reg = Region("sq64")
    K.ebt = nc.alloc_sbuf_tensor("ebt", [128, 8], F32).ap()
    K.ebt_reg = Region("ebt")
    K.sqs = nc.alloc_sbuf_tensor("sqs", [128, 512], BF16).ap()
    K.sqs_reg = Region("sqs")
    K.pb = nc.alloc_sbuf_tensor("pbt", [128, 512], BF16).ap()
    K.pb_reg = Region("pb")
    K.rlkf = nc.alloc_sbuf_tensor("rlkf", [32, 256], F32).ap()
    K.rlk = nc.alloc_sbuf_tensor("rlkb", [32, 256], BF16).ap()
    K.rlk_reg = Region("rlk")
    for (ap_, reg_) in [(K.rwa, K.rwa_reg), (K.rwb, K.rwb_reg),
                        (K.cols, K.cols_reg), (K.xf, K.xf_reg),
                        (K.negc, K.negc_reg), (K.lam, K.lam_reg),
                        (K.rsc, K.rsc_reg), (K.sq64, K.sq64_reg),
                        (K.ebt, K.ebt_reg),
                        (K.sqs, K.sqs_reg), (K.pb, K.pb_reg),
                        (K.rlkf, K.rlk_reg), (K.rlk, K.rlk_reg)]:
        p.step("dve", lambda e, a=ap_: e.memset(a, 0.0), reads=[],
               writes=[reg_], tag="init")
    return K


def precompute_l0a_scalars(K, lam_aps, lam_regs, rms_ap, rms_reg, tag="pre"):
    p = K.p
    C, cr = K.cols, K.cols_reg
    for i, (ap_, reg_) in enumerate(zip(lam_aps, lam_regs)):
        p.dma("sp", C[0:32, 8 + i:9 + i], ap_, reads=[reg_], writes=[cr], tag=tag)
    p.dma("sp", K.rsc[0:64, 1:2], rms_ap, reads=[rms_reg], writes=[K.rsc_reg],
          tag=tag)
    p.step("dve", lambda e: e.tensor_scalar(
        K.rsc[0:64, 0:1], K.rsc[0:64, 1:2], 1.0 - LAMBDA_INIT, None,
        op0=AluOpType.mult),
        reads=[K.rsc_reg], writes=[K.rsc_reg], tag=tag)
    for pair in range(2):
        p.step("dve", lambda e, a=8 + 2 * pair, pr=pair: e.tensor_tensor(
            C[0:32, 12 + pr:13 + pr], C[0:32, a:a + 1], C[0:32, a + 1:a + 2],
            op=AluOpType.mult),
            reads=[cr], writes=[cr], tag=tag)
        bank = K.ps.take()
        p.step("pe", lambda e, o=K.ps.ap(bank, 0, 1, 0, 1), pair=pair:
               e.matmul(o, K.cbuf["onescolf"][0:32, 0:1],
                        C[0:32, 12 + pair:13 + pair], start=True, stop=True),
               reads=[K.creg["onescolf"], cr], writes=[K.ps.R(bank)], tag=tag)
        p.step("act", lambda e, s=K.ps.ap(bank, 0, 1, 0, 1), pair=pair:
               e.activation(K.lam[0:1, 1 + pair:2 + pair], s, AF.Exp),
               reads=[K.ps.R(bank)], writes=[K.lam_reg], tag=tag)
    p.step("dve", lambda e: e.tensor_tensor(
        K.lam[0:1, 0:1], K.lam[0:1, 1:2], K.lam[0:1, 2:3], op=AluOpType.subtract),
        reads=[K.lam_reg], writes=[K.lam_reg], tag=tag)
    p.step("dve", lambda e: e.tensor_scalar(
        K.lam[0:1, 0:1], K.lam[0:1, 0:1], LAMBDA_INIT, None, op0=AluOpType.add),
        reads=[K.lam_reg], writes=[K.lam_reg], tag=tag)



def bound_sumsq_max(K, tiles_fn, ntiles, T, cell, tmpcell, tag, chunk=512):
    p = K.p
    C, cr = K.cols, K.cols_reg
    first = True
    for c0 in range(0, T, chunk):
        c1 = min(c0 + chunk, T)
        bank = K.ps.take()
        for i in range(ntiles):
            ap_, reg_ = tiles_fn(i, c0, c1)
            p.step("act", lambda e, s=ap_, d=K.sqs[0:128, 0:c1 - c0]:
                   e.activation(d, s, AF.Square),
                   reads=[reg_], writes=[K.sqs_reg], tag=tag)
            p.step("pe", lambda e, o=K.ps.ap(bank, 0, 1, 0, c1 - c0),
                   st=(i == 0), sp=(i == ntiles - 1):
                   e.matmul(o, K.cbuf["onescol"][0:128, 0:1],
                            K.sqs[0:128, 0:c1 - c0], start=st, stop=sp),
                   reads=[K.creg["onescol"], K.sqs_reg], writes=[K.ps.R(bank)],
                   tag=tag)
        dstcell = cell if first else tmpcell
        p.step("dve", lambda e, s=K.ps.ap(bank, 0, 1, 0, c1 - c0), d=dstcell:
               e.reduce_max(d, s, axis=AX.X),
               reads=[K.ps.R(bank)], writes=[cr], tag=tag)
        if not first:
            p.step("dve", lambda e: e.tensor_tensor(cell, cell, tmpcell,
                                                    op=AluOpType.max),
                   reads=[cr], writes=[cr], tag=tag)
        first = False


def bound_to_negcol(K, qcell, kcell, tmpcell, negcol, scale2=1.0, tag="bnd"):
    p = K.p
    cr = K.cols_reg
    p.step("dve", lambda e: e.tensor_tensor(tmpcell, qcell, kcell,
                                            op=AluOpType.mult),
           reads=[cr], writes=[cr], tag=tag)
    if scale2 != 1.0:
        p.step("dve", lambda e: e.tensor_scalar(tmpcell, tmpcell, scale2, None,
                                                op0=AluOpType.mult),
               reads=[cr], writes=[cr], tag=tag)
    p.step("act", lambda e: e.activation(tmpcell, tmpcell, AF.Ln),
           reads=[cr], writes=[cr], tag=tag)
    p.step("act", lambda e: e.activation(tmpcell, tmpcell, AF.Exp, scale=0.5),
           reads=[cr], writes=[cr], tag=tag)
    bank = K.ps.take()
    p.step("pe", lambda e, o=K.ps.ap(bank, 0, 128, 0, 1):
           e.matmul(o, K.cbuf["ones1x128"][0:1, 0:128], tmpcell,
                    start=True, stop=True),
           reads=[K.creg["ones1x128"], cr], writes=[K.ps.R(bank)], tag=tag)
    p.step("dve", lambda e, s=K.ps.ap(bank, 0, 128, 0, 1), d=negcol:
           e.tensor_scalar(d, s, -1.0, None, op0=AluOpType.mult),
           reads=[K.ps.R(bank)], writes=[K.negc_reg], tag=tag)


def load_cols(K, src_ap, src_reg, col0, n, tag="gc"):
    cols = []
    for kt in range(n):
        K.p.dma("sp", K.cols[0:128, col0 + kt:col0 + kt + 1],
                src_ap[kt * 128:(kt + 1) * 128].unsqueeze(1),
                reads=[src_reg], writes=[K.cols_reg], tag=tag)
        cols.append(K.cols[0:128, col0 + kt:col0 + kt + 1])
    return cols



def slice3d(ap3, e, rows, cols):
    return bass.AP(ap3.tensor, e * rows * cols, [[cols, rows], [1, cols]])


def stage_l0_ffn(K, XNT, Y, T, W, keeps=(), tag="l0f"):
    p = K.p
    NKB = T // 128
    keeps = [Y] + list(keeps)

    Va = K.arena.carve(keeps + [XNT], [
        ("xb", 8, T, BF16), ("bc", 2, T, F32), ("nx", 8, T, BF16)])
    for kt in range(8):
        p.step("act", lambda e, kt=kt, d=Y.tile(kt), s=XNT.tile(kt):
               e.copy(d, s), reads=[XNT.R(kt)], writes=Y.wr(kt), tag=tag + "sd")
    g1c = load_cols(K, W["g1"][0], W["g1"][1], 16, 8, tag + "g1")
    NX = Va["nx"]
    rms_emit(K, Y, Va["xb"], Va["bc"], NX, T, g1c, K.cols_reg, eps=1e-6,
             tag=tag + "r1")

    Vb = K.arena.carve(keeps + [NX], [
        ("wring", 4, 2048, BF16), ("wtmp", 2, 1024, BF16),
        ("qt2", 8, T, BF16), ("kct", 2, T, BF16), ("vct", 2, T, BF16),
        ("vcn", NKB, 256, BF16)])
    QT2, KCT, VCT, VCN = Vb["qt2"], Vb["kct"], Vb["vct"], Vb["vcn"]
    wsq = WStream(K, tag + "wq", W["Wq"][0], W["Wq"][1], 1024, 1024, True,
                  Vb["wring"], wtmp=Vb["wtmp"], mg=2)
    linear(K, wsq, src_from_view(NX, T),
           ep_to_view(K, QT2, dtype_scale=1.0 / 16.0, tag=tag + "qe"),
           0, T, tag=tag + "q")
    wskc = WStream(K, tag + "wk", W["Wkc"][0], W["Wkc"][1], 1024, 256, True,
                   Vb["wring"], wtmp=Vb["wtmp"], mg=2)
    linear(K, wskc, src_from_view(NX, T),
           ep_to_view(K, KCT, tag=tag + "ke"), 0, T, tag=tag + "k")
    wsvc = WStream(K, tag + "wv", W["Wvc"][0], W["Wvc"][1], 1024, 256, True,
                   Vb["wring"], wtmp=Vb["wtmp"], mg=2)
    linear(K, wsvc, src_from_view(NX, T),
           ep_to_view(K, VCT, tag=tag + "ve"), 0, T, tag=tag + "v")
    for dt in range(2):
        for tb in range(NKB):
            bank = K.ps.take()
            p.step("pe", lambda e, s=VCT.tile(dt, tb * 128, (tb + 1) * 128),
                   o=K.ps.bf16(bank, 0, 128, 0, 128), i=K.cbuf["ident"]:
                   e.transpose(o, s, i),
                   reads=[VCT.R(dt), K.creg["ident"]], writes=[K.ps.R(bank)],
                   tag=tag + "vt")
            p.step("dve", lambda e, s=K.ps.bf16(bank, 0, 128, 0, 128),
                   d=VCN.tile(tb, dt * 128, (dt + 1) * 128):
                   e.tensor_copy(d, s),
                   reads=[K.ps.R(bank)], writes=VCN.wr(tb), tag=tag + "vt")

    Vc_ = K.arena.carve(keeps + [QT2, KCT, VCN], [
        ("p", NKB, 512, BF16), ("ot2", 8, T, BF16)])
    P, OT2 = Vc_["p"], Vc_["ot2"]
    C = K.cols
    bound_sumsq_max(K, lambda i, c0, c1: (KCT.tile(i, c0, c1), KCT.R(i)), 2, T,
                    C[0:1, 44:45], C[0:1, 45:46], tag + "bk")
    for h in range(4):
        bound_sumsq_max(K, lambda i, c0, c1, h=h: (QT2.tile(2 * h + i, c0, c1),
                                                   QT2.R(2 * h + i)), 2, T,
                        C[0:1, 40:41], C[0:1, 41:42], tag + f"bq{h}")
        bound_to_negcol(K, C[0:1, 40:41], C[0:1, 44:45], C[0:1, 41:42],
                        K.negc[0:128, h:h + 1], tag=tag + f"bn{h}")
    for qc in range(T // 512):
        t0, t1 = qc * 512, (qc + 1) * 512
        for h in range(4):
            for kb in range(NKB):
                sbank = K.ps.take()
                for dt in range(2):
                    p.step("pe", lambda e, o=K.ps.ap(sbank, 0, 128, 0, 512),
                           l=KCT.tile(dt, kb * 128, (kb + 1) * 128),
                           r=QT2.tile(2 * h + dt, t0, t1),
                           st=(dt == 0), sp=(dt == 1):
                           e.matmul(o, l, r, start=st, stop=sp),
                           reads=[KCT.R(dt), QT2.R(2 * h + dt)],
                           writes=[K.ps.R(sbank)], tag=tag + "s")
                p.step("act", lambda e, s=K.ps.ap(sbank, 0, 128, 0, 512),
                       d=P.tile(kb), h=h:
                       e.activation(d, s, AF.Exp, bias=K.negc[0:128, h:h + 1]),
                       reads=[K.ps.R(sbank), K.negc_reg], writes=P.wr(kb),
                       tag=tag + "e")
            obanks = [K.ps.take(hold=True), K.ps.take(hold=True)]
            dbank = K.ps.take(hold=True)
            for kb in range(NKB):
                for half in range(2):
                    p.step("pe", lambda e, o=K.ps.ap(obanks[half], 0, 128, 0, 512),
                           l=VCN.tile(kb, half * 128, (half + 1) * 128),
                           r=P.tile(kb), st=(kb == 0), sp=(kb == NKB - 1):
                           e.matmul(o, l, r, start=st, stop=sp),
                           reads=[VCN.R(kb), P.R(kb)],
                           writes=[K.ps.R(obanks[half])], tag=tag + "pv")
                p.step("pe", lambda e, o=K.ps.ap(dbank, 0, 1, 0, 512),
                       l=K.cbuf["onescol"][0:128, 0:1], r=P.tile(kb),
                       st=(kb == 0), sp=(kb == NKB - 1):
                       e.matmul(o, l, r, start=st, stop=sp),
                       reads=[K.creg["onescol"], P.R(kb)],
                       writes=[K.ps.R(dbank)], tag=tag + "dn")
            p.step("dve", lambda e, s=K.ps.ap(dbank, 0, 1, 0, 512):
                   e.reciprocal(K.rwa[0:1, 0:512], s),
                   reads=[K.ps.R(dbank)], writes=[K.rwa_reg], tag=tag + "nr")
            bank2 = K.ps.take()
            p.step("pe", lambda e, o=K.ps.ap(bank2, 0, 128, 0, 512):
                   e.matmul(o, K.cbuf["ones1x128"][0:1, 0:128],
                            K.rwa[0:1, 0:512], start=True, stop=True),
                   reads=[K.creg["ones1x128"], K.rwa_reg],
                   writes=[K.ps.R(bank2)], tag=tag + "nb")
            p.step("dve", lambda e, s=K.ps.ap(bank2, 0, 128, 0, 512):
                   e.tensor_copy(K.xf[0:128, 0:512], s),
                   reads=[K.ps.R(bank2)], writes=[K.xf_reg], tag=tag + "nb")
            for half in range(2):
                p.step("dve", lambda e, s=K.ps.ap(obanks[half], 0, 128, 0, 512),
                       d=OT2.tile(2 * h + half, t0, t1):
                       e.tensor_tensor(d, s, K.xf[0:128, 0:512],
                                       op=AluOpType.mult),
                       reads=[K.ps.R(obanks[half]), K.xf_reg],
                       writes=OT2.wr(2 * h + half), tag=tag + "oe")
            K.ps.release(obanks[0]); K.ps.release(obanks[1]); K.ps.release(dbank)

    Vd = K.arena.carve(keeps + [OT2], [
        ("wring", 4, 2048, BF16), ("wtmp", 2, 1024, BF16)])
    wso = WStream(K, tag + "wo", W["Wo"][0], W["Wo"][1], 1024, 1024, True,
                  Vd["wring"], wtmp=Vd["wtmp"], mg=2)

    def ep_acc_y(mi, msz, bank, t0, t1):
        K.p.step("dve", lambda e, s=K.ps.ap(bank, 0, msz, 0, t1 - t0),
                 d=Y.tile(mi, t0, t1): e.tensor_tensor(d, d, s, op=AluOpType.add),
                 reads=[K.ps.R(bank), Y.R(mi)], writes=[Y.R(mi)], tag=tag + "ya")
    linear(K, wso, src_from_view(OT2, T), ep_acc_y, 0, T, tag=tag + "wo")

    Ve = K.arena.carve(keeps, [
        ("xb", 8, T, BF16), ("bc", 2, T, F32),
        ("xn2", 8, T, BF16), ("xn2lo", 8, T, BF16)])
    XN2, XN2LO = Ve["xn2"], Ve["xn2lo"]
    g2c = load_cols(K, W["g2"][0], W["g2"][1], 24, 8, tag + "g2")
    rms_emit(K, Y, Ve["xb"], Ve["bc"], XN2, T, g2c, K.cols_reg, eps=1e-6,
             tag=tag + "r2", LO=XN2LO)

    Vf = K.arena.carve(keeps + [XN2, XN2LO], [
        ("cmb", 1, T, F32), ("cmbr", 1, T, F32),
        ("wrgh", 8, 8, BF16), ("wrgl", 8, 8, BF16),
        ("wrsv", 1, 1024, F32)])
    Vf["xn2"] = XN2
    Vf["xn2lo"] = XN2LO
    _l0f_router(K, Vf, T, W, tag)
    CMB = Vf["cmb"]

    Vg = K.arena.carve(keeps + [XN2, CMB], [
        ("wring", 2, 2048, BF16), ("wring2", 2, 2048, BF16),
        ("wtmp", 2, 1024, BF16), ("cbb", 1, T, BF16),
        ("vstg", 2, T, BF16), ("hv", 8, T, BF16)])
    CBB, VSTG, HV = Vg["cbb"], Vg["vstg"], Vg["hv"]
    for e_ in range(9):
        is_shared = (e_ == 8)
        if is_shared:
            w1ap, w1rg = W["sW1"]
            w2ap, w2rg = W["sW2"]
            w3ap, w3rg = W["sW3"]
        else:
            w1ap, w1rg = W["eW1"][0][e_], W["eW1"][1]
            w2ap, w2rg = W["eW2"][0][e_], W["eW2"][1]
            w3ap, w3rg = W["eW3"][0][e_], W["eW3"][1]
        if not is_shared:
            p.dma("sp", K.rwa[0:1, 0:T], CMB.tile(0, 0, T, e_, e_ + 1),
                  reads=[CMB.R(0)], writes=[K.rwa_reg], tag=tag + f"cbr{e_}")
            for c0 in range(0, T, 512):
                c1 = min(c0 + 512, T)
                bank = K.ps.take()
                p.step("pe", lambda e, o=K.ps.ap(bank, 0, 128, 0, c1 - c0),
                       l=K.cbuf["ones1x128"][0:1, 0:128],
                       r=K.rwa[0:1, c0:c1]:
                       e.matmul(o, l, r, start=True, stop=True),
                       reads=[K.creg["ones1x128"], K.rwa_reg],
                       writes=[K.ps.R(bank)], tag=tag + "cb")
                p.step("dve", lambda e, s=K.ps.ap(bank, 0, 128, 0, c1 - c0),
                       d=CBB.tile(0, c0, c1): e.tensor_copy(d, s),
                       reads=[K.ps.R(bank)], writes=CBB.wr(0), tag=tag + "cb")
        ws3 = WStream(K, tag + f"w3_{e_}", w3ap, w3rg, 1024, 1024, True,
                      Vg["wring"], wtmp=Vg["wtmp"], mg=2)
        ws1 = WStream(K, tag + f"w1_{e_}", w1ap, w1rg, 1024, 1024, True,
                      Vg["wring2"], wtmp=Vg["wtmp"], mg=2)
        chunks = [(c0, min(c0 + 512, T)) for c0 in range(0, T, 512)]
        for g in range(ws3.n_groups):
            s3, r3 = ws3.stage_group(g)
            s1, r1 = ws1.stage_group(g)
            for m in range(2):
                mi = g * 2 + m
                for (t0, t1) in chunks:
                    bank = K.ps.take()
                    for kt in range(8):
                        p.step("pe", lambda e, o=K.ps.ap(bank, 0, 128, 0, t1 - t0),
                               l=ws3.entry_ap(s3, m, kt),
                               r=XN2.tile(kt, t0, t1),
                               st=(kt == 0), sp=(kt == 7):
                               e.matmul(o, l, r, start=st, stop=sp),
                               reads=[r3, XN2.R(kt)], writes=[K.ps.R(bank)],
                               tag=tag + "x3")
                    p.step("dve", lambda e, s=K.ps.ap(bank, 0, 128, 0, t1 - t0),
                           d=VSTG.tile(m, t0, t1): e.tensor_copy(d, s),
                           reads=[K.ps.R(bank)], writes=[VSTG.R(m)],
                           tag=tag + "x3e")
                for (t0, t1) in chunks:
                    bank = K.ps.take()
                    for kt in range(8):
                        p.step("pe", lambda e, o=K.ps.ap(bank, 0, 128, 0, t1 - t0),
                               l=ws1.entry_ap(s1, m, kt),
                               r=XN2.tile(kt, t0, t1),
                               st=(kt == 0), sp=(kt == 7):
                               e.matmul(o, l, r, start=st, stop=sp),
                               reads=[r1, XN2.R(kt)], writes=[K.ps.R(bank)],
                               tag=tag + "x1")
                    p.step("act", lambda e, s=K.ps.ap(bank, 0, 128, 0, t1 - t0),
                           d=K.sqs[0:128, 0:t1 - t0]:
                           e.activation(d, s, AF.Silu),
                           reads=[K.ps.R(bank)], writes=[K.sqs_reg],
                           tag=tag + "hs")
                    p.step("dve", lambda e, d=HV.tile(mi, t0, t1),
                           s=VSTG.tile(m, t0, t1):
                           e.tensor_tensor(d, K.sqs[0:128, 0:t1 - t0], s,
                                           op=AluOpType.mult),
                           reads=[K.sqs_reg, VSTG.R(m)], writes=HV.wr(mi),
                           tag=tag + "hm")
                    if not is_shared:
                        p.step("dve", lambda e, d=HV.tile(mi, t0, t1),
                               c=CBB.tile(0, t0, t1):
                               e.tensor_tensor(d, d, c, op=AluOpType.mult),
                               reads=[HV.R(mi), CBB.R(0)], writes=[HV.R(mi)],
                               tag=tag + "hc")
        ws2 = WStream(K, tag + f"w2_{e_}", w2ap, w2rg, 1024, 1024, True,
                      Vg["wring"], wtmp=Vg["wtmp"], mg=2)
        linear(K, ws2, src_from_view(HV, T), ep_acc_y, 0, T,
               tag=tag + f"x2_{e_}")


def _l0f_router(K, V, T, W, tag):
    p = K.p
    wrs = V["wrsv"].tile(0)
    wrs_reg = V["wrsv"].R(0)
    p.step("dve", lambda e: e.memset(V["wrsv"].tile(0, 0, 1024, 0, 32), 0.0),
           reads=[], writes=V["wrsv"].wr(0), tag=tag + "wr")
    p.dma("sp", V["wrsv"].tile(0, 0, 1024, 0, 8), W["Wr"][0],
          reads=[W["Wr"][1]], writes=[wrs_reg], tag=tag + "wr")
    p.dma("sp", K.ebt[0:128, 0:8], W["eb"][0].unsqueeze(0).partition_broadcast(128),
          reads=[W["eb"][1]], writes=[K.ebt_reg], tag=tag + "eb")
    WRGH, WRGL = V["wrgh"], V["wrgl"]
    for kt in range(8):
        bank = K.ps.take()
        p.step("pe", lambda e, s=wrs[0:8, kt * 128:(kt + 1) * 128],
               o=K.ps.ap(bank, 0, 128, 0, 8), i=K.cbuf["identf"][0:8, 0:8]:
               e.transpose(o, s, i),
               reads=[wrs_reg, K.creg["identf"]], writes=[K.ps.R(bank)],
               tag=tag + "wt")
        p.step("dve", lambda e, s=K.ps.ap(bank, 0, 128, 0, 8):
               e.tensor_copy(K.xf[0:128, 0:8], s),
               reads=[K.ps.R(bank)], writes=[K.xf_reg], tag=tag + "wt")
        p.step("dve", lambda e, kt=kt, d=WRGH.tile(kt):
               e.tensor_copy(d, K.xf[0:128, 0:8]),
               reads=[K.xf_reg], writes=WRGH.wr(kt), tag=tag + "wt")
        p.step("act", lambda e, kt=kt, s=WRGH.tile(kt):
               e.copy(K.xf[0:128, 8:16], s),
               reads=[WRGH.R(kt)], writes=[K.xf_reg], tag=tag + "wt")
        p.step("dve", lambda e, kt=kt, d=WRGL.tile(kt):
               e.tensor_tensor(d, K.xf[0:128, 0:8], K.xf[0:128, 8:16],
                               op=AluOpType.subtract),
               reads=[K.xf_reg], writes=WRGL.wr(kt), tag=tag + "wt")
    for tt in range(T // 128):
        bank = K.ps.take()
        passes = [(V["xn2"], WRGH), (V["xn2lo"], WRGH), (V["xn2"], WRGL)]
        idx = 0
        for (srcv, wv) in passes:
            for kt in range(8):
                p.step("pe", lambda e, o=K.ps.ap(bank, 0, 128, 0, 8),
                       l=srcv.tile(kt, tt * 128, (tt + 1) * 128),
                       r=wv.tile(kt), st=(idx == 0), sp=(idx == 23):
                       e.matmul(o, l, r, start=st, stop=sp),
                       reads=[srcv.R(kt), wv.R(kt)], writes=[K.ps.R(bank)],
                       tag=tag + "lg")
                idx += 1
        xf, xr = K.xf, K.xf_reg
        C, cr = K.cols, K.cols_reg
        p.step("dve", lambda e, s=K.ps.ap(bank, 0, 128, 0, 8):
               e.tensor_tensor(xf[0:128, 0:8], s, K.ebt[0:128, 0:8],
                               op=AluOpType.add),
               reads=[K.ps.R(bank), K.ebt_reg], writes=[xr], tag=tag + "lg")
        p.step("dve", lambda e: e.reduce_max(C[0:128, 48:49], xf[0:128, 0:8],
                                             axis=AX.X, negate=True),
               reads=[xr], writes=[cr], tag=tag + "sm")
        p.step("act", lambda e: e.activation(xf[0:128, 0:8], xf[0:128, 0:8],
                                             AF.Exp, bias=C[0:128, 48:49]),
               reads=[xr, cr], writes=[xr], tag=tag + "sm")
        p.step("dve", lambda e: e.tensor_reduce(C[0:128, 48:49], xf[0:128, 0:8],
                                                axis=AX.X, op=AluOpType.add),
               reads=[xr], writes=[cr], tag=tag + "sm")
        p.step("dve", lambda e: e.reciprocal(C[0:128, 49:50], C[0:128, 48:49]),
               reads=[cr], writes=[cr], tag=tag + "sm")
        p.step("dve", lambda e: e.tensor_scalar(xf[0:128, 0:8], xf[0:128, 0:8],
                                                C[0:128, 49:50], None,
                                                op0=AluOpType.mult),
               reads=[xr, cr], writes=[xr], tag=tag + "sm")
        def top1(rw_c0, mask_c0, v_col, j_col):
            p.step("dve", lambda e: e.tensor_reduce(
                C[0:128, v_col:v_col + 1], xf[0:128, rw_c0:rw_c0 + 8],
                axis=AX.X, op=AluOpType.max),
                reads=[xr], writes=[cr], tag=tag + "t1")
            p.step("dve", lambda e: e.tensor_scalar(
                xf[0:128, mask_c0:mask_c0 + 8], xf[0:128, rw_c0:rw_c0 + 8],
                C[0:128, v_col:v_col + 1], None, op0=AluOpType.is_ge),
                reads=[xr, cr], writes=[xr], tag=tag + "t1")
            p.step("dve", lambda e: e.tensor_scalar(
                xf[0:128, 24:32], xf[0:128, mask_c0:mask_c0 + 8], -64.0, 64.0,
                op0=AluOpType.mult, op1=AluOpType.add),
                reads=[xr], writes=[xr], tag=tag + "t1")
            p.step("dve", lambda e: e.tensor_tensor(
                xf[0:128, 24:32], xf[0:128, 24:32], K.cbuf["iota8"][0:128, 0:8],
                op=AluOpType.add),
                reads=[xr, K.creg["iota8"]], writes=[xr], tag=tag + "t1")
            p.step("dve", lambda e: e.tensor_reduce(
                C[0:128, j_col:j_col + 1], xf[0:128, 24:32], axis=AX.X,
                op=AluOpType.min),
                reads=[xr], writes=[cr], tag=tag + "t1")
            p.step("dve", lambda e: e.tensor_scalar(
                xf[0:128, mask_c0:mask_c0 + 8], K.cbuf["iota8"][0:128, 0:8],
                C[0:128, j_col:j_col + 1], None, op0=AluOpType.is_equal),
                reads=[K.creg["iota8"], cr], writes=[xr], tag=tag + "t1")
        top1(0, 8, 50, 51)
        p.step("dve", lambda e: e.tensor_scalar(
            xf[0:128, 24:32], xf[0:128, 8:16], 1e9, None, op0=AluOpType.mult),
            reads=[xr], writes=[xr], tag=tag + "t2")
        p.step("dve", lambda e: e.tensor_tensor(
            xf[0:128, 16:24], xf[0:128, 0:8], xf[0:128, 24:32],
            op=AluOpType.subtract),
            reads=[xr], writes=[xr], tag=tag + "t2")
        top1(16, 16, 52, 53)
        p.step("act", lambda e: e.activation(C[0:128, 54:55], C[0:128, 50:51],
                                             AF.Exp),
               reads=[cr], writes=[cr], tag=tag + "tw")
        p.step("act", lambda e: e.activation(C[0:128, 55:56], C[0:128, 52:53],
                                             AF.Exp),
               reads=[cr], writes=[cr], tag=tag + "tw")
        p.step("dve", lambda e: e.tensor_tensor(C[0:128, 56:57], C[0:128, 54:55],
                                                C[0:128, 55:56], op=AluOpType.add),
               reads=[cr], writes=[cr], tag=tag + "tw")
        p.step("dve", lambda e: e.reciprocal(C[0:128, 57:58], C[0:128, 56:57]),
               reads=[cr], writes=[cr], tag=tag + "tw")
        p.step("dve", lambda e: e.tensor_tensor(C[0:128, 54:55], C[0:128, 54:55],
                                                C[0:128, 57:58], op=AluOpType.mult),
               reads=[cr], writes=[cr], tag=tag + "tw")
        p.step("dve", lambda e: e.tensor_tensor(C[0:128, 55:56], C[0:128, 55:56],
                                                C[0:128, 57:58], op=AluOpType.mult),
               reads=[cr], writes=[cr], tag=tag + "tw")
        p.step("dve", lambda e: e.tensor_scalar(
            xf[0:128, 8:16], xf[0:128, 8:16], C[0:128, 54:55], None,
            op0=AluOpType.mult),
            reads=[xr, cr], writes=[xr], tag=tag + "cmb")
        p.step("dve", lambda e: e.tensor_scalar(
            xf[0:128, 16:24], xf[0:128, 16:24], C[0:128, 55:56], None,
            op0=AluOpType.mult),
            reads=[xr, cr], writes=[xr], tag=tag + "cmb")
        p.step("dve", lambda e: e.tensor_tensor(
            xf[0:128, 0:8], xf[0:128, 8:16], xf[0:128, 16:24], op=AluOpType.add),
            reads=[xr], writes=[xr], tag=tag + "cmb")
        bank2 = K.ps.take()
        p.step("pe", lambda e, o=K.ps.ap(bank2, 0, 8, 0, 128):
               e.transpose(o, xf[0:128, 0:8], K.cbuf["identf"]),
               reads=[xr, K.creg["identf"]], writes=[K.ps.R(bank2)], tag=tag + "ct")
        p.step("dve", lambda e, s=K.ps.ap(bank2, 0, 8, 0, 128), tt=tt:
               e.tensor_copy(V["cmb"].tile(0, tt * 128, (tt + 1) * 128, 0, 8), s),
               reads=[K.ps.R(bank2)], writes=V["cmb"].wr(0), tag=tag + "ct")



def bound_sumsq_max_b2(K, tiles_fn, ntiles, T, cell, tmpcell, tag, chunk=512):
    p = K.p
    C, cr = K.cols, K.cols_reg
    first = True
    for i in range(ntiles):
        for c0 in range(0, T, chunk):
            c1 = min(c0 + chunk, T)
            ap_, reg_ = tiles_fn(i, c0, c1)
            p.step("act", lambda e, s=ap_, d=K.sqs[0:128, 0:c1 - c0]:
                   e.activation(d, s, AF.Square),
                   reads=[reg_], writes=[K.sqs_reg], tag=tag)
            bank = K.ps.take()
            p.step("pe", lambda e, o=K.ps.ap(bank, 0, 2, 0, c1 - c0):
                   e.matmul(o, K.cbuf["e2"][0:128, 0:2],
                            K.sqs[0:128, 0:c1 - c0], start=True, stop=True),
                   reads=[K.creg["e2"], K.sqs_reg], writes=[K.ps.R(bank)],
                   tag=tag)
            dc = C[0:2, 60:61] if first else C[0:2, 61:62]
            p.step("dve", lambda e, s=K.ps.ap(bank, 0, 2, 0, c1 - c0), d=dc:
                   e.reduce_max(d, s, axis=AX.X),
                   reads=[K.ps.R(bank)], writes=[cr], tag=tag)
            if not first:
                p.step("dve", lambda e: e.tensor_tensor(
                    C[0:2, 60:61], C[0:2, 60:61], C[0:2, 61:62], op=AluOpType.max),
                    reads=[cr], writes=[cr], tag=tag)
            first = False
    bank = K.ps.take()
    p.step("pe", lambda e, o=K.ps.ap(bank, 0, 1, 0, 2):
           e.transpose(o, C[0:2, 60:61], K.cbuf["identf"][0:2, 0:2]),
           reads=[cr, K.creg["identf"]], writes=[K.ps.R(bank)], tag=tag)
    p.step("dve", lambda e, s=K.ps.ap(bank, 0, 1, 0, 2), d=cell:
           e.reduce_max(d, s, axis=AX.X),
           reads=[K.ps.R(bank)], writes=[cr], tag=tag)


def stage_l1_attn(K, XNT, T, W, ao_sp, xnt_sp, keeps=(), tag="l1a"):
    p = K.p
    NKB = T // 128
    keeps = list(keeps)

    V1 = K.arena.carve(keeps + [XNT], [
        ("wring", 4, 2048, BF16), ("wtmp", 2, 1024, BF16),
        ("qkt", 16, T, BF16), ("vn", NKB, 1040, BF16),
        ("ropec", 1, T, F32), ("ropes", 1, T, F32)])
    QKT, VN = V1["qkt"], V1["vn"]
    p.dma("sp", V1["ropec"].tile(0), W["ropec"][0], reads=[W["ropec"][1]],
          writes=V1["ropec"].wr(0), tag=tag + "tc")
    p.dma("sp", V1["ropes"].tile(0), W["ropes"][0], reads=[W["ropes"][1]],
          writes=V1["ropes"].wr(0), tag=tag + "ts")
    bq = load_cols(K, W["bqkv"][0], W["bqkv"][1], 8, 24, tag + "bq")

    for t_ in range(NKB):
        p.step("dve", lambda e, t_=t_: e.memset(VN.tile(t_), 1.0),
               reads=[], writes=VN.wr(t_), tag=tag + "vo")
    ws = WStream(K, tag + "w", W["wqkv"][0], W["wqkv"][1], 1024, 3072, True,
                 V1["wring"], wtmp=V1["wtmp"], mg=2)

    def ep_qkv(mi, msz, bank, t0, t1):
        if mi < 8:
            K.p.step("dve", lambda e, s=K.ps.ap(bank, 0, msz, 0, t1 - t0),
                     d=QKT.tile(mi, t0, t1), b=bq[mi]:
                     e.tensor_scalar(d, s, b, 0.125, op0=AluOpType.add,
                                     op1=AluOpType.mult),
                     reads=[K.ps.R(bank), K.cols_reg], writes=[QKT.R(mi)],
                     tag=tag + "eq")
        elif mi < 16:
            K.p.step("dve", lambda e, s=K.ps.ap(bank, 0, msz, 0, t1 - t0),
                     d=QKT.tile(mi, t0, t1), b=bq[mi]:
                     e.tensor_scalar(d, s, b, None, op0=AluOpType.add),
                     reads=[K.ps.R(bank), K.cols_reg], writes=[QKT.R(mi)],
                     tag=tag + "ek")
        else:
            K.p.step("dve", lambda e, s=K.ps.ap(bank, 0, msz, 0, t1 - t0),
                     d=K.sqs[0:128, 0:t1 - t0], b=bq[mi]:
                     e.tensor_scalar(d, s, b, None, op0=AluOpType.add),
                     reads=[K.ps.R(bank), K.cols_reg], writes=[K.sqs_reg],
                     tag=tag + "ev")
            h0 = 2 * (mi - 16)
            for tb in range((t1 - t0) // 128):
                bank2 = K.ps.take()
                K.p.step("pe", lambda e, tb=tb,
                         s=K.sqs[0:128, tb * 128:(tb + 1) * 128],
                         o=K.ps.bf16(bank2, 0, 128, 0, 128), i=K.cbuf["ident"]:
                         e.transpose(o, s, i),
                         reads=[K.sqs_reg, K.creg["ident"]],
                         writes=[K.ps.R(bank2)], tag=tag + "vt")
                vt = (t0 + tb * 128) // 128
                src_ = K.ps.bf16(bank2, 0, 128, 0, 128).rearrange(
                    "p (h c) -> p h c", c=64)
                dst = VN.tile(vt, 65 * h0, 65 * h0 + 130).rearrange(
                    "p (h c) -> p h c", c=65)[:, :, 0:64]
                K.p.step("dve", lambda e, s=src_, d=dst: e.tensor_copy(d, s),
                         reads=[K.ps.R(bank2)], writes=[VN.R(vt)],
                         tag=tag + "vc")
    linear(K, ws, src_from_view(XNT, T), ep_qkv, 0, T, tag=tag + "qkv")

    for t_ in range(16):
        for c0 in range(0, T, 512):
            c1 = c0 + 512
            bank = K.ps.take()
            p.step("pe", lambda e, o=K.ps.ap(bank, 0, 128, 0, 512),
                   l=K.cbuf["permf"], r=QKT.tile(t_, c0, c1):
                   e.matmul(o, l, r, start=True, stop=True),
                   reads=[K.creg["permf"], QKT.R(t_)], writes=[K.ps.R(bank)],
                   tag=tag + "rp")
            p.step("dve", lambda e, s=K.ps.ap(bank, 0, 128, 0, 512),
                   c=V1["ropes"].tile(0, c0, c1):
                   e.tensor_tensor(K.xf[0:128, 0:512], s, c, op=AluOpType.mult),
                   reads=[K.ps.R(bank), V1["ropes"].R(0)], writes=[K.xf_reg],
                   tag=tag + "rp")
            p.step("dve", lambda e, q=QKT.tile(t_, c0, c1),
                   c=V1["ropec"].tile(0, c0, c1):
                   e.tensor_tensor(K.xf[0:128, 512:1024], q, c, op=AluOpType.mult),
                   reads=[QKT.R(t_), V1["ropec"].R(0)], writes=[K.xf_reg],
                   tag=tag + "rp")
            p.step("dve", lambda e, d=QKT.tile(t_, c0, c1):
                   e.tensor_tensor(d, K.xf[0:128, 0:512], K.xf[0:128, 512:1024],
                                   op=AluOpType.add),
                   reads=[K.xf_reg], writes=[QKT.R(t_)], tag=tag + "rp")

    for kt in range(8):
        p.dma("sp", xnt_sp[0][kt * 128:(kt + 1) * 128, 0:T], XNT.tile(kt),
              reads=[XNT.R(kt)], writes=[xnt_sp[1]], tag=tag + "xsp")

    V2 = K.arena.carve(keeps + [QKT, VN], [
        ("p", NKB, 512, BF16), ("cmask", 4, 512, BF16), ("ot", 8, T, BF16)])
    P, OT = V2["p"], V2["ot"]
    for r in range(4):
        p.dma("sp", V2["cmask"].tile(r), W["cmask"][0][r * 128:(r + 1) * 128, 0:512],
              reads=[W["cmask"][1]], writes=V2["cmask"].wr(r), tag=tag + "tm")
    bound_sumsq_max_b2(K, lambda i, c0, c1: (QKT.tile(i, c0, c1), QKT.R(i)),
                       8, T, K.cols[0:1, 40:41], K.cols[0:1, 41:42], tag + "bq")
    bound_sumsq_max_b2(K, lambda i, c0, c1: (QKT.tile(8 + i, c0, c1), QKT.R(8 + i)),
                       8, T, K.cols[0:1, 44:45], K.cols[0:1, 45:46], tag + "bk")
    bound_to_negcol(K, K.cols[0:1, 40:41], K.cols[0:1, 44:45], K.cols[0:1, 41:42],
                    K.negc[0:128, 0:1], tag=tag + "bn")

    for qc in range(T // 512):
        t0, t1 = qc * 512, (qc + 1) * 512
        nkb = (qc + 1) * 4
        for h in range(16):
            qt = h // 2
            b = 64 * (h % 2)
            for kb in range(nkb):
                sbank = K.ps.take()
                p.step("pe", lambda e, o=K.ps.ap(sbank, 0, 128, 0, 512),
                       l=QKT.tile(8 + qt, kb * 128, (kb + 1) * 128, b, b + 64),
                       r=QKT.tile(qt, t0, t1, b, b + 64):
                       e.matmul(o, l, r, start=True, stop=True),
                       reads=[QKT.R(8 + qt), QKT.R(qt)],
                       writes=[K.ps.R(sbank)], tag=tag + "s")
                p.step("act", lambda e, s=K.ps.ap(sbank, 0, 128, 0, 512),
                       d=P.tile(kb):
                       e.activation(d, s, AF.Exp, bias=K.negc[0:128, 0:1]),
                       reads=[K.ps.R(sbank), K.negc_reg], writes=P.wr(kb),
                       tag=tag + "e")
                r = kb * 128 - t0
                if 0 <= r < 512:
                    p.step("dve", lambda e, d=P.tile(kb), m=V2["cmask"].tile(r // 128):
                           e.tensor_tensor(d, d, m, op=AluOpType.mult),
                           reads=[P.R(kb), V2["cmask"].R(r // 128)],
                           writes=[P.R(kb)], tag=tag + "m")
            obank = K.ps.take(hold=True)
            for kb in range(nkb):
                p.step("pe", lambda e, o=K.ps.ap(obank, 0, 65, 0, 512),
                       l=VN.tile(kb, 65 * h, 65 * h + 65), r=P.tile(kb),
                       st=(kb == 0), sp=(kb == nkb - 1):
                       e.matmul(o, l, r, start=st, stop=sp),
                       reads=[VN.R(kb), P.R(kb)], writes=[K.ps.R(obank)],
                       tag=tag + "pv")
            p.step("dve", lambda e, s=K.ps.ap(obank, 64, 65, 0, 512):
                   e.reciprocal(K.rwa[0:1, 0:512], s),
                   reads=[K.ps.R(obank)], writes=[K.rwa_reg], tag=tag + "nr")
            bank2 = K.ps.take()
            p.step("pe", lambda e, o=K.ps.ap(bank2, 0, 64, 0, 512):
                   e.matmul(o, K.cbuf["ones1x128"][0:1, 0:64],
                            K.rwa[0:1, 0:512], start=True, stop=True),
                   reads=[K.creg["ones1x128"], K.rwa_reg],
                   writes=[K.ps.R(bank2)], tag=tag + "nb")
            p.step("dve", lambda e, s=K.ps.ap(bank2, 0, 64, 0, 512):
                   e.tensor_copy(K.xf[0:64, 0:512], s),
                   reads=[K.ps.R(bank2)], writes=[K.xf_reg], tag=tag + "nb")
            rb = 64 * (h % 2)
            p.step("dve", lambda e, s=K.ps.ap(obank, 0, 64, 0, 512),
                   d=OT.tile(h // 2, t0, t1, rb, rb + 64):
                   e.tensor_tensor(d, s, K.xf[0:64, 0:512], op=AluOpType.mult),
                   reads=[K.ps.R(obank), K.xf_reg],
                   writes=[OT.R(h // 2)], tag=tag + "oe")
            K.ps.release(obank)

    V3 = K.arena.carve(keeps + [OT], [
        ("wring", 4, 2048, BF16), ("wtmp", 2, 1024, BF16)])
    bo = load_cols(K, W["bout"][0], W["bout"][1], 32, 8, tag + "bo")
    wso = WStream(K, tag + "wo", W["wout"][0], W["wout"][1], 1024, 1024, True,
                  V3["wring"], wtmp=V3["wtmp"], mg=2)

    def ep_ao(mi, msz, bank, t0, t1):
        stg = (K.sqs, K.sqs_reg) if mi % 2 == 0 else (K.pb, K.pb_reg)
        K.p.step("dve", lambda e, s=K.ps.ap(bank, 0, msz, 0, t1 - t0),
                 d=stg[0][0:msz, 0:t1 - t0], b=bo[mi]:
                 e.tensor_scalar(d, s, b, None, op0=AluOpType.add),
                 reads=[K.ps.R(bank), K.cols_reg], writes=[stg[1]],
                 tag=tag + "ao")
        K.p.dma("sp", ao_sp[0][mi * 128:mi * 128 + msz, t0:t1],
                stg[0][0:msz, 0:t1 - t0],
                reads=[stg[1]], writes=[ao_sp[1]], tag=tag + "aosp")
    linear(K, wso, src_from_view(OT, T), ep_ao, 0, T, tag=tag + "wo")



def ln_shift_emit(K, YB, XN1, t0, t1, gcols, bcols, SQ8, BC, XF2, tag="lns"):
    p = K.p
    TL = t1 - t0
    RWA, RWB = K.rwa, K.rwb
    rA, rB = K.rwa_reg, K.rwb_reg

    def zsrc(kt, c0, c1):
        off = 1 if kt < 4 else 0
        return YB.tile(kt, t0 + c0 + off, t0 + c1 + off), [YB.R(kt)], 128
    colsum(K, zsrc, 8, RWA[0:1, 0:TL], [rA], 0, TL, scale=1.0 / 1024.0,
           tag=tag + "mu")
    bcast_row(K, RWA[0:1, 0:TL], [rA], lambda a, b: BC.tile(1, a, b),
              BC.wr(1), 0, TL, tag=tag + "b1")
    p.step("act", lambda e: e.activation(RWA[0:1, 0:TL], RWA[0:1, 0:TL],
                                         AF.Square),
           reads=[rA], writes=[rA], tag=tag)
    for kt in range(8):
        off = 1 if kt < 4 else 0
        p.step("act", lambda e, kt=kt, off=off,
               s=YB.tile(kt, t0 + off, t1 + off), d=SQ8.tile(kt):
               e.activation(d, s, AF.Square),
               reads=[YB.R(kt)], writes=SQ8.wr(kt), tag=tag + "sq")
    colsum(K, lambda kt, c0, c1: (SQ8.tile(kt, c0, c1), [SQ8.R(kt)], 128),
           8, RWB[0:1, 0:TL], [rB], 0, TL, scale=1.0 / 1024.0, tag=tag + "s2")
    p.step("dve", lambda e: e.tensor_tensor(RWB[0:1, 0:TL], RWB[0:1, 0:TL],
                                            RWA[0:1, 0:TL], op=AluOpType.subtract),
           reads=[rA, rB], writes=[rB], tag=tag)
    p.step("act", lambda e: e.activation(RWB[0:1, 0:TL], RWB[0:1, 0:TL],
                                         AF.Ln, bias=K.cbuf["misc"][0:1, 0:1]),
           reads=[rB, K.creg["misc"]], writes=[rB], tag=tag)
    p.step("act", lambda e: e.activation(RWB[0:1, 0:TL], RWB[0:1, 0:TL],
                                         AF.Exp, scale=-0.5),
           reads=[rB], writes=[rB], tag=tag)
    bcast_row(K, RWB[0:1, 0:TL], [rB], lambda a, b: BC.tile(0, a, b),
              BC.wr(0), 0, TL, tag=tag + "b0")
    for kt in range(8):
        off = 1 if kt < 4 else 0
        x2 = XF2.tile(kt % 2)
        x2r = XF2.R(kt % 2)
        p.step("dve", lambda e, kt=kt, off=off, x2=x2: e.tensor_tensor(
            x2, YB.tile(kt, t0 + off, t1 + off), BC.tile(1, 0, TL),
            op=AluOpType.subtract),
            reads=[YB.R(kt), BC.R(1)], writes=XF2.wr(kt % 2), tag=tag)
        p.step("dve", lambda e, x2=x2: e.tensor_tensor(
            x2, x2, BC.tile(0, 0, TL), op=AluOpType.mult),
            reads=[x2r, BC.R(0)], writes=[x2r], tag=tag)
        p.step("dve", lambda e, kt=kt, x2=x2, g=gcols[kt], b=bcols[kt]:
               e.tensor_scalar(XN1.tile(kt, t0, t1), x2, g, b,
                               op0=AluOpType.mult, op1=AluOpType.add),
               reads=[x2r, K.cols_reg], writes=XN1.wr(kt), tag=tag)


def stage_l1_ffn(K, XNT, Y, T, W, keeps=(), tag="l1f", tap=None):
    p = K.p
    for kt in range(8):
        p.step("act", lambda e, kt=kt, d=Y.tile(kt), s=XNT.tile(kt):
               e.copy(d, s), reads=[XNT.R(kt)], writes=Y.wr(kt), tag=tag + "sd")
    keeps = [Y] + list(keeps)
    for li in range(6):
        V = K.arena.carve(keeps, [
            ("wqs", 12, 2048, BF16), ("wos", 4, 2048, BF16),
            ("yw", 16, 257, BF16), ("qw", 8, 256, BF16), ("kw", 16, 256, BF16),
            ("vw", 4, 1024, BF16), ("xcs", 1, 256, BF16), ("xss", 1, 256, BF16),
            ("xck", 1, 256, BF16), ("xsk", 1, 256, BF16),
            ("pt", 4, 256, BF16), ("otw", 8, 256, BF16), ("rlq", 2, 32, F32),
            ("lmask", 2, 512, BF16),
            ("sx", 2, 512, F32), ("spb", 2, 512, BF16), ("scl", 8, 2, F32),
            ("sq2", 2, 256, BF16), ("srl", 1, 512, F32), ("rlf", 2, 256, F32),
            ("rlb", 1, 256, BF16),
        ])
        _l1f_attn(K, Y, V, T, W, li, tag=f"{tag}a{li}")
        if tap is not None:
            tap(li, "a")
        _l1f_ffn(K, Y, T, W, li, keeps, tag=f"{tag}f{li}")
        if tap is not None:
            tap(li, "b")


def _l1f_attn(K, Y, V, T, W, li, tag):
    p = K.p
    NW = T // 256
    wqk = WStream(K, tag + "wq", slice3d(W["Wqkv"][0], li, 1024, 3072),
                  W["Wqkv"][1], 1024, 3072, False, V["wqs"], mg=2)
    for g in range(wqk.n_groups):
        wqk.stage_group(g)
    wos = WStream(K, tag + "wo", slice3d(W["Wo"][0], li, 1024, 1024),
                  W["Wo"][1], 1024, 1024, False, V["wos"], mg=2)
    for g in range(wos.n_groups):
        wos.stage_group(g)
    for rep in range(2):
        p.dma("sp", K.cols[rep * 64:rep * 64 + 64, 60:61],
              bass.AP(W["qs"][0].tensor, li * 64, [[1, 64], [1, 1]]),
              reads=[W["qs"][1]], writes=[K.cols_reg], tag=tag + "qs")
        p.dma("sp", K.cols[rep * 64:rep * 64 + 64, 61:62],
              bass.AP(W["ks"][0].tensor, li * 64, [[1, 64], [1, 1]]),
              reads=[W["ks"][1]], writes=[K.cols_reg], tag=tag + "ks")
    qscol = K.cols[0:128, 60:61]
    kscol = K.cols[0:128, 61:62]
    for qt in range(2):
        p.dma("pool", V["lmask"].tile(qt),
              W["lmask0"][0][qt * 128:(qt + 1) * 128, 0:512],
              reads=[W["lmask0"][1]], writes=V["lmask"].wr(qt), tag=tag + "mk0")
    for t_ in range(V["kw"].nt):
        p.step("dve", lambda e, t_=t_: e.memset(V["kw"].tile(t_), 0.0),
               reads=[], writes=V["kw"].wr(t_), tag=tag + "z")
    for t_ in range(V["vw"].nt):
        p.step("dve", lambda e, t_=t_: e.memset(V["vw"].tile(t_), 0.0),
               reads=[], writes=V["vw"].wr(t_), tag=tag + "z")
    for kt in range(8):
        p.step("dve", lambda e, kt=kt, d=V["yw"].tile(kt, 0, 1): e.memset(d, 0.0),
               reads=[], writes=V["yw"].wr(kt), tag=tag + "yz")
    YLOC = [Region(f"{tag}yl{kt}") for kt in range(8)]

    for w in range(NW):
        t0 = w * 256
        ys = (w % 2) * 8
        nb = ((w + 1) % 2) * 8
        for kt in range(8):
            p.step("act", lambda e, kt=kt, ys=ys, s=Y.tile(kt, t0, t0 + 256),
                   d=V["yw"].tile(ys + kt, 1, 257): e.copy(d, s),
                   reads=[Y.R(kt)], writes=V["yw"].wr(ys + kt), tag=tag + "yc")
            if w + 1 < NW:
                p.step("act", lambda e, kt=kt, nb=nb,
                       s=Y.tile(kt, t0 + 255, t0 + 256),
                       d=V["yw"].tile(nb + kt, 0, 1): e.copy(d, s),
                       reads=[Y.R(kt)], writes=V["yw"].wr(nb + kt),
                       tag=tag + "yb")
        for (nm, dr) in (("xcs", "xposcs"), ("xss", "xposss"),
                         ("xck", "xposck"), ("xsk", "xpossk")):
            p.dma("sp", V[nm].tile(0), W[dr][0][0:128, t0:t0 + 256],
                  reads=[W[dr][1]], writes=V[nm].wr(0), tag=tag + "tb")
        kslot = w % 2
        for mi in range(24):
            bank = K.ps.take()
            for kt in range(8):
                off = 1 if kt < 4 else 0
                g = mi // wqk.mg
                p.step("pe", lambda e, o=K.ps.ap(bank, 0, 128, 0, 256),
                       l=wqk.entry_ap(g % wqk.nring, mi % wqk.mg, kt),
                       r=V["yw"].tile(ys + kt, off, off + 256),
                       st=(kt == 0), sp=(kt == 7):
                       e.matmul(o, l, r, start=st, stop=sp),
                       reads=[wqk.slot_regs[g % wqk.nring], V["yw"].R(ys + kt)],
                       writes=[K.ps.R(bank)], tag=tag + "qkv")
            if mi < 16:
                isq = mi < 8
                ht = mi % 8
                sq = V["sq2"].tile(mi % 2)
                sqr = V["sq2"].R(mi % 2)
                rlf = V["rlf"].tile(mi % 2, 0, 256, 0, 2)
                rlr = V["rlf"].R(mi % 2)
                p.step("act", lambda e, s=K.ps.ap(bank, 0, 128, 0, 256), sq=sq:
                       e.activation(sq, s, AF.Square),
                       reads=[K.ps.R(bank)], writes=V["sq2"].wr(mi % 2),
                       tag=tag + "l2")
                bank2 = K.ps.take()
                p.step("pe", lambda e, o=K.ps.ap(bank2, 0, 2, 0, 256), sq=sq:
                       e.matmul(o, K.cbuf["e2"][0:128, 0:2], sq,
                                start=True, stop=True),
                       reads=[K.creg["e2"], sqr], writes=[K.ps.R(bank2)],
                       tag=tag + "l2")
                p.step("act", lambda e, s=K.ps.ap(bank2, 0, 2, 0, 256), d=rlf:
                       e.activation(d, s, AF.Ln,
                                    bias=K.cbuf["misc"][0:2, 4:5]),
                       reads=[K.ps.R(bank2), K.creg["misc"]],
                       writes=V["rlf"].wr(mi % 2), tag=tag + "l2")
                p.step("act", lambda e, d=rlf: e.activation(d, d, AF.Exp,
                                                            scale=0.5),
                       reads=[rlr], writes=[rlr], tag=tag + "l2")
                p.step("dve", lambda e, d=rlf: e.tensor_scalar(
                    d, d, 1e-12, None, op0=AluOpType.max),
                    reads=[rlr], writes=[rlr], tag=tag + "l2")
                p.step("dve", lambda e, d=rlf: e.reciprocal(d, d),
                       reads=[rlr], writes=[rlr], tag=tag + "l2")
                gc = qscol if isq else kscol
                p.step("dve", lambda e, s=K.ps.ap(bank, 0, 128, 0, 256), g=gc,
                       sq=sq:
                       e.tensor_scalar(sq, s, g, None, op0=AluOpType.mult),
                       reads=[K.ps.R(bank), K.cols_reg], writes=[sqr],
                       tag=tag + "g")
                bank3 = K.ps.take()
                p.step("pe", lambda e, o=K.ps.ap(bank3, 0, 128, 0, 256), sq=sq:
                       e.matmul(o, K.cbuf["permh"], sq, start=True, stop=True),
                       reads=[K.creg["permh"], sqr], writes=[K.ps.R(bank3)],
                       tag=tag + "rp")
                ct = V["xcs"] if isq else V["xck"]
                st_ = V["xss"] if isq else V["xsk"]
                srl = V["srl"]
                p.step("dve", lambda e, c=ct.tile(0), sq=sq:
                       e.tensor_tensor(srl.tile(0, 0, 256), sq, c,
                                       op=AluOpType.mult),
                       reads=[sqr, ct.R(0)], writes=srl.wr(0), tag=tag + "rp")
                p.step("dve", lambda e, s=K.ps.ap(bank3, 0, 128, 0, 256),
                       c=st_.tile(0):
                       e.tensor_tensor(srl.tile(0, 256, 512), s, c,
                                       op=AluOpType.mult),
                       reads=[K.ps.R(bank3), st_.R(0)], writes=[srl.R(0)],
                       tag=tag + "rp")
                if isq:
                    p.step("dve", lambda e, ht=ht, d=V["qw"].tile(ht):
                           e.tensor_tensor(d, srl.tile(0, 0, 256),
                                           srl.tile(0, 256, 512),
                                           op=AluOpType.add),
                           reads=[srl.R(0)], writes=V["qw"].wr(ht), tag=tag + "q")
                    for qt in range(2):
                        bank4 = K.ps.take()
                        p.step("pe", lambda e, qt=qt,
                               o=K.ps.ap(bank4, 0, 128, 0, 2),
                               s=V["rlf"].tile(mi % 2, qt * 128,
                                               (qt + 1) * 128, 0, 2):
                               e.transpose(o, s, K.cbuf["identf"][0:2, 0:2]),
                               reads=[rlr, K.creg["identf"]],
                               writes=[K.ps.R(bank4)], tag=tag + "rq")
                        p.step("dve", lambda e, qt=qt, ht=ht,
                               s=K.ps.ap(bank4, 0, 128, 0, 2):
                               e.tensor_copy(V["rlq"].tile(qt, 2 * ht, 2 * ht + 2),
                                             s), reads=[K.ps.R(bank4)],
                               writes=[V["rlq"].R(qt)], tag=tag + "rq")
                else:
                    p.step("dve", lambda e: e.tensor_tensor(
                        srl.tile(0, 0, 256), srl.tile(0, 0, 256),
                        srl.tile(0, 256, 512), op=AluOpType.add),
                        reads=[srl.R(0)], writes=[srl.R(0)], tag=tag + "k")
                    p.step("dve", lambda e, d=V["rlb"].tile(0, 0, 256, 0, 2),
                           s=rlf: e.tensor_copy(d, s),
                           reads=[rlr], writes=V["rlb"].wr(0), tag=tag + "k")
                    bank4 = K.ps.take()
                    p.step("pe", lambda e, o=K.ps.ap(bank4, 0, 128, 0, 256):
                           e.matmul(o, K.cbuf["exp2"][0:2, 0:128],
                                    V["rlb"].tile(0, 0, 256, 0, 2),
                                    start=True, stop=True),
                           reads=[K.creg["exp2"], V["rlb"].R(0)],
                           writes=[K.ps.R(bank4)], tag=tag + "k")
                    p.step("dve", lambda e, ht=ht, kslot=kslot,
                           s=K.ps.ap(bank4, 0, 128, 0, 256),
                           d=V["kw"].tile(kslot * 8 + ht):
                           e.tensor_tensor(d, srl.tile(0, 0, 256), s,
                                           op=AluOpType.mult),
                           reads=[srl.R(0), K.ps.R(bank4)],
                           writes=[V["kw"].R(kslot * 8 + ht)], tag=tag + "k")
            else:
                sq = V["sq2"].tile(mi % 2)
                sqr = V["sq2"].R(mi % 2)
                p.step("dve", lambda e, s=K.ps.ap(bank, 0, 128, 0, 256), sq=sq:
                       e.tensor_copy(sq, s),
                       reads=[K.ps.R(bank)], writes=V["sq2"].wr(mi % 2),
                       tag=tag + "v")
                bank2 = K.ps.take()
                for tb in range(2):
                    p.step("pe", lambda e, tb=tb,
                           s=V["sq2"].tile(mi % 2, tb * 128, (tb + 1) * 128),
                           o=K.ps.bf16(bank2, 0, 128, tb * 128, tb * 128 + 128),
                           i=K.cbuf["ident"]:
                           e.transpose(o, s, i),
                           reads=[sqr, K.creg["ident"]],
                           writes=[K.ps.R(bank2)], tag=tag + "v")
                for tb in range(2):
                    vt = kslot * 2 + tb
                    p.step("dve", lambda e, tb=tb,
                           s=K.ps.bf16(bank2, 0, 128, tb * 128, tb * 128 + 128),
                           d=V["vw"].tile(vt, 128 * (mi - 16), 128 * (mi - 16) + 128):
                           e.tensor_copy(d, s),
                           reads=[K.ps.R(bank2)], writes=[V["vw"].R(vt)],
                           tag=tag + "v")
        pslot = (w + 1) % 2
        for h in range(16):
            qt_t = h // 2
            b = 64 * (h % 2)
            obank = K.ps.take(hold=True)
            for qt in range(2):
                u = (2 * h + qt) % 2
                uc = (2 * h + qt) % 8
                SX = V["sx"]
                PB = V["spb"]
                SCt = V["scl"].tile(uc, 0, 2, 0, 128)
                SCr = V["scl"].R(uc)
                sbank = K.ps.take(hold=True)
                for kh in range(2):
                    ks_ = pslot if kh == 0 else kslot
                    p.step("pe", lambda e, o=K.ps.ap(sbank, 0, 128, kh * 256,
                                                     kh * 256 + 256),
                           l=V["qw"].tile(qt_t, qt * 128, (qt + 1) * 128, b, b + 64),
                           r=V["kw"].tile(ks_ * 8 + qt_t, 0, 256, b, b + 64):
                           e.matmul(o, l, r, start=True, stop=True),
                           reads=[V["qw"].R(qt_t), V["kw"].R(ks_ * 8 + qt_t)],
                           writes=[K.ps.R(sbank)], tag=tag + "s")
                p.step("dve", lambda e, u=u, s=K.ps.ap(sbank, 0, 128, 0, 512),
                       r=V["rlq"].tile(qt, h, h + 1):
                       e.tensor_scalar(SX.tile(u), s, r, 8.0,
                                       op0=AluOpType.mult, op1=AluOpType.mult),
                       reads=[K.ps.R(sbank), V["rlq"].R(qt)], writes=SX.wr(u),
                       tag=tag + "sc")
                K.ps.release(sbank)
                p.step("dve", lambda e, u=u, m=V["lmask"].tile(qt):
                       e.tensor_tensor(SX.tile(u), SX.tile(u), m,
                                       op=AluOpType.add),
                       reads=[SX.R(u), V["lmask"].R(qt)], writes=[SX.R(u)],
                       tag=tag + "mk")
                p.step("dve", lambda e, u=u, d=SCt[0:128, 0:1]:
                       e.reduce_max(d, SX.tile(u), axis=AX.X, negate=True),
                       reads=[SX.R(u)], writes=V["scl"].wr(uc), tag=tag + "mx")
                p.step("act", lambda e, u=u: e.activation(
                    PB.tile(u), SX.tile(u), AF.Exp,
                    bias=SCt[0:128, 0:1], accum_out=SCt[0:128, 1:2]),
                    reads=[SX.R(u), SCr], writes=[*PB.wr(u), SCr],
                    tag=tag + "ex")
                p.step("dve", lambda e: e.reciprocal(SCt[0:128, 0:1],
                                                     SCt[0:128, 1:2]),
                       reads=[SCr], writes=[SCr], tag=tag + "rc")
                p.step("dve", lambda e, u=u: e.tensor_scalar(
                    PB.tile(u), PB.tile(u), SCt[0:128, 0:1],
                    None, op0=AluOpType.mult),
                    reads=[PB.R(u), SCr], writes=[PB.R(u)], tag=tag + "nm")
                ptb = K.ps.take()
                for kt4 in range(4):
                    p.step("pe", lambda e, kt4=kt4, u=u,
                           s=PB.tile(u, kt4 * 128, (kt4 + 1) * 128),
                           o=K.ps.bf16(ptb, 0, 128, kt4 * 128, kt4 * 128 + 128),
                           i=K.cbuf["ident"]:
                           e.transpose(o, s, i),
                           reads=[PB.R(u), K.creg["ident"]],
                           writes=[K.ps.R(ptb)], tag=tag + "pt")
                for kt4 in range(4):
                    p.step("dve", lambda e, kt4=kt4,
                           s=K.ps.bf16(ptb, 0, 128, kt4 * 128, kt4 * 128 + 128),
                           d=V["pt"].tile(kt4, qt * 128, (qt + 1) * 128):
                           e.tensor_copy(d, s),
                           reads=[K.ps.R(ptb)], writes=V["pt"].wr(kt4),
                           tag=tag + "pt")
            for kt4 in range(4):
                vt = (pslot if kt4 < 2 else kslot) * 2 + (kt4 % 2)
                p.step("pe", lambda e, o=K.ps.ap(obank, 0, 64, 0, 256),
                       l=V["vw"].tile(vt, 64 * h, 64 * h + 64),
                       r=V["pt"].tile(kt4), st=(kt4 == 0), sp=(kt4 == 3):
                       e.matmul(o, l, r, start=st, stop=sp),
                       reads=[V["vw"].R(vt), V["pt"].R(kt4)],
                       writes=[K.ps.R(obank)], tag=tag + "pv")
            rb = 64 * (h % 2)
            p.step("dve", lambda e, s=K.ps.ap(obank, 0, 64, 0, 256),
                   d=V["otw"].tile(h // 2, 0, 256, rb, rb + 64):
                   e.tensor_copy(d, s),
                   reads=[K.ps.R(obank)], writes=[V["otw"].R(h // 2)],
                   tag=tag + "oe")
            K.ps.release(obank)
        if w == 0:
            for qt in range(2):
                p.dma("pool", V["lmask"].tile(qt),
                      W["lmask"][0][qt * 128:(qt + 1) * 128, 0:512],
                      reads=[W["lmask"][1]], writes=[V["lmask"].R(qt)],
                      tag=tag + "mk1")
        for mi in range(8):
            bank = K.ps.take()
            for kt in range(8):
                g = mi // wos.mg
                p.step("pe", lambda e, o=K.ps.ap(bank, 0, 128, 0, 256),
                       l=wos.entry_ap(g % wos.nring, mi % wos.mg, kt),
                       r=V["otw"].tile(kt), st=(kt == 0), sp=(kt == 7):
                       e.matmul(o, l, r, start=st, stop=sp),
                       reads=[wos.slot_regs[g % wos.nring], V["otw"].R(kt)],
                       writes=[K.ps.R(bank)], tag=tag + "woA")
            rd = [K.ps.R(bank), Y.R(mi)]
            if w + 1 < NW:
                rd.append(V["yw"].R(nb + mi))
            wr_ = [YLOC[mi], Y.R(mi)] if w == NW - 1 else [YLOC[mi]]
            p.step("dve", lambda e, mi=mi, s=K.ps.ap(bank, 0, 128, 0, 256),
                   d=Y.tile(mi, t0, t0 + 256):
                   e.tensor_tensor(d, d, s, op=AluOpType.add),
                   reads=rd, writes=wr_, tag=tag + "ya")


def _l1f_ffn(K, Y, T, W, li, keeps, tag=""):
    p = K.p
    DI = 2730
    TCW = min(1024, T)
    NCH = ceil_div(T, 512)
    Va = K.arena.carve(keeps, [
        ("ybf", 8, T + 1, BF16), ("xn1", 8, T, BF16),
        ("sq8", 8, TCW, BF16), ("bcl", 2, TCW, F32), ("lxf", 2, TCW, F32)])
    YB, XN1 = Va["ybf"], Va["xn1"]
    for kt in range(8):
        p.step("act", lambda e, kt=kt, s=Y.tile(kt), d=YB.tile(kt, 1, T + 1):
               e.copy(d, s), reads=[Y.R(kt)], writes=YB.wr(kt), tag=tag + "yb")
        p.step("dve", lambda e, kt=kt, d=YB.tile(kt, 0, 1): e.memset(d, 0.0),
               reads=[], writes=[YB.R(kt)], tag=tag + "yb")
    lgc = load_cols(K, bass.AP(W["lng"][0].tensor, li * 1024, [[1, 1024]]),
                    W["lng"][1], 16, 8, tag + "lg")
    lbc = load_cols(K, bass.AP(W["lnb"][0].tensor, li * 1024, [[1, 1024]]),
                    W["lnb"][1], 24, 8, tag + "lb")
    for tc in range(T // TCW):
        ln_shift_emit(K, YB, XN1, tc * TCW, (tc + 1) * TCW, lgc, lbc,
                      Va["sq8"], Va["bcl"], Va["lxf"], tag=tag + "ln")

    Vb = K.arena.carve(keeps + [XN1], [
        ("wring", 4, 2048, BF16), ("wring2", 4, 2048, BF16),
        ("gtq", 6 * NCH, 512, BF16), ("hbq", 6 * NCH, 512, BF16),
        ("fb1", 1, 16, F32), ("fst", 2, 512, BF16)])
    GTQ, HBQ = Vb["gtq"], Vb["hbq"]
    FB = Vb["fb1"]
    b2c = load_cols(K, bass.AP(W["b2"][0].tensor, li * 1024, [[1, 1024]]),
                    W["b2"][1], 0, 8, tag + "b2")
    for kt in range(8):
        p.step("dve", lambda e, kt=kt, d=Y.tile(kt), b=b2c[kt]:
               e.tensor_scalar(d, d, b, None, op0=AluOpType.add),
               reads=[Y.R(kt), K.cols_reg], writes=[Y.R(kt)], tag=tag + "b2a")
    xsrc = src_from_view(XN1, T)
    qsz = [768, 768, 768, 426]
    qoff = [0, 768, 1536, 2304]
    for fq in range(4):
        f0, fn = qoff[fq], qsz[fq]
        nqt = ceil_div(fn, 128)
        for mi in range(nqt):
            msz = min(128, fn - mi * 128)
            p.dma("sp", FB.tile(0, mi, mi + 1, 0, msz),
                  bass.AP(W["b1"][0].tensor, li * 5460 + DI + f0 + mi * 128,
                          [[1, msz], [1, 1]]),
                  reads=[W["b1"][1]], writes=FB.wr(0), tag=tag + "b1g")
            p.dma("sp", FB.tile(0, 8 + mi, 9 + mi, 0, msz),
                  bass.AP(W["b1"][0].tensor, li * 5460 + f0 + mi * 128,
                          [[1, msz], [1, 1]]),
                  reads=[W["b1"][1]], writes=[FB.R(0)], tag=tag + "b1a")
        wsg = WStream(K, tag + f"wg{fq}", slice3d(W["W1"][0], li, 1024, 5460),
                      W["W1"][1], 1024, fn, False, Vb["wring"], mg=2,
                      col0=DI + f0)

        def ep_g(mi, msz, bank, a, b):
            ci = a // 512
            K.p.step("act", lambda e, s=K.ps.ap(bank, 0, msz, 0, b - a),
                     d=GTQ.tile(mi * NCH + ci, 0, b - a, 0, msz),
                     bc=FB.tile(0, mi, mi + 1, 0, msz):
                     e.activation(d, s, AF.Gelu, bias=bc),
                     reads=[K.ps.R(bank), FB.R(0)],
                     writes=GTQ.wr(mi * NCH + ci), tag=tag + "eg")
        linear(K, wsg, xsrc, ep_g, 0, T, tag=tag + f"g{fq}")
        wsa = WStream(K, tag + f"wa{fq}", slice3d(W["W1"][0], li, 1024, 5460),
                      W["W1"][1], 1024, fn, False, Vb["wring2"], mg=2, col0=f0)

        def ep_a(mi, msz, bank, a, b):
            ci = a // 512
            r = (mi + ci) % 2
            K.p.step("dve", lambda e, s=K.ps.ap(bank, 0, msz, 0, b - a),
                     d=Vb["fst"].tile(r, 0, b - a, 0, msz),
                     bc=FB.tile(0, 8 + mi, 9 + mi, 0, msz):
                     e.tensor_scalar(d, s, bc, None, op0=AluOpType.add),
                     reads=[K.ps.R(bank), FB.R(0)], writes=Vb["fst"].wr(r),
                     tag=tag + "ea")
            K.p.step("dve", lambda e, s=Vb["fst"].tile(r, 0, b - a, 0, msz),
                     g=GTQ.tile(mi * NCH + ci, 0, b - a, 0, msz),
                     d=HBQ.tile(mi * NCH + ci, 0, b - a, 0, msz):
                     e.tensor_tensor(d, s, g, op=AluOpType.mult),
                     reads=[Vb["fst"].R(r), GTQ.R(mi * NCH + ci)],
                     writes=HBQ.wr(mi * NCH + ci), tag=tag + "ea")
        linear(K, wsa, xsrc, ep_a, 0, T, tag=tag + f"a{fq}")

        def hsrc(kt, ksz, a, b):
            ti = kt * NCH + a // 512
            return HBQ.tile(ti, 0, b - a, 0, ksz), [HBQ.R(ti)]
        ws2 = WStream(K, tag + f"w2{fq}", slice3d(W["W2"][0], li, 2730, 1024),
                      W["W2"][1], fn, 1024, False, Vb["wring"], mg=2, k0=f0)

        def ep_y(mi, msz, bank, a, b):
            K.p.step("dve", lambda e, mi=mi, s=K.ps.ap(bank, 0, msz, 0, b - a),
                     d=Y.tile(mi, a, b):
                     e.tensor_tensor(d, d, s, op=AluOpType.add),
                     reads=[K.ps.R(bank), Y.R(mi)], writes=[Y.R(mi)],
                     tag=tag + "ey")
        linear(K, ws2, hsrc, ep_y, 0, T, tag=tag + f"y{fq}")



def emit_batch(K, x_in, out, T, WD, SCR, tag="b"):
    p = K.p
    tp = getattr(K, "tapd", None) if tag == "b0" else None
    V0 = K.arena.carve([], [("xt", 8, T, F32), ("xnr", 2, 1024, F32)])
    XT = V0["xt"]
    load_xt(K, x_in[0], x_in[1], XT, V0["xnr"], T, tag=tag + "ld")

    for lay in range(2):
        lt = f"{tag}L{lay}"
        Vl = K.arena.carve([XT], [("xb", 8, T, BF16), ("bc", 2, T, F32),
                                  ("xnt", 8, T, BF16)])
        XNT = Vl["xnt"]
        ln_emit(K, XT, Vl["xb"], Vl["bc"], XNT, T, tag=lt + "ln")
        if tp:
            for kt in range(8):
                r0 = 2 * lay * 1024 + kt * 128
                p.dma("sp", tp["b"][0][r0:r0 + 128, 0:T], XNT.tile(kt),
                      reads=[XNT.R(kt)], writes=[tp["b"][1]], tag=lt + "tpx")
        for kt in range(8):
            p.dma("sp", SCR["xt"][0][kt * 128:(kt + 1) * 128, 0:T], XT.tile(kt),
                  reads=[XT.R(kt)], writes=[SCR["xt"][1]], tag=lt + "xtsp")
        if lay == 0:
            stage_l0_attn(K, XNT, T, WD["l0a"], SCR["ao"], SCR["xnt"],
                          tag=lt + "A")
        else:
            stage_l1_attn(K, XNT, T, WD["l1a"], SCR["ao"], SCR["xnt"],
                          tag=lt + "A")
        Vf = K.arena.carve([], [("y", 8, T, F32), ("xntr", 8, T, BF16)])
        Y, XNTR = Vf["y"], Vf["xntr"]
        for kt in range(8):
            p.dma("sp", XNTR.tile(kt), SCR["xnt"][0][kt * 128:(kt + 1) * 128, 0:T],
                  reads=[SCR["xnt"][1]], writes=XNTR.wr(kt), tag=lt + "xnr")
        if lay == 0:
            stage_l0_ffn(K, XNTR, Y, T, WD["l0f"], tag=lt + "F")
        else:
            l1cb = None
            if tp:
                tslot = {(0, "a"): 5, (0, "b"): 6, (1, "b"): 7,
                         (3, "b"): 8, (5, "b"): 9}

                def l1cb(li, ph):
                    sl = tslot.get((li, ph))
                    if sl is None:
                        return
                    for kt in range(8):
                        r0 = sl * 1024 + kt * 128
                        p.dma("sp", tp["f"][0][r0:r0 + 128, 0:T], Y.tile(kt),
                              reads=[Y.R(kt)], writes=[tp["f"][1]],
                              tag=lt + f"tpy{sl}")
            stage_l1_ffn(K, XNTR, Y, T, WD["l1f"], tag=lt + "F", tap=l1cb)
        Vc = K.arena.carve([Y], [("xt", 8, T, F32), ("aor", 8, T, BF16)])
        XT = Vc["xt"]
        AOR = Vc["aor"]
        for kt in range(8):
            p.dma("sp", XT.tile(kt), SCR["xt"][0][kt * 128:(kt + 1) * 128, 0:T],
                  reads=[SCR["xt"][1]], writes=XT.wr(kt), tag=lt + "xtr")
            p.dma("sp", AOR.tile(kt), SCR["ao"][0][kt * 128:(kt + 1) * 128, 0:T],
                  reads=[SCR["ao"][1]], writes=AOR.wr(kt), tag=lt + "aor")
        if tp:
            for kt in range(8):
                r0 = (2 * lay + 1) * 1024 + kt * 128
                p.dma("sp", tp["b"][0][r0:r0 + 128, 0:T], AOR.tile(kt),
                      reads=[AOR.R(kt)], writes=[tp["b"][1]], tag=lt + "tpa")
                r1 = 2 * lay * 1024 + kt * 128
                p.dma("sp", tp["f"][0][r1:r1 + 128, 0:T], Y.tile(kt),
                      reads=[Y.R(kt)], writes=[tp["f"][1]], tag=lt + "tpy")
        for kt in range(8):
            p.step("dve", lambda e, kt=kt, d=XT.tile(kt), s=Y.tile(kt):
                   e.tensor_tensor(d, d, s, op=AluOpType.add),
                   reads=[Y.R(kt), XT.R(kt)], writes=[XT.R(kt)], tag=lt + "cb1")
            p.step("dve", lambda e, kt=kt, d=XT.tile(kt), s=AOR.tile(kt):
                   e.tensor_tensor(d, d, s, op=AluOpType.add),
                   reads=[AOR.R(kt), XT.R(kt)], writes=[XT.R(kt)], tag=lt + "cb2")
        if tp and lay == 0:
            for kt in range(8):
                r0 = 4096 + kt * 128
                p.dma("sp", tp["f"][0][r0:r0 + 128, 0:T], XT.tile(kt),
                      reads=[XT.R(kt)], writes=[tp["f"][1]], tag=lt + "tpx1")

    Vs = K.arena.carve([XT], [("xnr", 2, 1024, F32)])
    store_xt(K, XT, out[0], out[1], Vs["xnr"], T, tag=tag + "st")


def declare_weights(nc, T, b):
    W = {}

    def D(name, shape):
        return (nc.dram_tensor(name, shape, F32, kind="ExternalInput").ap(),
                Region("d_" + name))
    w = {}
    w["L0_attn_Wqkv"] = D("L0_attn_Wqkv", [1024, 3072])
    w["L0_attn_Wproj"] = D("L0_attn_Wproj", [1024, 1024])
    for n in ["lambda_q1", "lambda_k1", "lambda_q2", "lambda_k2"]:
        w["L0_attn_" + n] = D("L0_attn_" + n, [32])
    w["L0_attn_rms_scale"] = D("L0_attn_rms_scale", [64])
    for n, sh in [("g1", [1024]), ("g2", [1024]), ("Wq", [1024, 1024]),
                  ("Wkc", [256, 1024]), ("Wvc", [256, 1024]),
                  ("Wo", [1024, 1024]), ("Wr", [8, 1024]),
                  ("expert_bias", [8]), ("sW1", [1024, 1024]),
                  ("sW2", [1024, 1024]), ("sW3", [1024, 1024]),
                  ("eW1", [8, 1024, 1024]), ("eW2", [8, 1024, 1024]),
                  ("eW3", [8, 1024, 1024])]:
        w["L0_ffn_" + n] = D("L0_ffn_" + n, sh)
    for n, sh in [("w_qkv", [3072, 1024]), ("b_qkv", [3072]),
                  ("w_out", [1024, 1024]), ("b_out", [1024])]:
        w["L1_attn_" + n] = D("L1_attn_" + n, sh)
    for n, sh in [("Wqkv", [6, 1024, 3072]), ("q_scale", [6, 64]),
                  ("k_scale", [6, 64]), ("Wo", [6, 1024, 1024]),
                  ("ln_g", [6, 1024]), ("ln_b", [6, 1024]),
                  ("W1", [6, 1024, 5460]), ("b1", [6, 5460]),
                  ("W2", [6, 2730, 1024]), ("b2", [6, 1024])]:
        w["L1_ffn_" + n] = D("L1_ffn_" + n, sh)
    def DB(name, shape, dt):
        return (nc.dram_tensor(name, shape, dt, kind="ExternalInput").ap(),
                Region("d_" + name))
    w["t_ropec"] = DB("t_ropec", [128, T], F32)
    w["t_ropes"] = DB("t_ropes", [128, T], F32)
    w["t_cmask"] = DB("t_cmask", [512, 512], BF16)
    for n in ["xposcs", "xposss", "xposck", "xpossk"]:
        w["t_" + n] = DB("t_" + n, [128, T], BF16)
    w["t_lmask"] = DB("t_lmask", [256, 512], F32)
    w["t_lmask0"] = DB("t_lmask0", [256, 512], F32)

    WD = {
        "l0a": {"wqkv": w["L0_attn_Wqkv"], "wproj": w["L0_attn_Wproj"]},
        "l0f": {"Wq": w["L0_ffn_Wq"], "Wkc": w["L0_ffn_Wkc"],
                "Wvc": w["L0_ffn_Wvc"], "Wo": w["L0_ffn_Wo"],
                "Wr": w["L0_ffn_Wr"], "eb": w["L0_ffn_expert_bias"],
                "g1": w["L0_ffn_g1"], "g2": w["L0_ffn_g2"],
                "sW1": w["L0_ffn_sW1"], "sW2": w["L0_ffn_sW2"],
                "sW3": w["L0_ffn_sW3"],
                "eW1": ([slice3d(w["L0_ffn_eW1"][0], e, 1024, 1024)
                         for e in range(8)], w["L0_ffn_eW1"][1]),
                "eW2": ([slice3d(w["L0_ffn_eW2"][0], e, 1024, 1024)
                         for e in range(8)], w["L0_ffn_eW2"][1]),
                "eW3": ([slice3d(w["L0_ffn_eW3"][0], e, 1024, 1024)
                         for e in range(8)], w["L0_ffn_eW3"][1])},
        "l1a": {"wqkv": w["L1_attn_w_qkv"], "bqkv": w["L1_attn_b_qkv"],
                "wout": w["L1_attn_w_out"], "bout": w["L1_attn_b_out"],
                "ropec": w["t_ropec"], "ropes": w["t_ropes"],
                "cmask": w["t_cmask"]},
        "l1f": {"Wqkv": w["L1_ffn_Wqkv"], "qs": w["L1_ffn_q_scale"],
                "ks": w["L1_ffn_k_scale"], "Wo": w["L1_ffn_Wo"],
                "lng": w["L1_ffn_ln_g"], "lnb": w["L1_ffn_ln_b"],
                "W1": w["L1_ffn_W1"], "b1": w["L1_ffn_b1"],
                "W2": w["L1_ffn_W2"], "b2": w["L1_ffn_b2"],
                "xposcs": w["t_xposcs"], "xposss": w["t_xposss"],
                "xposck": w["t_xposck"], "xpossk": w["t_xpossk"],
                "lmask": w["t_lmask"], "lmask0": w["t_lmask0"]},
        "raw": w,
    }
    return WD


def build_core_program(T, NB, tap=False):
    import contextlib
    nc = bass.Bass("TRN2", target_bir_lowering=False)
    p = Prog(nc)
    cfg = {"T": T}
    x_d = nc.dram_tensor("x", [NB, T, 1024], F32, kind="ExternalInput").ap()
    o_d = nc.dram_tensor("out", [NB, T, 1024], F32, kind="ExternalOutput").ap()
    r_x = Region("d_x")
    r_o = Region("d_out")
    WD = declare_weights(nc, T, 0)
    K = make_ctx(nc, p, cfg)
    K.arena = Arena(nc, "arena", (nc.sbuf_bytes_remaining // 2) - 1024)
    K.arena.memset_all(p)
    SCR = {}
    for n, dt in [("xt", F32), ("ao", BF16), ("xnt", BF16)]:
        t = nc.dram_tensor("scr_" + n, [1024, T], dt, kind="Internal").ap()
        SCR[n] = (t, Region("scr_" + n))
    if tap:
        tb = nc.dram_tensor("o_tapb", [4096, T], BF16, kind="ExternalOutput").ap()
        tf = nc.dram_tensor("o_tapf", [10240, T], F32, kind="ExternalOutput").ap()
        K.tapd = {"b": (tb, Region("d_tapb")), "f": (tf, Region("d_tapf"))}
    raw = WD["raw"]
    precompute_l0a_scalars(
        K, [raw["L0_attn_" + n][0].unsqueeze(1) for n in
            ["lambda_q1", "lambda_k1", "lambda_q2", "lambda_k2"]],
        [raw["L0_attn_" + n][1] for n in
         ["lambda_q1", "lambda_k1", "lambda_q2", "lambda_k2"]],
        raw["L0_attn_rms_scale"][0].unsqueeze(1), raw["L0_attn_rms_scale"][1])
    for b in range(NB):
        xb_ap = bass.AP(x_d.tensor, b * T * 1024, [[1024, T], [1, 1024]])
        ob_ap = bass.AP(o_d.tensor, b * T * 1024, [[1024, T], [1, 1024]])
        emit_batch(K, (xb_ap, r_x), (ob_ap, r_o), T, WD, SCR, tag=f"b{b}")
    fin = [r_o]
    if tap:
        fin += [K.tapd["b"][1], K.tapd["f"][1]]
    p.finish(fin)
    es = contextlib.ExitStack()
    sems = make_sems(nc, es)
    with nc.Block() as block:
        p.realize(block, sems)
    es.close()
    return nc, p


def input_map_for_core(xs, ins, T):
    ca = const_arrays(T)
    m = {"x": np.ascontiguousarray(xs)}
    for k in ins:
        if k != "L0_attn_x":
            m[k] = np.ascontiguousarray(np.asarray(ins[k], np.float32))
    m["t_ropec"] = ca["c_ropec"]
    m["t_ropes"] = ca["c_ropes"]
    m["t_cmask"] = np.asarray(ca["c_cmask"]).reshape(512, 512)
    for n in ["xposcs", "xposss", "xposck", "xpossk"]:
        m["t_" + n] = np.asarray(ca["c_" + n])
    m["t_lmask"] = np.asarray(ca["c_lmask"]).reshape(256, 512)
    m["t_lmask0"] = np.asarray(ca["c_lmask0"]).reshape(256, 512)
    for (name, shape, dt) in CONST_SPECS:
        m["c_" + name] = ca["c_" + name]
    return m


def kernel(**inputs):
    T = 2048
    NB = 2
    x = np.asarray(inputs["L0_attn_x"], np.float32)
    nc, _ = build_core_program(T, NB)
    in_maps = []
    for c in range(8):
        xs = x[c * NB:(c + 1) * NB]
        in_maps.append(input_map_for_core(xs, inputs, T))
    res = run_bass_kernel_spmd(nc, in_maps, core_ids=list(range(8)))
    return np.concatenate([r["out"] for r in res.results], axis=0)
